# version 05
# baseline (speedup 1.0000x reference)
import numpy as np
import ml_dtypes
from contextlib import ExitStack

import concourse.bass as bass
import concourse.mybir as mybir
from concourse.bass_utils import run_bass_kernel_spmd

F32 = mybir.dt.float32
BF16 = mybir.dt.bfloat16
AF = mybir.ActivationFunctionType
OP = mybir.AluOpType
AX = mybir.AxisListType
NPBF = ml_dtypes.bfloat16

NCORES = 8
B = 2
T = 2048
D = 1024
KT = D // 128
PAN = 512
NPAN = T // PAN
NH, DK, DV = 4, 256, 256
CH = 128
NCH = T // CH
E0, F0 = 8, 4096
H1, DH1 = 16, 64
F1, E1 = 4096, 8


class _Op:
    __slots__ = ("eng", "fn", "waits", "inc", "amount", "val", "kind", "semkey")
    def __init__(self, eng, fn, kind):
        self.eng = eng; self.fn = fn; self.waits = []
        self.inc = False; self.amount = 1; self.val = None; self.kind = kind
        self.semkey = None

class Buf:
    __slots__ = ("name", "writer", "readers")
    def __init__(self, name):
        self.name = name; self.writer = None; self.readers = []

class TB:
    __slots__ = ("t", "b")
    def __init__(self, t, name):
        self.t = t; self.b = Buf(name)
    def __getitem__(self, idx):
        return self.t[idx]
    @property
    def ap(self):
        return self.t[:]

class View:
    __slots__ = ("_ap", "b")
    def __init__(self, ap, buf):
        self._ap = ap; self.b = buf
    @property
    def ap(self):
        return self._ap


def rr2(tb, pat, **kw):
    return View(tb.t[:, :].rearrange(pat, **kw), tb.b)

def _buf(x):
    return getattr(x, "b", x)

class Prog:
    ENGS = ("tensor", "vector", "scalar", "gpsimd", "sync")
    NDS = 4

    def __init__(self, nc):
        self.nc = nc
        self.ops = {e: [] for e in self.ENGS}
        self.dma_n = {e: 0 for e in self.ENGS}
        self.dma_sem_tot = {}
        self.pbank = {}
        self.pbufs = {}

    def _emit(self, eng, fn, r=(), w=(), kind="eng", amount=1):
        op = _Op(eng, fn, kind)
        op.amount = amount
        assert (kind == "dma") == (eng in ("sync", "gpsimd")), (kind, eng)
        if kind == "dma":
            idx = self.dma_n[eng]
            self.dma_n[eng] = idx + 1
            op.semkey = (eng, idx % self.NDS)
            op.val = 16 * (idx // self.NDS + 1)
            op.inc = True
            if op.val > 16:
                op.waits.append(("dma", op.semkey, op.val - 16))
            self.dma_sem_tot[op.semkey] = op.val
        deps = []
        for x in r:
            b = _buf(x)
            if b.name in self.pbufs:
                for kk in self.pbufs[b.name]:
                    if kk in self.pbank:
                        self.pbank[kk]["stale"] = True
            if b.writer is not None:
                deps.append(b.writer)
        for x in w:
            b = _buf(x)
            if b.writer is not None:
                deps.append(b.writer)
            deps.extend(b.readers)
        seen = set()
        dma_wait = {}
        for d in deps:
            if d is op or id(d) in seen:
                continue
            seen.add(id(d))
            if d.kind == "dma":
                k = d.semkey
                dma_wait[k] = max(dma_wait.get(k, 0), d.val)
                continue
            if d.kind == "eng" and d.eng == eng == "tensor":
                continue
            d.inc = True
            op.waits.append(d)
        for k, v in dma_wait.items():
            if op.kind == "dma" and k == op.semkey:
                op.waits = [wt for wt in op.waits
                            if not (isinstance(wt, tuple) and wt[1] == k and wt[2] < v)]
                if not any(isinstance(wt, tuple) and wt[1] == k and wt[2] >= v
                           for wt in op.waits):
                    op.waits.append(("dma", k, v))
            else:
                op.waits.append(("dma", k, v))
        self.ops[eng].append(op)
        for x in r:
            _buf(x).readers.append(op)
        for x in w:
            b = _buf(x)
            b.writer = op
            b.readers = []
        return op

    def t(self, fn, r=(), w=()):
        return self._emit("tensor", fn, r, w)
    def v(self, fn, r=(), w=()):
        return self._emit("vector", fn, r, w)
    def s(self, fn, r=(), w=()):
        return self._emit("scalar", fn, r, w)
    def g(self, fn, r=(), w=()):
        return self._emit("gpsimd", fn, r, w)
    def load(self, dst, src_ap, r=(), w=(), slow=False, q="sync"):
        def fn(e, dst=dst, src_ap=src_ap, slow=slow):
            return e.dma_start(out=(dst.ap if hasattr(dst, "ap") else dst),
                               in_=src_ap, allow_slow_non_contiguous=slow)
        return self._emit(q, fn, tuple(r), (dst,) + tuple(w), kind="dma", amount=16)
    def store(self, dst_ap, src_ap, dstbuf, r=(), slow=False, q="gpsimd"):
        def fn(e, dst_ap=dst_ap, src_ap=src_ap, slow=slow):
            return e.dma_start(out=dst_ap, in_=src_ap,
                               allow_slow_non_contiguous=slow)
        return self._emit(q, fn, tuple(r), (dstbuf,), kind="dma", amount=16)

    def barrier(self):
        self.close_banks()
        lasts = [("dma", k, v) for k, v in self.dma_sem_tot.items()]
        for e in self.ENGS:
            for op in reversed(self.ops[e]):
                if op.kind == "eng" and op.fn is not None:
                    op.inc = True
                    lasts.append(op)
                    break
        for e in self.ENGS:
            bop = _Op(e, None, "eng")
            bop.waits = [d for d in lasts
                         if not (not isinstance(d, tuple) and d.eng == e)]
            self.ops[e].append(bop)

    def mm_flags(self, out_ap, acc, bufname):
        off_b = out_ap.offset * mybir.dt.size(out_ap.tensor.dtype)
        key = (out_ap.tensor.name, off_b // 2048)
        flags = {"start": False, "stop": False}
        ent = self.pbank.get(key)
        if acc:
            if ent is None:
                flags["start"] = True
                self.pbank[key] = {"flags": flags, "stale": False}
                self.pbufs.setdefault(bufname, set()).add(key)
            else:
                ent["flags"] = flags
        else:
            if ent is None or ent["stale"]:
                if ent is not None:
                    ent["flags"]["stop"] = True
                flags["start"] = True
                self.pbank[key] = {"flags": flags, "stale": False}
            else:
                ent["flags"] = flags
        self.pbufs.setdefault(bufname, set()).add(key)
        return flags

    def close_banks(self):
        for ent in self.pbank.values():
            ent["flags"]["stop"] = True
        self.pbank.clear()
        self.pbufs.clear()

    def build(self, stack):
        nc = self.nc
        self.close_banks()
        self.barrier()
        sems = {}
        for e in ("tensor", "vector", "scalar"):
            c = 0
            for op in self.ops[e]:
                if op.inc:
                    c += op.amount
                    op.val = c
            sems[e] = stack.enter_context(nc.semaphore(f"sem_{e}"))
        dsems = {}
        for k in self.dma_sem_tot:
            dsems[k] = stack.enter_context(nc.semaphore(f"dsem_{k[0]}_{k[1]}"))
        block = stack.enter_context(nc.Block())
        def mk(e):
            def emitf(engq):
                for op in self.ops[e]:
                    for d in op.waits:
                        if isinstance(d, tuple):
                            engq.wait_ge(dsems[d[1]], d[2])
                        else:
                            engq.wait_ge(sems[d.eng], d.val)
                    if op.fn is not None:
                        instr = op.fn(engq)
                        if op.inc:
                            if op.kind == "dma":
                                instr.then_inc(dsems[op.semkey], 16)
                            else:
                                instr.then_inc(sems[e], op.amount)
            return emitf
        block.tensor(mk("tensor"))
        block.vector(mk("vector"))
        block.scalar(mk("scalar"))
        block.gpsimd(mk("gpsimd"))
        block.sync(mk("sync"))


def host_consts():
    i = np.arange(128)
    tril_s = (i[:, None] > i[None, :]).astype(np.float32)
    triu_s = (i[:, None] < i[None, :]).astype(np.float32)
    triu_i = (i[:, None] <= i[None, :]).astype(np.float32)
    c = {}
    c["eye_f"] = np.eye(128, dtype=np.float32)
    c["eye_b"] = np.eye(128, dtype=np.float32).astype(NPBF)
    c["ones_col_f"] = np.ones((128, 1), dtype=np.float32)
    c["ones_col_b"] = np.ones((128, 1), dtype=np.float32).astype(NPBF)
    c["ones_row_f"] = np.ones((1, 128), dtype=np.float32)
    c["mask_sl4"] = np.tile(tril_s, (1, 4)).copy()
    c["mask_su4"] = np.tile(triu_s, (1, 4)).copy()
    c["mask_ui4"] = np.tile(triu_i, (1, 4)).copy()
    sr16 = np.zeros((16, 16 * 128), dtype=np.float32)
    for r in range(16):
        sr16[r, r * 128:(r + 1) * 128] = 1.0
    c["sel_row16"] = sr16
    sr8 = np.zeros((8, 8 * 128), dtype=np.float32)
    for r in range(8):
        sr8[r, r * 128:(r + 1) * 128] = 1.0
    c["sel_row8"] = sr8
    sr4 = np.zeros((4, 4 * 128), dtype=np.float32)
    for r in range(4):
        sr4[r, r * 128:(r + 1) * 128] = 1.0
    c["sel_row4"] = sr4
    sc = np.zeros((128, 16 * 12), dtype=np.float32)
    for br in range(4):
        sc[:, (3 * br + 0) * 16 + 4 * br + 0] = 1.0
        sc[:, (3 * br + 1) * 16 + 4 * br + 1] = 1.0
        sc[:, (3 * br + 1) * 16 + 4 * br + 3] = 1.0
        sc[:, (3 * br + 2) * 16 + 4 * br + 2] = 1.0
    c["selc_st"] = sc
    s4 = np.zeros((128, 16), dtype=np.float32)
    for h in range(4):
        s4[:, 4 * h + h] = 1.0
    c["selc4"] = s4
    c["selc_st_b"] = sc.astype(NPBF)
    st_scale = np.full((16, 1), 1.0 / 256.0, np.float32)
    for br in range(4):
        st_scale[4 * br + 3, 0] = 1.0
    c["st_scale"] = st_scale
    m3b = np.zeros((16, 512), np.float32)
    for br in range(4):
        m3b[4 * br + 3, :] = 1.0
    c["m3big"] = m3b
    r416 = np.zeros((4, 16), np.float32)
    for h in range(4):
        r416[h, 4 * h:4 * h + 4] = 1.0
    c["rep4_16"] = r416
    fv = np.array([1e-5, 1e-6, -1.0, 0.125, 1.0 / 256.0], np.float32)
    c["fvals"] = np.tile(fv[None, :], (128, 1))
    sh = np.zeros((16, 16), dtype=np.float32)
    for br in range(4):
        sh[4 * br + 0, 4 * br + 1] = 1.0
    c["shift16"] = sh
    m3 = np.zeros((16, 1), dtype=np.float32)
    for br in range(4):
        m3[4 * br + 3, 0] = 1.0
    c["m3row"] = m3
    return c

CONST_SPECS = {k: v for k, v in host_consts().items()}


class Ctx:
    def __init__(self, nc, stack, P, mode, carried):
        self.nc = nc; self.stack = stack; self.stack0 = stack; self.P = P
        self.mode = mode
        self.carried = carried
        self.dram = {}
        self.outputs = set()
        self.sb_cache = {}

    def din(self, name, shape, dtype=F32):
        if name not in self.dram:
            t = self.nc.dram_tensor(name, list(shape), dtype, kind="ExternalInput")
            self.dram[name] = TB(t.ap(), name)
        return self.dram[name]

    def dout(self, name, shape, dtype=F32):
        if name not in self.dram:
            t = self.nc.dram_tensor(name, list(shape), dtype, kind="ExternalOutput")
            self.dram[name] = TB(t.ap(), name)
            self.outputs.add(name)
        return self.dram[name]

    def dscratch(self, name, shape, dtype=F32):
        if name not in self.dram:
            kind = "Internal" if self.mode == "fused" else None
            if kind:
                t = self.nc.dram_tensor(name, list(shape), dtype, kind=kind)
                self.dram[name] = TB(t.ap(), name)
        return self.dram.get(name)

    def sb(self, name, p, f, dtype=F32):
        t = self.stack.enter_context(self.nc.sbuf_tensor(name, [p, f], dtype))
        return TB(t, name)

    def ps(self, name, p, f, dtype=F32):
        t = self.stack.enter_context(self.nc.psum_tensor(name, [p, f], dtype))
        return TB(t, name)

    def const(self, name):
        if name in self.sb_cache:
            return self.sb_cache[name]
        arr = CONST_SPECS[name]
        dt = BF16 if arr.dtype == np.dtype(NPBF) else F32
        d = self.din("c_" + name, arr.shape, dt)
        t = self.stack0.enter_context(self.nc.sbuf_tensor("s_" + name,
                                                          list(arr.shape), dt))
        s = TB(t, "s_" + name)
        self.P.load(s, d.ap, r=[d])
        self.sb_cache[name] = s
        return s


def mm(P, out_ap, lhsT_ap, rhs_ap, r, w, start=True, stop=True):
    bufname = _buf(w[0]).name
    flags = P.mm_flags(out_ap, acc=(not start), bufname=bufname)
    return P.t(lambda e: e.matmul(out_ap, lhsT_ap, rhs_ap,
                                  start=flags["start"], stop=flags["stop"]),
               r=r, w=w)

def transpose(P, out_ap, in_ap, eye_tb, r, w, kdim=128):
    bufname = _buf(w[0]).name
    flags = P.mm_flags(out_ap, acc=False, bufname=bufname)
    return P.t(lambda e: e.matmul(out_ap, in_ap, eye_tb[0:kdim, 0:kdim],
                                  start=flags["start"], stop=flags["stop"],
                                  is_transpose=True),
               r=list(r) + [eye_tb], w=w)

def tt(P, out_ap, a_ap, b_ap, op, r, w, eng="v"):
    f = lambda e: e.tensor_tensor(out_ap, a_ap, b_ap, op)
    return getattr(P, eng)(f, r=r, w=w)

def ts(P, out_ap, in_ap, s1, op0, r, w, s2=None, op1=None, eng="v"):
    def f(e):
        if s2 is None and op1 is None:
            return e.tensor_scalar(out_ap, in_ap, s1, None, op0)
        return e.tensor_scalar(out_ap, in_ap, s1, s2, op0, op1)
    return getattr(P, eng)(f, r=r, w=w)

def stt(P, out_ap, in0_ap, scalar, in1_ap, op0, op1, r, w):
    return P.v(lambda e: e.scalar_tensor_tensor(out_ap, in0_ap, scalar, in1_ap, op0, op1),
               r=r, w=w)

def act(P, out_ap, in_ap, func, r, w, bias=None, scale=None):
    r = list(r)
    if isinstance(bias, float) or isinstance(scale, float):
        fc = getattr(P, "fconst", None)
        if fc is not None:
            r.append(fc)
    def f(e):
        kw = {}
        if bias is not None:
            kw["bias"] = bias
        if scale is not None:
            kw["scale"] = scale
        return e.activation(out_ap, in_ap, func, **kw)
    return P.s(f, r=r, w=w)

def vcopy(P, out_ap, in_ap, r, w, eng="v"):
    if eng == "s":
        f = lambda e: e.copy(out_ap, in_ap)
    else:
        f = lambda e: e.tensor_copy(out_ap, in_ap)
    return getattr(P, eng)(f, r=r, w=w)

def memset(P, ap, val, w, eng="v"):
    return getattr(P, eng)(lambda e: e.memset(ap, val), r=(), w=w)

def reduce(P, out_ap, in_ap, op, r, w):
    return P.v(lambda e: e.tensor_reduce(out_ap, in_ap, AX.X, op), r=r, w=w)

def recip(P, out_ap, in_ap, r, w):
    return P.v(lambda e: e.reciprocal(out_ap, in_ap), r=r, w=w)

def emit_ln_panel(P, ctx, xs, dst, tmp, pr_sum, pr_sq, rowbuf, pr_bc):
    oc = ctx.const("ones_col_f")
    for k in range(KT):
        mm(P, pr_sum[0:1, :], oc[:, 0:1], xs[k].ap, r=[xs[k], oc], w=[pr_sum],
           start=(k == 0), stop=(k == KT - 1))
    for k in range(KT):
        act(P, tmp.ap, xs[k].ap, AF.Square, r=[xs[k]], w=[tmp])
        mm(P, pr_sq[0:1, :], oc[:, 0:1], tmp.ap, r=[tmp, oc], w=[pr_sq],
           start=(k == 0), stop=(k == KT - 1))
    mean = rowbuf[0:1, 0:PAN]; m2 = rowbuf[0:1, PAN:2 * PAN]
    var = rowbuf[0:1, 2 * PAN:3 * PAN]; rstd = rowbuf[0:1, 3 * PAN:4 * PAN]
    ts(P, mean, pr_sum[0:1, :], 1.0 / D, OP.mult, r=[pr_sum], w=[rowbuf])
    act(P, m2, mean, AF.Square, r=[rowbuf], w=[rowbuf])
    ts(P, var, pr_sq[0:1, :], 1.0 / D, OP.mult, r=[pr_sq, rowbuf], w=[rowbuf])
    tt(P, var, var, m2, OP.subtract, r=[rowbuf], w=[rowbuf])
    act(P, rstd, var, AF.Sqrt, r=[rowbuf], w=[rowbuf], bias=1e-5)
    recip(P, rstd, rstd, r=[rowbuf], w=[rowbuf])
    orow = ctx.const("ones_row_f")
    mm(P, pr_bc[:, 0:PAN], orow[0:1, :], mean, r=[rowbuf, orow], w=[pr_bc])
    mm(P, pr_bc[:, PAN:2 * PAN], orow[0:1, :], rstd, r=[rowbuf, orow], w=[pr_bc])
    for k in range(KT):
        tt(P, tmp.ap, xs[k].ap, pr_bc[:, 0:PAN], OP.subtract, r=[xs[k], pr_bc], w=[tmp])
        tt(P, dst[k].ap, tmp.ap, pr_bc[:, PAN:2 * PAN], OP.mult, r=[tmp, pr_bc], w=[dst[k]])


def emit_s_in(P, ctx):
    x_in = ctx.din("x", [B, T, D], F32)
    xT = ctx.dscratch("xT", [B, D, T], F32)
    eye = ctx.const("eye_f")
    _scope = scope(ctx); _scope.__enter__()
    ld = [ctx.sb(f"si_ld{i}", 128, D) for i in range(2)]
    pt = ctx.ps("si_ps", 128, 1024)
    gat = [ctx.sb(f"si_g{i}", 128, 1024) for i in range(2)]
    for b in range(B):
        for tt_ in range(T // 128):
            l = ld[tt_ % 2]; g = gat[tt_ % 2]
            P.load(l, x_in.t[b, tt_ * 128:(tt_ + 1) * 128, :], r=[x_in])
            for k in range(KT):
                transpose(P, pt[:, k * 128:(k + 1) * 128], l[:, k * 128:(k + 1) * 128],
                          eye, r=[l], w=[pt])
                vcopy(P, g[:, k * 128:(k + 1) * 128], pt[:, k * 128:(k + 1) * 128],
                      r=[pt], w=[g])
            for k in range(KT):
                P.store(xT.t[b, k * 128:(k + 1) * 128, tt_ * 128:(tt_ + 1) * 128],
                        g.t[:, k * 128:(k + 1) * 128], xT, r=[g])
    _scope.__exit__(None, None, None)
    P.barrier()

def emit_s_out(P, ctx, src="x4T"):
    x4T = ctx.dscratch(src, [B, D, T], F32)
    out = ctx.dout("out", [B, T, D], F32)
    eye = ctx.const("eye_f")
    _scope = scope(ctx); _scope.__enter__()
    ld = [ctx.sb(f"so_ld{i}", 128, 1024) for i in range(2)]
    pt = ctx.ps("so_ps", 128, 1024)
    gat = [ctx.sb(f"so_g{i}", 128, 1024) for i in range(2)]
    for b in range(B):
        for tt_ in range(T // 128):
            l = ld[tt_ % 2]; g = gat[tt_ % 2]
            for k in range(KT):
                P.load(TB(l.t[:, k * 128:(k + 1) * 128], f"so{k}"),
                       x4T.t[b, k * 128:(k + 1) * 128, tt_ * 128:(tt_ + 1) * 128],
                       r=[x4T], w=[l])
            for k in range(KT):
                transpose(P, pt[:, k * 128:(k + 1) * 128], l[:, k * 128:(k + 1) * 128],
                          eye, r=[l], w=[pt])
                vcopy(P, g[:, k * 128:(k + 1) * 128], pt[:, k * 128:(k + 1) * 128],
                      r=[pt], w=[g])
            P.store(out.t[b, tt_ * 128:(tt_ + 1) * 128, :], g.ap, out, r=[g])
    _scope.__exit__(None, None, None)
    P.barrier()


from contextlib import contextmanager

@contextmanager
def scope(ctx):
    old = ctx.stack
    with ExitStack() as st:
        ctx.stack = st
        yield
        ctx.P.barrier()
    ctx.stack = old

def _ln_alloc(ctx, tag):
    a = {}
    a["tmp"] = ctx.sb(f"{tag}_lntmp", 128, PAN)
    a["row"] = ctx.sb(f"{tag}_lnrow", 1, 4 * PAN)
    a["psum"] = ctx.ps(f"{tag}_lnpsS", 1, PAN)
    a["psq"] = ctx.ps(f"{tag}_lnpsQ", 1, PAN)
    a["pbc"] = ctx.ps(f"{tag}_lnpbc", 128, 2 * PAN)
    return a

def emit_ln_to(P, ctx, a, xs, dst):
    emit_ln_panel(P, ctx, xs, dst, a["tmp"], a["psum"], a["psq"], a["row"], a["pbc"])


def emit_s0a(P, ctx):
    xT = ctx.dscratch("xT", [B, D, T], F32)
    hlnT = ctx.dscratch("hlnT", [B, D, T], F32)
    qnT = ctx.dscratch("qnT", [B, D, T], F32)
    knT = ctx.dscratch("knT", [B, D, T], F32)
    vT = ctx.dscratch("vT", [B, D, T], F32)
    eye = ctx.const("eye_f")
    oc = ctx.const("ones_col_f")
    with scope(ctx):
        xs = [ctx.sb(f"s0a_x{k}", 128, PAN) for k in range(KT)]
        hl = [ctx.sb(f"s0a_h{k}", 128, PAN) for k in range(KT)]
        la = _ln_alloc(ctx, "s0a")
        for b in range(B):
            for p in range(NPAN):
                for k in range(KT):
                    P.load(xs[k], xT.t[b, k * 128:(k + 1) * 128, p * PAN:(p + 1) * PAN],
                           r=[xT], q=("sync" if k % 2 == 0 else "gpsimd"))
                emit_ln_to(P, ctx, la, xs, hl)
                for k in range(KT):
                    P.store(hlnT.t[b, k * 128:(k + 1) * 128, p * PAN:(p + 1) * PAN],
                            hl[k].ap, hlnT, r=[hl[k]])
        P.barrier()
        wnames = [("L0_attn_Wq", "L0_attn_cwq", qnT, True),
                  ("L0_attn_Wk", "L0_attn_cwk", knT, True),
                  ("L0_attn_Wv", "L0_attn_cwv", vT, False)]
        W_d = {nm: ctx.din(nm, [D, D]) for nm, _, _, _ in wnames}
        cw_d = {nm: ctx.din(nm, [D, 4]) for _, nm, _, _ in wnames}
        ws = [ctx.sb(f"s0a_w{k}", 128, D) for k in range(KT)]
        cw_sb = ctx.sb("s0a_cw", 128, 32)
        dg = [ctx.sb(f"s0a_dg{i}", 128, 128) for i in range(32)]
        pre = [ctx.sb(f"s0a_pre{k}", 128, PAN + 3) for k in range(KT)]
        post = [ctx.sb(f"s0a_post{k}", 128, PAN) for k in range(KT)]
        tail = [ctx.sb(f"s0a_tl{k}", 128, 3) for k in range(KT)]
        sq = ctx.sb("s0a_sq", 128, PAN)
        nrow = ctx.sb("s0a_nrow", 1, PAN)
        pp = [ctx.ps(f"s0a_pp{i}", 128, PAN) for i in range(2)]
        pn = ctx.ps("s0a_pn", 1, PAN)
        pb = ctx.ps("s0a_pb", 128, PAN)
        for wname, cwname, dstT, do_l2 in wnames:
            Wd = W_d[wname]
            for k in range(KT):
                P.load(ws[k], Wd.t[k * 128:(k + 1) * 128, :], r=[Wd],
                       q=("sync" if k % 2 == 0 else "gpsimd"))
            P.load(rr2(cw_sb, "p (a j) -> p a j", j=4),
                   cw_d[cwname].t[:, :].rearrange("(a p) j -> p a j", p=128),
                   r=[cw_d[cwname]], w=[cw_sb], slow=True)
            for k in range(KT):
                for tap in range(4):
                    ts(P, dg[k * 4 + tap].ap, eye.ap, cw_sb[:, k * 4 + tap:k * 4 + tap + 1],
                       OP.mult, r=[eye, cw_sb], w=[dg[k * 4 + tap]])
            for b in range(B):
                for k in range(KT):
                    memset(P, tail[k].ap, 0.0, w=[tail[k]])
                for p in range(NPAN):
                    for k in range(KT):
                        P.load(hl[k], hlnT.t[b, k * 128:(k + 1) * 128, p * PAN:(p + 1) * PAN],
                               r=[hlnT], q=("sync" if k % 2 == 0 else "gpsimd"))
                    for mk_ in range(KT):
                        pj = pp[mk_ % 2]
                        for k in range(KT):
                            mm(P, pj.ap, ws[k][:, mk_ * 128:(mk_ + 1) * 128], hl[k].ap,
                               r=[ws[k], hl[k]], w=[pj], start=(k == 0), stop=(k == KT - 1))
                        vcopy(P, pre[mk_][:, 3:3 + PAN], pj.ap, r=[pj], w=[pre[mk_]])
                        vcopy(P, pre[mk_][:, 0:3], tail[mk_].ap, r=[tail[mk_]], w=[pre[mk_]])
                        vcopy(P, tail[mk_].ap, pre[mk_][:, PAN:PAN + 3],
                              r=[pre[mk_]], w=[tail[mk_]], eng="s")
                    for k in range(KT):
                        pj = pp[k % 2]
                        for tap in range(4):
                            mm(P, pj.ap, dg[k * 4 + tap].ap, pre[k][:, tap:tap + PAN],
                               r=[dg[k * 4 + tap], pre[k]], w=[pj],
                               start=(tap == 0), stop=(tap == 3))
                        act(P, post[k].ap, pj.ap, AF.Silu, r=[pj], w=[post[k]])
                    if do_l2:
                        for h in range(NH):
                            for j in range(2):
                                k = 2 * h + j
                                act(P, sq.ap, post[k].ap, AF.Square, r=[post[k]], w=[sq])
                                mm(P, pn.ap, oc[:, 0:1], sq.ap, r=[sq, oc], w=[pn],
                                   start=(j == 0), stop=(j == 1))
                            act(P, nrow.ap, pn.ap, AF.Sqrt, r=[pn], w=[nrow], bias=1e-6)
                            recip(P, nrow.ap, nrow.ap, r=[nrow], w=[nrow])
                            orow = ctx.const("ones_row_f")
                            mm(P, pb.ap, orow[0:1, :], nrow.ap, r=[nrow, orow], w=[pb])
                            for j in range(2):
                                k = 2 * h + j
                                tt(P, post[k].ap, post[k].ap, pb.ap, OP.mult,
                                   r=[post[k], pb], w=[post[k]])
                    for k in range(KT):
                        P.store(dstT.t[b, k * 128:(k + 1) * 128, p * PAN:(p + 1) * PAN],
                                post[k].ap, dstT, r=[post[k]])
            P.barrier()
    P.barrier()


def emit_s0a2(P, ctx):
    hlnT = ctx.dscratch("hlnT", [B, D, T], F32)
    idT = ctx.dscratch("idT", [B, D, T], F32)
    betaT = ctx.dscratch("betaT", [B, NH, T], F32)
    Wid = ctx.din("L0_attn_Wid", [D, D])
    Wb = ctx.din("L0_attn_Wb", [D, NH])
    with scope(ctx):
        hl = [ctx.sb(f"s0a2_h{k}", 128, PAN) for k in range(KT)]
        ws = [ctx.sb(f"s0a2_w{k}", 128, D) for k in range(KT)]
        wb_sb = ctx.sb("s0a2_wb", 128, 32)
        out = [ctx.sb(f"s0a2_o{k}", 128, PAN) for k in range(KT)]
        brow = ctx.sb("s0a2_brow", 1, PAN)
        pp = [ctx.ps(f"s0a2_pp{i}", 128, PAN) for i in range(2)]
        pnb = ctx.ps("s0a2_pnb", 1, PAN)
        for k in range(KT):
            P.load(ws[k], Wid.t[k * 128:(k + 1) * 128, :], r=[Wid],
                   q=("sync" if k % 2 == 0 else "gpsimd"))
        P.load(rr2(wb_sb, "p (a j) -> p a j", j=4),
               Wb.t[:, :].rearrange("(a p) j -> p a j", p=128), r=[Wb], w=[wb_sb], slow=True)
        for b in range(B):
            for p in range(NPAN):
                for k in range(KT):
                    P.load(hl[k], hlnT.t[b, k * 128:(k + 1) * 128, p * PAN:(p + 1) * PAN],
                           r=[hlnT], q=("sync" if k % 2 == 0 else "gpsimd"))
                for mk_ in range(KT):
                    pj = pp[mk_ % 2]
                    for k in range(KT):
                        mm(P, pj.ap, ws[k][:, mk_ * 128:(mk_ + 1) * 128], hl[k].ap,
                           r=[ws[k], hl[k]], w=[pj], start=(k == 0), stop=(k == KT - 1))
                    vcopy(P, out[mk_].ap, pj.ap, r=[pj], w=[out[mk_]])
                    P.store(idT.t[b, mk_ * 128:(mk_ + 1) * 128, p * PAN:(p + 1) * PAN],
                            out[mk_].ap, idT, r=[out[mk_]])
                for h in range(NH):
                    for k in range(KT):
                        mm(P, pnb.ap, wb_sb[:, k * 4 + h:k * 4 + h + 1], hl[k].ap,
                           r=[wb_sb, hl[k]], w=[pnb], start=(k == 0), stop=(k == KT - 1))
                    act(P, brow.ap, pnb.ap, AF.Sigmoid, r=[pnb], w=[brow])
                    P.store(betaT.t[b, h:h + 1, p * PAN:(p + 1) * PAN],
                            brow.ap, betaT, r=[brow])
    P.barrier()

DBG_S0B = False


def emit_s0b(P, ctx):
    qnT = ctx.dscratch("qnT", [B, D, T], F32)
    knT = ctx.dscratch("knT", [B, D, T], F32)
    vT = ctx.dscratch("vT", [B, D, T], F32)
    betaT = ctx.dscratch("betaT", [B, NH, T], F32)
    doutT = ctx.dscratch("doutT", [B, D, T], F32)
    eye = ctx.const("eye_f")
    sr4 = ctx.const("sel_row4")
    m_sl = ctx.const("mask_sl4")
    m_su = ctx.const("mask_su4")
    m_ui = ctx.const("mask_ui4")
    with scope(ctx):
        qc = [ctx.sb(f"s0b_q{i}", 128, 1024) for i in range(2)]
        kc = [ctx.sb(f"s0b_k{i}", 128, 1024) for i in range(2)]
        vc = [ctx.sb(f"s0b_v{i}", 128, 1024) for i in range(2)]
        bc = [ctx.sb(f"s0b_b{i}", NH, CH) for i in range(2)]
        kb = ctx.sb("s0b_kb", 128, 1024)
        vn = ctx.sb("s0b_vn", 128, 1024)
        kn = ctx.sb("s0b_kn", 128, 1024)
        bn = ctx.sb("s0b_bn", 128, NH)
        Bp = [ctx.sb(f"s0b_B{i}", 128, 512) for i in range(2)]
        Cp = [ctx.sb(f"s0b_C{j}", 128, 512) for j in range(7)]
        za = ctx.sb("s0b_za", 128, 1024)
        zb = ctx.sb("s0b_zb", 128, 1024)
        Asb = ctx.sb("s0b_A", 128, 512)
        osb = ctx.sb("s0b_o", 128, 1024)
        Ssb = ctx.sb("s0b_S", 128, 2048)
        psS = ctx.ps("s0b_psS", 128, 2048)
        psa = ctx.ps("s0b_psa", 128, 512)
        psb = ctx.ps("s0b_psb", 128, 512)
        psw = ctx.ps("s0b_psw", 128, 1024)
        for b in range(B):
            for c in range(NCH):
                par = c % 2
                q_, k_, v_, b_ = qc[par], kc[par], vc[par], bc[par]
                lq = "sync" if par == 0 else "gpsimd"
                for k in range(KT):
                    P.load(TB(q_.t[:, k * 128:(k + 1) * 128], "l"),
                           qnT.t[b, k * 128:(k + 1) * 128, c * CH:(c + 1) * CH],
                           r=[qnT], w=[q_], q=lq)
                    P.load(TB(k_.t[:, k * 128:(k + 1) * 128], "l"),
                           knT.t[b, k * 128:(k + 1) * 128, c * CH:(c + 1) * CH],
                           r=[knT], w=[k_], q=lq)
                    P.load(TB(v_.t[:, k * 128:(k + 1) * 128], "l"),
                           vT.t[b, k * 128:(k + 1) * 128, c * CH:(c + 1) * CH],
                           r=[vT], w=[v_], q=lq)
                P.load(b_, betaT.t[b, :, c * CH:(c + 1) * CH], r=[betaT], q=lq)
                transpose(P, psb[:, 0:NH], b_.ap, eye, r=[b_], w=[psb], kdim=NH)
                vcopy(P, bn.ap, psb[:, 0:NH], r=[psb], w=[bn])
                for k in range(KT):
                    h = k // 2
                    mm(P, psw[:, k * 128:(k + 1) * 128], sr4[:, h * 128:(h + 1) * 128],
                       b_.ap, r=[b_, sr4], w=[psw])
                tt(P, kb.ap, k_.ap, psw.ap, OP.mult, r=[k_, psw], w=[kb])
                for h in range(NH):
                    for j in range(2):
                        k = 2 * h + j
                        mm(P, psa[:, h * 128:(h + 1) * 128], kb[:, k * 128:(k + 1) * 128],
                           k_.t[:, k * 128:(k + 1) * 128], r=[kb, k_], w=[psa],
                           start=(j == 0), stop=(j == 1))
                        mm(P, psb[:, h * 128:(h + 1) * 128], k_.t[:, k * 128:(k + 1) * 128],
                           kb[:, k * 128:(k + 1) * 128], r=[kb, k_], w=[psb],
                           start=(j == 0), stop=(j == 1))
                tt(P, Bp[0].ap, psa.ap, m_sl.ap, OP.mult, r=[psa, m_sl], w=[Bp[0]])
                tt(P, Cp[0].ap, psb.ap, m_su.ap, OP.mult, r=[psb, m_su], w=[Cp[0]])
                bcur = Bp[0]
                for j in range(1, 7):
                    bnext = Bp[j % 2]
                    for h in range(NH):
                        sl = slice(h * 128, (h + 1) * 128)
                        mm(P, psa[:, sl], Cp[j - 1][:, sl], bcur[:, sl],
                           r=[Cp[j - 1], bcur], w=[psa])
                        mm(P, psb[:, sl], bcur[:, sl], Cp[j - 1][:, sl],
                           r=[Cp[j - 1], bcur], w=[psb])
                    vcopy(P, bnext.ap, psa.ap, r=[psa], w=[bnext])
                    vcopy(P, Cp[j].ap, psb.ap, r=[psb], w=[Cp[j]], eng="s")
                    bcur = bnext
                for k in range(KT):
                    transpose(P, psw[:, k * 128:(k + 1) * 128],
                              v_.t[:, k * 128:(k + 1) * 128], eye, r=[v_], w=[psw])
                vcopy(P, vn.ap, psw.ap, r=[psw], w=[vn])
                for k in range(KT):
                    transpose(P, psw[:, k * 128:(k + 1) * 128],
                              k_.t[:, k * 128:(k + 1) * 128], eye, r=[k_], w=[psw])
                vcopy(P, kn.ap, psw.ap, r=[psw], w=[kn])
                if c == 0:
                    for h in range(NH):
                        sl = slice(h * 256, (h + 1) * 256)
                        ts(P, za.t[:, sl], vn.t[:, sl], bn[:, h:h + 1], OP.mult,
                           r=[vn, bn], w=[za])
                else:
                    for h in range(NH):
                        for j in range(2):
                            k = 2 * h + j
                            mm(P, psw[:, h * 256:(h + 1) * 256],
                               kb[:, k * 128:(k + 1) * 128],
                               Ssb.t[:, h * 512 + j * 256:h * 512 + (j + 1) * 256],
                               r=[kb, Ssb], w=[psw], start=(j == 0), stop=(j == 1))
                    if DBG_S0B:
                        dbgP = ctx.dscratch("dbgP", [B, NCH, 128, 1024], F32)
                        vcopy(P, osb.ap, psw.ap, r=[psw], w=[osb])
                        P.store(dbgP.t[b, c], osb.ap, dbgP, r=[osb], q="sync")
                    for h in range(NH):
                        sl = slice(h * 256, (h + 1) * 256)
                        def f(e, o=za.t[:, sl], i0=vn.t[:, sl], s=bn[:, h:h + 1],
                              i1=psw[:, sl]):
                            return e.scalar_tensor_tensor(o, i0, s, i1,
                                                          OP.mult, OP.subtract)
                        P.v(f, r=[vn, bn, psw], w=[za])
                if DBG_S0B:
                    dbgM = ctx.dscratch("dbgM", [B, NCH, 128, 1024], F32)
                    P.store(dbgM.t[b, c], za.ap, dbgM, r=[za])
                    dbgB = ctx.dscratch("dbgB", [B, NCH, 128, 512], F32)
                    P.store(dbgB.t[b, c], Cp[0].ap, dbgB, r=[Cp[0]])
                zc, zn = za, zb
                for j in range(6, -1, -1):
                    for h in range(NH):
                        mm(P, psw[:, h * 256:(h + 1) * 256], Cp[j][:, h * 128:(h + 1) * 128],
                           zc.t[:, h * 256:(h + 1) * 256], r=[Cp[j], zc], w=[psw])
                    op1 = OP.subtract if j == 0 else OP.add
                    sgn = -1.0 if j == 0 else 1.0
                    def f(e, sgn=sgn, zn=zn, zc=zc):
                        return e.scalar_tensor_tensor(zn.ap, psw.ap, sgn, zc.ap,
                                                      OP.mult, OP.add)
                    P.v(f, r=[psw, zc], w=[zn])
                    zc, zn = zn, zc
                uadj = zc
                for h in range(NH):
                    for j in range(2):
                        k = 2 * h + j
                        mm(P, psa[:, h * 128:(h + 1) * 128], k_.t[:, k * 128:(k + 1) * 128],
                           q_.t[:, k * 128:(k + 1) * 128], r=[k_, q_], w=[psa],
                           start=(j == 0), stop=(j == 1))
                tt(P, Asb.ap, psa.ap, m_ui.ap, OP.mult, r=[psa, m_ui], w=[Asb])
                for h in range(NH):
                    for dt in range(2):
                        k = 2 * h + dt
                        sl = slice(k * 128, (k + 1) * 128)
                        first = True
                        if c > 0:
                            for j in range(2):
                                mm(P, psw[:, sl],
                                   Ssb.t[:, h * 512 + j * 256 + dt * 128:h * 512 + j * 256 + (dt + 1) * 128],
                                   q_.t[:, (2 * h + j) * 128:(2 * h + j + 1) * 128],
                                   r=[Ssb, q_], w=[psw], start=(j == 0), stop=False)
                            first = False
                        mm(P, psw[:, sl], uadj.t[:, h * 256 + dt * 128:h * 256 + (dt + 1) * 128],
                           Asb[:, h * 128:(h + 1) * 128], r=[uadj, Asb], w=[psw],
                           start=first, stop=True)
                vcopy(P, osb.ap, psw.ap, r=[psw], w=[osb])
                for k in range(KT):
                    P.store(doutT.t[b, k * 128:(k + 1) * 128, c * CH:(c + 1) * CH],
                            osb.t[:, k * 128:(k + 1) * 128], doutT, r=[osb],
                            q=("gpsimd" if par == 0 else "sync"))
                if c < NCH - 1:
                    for h in range(NH):
                        for j in range(2):
                            mm(P, psS[:, h * 512 + j * 256:h * 512 + (j + 1) * 256],
                               kn.t[:, h * 256 + j * 128:h * 256 + (j + 1) * 128],
                               uadj.t[:, h * 256:(h + 1) * 256], r=[kn, uadj], w=[psS],
                               start=True)
                if c < NCH - 1:
                    if c == 0:
                        vcopy(P, Ssb.ap, psS.ap, r=[psS], w=[Ssb])
                    else:
                        tt(P, Ssb.ap, Ssb.ap, psS.ap, OP.add, r=[Ssb, psS], w=[Ssb])
                if DBG_S0B:
                    dbgU = ctx.dscratch("dbgU", [B, NCH, 128, 1024], F32)
                    dbgS = ctx.dscratch("dbgS", [B, NCH, 128, 2048], F32)
                    P.store(dbgU.t[b, c], uadj.ap, dbgU, r=[uadj])
                    P.store(dbgS.t[b, c], Ssb.ap, dbgS, r=[Ssb])
            P.barrier()
    P.barrier()


def emit_s0c1(P, ctx):
    vT = ctx.dscratch("vT", [B, D, T], F32)
    lsT = ctx.dscratch("lsT", [B, D, T], F32)
    llT = ctx.dscratch("llT", [B, D, T], F32)
    firs_d = ctx.din("L0_attn_firs", [4, 256, 3])
    firl_d = ctx.din("L0_attn_firl", [4, 256, 63])
    eyeb = ctx.const("eye_b")
    with scope(ctx):
        fls = ctx.sb("s0c1_fls", 128, 24)
        fll = ctx.sb("s0c1_fll", 128, 504)
        P.load(View(fls.t[:, :].rearrange("p (h a t) -> p h a t", h=4, a=2), fls.b),
               firs_d.t[:, :, :].rearrange("h (a p) t -> p h a t", p=128),
               r=[firs_d], slow=True)
        P.load(View(fll.t[:, :].rearrange("p (h a t) -> p h a t", h=4, a=2), fll.b),
               firl_d.t[:, :, :].rearrange("h (a p) t -> p h a t", p=128),
               r=[firl_d], slow=True)
        vbf = [ctx.sb(f"s0c1_vbf{k}", 128, 62 + T, BF16) for k in range(KT)]
        dgl = [ctx.sb(f"s0c1_dg{i}", 128, 128, BF16) for i in range(66)]
        vstg = [ctx.sb(f"s0c1_vstg{i}", 128, PAN) for i in range(2)]
        ous = [ctx.sb(f"s0c1_ou{i}", 128, PAN) for i in range(4)]
        psf = ctx.ps("s0c1_psf", 128, 1024)
        for b in range(B):
            for k in range(KT):
                memset(P, vbf[k][:, 0:62], 0.0, w=[vbf[k]])
                for p in range(NPAN):
                    vs = vstg[p % 2]
                    P.load(vs, vT.t[b, k * 128:(k + 1) * 128, p * PAN:(p + 1) * PAN],
                           r=[vT], w=[vs], q=("sync" if p % 2 == 0 else "gpsimd"))
                    vcopy(P, vbf[k][:, 62 + p * PAN:62 + (p + 1) * PAN], vs.ap,
                          r=[vs], w=[vbf[k]])
            for k in range(KT):
                for tp_ in range(63):
                    ts(P, dgl[tp_].ap, eyeb.ap, fll[:, (4 * k + 0) * 63 + tp_:(4 * k) * 63 + tp_ + 1]
                       if False else fll[:, k * 63 + tp_:k * 63 + tp_ + 1],
                       OP.mult, r=[eyeb, fll], w=[dgl[tp_]])
                for tp_ in range(3):
                    ts(P, dgl[63 + tp_].ap, eyeb.ap, fls[:, k * 3 + tp_:k * 3 + tp_ + 1],
                       OP.mult, r=[eyeb, fls], w=[dgl[63 + tp_]])
                for p in range(NPAN):
                    for tp_ in range(63):
                        mm(P, psf[:, 512:1024], dgl[tp_].ap,
                           vbf[k][:, p * PAN + tp_:p * PAN + tp_ + PAN],
                           r=[dgl[tp_], vbf[k]], w=[psf],
                           start=(tp_ == 0), stop=(tp_ == 62))
                    for tp_ in range(3):
                        mm(P, psf[:, 0:512], dgl[63 + tp_].ap,
                           vbf[k][:, 60 + p * PAN + tp_:60 + p * PAN + tp_ + PAN],
                           r=[dgl[63 + tp_], vbf[k]], w=[psf],
                           start=(tp_ == 0), stop=(tp_ == 2))
                    o1 = ous[2 * (p % 2)]; o2 = ous[2 * (p % 2) + 1]
                    vcopy(P, o1.ap, psf[:, 512:1024], r=[psf], w=[o1])
                    vcopy(P, o2.ap, psf[:, 0:512], r=[psf], w=[o2], eng="s")
                    P.store(llT.t[b, k * 128:(k + 1) * 128, p * PAN:(p + 1) * PAN],
                            o1.ap, llT, r=[o1])
                    P.store(lsT.t[b, k * 128:(k + 1) * 128, p * PAN:(p + 1) * PAN],
                            o2.ap, lsT, r=[o2], q="sync")
    P.barrier()


def emit_s0c2(P, ctx):
    xT = ctx.dscratch("xT", [B, D, T], F32)
    hlnT = ctx.dscratch("hlnT", [B, D, T], F32)
    vT = ctx.dscratch("vT", [B, D, T], F32)
    doutT = ctx.dscratch("doutT", [B, D, T], F32)
    idT = ctx.dscratch("idT", [B, D, T], F32)
    lsT = ctx.dscratch("lsT", [B, D, T], F32)
    llT = ctx.dscratch("llT", [B, D, T], F32)
    x1T = ctx.dscratch("x1T", [B, D, T], F32)
    gW1 = ctx.din("L0_attn_gW1", [1040, 1024])
    gW2 = ctx.din("L0_attn_gW2", [1024, 4])
    gb1 = ctx.din("L0_attn_gb1", [1024])
    gb2 = ctx.din("L0_attn_gb2", [4])
    ltm = ctx.din("L0_attn_log_temp", [4, 4])
    flp = ctx.din("L0_attn_floor_param", [4, 4])
    crl = ctx.din("L0_attn_conv_res_logit", [4])
    alp = ctx.din("L0_attn_alpha_id", [4])
    onw = ctx.din("L0_attn_o_norm_w", [256])
    Wo = ctx.din("L0_attn_Wo", [D, D])
    eye = ctx.const("eye_f")
    orow = ctx.const("ones_row_f")
    sr16 = ctx.const("sel_row16")
    sr4 = ctx.const("sel_row4")
    selc = ctx.const("selc_st")
    stsc = ctx.const("st_scale")
    m3b = ctx.const("m3big")
    r416 = ctx.const("rep4_16")
    sh16 = ctx.const("shift16")
    selc4 = ctx.const("selc4")
    with scope(ctx):
        gwa = [ctx.sb(f"s0c2_gwa{k}", 128, 1024) for k in range(KT)]
        for k in range(KT):
            P.load(gwa[k], gW1.t[k * 128:(k + 1) * 128, :], r=[gW1],
                   q=("sync" if k % 2 == 0 else "gpsimd"))
        gwb = ctx.sb("s0c2_gwb", 16, 1024)
        P.load(gwb, gW1.t[1024:1040, :], r=[gW1])
        gw2 = ctx.sb("s0c2_gw2", 128, 32)
        P.load(rr2(gw2, "p (a j) -> p a j", j=4),
               gW2.t[:, :].rearrange("(a p) j -> p a j", p=128), r=[gW2], w=[gw2], slow=True)
        gb1s = ctx.sb("s0c2_gb1", 128, KT)
        P.load(gb1s, gb1.t[:].rearrange("(a p) -> p a", p=128), r=[gb1], slow=True)
        gb2s = ctx.sb("s0c2_gb2", 4, 1)
        P.load(gb2s, gb2.t[:].rearrange("(j o) -> j o", o=1), r=[gb2], slow=True)
        wstg = ctx.sb("s0c2_wstg", 128, 1024)
        wob = [ctx.sb(f"s0c2_wob{k}", 128, 1024, BF16) for k in range(KT)]
        for k in range(KT):
            P.load(wstg, Wo.t[k * 128:(k + 1) * 128, :], r=[Wo], w=[wstg])
            vcopy(P, wob[k].ap, wstg.ap, r=[wstg], w=[wob[k]])
        onws = ctx.sb("s0c2_onw", 128, 2)
        P.load(onws, onw.t[:].rearrange("(a p) -> p a", p=128), r=[onw], slow=True)
        r16 = ctx.sb("s0c2_r16", 1, 16)
        r16b = ctx.sb("s0c2_r16b", 1, 16)
        r4 = ctx.sb("s0c2_r4", 1, 4)
        c4 = ctx.sb("s0c2_c4", 4, 1)
        invt = ctx.sb("s0c2_invt", 128, 64)
        fl64 = ctx.sb("s0c2_fl64", 128, 64)
        omf64 = ctx.sb("s0c2_omf64", 128, 64)
        crlb = ctx.sb("s0c2_crlb", 128, 4)
        alpb = ctx.sb("s0c2_alpb", 128, 4)
        ps_nat = ctx.ps("s0c2_psnat", 128, 64)
        ps_g = ctx.ps("s0c2_psg", 128, 512)
        ps_s1 = ctx.ps("s0c2_pss1", 128, 512)
        ps_t4 = ctx.ps("s0c2_pst4", 16, 512)
        ps_big = ctx.ps("s0c2_psbig", 128, 2048)
        P.load(r16, ltm.t[:, :].rearrange("(o h) j -> o (h j)", o=1), r=[ltm])
        act(P, r16.ap, r16.ap, AF.Exp, r=[r16], w=[r16], scale=-1.0)
        mm(P, ps_nat[:, 0:16], orow[0:1, :], r16.ap, r=[r16, orow], w=[ps_nat])
        for j in range(4):
            vcopy(P, invt[:, 16 * j:16 * (j + 1)], ps_nat[:, 0:16], r=[ps_nat], w=[invt])
        P.load(r16b, flp.t[:, :].rearrange("(o h) j -> o (h j)", o=1), r=[flp])
        act(P, r16b.ap, r16b.ap, AF.Sigmoid, r=[r16b], w=[r16b])
        ts(P, r16b.ap, r16b.ap, 0.05, OP.mult, r=[r16b], w=[r16b])
        mm(P, ps_nat[:, 0:16], orow[0:1, :], r16b.ap, r=[r16b, orow], w=[ps_nat])
        for j in range(4):
            vcopy(P, fl64[:, 16 * j:16 * (j + 1)], ps_nat[:, 0:16], r=[ps_nat], w=[fl64])
        reduce(P, r4.ap, r16b.t[0:1, :].rearrange("o (h j) -> o h j", j=4), OP.add,
               r=[r16b], w=[r4])
        ts(P, r4.ap, r4.ap, -1.0, OP.mult, r=[r4], w=[r4], s2=1.0, op1=OP.add)
        transpose(P, ps_t4[0:4, 0:1], r4.ap, eye, r=[r4], w=[ps_t4], kdim=1)
        vcopy(P, c4.ap, ps_t4[0:4, 0:1], r=[ps_t4], w=[c4])
        mm(P, ps_nat[0:1, 0:16], c4.ap, r416.ap, r=[c4, r416], w=[ps_nat])
        vcopy(P, r16.ap, ps_nat[0:1, 0:16], r=[ps_nat], w=[r16])
        mm(P, ps_nat[:, 16:32], orow[0:1, :], r16.ap, r=[r16, orow], w=[ps_nat])
        for j in range(4):
            vcopy(P, omf64[:, 16 * j:16 * (j + 1)], ps_nat[:, 16:32], r=[ps_nat], w=[omf64])
        P.load(r4, crl.t[:].rearrange("(o h) -> o h", o=1), r=[crl], w=[r4])
        act(P, r4.ap, r4.ap, AF.Sigmoid, r=[r4], w=[r4])
        mm(P, ps_nat[:, 0:4], orow[0:1, :], r4.ap, r=[r4, orow], w=[ps_nat])
        vcopy(P, crlb.ap, ps_nat[:, 0:4], r=[ps_nat], w=[crlb])
        P.load(r4, alp.t[:].rearrange("(o h) -> o h", o=1), r=[alp], w=[r4])
        mm(P, ps_nat[:, 0:4], orow[0:1, :], r4.ap, r=[r4, orow], w=[ps_nat])
        vcopy(P, alpb.ap, ps_nat[:, 0:4], r=[ps_nat], w=[alpb])
        hl = [ctx.sb(f"s0c2_hl{k}", 128, PAN) for k in range(KT)]
        xps = [ctx.sb(f"s0c2_xp{k}", 128, PAN) for k in range(KT)]
        onT = [ctx.sb(f"s0c2_on{k}", 128, PAN, BF16) for k in range(KT)]
        brt = {nm: [ctx.sb(f"s0c2_{nm}{j}", 128, PAN) for j in range(2)]
               for nm in ("ls", "ll", "do", "vp", "id")}
        sqt = ctx.sb("s0c2_sqt", 128, PAN)
        abt = ctx.sb("s0c2_abt", 128, PAN)
        Gsb = ctx.sb("s0c2_G", 16, PAN)
        m2sb = ctx.sb("s0c2_m2", 16, PAN)
        sts = [ctx.sb(f"s0c2_st{h}", 16, PAN) for h in range(NH)]
        shsb = ctx.sb("s0c2_sh", 128, PAN)
        tg = ctx.sb("s0c2_tg", 128, PAN)
        ghm = ctx.sb("s0c2_ghm", 128, PAN)
        lgs = [ctx.sb(f"s0c2_lg{h}", 4, PAN) for h in range(NH)]
        lgt = ctx.sb("s0c2_lgt", 4, PAN)
        nats = ctx.sb("s0c2_nats", 128, 64)
        ex = ctx.sb("s0c2_ex", 128, 64)
        sums = ctx.sb("s0c2_sums", 128, 16)
        prT = ctx.sb("s0c2_prT", 16, PAN)
        oac = [ctx.sb(f"s0c2_oac{d}", 128, PAN) for d in range(2)]
        tmx = ctx.sb("s0c2_tmx", 128, PAN)
        rr = ctx.sb("s0c2_rr", 4, PAN)
        x1s = ctx.sb("s0c2_x1s", 128, PAN)

        def load_branches(b, h, sl_t):
            for j in range(2):
                k = 2 * h + j
                for nm, src in (("ls", lsT), ("ll", llT), ("do", doutT), ("vp", vT),
                                ("id", idT)):
                    P.load(brt[nm][j], src.t[b, k * 128:(k + 1) * 128, sl_t], r=[src],
                           q=("sync" if (j + ord(nm[0])) % 2 == 0 else "gpsimd"))

        for b in range(B):
            for p in range(NPAN):
                sl_t = slice(p * PAN, (p + 1) * PAN)
                for k in range(KT):
                    P.load(hl[k], hlnT.t[b, k * 128:(k + 1) * 128, sl_t], r=[hlnT],
                           q=("sync" if k % 2 == 0 else "gpsimd"))
                for h in range(NH):
                    load_branches(b, h, sl_t)
                    for br, nm in enumerate(("ls", "ll", "do", "vp")):
                        for j in range(2):
                            tb_ = brt[nm][j]
                            first = (br == 0 and j == 0)
                            mm(P, ps_t4.ap, selc[:, (3 * br + 0) * 16:(3 * br + 1) * 16],
                               tb_.ap, r=[tb_, selc], w=[ps_t4], start=first, stop=False)
                            act(P, sqt.ap, tb_.ap, AF.Square, r=[tb_], w=[sqt])
                            mm(P, ps_t4.ap, selc[:, (3 * br + 1) * 16:(3 * br + 2) * 16],
                               sqt.ap, r=[sqt, selc], w=[ps_t4], start=False, stop=False)
                            act(P, abt.ap, tb_.ap, AF.Abs, r=[tb_], w=[abt])
                            mm(P, ps_t4.ap, selc[:, (3 * br + 2) * 16:(3 * br + 3) * 16],
                               abt.ap, r=[abt, selc], w=[ps_t4], start=False,
                               stop=(br == 3 and j == 1))
                    ts(P, Gsb.ap, ps_t4.ap, stsc.t[:, 0:1], OP.mult, r=[ps_t4, stsc], w=[Gsb])
                    mm(P, ps_t4.ap, sh16.ap, Gsb.ap, r=[sh16, Gsb], w=[ps_t4])
                    act(P, m2sb.ap, ps_t4.ap, AF.Square, r=[ps_t4], w=[m2sb])
                    tt(P, Gsb.ap, Gsb.ap, m2sb.ap, OP.subtract, r=[Gsb, m2sb], w=[Gsb])
                    ts(P, m2sb.ap, Gsb.ap, 0.0, OP.max, r=[Gsb], w=[m2sb])
                    act(P, m2sb.ap, m2sb.ap, AF.Sqrt, r=[m2sb], w=[m2sb])
                    tt(P, m2sb.ap, m2sb.ap, Gsb.ap, OP.subtract, r=[m2sb, Gsb], w=[m2sb])
                    tt(P, m2sb.ap, m2sb.ap, m3b[:, 0:PAN], OP.mult, r=[m2sb, m3b], w=[m2sb])
                    tt(P, sts[h].ap, Gsb.ap, m2sb.ap, OP.add, r=[Gsb, m2sb], w=[sts[h]])
                for h in range(NH):
                    memset(P, lgs[h].ap, 0.0, w=[lgs[h]])
                for mt in range(KT):
                    for k in range(KT):
                        mm(P, ps_g.ap, gwa[k][:, mt * 128:(mt + 1) * 128], hl[k].ap,
                           r=[gwa[k], hl[k]], w=[ps_g], start=(k == 0), stop=(k == KT - 1))
                    vcopy(P, shsb.ap, ps_g.ap, r=[ps_g], w=[shsb])
                    for h in range(NH):
                        mm(P, ps_s1.ap, gwb[:, mt * 128:(mt + 1) * 128], sts[h].ap,
                           r=[gwb, sts[h]], w=[ps_s1])
                        tt(P, tg.ap, ps_s1.ap, shsb.ap, OP.add, r=[ps_s1, shsb], w=[tg])
                        act(P, ghm.ap, tg.ap, AF.Gelu, r=[tg, gb1s], w=[ghm],
                            bias=gb1s[:, mt:mt + 1])
                        mm(P, ps_s1[0:4, :], gw2[:, mt * 4:(mt + 1) * 4], ghm.ap,
                           r=[gw2, ghm], w=[ps_s1])
                        vcopy(P, lgt.ap, ps_s1[0:4, :], r=[ps_s1], w=[lgt])
                        tt(P, lgs[h].ap, lgs[h].ap, lgt.ap, OP.add, r=[lgs[h], lgt],
                           w=[lgs[h]])
                for h in range(NH):
                    ts(P, lgs[h].ap, lgs[h].ap, gb2s.t[:, 0:1], OP.add, r=[lgs[h], gb2s],
                       w=[lgs[h]])
                for s in range(4):
                    for h in range(NH):
                        transpose(P, ps_nat[:, s * 16 + 4 * h:s * 16 + 4 * h + 4],
                                  lgs[h][:, s * 128:(s + 1) * 128], eye,
                                  r=[lgs[h]], w=[ps_nat], kdim=4)
                vcopy(P, nats.ap, ps_nat.ap, r=[ps_nat], w=[nats])
                tt(P, nats.ap, nats.ap, invt.ap, OP.mult, r=[nats, invt], w=[nats])
                act(P, ex.ap, nats.ap, AF.Exp, r=[nats], w=[ex])
                reduce(P, sums.ap, ex.t[:, :].rearrange("p (s j) -> p s j", j=4), OP.add,
                       r=[ex], w=[sums])
                recip(P, sums.ap, sums.ap, r=[sums], w=[sums])
                for s in range(4):
                    for h in range(NH):
                        c_ = s * 4 + h
                        ts(P, ex[:, 16 * s + 4 * h:16 * s + 4 * h + 4],
                           ex[:, 16 * s + 4 * h:16 * s + 4 * h + 4],
                           sums[:, c_:c_ + 1], OP.mult, r=[ex, sums], w=[ex])
                tt(P, ex.ap, ex.ap, omf64.ap, OP.mult, r=[ex, omf64], w=[ex])
                tt(P, ex.ap, ex.ap, fl64.ap, OP.add, r=[ex, fl64], w=[ex])
                for s in range(4):
                    transpose(P, ps_t4[:, s * 128:(s + 1) * 128], ex[:, s * 16:(s + 1) * 16],
                              eye, r=[ex], w=[ps_t4], kdim=128)
                vcopy(P, prT.ap, ps_t4.ap, r=[ps_t4], w=[prT])
                for k in range(KT):
                    P.load(xps[k], xT.t[b, k * 128:(k + 1) * 128, sl_t], r=[xT],
                           q=("sync" if k % 2 == 0 else "gpsimd"))
                for h in range(NH):
                    load_branches(b, h, sl_t)
                    for j in range(4):
                        mm(P, ps_big[:, j * 512:(j + 1) * 512],
                           sr16[:, (4 * h + j) * 128:(4 * h + j + 1) * 128], prT.ap,
                           r=[sr16, prT], w=[ps_big])
                    for dt in range(2):
                        def f0(e, dt=dt, h=h):
                            return e.scalar_tensor_tensor(
                                oac[dt].ap, ps_big[:, 0:512], crlb[:, h:h + 1],
                                brt["ls"][dt].ap, OP.add, OP.mult)
                        P.v(f0, r=[ps_big, crlb, brt["ls"][dt]], w=[oac[dt]])
                        for (jj, nm) in ((1, "ll"), (2, "do"), (3, "vp")):
                            tt(P, tmx.ap, ps_big[:, jj * 512:(jj + 1) * 512],
                               brt[nm][dt].ap, OP.mult, r=[ps_big, brt[nm][dt]], w=[tmx])
                            tt(P, oac[dt].ap, oac[dt].ap, tmx.ap, OP.add,
                               r=[oac[dt], tmx], w=[oac[dt]])
                        ts(P, tmx.ap, brt["id"][dt].ap, alpb[:, h:h + 1], OP.mult,
                           r=[brt["id"][dt], alpb], w=[tmx])
                        tt(P, oac[dt].ap, oac[dt].ap, tmx.ap, OP.add, r=[oac[dt], tmx],
                           w=[oac[dt]])
                        act(P, sqt.ap, oac[dt].ap, AF.Square, r=[oac[dt]], w=[sqt])
                        mm(P, ps_t4[0:4, :], selc4[:, 4 * h:4 * h + 4], sqt.ap,
                           r=[sqt, selc4], w=[ps_t4], start=(dt == 0), stop=(dt == 1))
                    act(P, rr.ap, ps_t4[0:4, :], AF.Sqrt, r=[ps_t4], w=[rr],
                        scale=1.0 / 256.0, bias=1e-5)
                    recip(P, rr.ap, rr.ap, r=[rr], w=[rr])
                    mm(P, ps_g.ap, sr4[:, h * 128:(h + 1) * 128], rr.ap, r=[rr, sr4],
                       w=[ps_g])
                    for dt in range(2):
                        k = 2 * h + dt
                        def f1(e, k=k, dt=dt):
                            return e.scalar_tensor_tensor(
                                onT[k].ap, oac[dt].ap, onws[:, dt:dt + 1], ps_g.ap,
                                OP.mult, OP.mult)
                        P.v(f1, r=[oac[dt], onws, ps_g], w=[onT[k]])
                for mt in range(KT):
                    pj = ps_s1 if mt % 2 == 0 else ps_g
                    for k in range(KT):
                        mm(P, pj.ap, wob[k][:, mt * 128:(mt + 1) * 128], onT[k].ap,
                           r=[wob[k], onT[k]], w=[pj], start=(k == 0), stop=(k == KT - 1))
                    tt(P, x1s.ap, pj.ap, xps[mt].ap, OP.add, r=[pj, xps[mt]], w=[x1s])
                    P.store(x1T.t[b, mt * 128:(mt + 1) * 128, sl_t], x1s.ap, x1T, r=[x1s])
            P.barrier()
    P.barrier()


def emit_s1g(P, ctx):
    x1T = ctx.dscratch("x1T", [B, D, T], F32)
    hlnT1b = ctx.dscratch("hlnT1b", [B, D, T], BF16)
    smaskT = ctx.dscratch("smaskT", [B, E0, T], F32)
    wred = ctx.din("L0_ffn_w_red", [D, 16])
    wg_d = ctx.din("L0_ffn_wg", [E0, 16])
    eye = ctx.const("eye_f")
    eyeb = ctx.const("eye_b")
    with scope(ctx):
        xs = [ctx.sb(f"s1g_x{k}", 128, PAN) for k in range(KT)]
        hlb = [ctx.sb(f"s1g_h{k}", 128, PAN, BF16) for k in range(KT)]
        la = _ln_alloc(ctx, "s1g")
        wrs = ctx.sb("s1g_wrs", 128, 128)
        wrb = ctx.sb("s1g_wrb", 128, 128, BF16)
        P.load(rr2(wrs, "p (a j) -> p a j", j=16),
               wred.t[:, :].rearrange("(a p) j -> p a j", p=128), r=[wred], w=[wrs], slow=True)
        vcopy(P, wrb.ap, wrs.ap, r=[wrs], w=[wrb])
        wgs = ctx.sb("s1g_wgs", E0, 16)
        P.load(wgs, wg_d.ap, r=[wg_d])
        wsq = ctx.sb("s1g_wsq", E0, 16)
        wnr = ctx.sb("s1g_wnr", E0, 1)
        wgn = ctx.sb("s1g_wgn", E0, 16, BF16)
        wgnT = ctx.sb("s1g_wgnT", 16, E0, BF16)
        act(P, wsq.ap, wgs.ap, AF.Square, r=[wgs], w=[wsq])
        reduce(P, wnr.ap, wsq.ap, OP.add, r=[wsq], w=[wnr])
        act(P, wnr.ap, wnr.ap, AF.Sqrt, r=[wnr], w=[wnr])
        ts(P, wnr.ap, wnr.ap, 1e-4, OP.max, r=[wnr], w=[wnr])
        recip(P, wnr.ap, wnr.ap, r=[wnr], w=[wnr])
        ts(P, wgn.ap, wgs.ap, wnr.t[:, 0:1], OP.mult, r=[wgs, wnr], w=[wgn])
        ps_w = ctx.ps("s1g_psw", 16, E0, BF16)
        transpose(P, ps_w.ap, wgn.ap, eyeb, r=[wgn], w=[ps_w], kdim=E0)
        vcopy(P, wgnT.ap, ps_w.ap, r=[ps_w], w=[wgnT])
        red = ctx.sb("s1g_red", 16, PAN, BF16)
        lg8 = ctx.sb("s1g_lg8", E0, PAN)
        nat = ctx.sb("s1g_nat", 128, 32)
        exl = ctx.sb("s1g_exl", 128, 32)
        exs = ctx.sb("s1g_exs", 128, 32)
        msk = ctx.sb("s1g_msk", 128, 32)
        lmx = ctx.sb("s1g_lmx", 128, 4)
        ssum = ctx.sb("s1g_ssum", 128, 4)
        smT = ctx.sb("s1g_smT", E0, PAN)
        ps16 = ctx.ps("s1g_ps16", 16, PAN)
        ps8 = ctx.ps("s1g_ps8", E0, PAN)
        psn = ctx.ps("s1g_psn", 128, 32)
        for b in range(B):
            for p in range(NPAN):
                sl_t = slice(p * PAN, (p + 1) * PAN)
                for k in range(KT):
                    P.load(xs[k], x1T.t[b, k * 128:(k + 1) * 128, sl_t], r=[x1T],
                           q=("sync" if k % 2 == 0 else "gpsimd"))
                emit_ln_to(P, ctx, la, xs, hlb)
                for k in range(KT):
                    P.store(hlnT1b.t[b, k * 128:(k + 1) * 128, sl_t], hlb[k].ap,
                            hlnT1b, r=[hlb[k]])
                for k in range(KT):
                    mm(P, ps16.ap, wrb[:, k * 16:(k + 1) * 16], hlb[k].ap,
                       r=[wrb, hlb[k]], w=[ps16], start=(k == 0), stop=(k == KT - 1))
                vcopy(P, red.ap, ps16.ap, r=[ps16], w=[red])
                mm(P, ps8.ap, wgnT.ap, red.ap, r=[wgnT, red], w=[ps8])
                vcopy(P, lg8.ap, ps8.ap, r=[ps8], w=[lg8])
                for s in range(4):
                    transpose(P, psn[:, s * 8:(s + 1) * 8], lg8[:, s * 128:(s + 1) * 128],
                              eye, r=[lg8], w=[psn], kdim=8)
                vcopy(P, nat.ap, psn.ap, r=[psn], w=[nat])
                for s in range(4):
                    sl8 = slice(s * 8, (s + 1) * 8)
                    def fr(e, sl8=sl8, s=s):
                        return e.tensor_reduce(lmx[:, s:s + 1], nat[:, sl8], AX.X, OP.max)
                    P.v(fr, r=[nat], w=[lmx])
                    ts(P, exl[:, sl8], nat[:, sl8], lmx[:, s:s + 1], OP.subtract,
                       r=[nat, lmx], w=[exl])
                act(P, exs.ap, exl.ap, AF.Exp, r=[exl], w=[exs])
                for s in range(4):
                    sl8 = slice(s * 8, (s + 1) * 8)
                    def fr2(e, sl8=sl8, s=s):
                        return e.tensor_reduce(ssum[:, s:s + 1], exs[:, sl8], AX.X, OP.add)
                    P.v(fr2, r=[exs], w=[ssum])
                recip(P, ssum.ap, ssum.ap, r=[ssum], w=[ssum])
                for s in range(4):
                    sl8 = slice(s * 8, (s + 1) * 8)
                    ts(P, msk[:, sl8], nat[:, sl8], lmx[:, s:s + 1], OP.is_equal,
                       r=[nat, lmx], w=[msk])
                    ts(P, msk[:, sl8], msk[:, sl8], ssum[:, s:s + 1], OP.mult,
                       r=[msk, ssum], w=[msk])
                for s in range(4):
                    transpose(P, ps8[:, s * 128:(s + 1) * 128], msk[:, s * 8:(s + 1) * 8],
                              eye, r=[msk], w=[ps8], kdim=128)
                vcopy(P, smT.ap, ps8.ap, r=[ps8], w=[smT])
                P.store(smaskT.t[b, :, sl_t], smT.ap, smaskT, r=[smT])
    P.barrier()


def emit_s1w(P, ctx):
    w1 = ctx.din("L0_ffn_w1", [E0, D, F0])
    w2 = ctx.din("L0_ffn_w2", [E0, F0, D])
    w1b = ctx.dscratch("w1b", [E0, KT, 128, F0], BF16)
    w2b = ctx.dscratch("w2b", [E0, 32, 128, D], BF16)
    with scope(ctx):
        stg = [ctx.sb(f"s1w_stg{i}", 128, F0) for i in range(2)]
        ob = [ctx.sb(f"s1w_ob{i}", 128, F0, BF16) for i in range(2)]
        i = 0
        for e in range(E0):
            for k in range(KT):
                s_ = stg[i % 2]; o_ = ob[i % 2]
                P.load(s_, w1.t[e, k * 128:(k + 1) * 128, :], r=[w1], w=[s_],
                       q=("sync" if i % 2 == 0 else "gpsimd"))
                vcopy(P, o_.ap, s_.ap, r=[s_], w=[o_], eng=("v" if i % 2 == 0 else "s"))
                P.store(w1b.t[e, k, :, :], o_.ap, w1b, r=[o_],
                        q=("gpsimd" if i % 2 == 0 else "sync"))
                i += 1
        for e in range(E0):
            for f in range(32):
                s_ = stg[i % 2]; o_ = ob[i % 2]
                P.load(TB(s_.t[:, 0:D], "stg"), w2.t[e, f * 128:(f + 1) * 128, :],
                       r=[w2], w=[s_], q=("sync" if i % 2 == 0 else "gpsimd"))
                vcopy(P, o_.t[:, 0:D], s_.t[:, 0:D], r=[s_], w=[o_],
                      eng=("v" if i % 2 == 0 else "s"))
                P.store(w2b.t[e, f, :, :], o_.t[:, 0:D], w2b, r=[o_],
                        q=("gpsimd" if i % 2 == 0 else "sync"))
                i += 1
    P.barrier()


def emit_s1e(P, ctx):
    x1T = ctx.dscratch("x1T", [B, D, T], F32)
    hlnT1b = ctx.dscratch("hlnT1b", [B, D, T], BF16)
    smaskT = ctx.dscratch("smaskT", [B, E0, T], F32)
    w1b = ctx.dscratch("w1b", [E0, KT, 128, F0], BF16)
    w2b = ctx.dscratch("w2b", [E0, 32, 128, D], BF16)
    x2T = ctx.dscratch("x2T", [B, D, T], F32)
    b1d = ctx.din("L0_ffn_b1", [E0, F0])
    b2d = ctx.din("L0_ffn_b2", [E0, D])
    sr8 = ctx.const("sel_row8")
    with scope(ctx):
        x2a = [ctx.sb(f"s1e_x2a{k}", 128, T) for k in range(KT)]
        smk = ctx.sb("s1e_smk", E0, T)
        b1a = ctx.sb("s1e_b1a", 128, 256)
        b2a = ctx.sb("s1e_b2a", 128, 64)
        P.load(rr2(b1a, "p (e a) -> p e a", e=8),
               b1d.t[:, :].rearrange("e (a p) -> p e a", p=128), r=[b1d], w=[b1a], slow=True)
        P.load(rr2(b2a, "p (e a) -> p e a", e=8),
               b2d.t[:, :].rearrange("e (a p) -> p e a", p=128), r=[b2d], w=[b2a], slow=True)
        w1h = [ctx.sb(f"s1e_w1h{k}", 128, 2048, BF16) for k in range(KT)]
        w2h = [ctx.sb(f"s1e_w2h{j}", 128, D, BF16) for j in range(16)]
        hlb = [ctx.sb(f"s1e_hl{k}", 128, PAN, BF16) for k in range(KT)]
        hbf = [ctx.sb(f"s1e_hb{j}", 128, PAN, BF16) for j in range(16)]
        smbc = ctx.sb("s1e_smbc", 128, PAN)
        tmx = ctx.sb("s1e_tmx", 128, PAN)
        ps_h = [ctx.ps(f"s1e_psh{i}", 128, PAN) for i in range(2)]
        ps_y = [ctx.ps(f"s1e_psy{i}", 128, PAN) for i in range(2)]
        ps_sm = ctx.ps("s1e_pssm", 128, PAN)
        for b in range(B):
            for k in range(KT):
                P.load(x2a[k], x1T.t[b, k * 128:(k + 1) * 128, :], r=[x1T], w=[x2a[k]],
                       q=("sync" if k % 2 == 0 else "gpsimd"))
            P.load(smk, smaskT.t[b, :, :], r=[smaskT])
            for e in range(E0):
                for fh in range(2):
                    for k in range(KT):
                        P.load(w1h[k], w1b.t[e, k, :, fh * 2048:(fh + 1) * 2048],
                               r=[w1b], w=[w1h[k]], q=("sync" if k % 2 == 0 else "gpsimd"))
                    for j in range(16):
                        P.load(w2h[j], w2b.t[e, fh * 16 + j, :, :], r=[w2b], w=[w2h[j]],
                               q=("sync" if j % 2 == 0 else "gpsimd"))
                    for p in range(NPAN):
                        sl_t = slice(p * PAN, (p + 1) * PAN)
                        for k in range(KT):
                            P.load(hlb[k], hlnT1b.t[b, k * 128:(k + 1) * 128, sl_t],
                                   r=[hlnT1b], w=[hlb[k]],
                                   q=("sync" if k % 2 == 0 else "gpsimd"))
                        mm(P, ps_sm.ap, sr8[:, e * 128:(e + 1) * 128], smk[:, sl_t],
                           r=[sr8, smk], w=[ps_sm])
                        vcopy(P, smbc.ap, ps_sm.ap, r=[ps_sm], w=[smbc])
                        for ft in range(16):
                            pj = ps_h[ft % 2]
                            for k in range(KT):
                                mm(P, pj.ap, w1h[k][:, ft * 128:(ft + 1) * 128], hlb[k].ap,
                                   r=[w1h[k], hlb[k]], w=[pj], start=(k == 0),
                                   stop=(k == KT - 1))
                            act(P, hbf[ft].ap, pj.ap, AF.Gelu, r=[pj, b1a], w=[hbf[ft]],
                                bias=b1a[:, e * 32 + fh * 16 + ft:e * 32 + fh * 16 + ft + 1])
                        for mt in range(KT):
                            pj = ps_y[mt % 2]
                            for j in range(16):
                                mm(P, pj.ap, w2h[j][:, mt * 128:(mt + 1) * 128], hbf[j].ap,
                                   r=[w2h[j], hbf[j]], w=[pj], start=(j == 0),
                                   stop=(j == 15))
                            if fh == 1:
                                def fb(e_, pj=pj, mt=mt, e=e):
                                    return e_.scalar_tensor_tensor(
                                        tmx.ap, pj.ap, b2a[:, e * 8 + mt:e * 8 + mt + 1],
                                        smbc.ap, OP.add, OP.mult)
                                P.v(fb, r=[pj, b2a, smbc], w=[tmx])
                            else:
                                tt(P, tmx.ap, pj.ap, smbc.ap, OP.mult, r=[pj, smbc],
                                   w=[tmx])
                            tt(P, x2a[mt][:, sl_t], x2a[mt][:, sl_t], tmx.ap, OP.add,
                               r=[x2a[mt], tmx], w=[x2a[mt]])
            for k in range(KT):
                P.store(x2T.t[b, k * 128:(k + 1) * 128, :], x2a[k].ap, x2T, r=[x2a[k]],
                        q=("gpsimd" if k % 2 == 0 else "sync"))
            P.barrier()
    P.barrier()


def emit_s2(P, ctx):
    x2T = ctx.dscratch("x2T", [B, D, T], F32)
    x3T = ctx.dscratch("x3T", [B, D, T], F32)
    Wq = ctx.din("L1_attn_Wq", [H1, D, DH1])
    Wk = ctx.din("L1_attn_Wk", [H1, D, DH1])
    Wv = ctx.din("L1_attn_Wv", [H1, D, DH1])
    Wo = ctx.din("L1_attn_Wo", [D, D])
    bo = ctx.din("L1_attn_bo", [D])
    eye = ctx.const("eye_f")
    eyeb = ctx.const("eye_b")
    ocb = ctx.const("ones_col_b")
    orow = ctx.const("ones_row_f")
    m_ui = ctx.const("mask_ui4")
    with scope(ctx):
        ps_s = [ctx.ps(f"s2_pss{i}", 128, PAN) for i in range(2)]
        ps_w2 = ctx.ps("s2_psw2", 128, 1024)
        ps_sum = ctx.ps("s2_pssum", 1, PAN)
        ps_t = ctx.ps("s2_pst", 128, 64, BF16)
        xs = [ctx.sb(f"s2_x{k}", 128, PAN) for k in range(KT)]
        hlb = [ctx.sb(f"s2_hl{k}", 128, T, BF16) for k in range(KT)]
        hpan = [TB(hlb[k].t, f"alias_hl{k}") for k in range(KT)]
        la = {"tmp": ctx.sb("s2_lntmp", 128, PAN), "row": ctx.sb("s2_lnrow", 1, 4 * PAN),
              "psum": ps_sum, "psq": ps_s[1], "pbc": ps_w2}
        qw = {}
        wstg = ctx.sb("s2_wstg", 128, 512)
        for nm, Wd in (("q", Wq), ("k", Wk), ("v", Wv)):
            for h in range(H1):
                t_ = ctx.sb(f"s2_w{nm}{h}", 128, 512, BF16)
                P.load(rr2(wstg, "p (a j) -> p a j", j=DH1),
                       Wd.t[h].rearrange("(a p) j -> p a j", p=128),
                       r=[Wd], w=[wstg], slow=True,
                       q=("sync" if h % 2 == 0 else "gpsimd"))
                vcopy(P, t_.ap, wstg.ap, r=[wstg], w=[t_],
                      eng=("v" if h % 2 == 0 else "s"))
                qw[(nm, h)] = t_
        woT = [ctx.sb(f"s2_woT{k}", 128, 1024, BF16) for k in range(KT)]
        wst2 = ctx.sb("s2_wst2", 128, 1024)
        for m in range(KT):
            P.load(wst2, Wo.t[m * 128:(m + 1) * 128, :], r=[Wo], w=[wst2])
            for k in range(KT):
                transpose(P, ps_w2[:, 0:128], wst2[:, k * 128:(k + 1) * 128], eye,
                          r=[wst2], w=[ps_w2])
                vcopy(P, woT[k][:, m * 128:(m + 1) * 128], ps_w2[:, 0:128],
                      r=[ps_w2], w=[woT[k]])
        bos = ctx.sb("s2_bos", 128, KT)
        P.load(bos, bo.t[:].rearrange("(a p) -> p a", p=128), r=[bo], slow=True)
        qT = ctx.sb("s2_qT", 64, T, BF16)
        kT = ctx.sb("s2_kT", 64, T, BF16)
        vTs = ctx.sb("s2_vT", 64, T, BF16)
        vn = [ctx.sb(f"s2_vn{tt_}", 128, 64, BF16) for tt_ in range(T // 128)]
        pex = ctx.sb("s2_pex", 128, PAN, BF16)
        srow = ctx.sb("s2_srow", 1, PAN)
        of32 = ctx.sb("s2_of32", 64, PAN)
        oall = [ctx.sb(f"s2_oa{k}", 128, T, BF16) for k in range(KT)]
        x3s = ctx.sb("s2_x3s", 128, PAN)
        for b in range(B):
            for p in range(NPAN):
                for k in range(KT):
                    P.load(xs[k], x2T.t[b, k * 128:(k + 1) * 128, p * PAN:(p + 1) * PAN],
                           r=[x2T], q=("sync" if k % 2 == 0 else "gpsimd"))
                dsts = [View(hlb[k].t[:, p * PAN:(p + 1) * PAN], hlb[k].b)
                        for k in range(KT)]
                emit_ln_panel(P, ctx, xs, dsts, la["tmp"], la["psum"], la["psq"],
                              la["row"], la["pbc"])
            for h in range(H1):
                for nm, dst in (("q", qT), ("k", kT), ("v", vTs)):
                    w_ = qw[(nm, h)]
                    for p in range(NPAN):
                        pj = ps_s[p % 2]
                        for k in range(KT):
                            mm(P, pj[0:64, :], w_[:, k * 64:(k + 1) * 64],
                               hlb[k][:, p * PAN:(p + 1) * PAN], r=[w_, hlb[k]], w=[pj],
                               start=(k == 0), stop=(k == KT - 1))
                        vcopy(P, dst[0:64, p * PAN:(p + 1) * PAN], pj[0:64, :],
                              r=[pj], w=[dst])
                for tt_ in range(T // 128):
                    transpose(P, ps_t[:, 0:64], vTs[0:64, tt_ * 128:(tt_ + 1) * 128],
                              eyeb, r=[vTs], w=[ps_t], kdim=64)
                    vcopy(P, vn[tt_].ap, ps_t[:, 0:64], r=[ps_t], w=[vn[tt_]])
                for ip in range(NPAN):
                    for jt in range(4 * ip + 4):
                        off = max(0, jt - 4 * ip)
                        cols = PAN - off * 128
                        pj = ps_s[jt % 2]
                        mm(P, pj[:, 0:cols], kT[0:64, jt * 128:(jt + 1) * 128],
                           qT[0:64, ip * PAN + off * 128:(ip + 1) * PAN],
                           r=[kT, qT], w=[pj])
                        act(P, pex[:, 0:cols], pj[:, 0:cols], AF.Exp, r=[pj], w=[pex],
                            scale=0.125)
                        if jt >= 4 * ip:
                            tt(P, pex[:, 0:128], pex[:, 0:128], m_ui[:, 0:128], OP.mult,
                               r=[pex, m_ui], w=[pex])
                        mm(P, ps_sum[0:1, off * 128:PAN], ocb[:, 0:1], pex[:, 0:cols],
                           r=[pex, ocb], w=[ps_sum], start=(jt == 0), stop=True)
                        mm(P, ps_w2[0:64, off * 128:PAN], vn[jt].ap, pex[:, 0:cols],
                           r=[vn[jt], pex], w=[ps_w2], start=(jt == 0), stop=True)
                    vcopy(P, srow.ap, ps_sum[0:1, :], r=[ps_sum], w=[srow])
                    recip(P, srow.ap, srow.ap, r=[srow], w=[srow])
                    mm(P, ps_w2[0:64, 512:1024], orow[0:1, 0:64], srow.ap,
                       r=[srow, orow], w=[ps_w2])
                    vcopy(P, of32.ap, ps_w2[0:64, 0:512], r=[ps_w2], w=[of32])
                    kk = h // 2; po = (h % 2) * 64
                    tt(P, oall[kk].t[po:po + 64, ip * PAN:(ip + 1) * PAN], of32.ap,
                       ps_w2[0:64, 512:1024], OP.mult, r=[of32, ps_w2], w=[oall[kk]])
            for p in range(NPAN):
                sl_t = slice(p * PAN, (p + 1) * PAN)
                for k in range(KT):
                    P.load(xs[k], x2T.t[b, k * 128:(k + 1) * 128, sl_t], r=[x2T],
                           q=("sync" if k % 2 == 0 else "gpsimd"))
                for mt in range(KT):
                    pj = ps_s[mt % 2]
                    for k in range(KT):
                        mm(P, pj.ap, woT[k][:, mt * 128:(mt + 1) * 128],
                           oall[k][:, sl_t], r=[woT[k], oall[k]], w=[pj],
                           start=(k == 0), stop=(k == KT - 1))
                    def fo(e, pj=pj, mt=mt):
                        return e.scalar_tensor_tensor(
                            x3s.ap, pj.ap, bos[:, mt:mt + 1], xs[mt].ap, OP.add, OP.add)
                    P.v(fo, r=[pj, bos, xs[mt]], w=[x3s])
                    P.store(x3T.t[b, mt * 128:(mt + 1) * 128, sl_t], x3s.ap, x3T, r=[x3s])
            P.barrier()
    P.barrier()


def emit_s3a(P, ctx):
    x3T = ctx.dscratch("x3T", [B, D, T], F32)
    hlnT3b = ctx.dscratch("hlnT3b", [B, D, T], BF16)
    gsc = ctx.dscratch("gsc", [B, E1, 1], F32)
    gw1 = ctx.din("L1_ffn_gate_w1", [D, D])
    gb1_ = ctx.din("L1_ffn_gate_b1", [D])
    gw2_ = ctx.din("L1_ffn_gate_w2", [E1, D])
    gb2_ = ctx.din("L1_ffn_gate_b2", [E1])
    eye = ctx.const("eye_f")
    with scope(ctx):
        xs = [ctx.sb(f"s3a_x{k}", 128, PAN) for k in range(KT)]
        hlb = [ctx.sb(f"s3a_h{k}", 128, PAN, BF16) for k in range(KT)]
        la = _ln_alloc(ctx, "s3a")
        wst = ctx.sb("s3a_wst", 128, 1024)
        gw1T = [ctx.sb(f"s3a_g1T{k}", 128, 1024, BF16) for k in range(KT)]
        pst = ctx.ps("s3a_pst", 128, 128)
        for o in range(KT):
            P.load(wst, gw1.t[o * 128:(o + 1) * 128, :], r=[gw1], w=[wst])
            for k in range(KT):
                transpose(P, pst.ap, wst[:, k * 128:(k + 1) * 128], eye, r=[wst], w=[pst])
                vcopy(P, gw1T[k][:, o * 128:(o + 1) * 128], pst.ap, r=[pst], w=[gw1T[k]])
        gw2s = ctx.sb("s3a_gw2s", E1, 1024)
        P.load(gw2s, gw2_.ap, r=[gw2_])
        gw2T = [ctx.sb(f"s3a_g2T{k}", 128, E1, BF16) for k in range(KT)]
        for k in range(KT):
            transpose(P, pst[:, 0:E1], gw2s[:, k * 128:(k + 1) * 128], eye,
                      r=[gw2s], w=[pst], kdim=E1)
            vcopy(P, gw2T[k].ap, pst[:, 0:E1], r=[pst], w=[gw2T[k]])
        gb1c = ctx.sb("s3a_gb1c", 128, KT)
        P.load(gb1c, gb1_.t[:].rearrange("(a p) -> p a", p=128), r=[gb1_], slow=True)
        gb2c = ctx.sb("s3a_gb2c", E1, 1)
        P.load(gb2c, gb2_.t[:].rearrange("(e o) -> e o", o=1), r=[gb2_], slow=True)
        h1b = [ctx.sb(f"s3a_h1{k}", 128, PAN, BF16) for k in range(KT)]
        gac = ctx.sb("s3a_gac", E1, 1)
        gtmp = ctx.sb("s3a_gtmp", E1, 1)
        ps_m = [ctx.ps(f"s3a_psm{i}", 128, PAN) for i in range(2)]
        ps_g8 = ctx.ps("s3a_psg8", E1, PAN)
        for b in range(B):
            memset(P, gac.ap, 0.0, w=[gac])
            for p in range(NPAN):
                sl_t = slice(p * PAN, (p + 1) * PAN)
                for k in range(KT):
                    P.load(xs[k], x3T.t[b, k * 128:(k + 1) * 128, sl_t], r=[x3T],
                           q=("sync" if k % 2 == 0 else "gpsimd"))
                emit_ln_to(P, ctx, la, xs, hlb)
                for k in range(KT):
                    P.store(hlnT3b.t[b, k * 128:(k + 1) * 128, sl_t], hlb[k].ap,
                            hlnT3b, r=[hlb[k]])
                for mt in range(KT):
                    pj = ps_m[mt % 2]
                    for k in range(KT):
                        mm(P, pj.ap, gw1T[k][:, mt * 128:(mt + 1) * 128], hlb[k].ap,
                           r=[gw1T[k], hlb[k]], w=[pj], start=(k == 0), stop=(k == KT - 1))
                    act(P, h1b[mt].ap, pj.ap, AF.Relu, r=[pj, gb1c], w=[h1b[mt]],
                        bias=gb1c[:, mt:mt + 1])
                for k in range(KT):
                    mm(P, ps_g8.ap, gw2T[k].ap, h1b[k].ap, r=[gw2T[k], h1b[k]], w=[ps_g8],
                       start=(k == 0), stop=(k == KT - 1))
                reduce(P, gtmp.ap, ps_g8.ap, OP.add, r=[ps_g8], w=[gtmp])
                tt(P, gac.ap, gac.ap, gtmp.ap, OP.add, r=[gac, gtmp], w=[gac])
            ts(P, gac.ap, gac.ap, 1.0 / T, OP.mult, r=[gac], w=[gac])
            tt(P, gac.ap, gac.ap, gb2c.ap, OP.add, r=[gac, gb2c], w=[gac])
            P.store(gsc.t[b], gac.ap, gsc, r=[gac])
    P.barrier()


def emit_s3b(P, ctx):
    gsc = ctx.dscratch("gsc", [B, E1, 1], F32)
    mw1b = ctx.dscratch("mw1b", [B, KT, 128, F1], BF16)
    mw2b = ctx.dscratch("mw2b", [B, 32, 128, D], BF16)
    mb1d = ctx.dscratch("mb1d", [B, 128, 32], F32)
    mb2d = ctx.dscratch("mb2d", [B, 128, E1], F32)
    W1 = ctx.din("L1_ffn_W1", [F1, D])
    W2 = ctx.din("L1_ffn_W2", [D, F1])
    TW1 = ctx.din("L1_ffn_TW1", [E1, F1, D])
    TW2 = ctx.din("L1_ffn_TW2", [E1, D, F1])
    b1_ = ctx.din("L1_ffn_b1", [F1])
    b2_ = ctx.din("L1_ffn_b2", [D])
    Tb1 = ctx.din("L1_ffn_Tb1", [E1, F1])
    Tb2 = ctx.din("L1_ffn_Tb2", [E1, D])
    eye = ctx.const("eye_f")
    sr8 = ctx.const("sel_row8")
    with scope(ctx):
        g8 = ctx.sb("s3b_g8", E1, 1)
        gcol = ctx.sb("s3b_gcol", 128, E1)
        dgs = [ctx.sb(f"s3b_dg{e}", 128, 128) for e in range(E1)]
        stg = [ctx.sb(f"s3b_stg{i}", 128, F1) for i in range(2)]
        gat = ctx.sb("s3b_gat", 128, F1, BF16)
        ps32 = ctx.ps("s3b_ps32", 128, 4096)
        tb1s = ctx.sb("s3b_tb1s", E1, F1)
        tb2s = ctx.sb("s3b_tb2s", E1, D)
        bvec = ctx.sb("s3b_bvec", 128, 32)
        mb = ctx.sb("s3b_mb", 128, 32)
        P.load(tb1s, Tb1.ap, r=[Tb1])
        P.load(tb2s, Tb2.ap, r=[Tb2])
        for b in range(B):
            P.load(g8, gsc.t[b], r=[gsc], w=[g8])
            for e in range(E1):
                mm(P, ps32[:, e:e + 1], sr8[:, e * 128:(e + 1) * 128], g8.ap,
                   r=[g8, sr8], w=[ps32])
            vcopy(P, gcol.ap, ps32[:, 0:E1], r=[ps32], w=[gcol])
            for e in range(E1):
                ts(P, dgs[e].ap, eye.ap, gcol[:, e:e + 1], OP.mult, r=[eye, gcol],
                   w=[dgs[e]])
            for ft in range(32):
                s_ = stg[0]
                P.load(View(s_.t[:, 0:D], s_.b), W1.t[ft * 128:(ft + 1) * 128, :],
                       r=[W1], w=[s_])
                for hc in range(KT):
                    mm(P, ps32[:, hc * 128:(hc + 1) * 128],
                       s_[:, hc * 128:(hc + 1) * 128], eye.ap, r=[s_, eye], w=[ps32],
                       start=True, stop=False)
                for e in range(E1):
                    s_ = stg[(e + 1) % 2]
                    P.load(View(s_.t[:, 0:D], s_.b),
                           TW1.t[e, ft * 128:(ft + 1) * 128, :], r=[TW1], w=[s_],
                           q=("sync" if e % 2 == 0 else "gpsimd"))
                    for hc in range(KT):
                        mm(P, ps32[:, hc * 128:(hc + 1) * 128],
                           s_[:, hc * 128:(hc + 1) * 128], dgs[e].ap, r=[s_, dgs[e]],
                           w=[ps32], start=False, stop=(e == E1 - 1))
                vcopy(P, gat[:, 0:1024], ps32[:, 0:1024], r=[ps32], w=[gat])
                for hc in range(KT):
                    P.store(mw1b.t[b, hc, :, ft * 128:(ft + 1) * 128],
                            gat.t[:, hc * 128:(hc + 1) * 128], mw1b, r=[gat],
                            q=("gpsimd" if hc % 2 == 0 else "sync"))
            for ht in range(KT):
                s_ = stg[0]
                P.load(s_, W2.t[ht * 128:(ht + 1) * 128, :], r=[W2], w=[s_])
                for fc in range(32):
                    mm(P, ps32[:, fc * 128:(fc + 1) * 128],
                       s_[:, fc * 128:(fc + 1) * 128], eye.ap, r=[s_, eye], w=[ps32],
                       start=True, stop=False)
                for e in range(E1):
                    s_ = stg[(e + 1) % 2]
                    P.load(s_, TW2.t[e, ht * 128:(ht + 1) * 128, :], r=[TW2], w=[s_],
                           q=("sync" if e % 2 == 0 else "gpsimd"))
                    for fc in range(32):
                        mm(P, ps32[:, fc * 128:(fc + 1) * 128],
                           s_[:, fc * 128:(fc + 1) * 128], dgs[e].ap, r=[s_, dgs[e]],
                           w=[ps32], start=False, stop=(e == E1 - 1))
                vcopy(P, gat.ap, ps32.ap, r=[ps32], w=[gat])
                for fc in range(32):
                    P.store(mw2b.t[b, fc, :, ht * 128:(ht + 1) * 128],
                            gat.t[:, fc * 128:(fc + 1) * 128], mw2b, r=[gat],
                            q=("gpsimd" if fc % 2 == 0 else "sync"))
            P.load(bvec, b1_.t[:].rearrange("(a p) -> p a", p=128), r=[b1_], w=[bvec],
                   slow=True)
            for ft in range(32):
                mm(P, ps32[:, ft:ft + 1], tb1s[:, ft * 128:(ft + 1) * 128], g8.ap,
                   r=[tb1s, g8], w=[ps32], start=True, stop=True)
            tt(P, mb.ap, ps32[:, 0:32], bvec.ap, OP.add, r=[ps32, bvec], w=[mb])
            P.store(mb1d.t[b], mb.ap, mb1d, r=[mb])
            P.load(View(bvec.t[:, 0:E1], bvec.b),
                   b2_.t[:].rearrange("(a p) -> p a", p=128), r=[b2_], w=[bvec],
                   slow=True)
            for ht in range(KT):
                mm(P, ps32[:, ht:ht + 1], tb2s[:, ht * 128:(ht + 1) * 128], g8.ap,
                   r=[tb2s, g8], w=[ps32], start=True, stop=True)
            tt(P, mb[:, 0:E1], ps32[:, 0:E1], bvec[:, 0:E1], OP.add, r=[ps32, bvec],
               w=[mb])
            P.store(mb2d.t[b], mb.t[:, 0:E1], mb2d, r=[mb])
            P.barrier()
    P.barrier()


def emit_s3c(P, ctx):
    x3T = ctx.dscratch("x3T", [B, D, T], F32)
    hlnT3b = ctx.dscratch("hlnT3b", [B, D, T], BF16)
    mw1b = ctx.dscratch("mw1b", [B, KT, 128, F1], BF16)
    mw2b = ctx.dscratch("mw2b", [B, 32, 128, D], BF16)
    mb1d = ctx.dscratch("mb1d", [B, 128, 32], F32)
    mb2d = ctx.dscratch("mb2d", [B, 128, E1], F32)
    x4T = ctx.dscratch("x4T", [B, D, T], F32)
    with scope(ctx):
        ya = [ctx.sb(f"s3c_ya{k}", 128, T) for k in range(KT)]
        mw1s = [ctx.sb(f"s3c_w1{k}", 128, PAN, BF16) for k in range(KT)]
        mw2s = [ctx.sb(f"s3c_w2{j}", 128, D, BF16) for j in range(4)]
        hlb = [ctx.sb(f"s3c_hl{k}", 128, PAN, BF16) for k in range(KT)]
        ubf = [ctx.sb(f"s3c_u{j}", 128, PAN, BF16) for j in range(4)]
        mb1s = ctx.sb("s3c_mb1", 128, 32)
        mb2s = ctx.sb("s3c_mb2", 128, E1)
        xps = [ctx.sb(f"s3c_xp{k}", 128, PAN) for k in range(KT)]
        x4s = ctx.sb("s3c_x4s", 128, PAN)
        ps_u = [ctx.ps(f"s3c_psu{i}", 128, PAN) for i in range(2)]
        ps_y = [ctx.ps(f"s3c_psy{i}", 128, PAN) for i in range(2)]
        for b in range(B):
            P.load(mb1s, mb1d.t[b], r=[mb1d], w=[mb1s])
            P.load(mb2s, mb2d.t[b], r=[mb2d], w=[mb2s])
            for fc in range(KT):
                for k in range(KT):
                    P.load(mw1s[k], mw1b.t[b, k, :, fc * PAN:(fc + 1) * PAN],
                           r=[mw1b], w=[mw1s[k]], q=("sync" if k % 2 == 0 else "gpsimd"))
                for j in range(4):
                    P.load(mw2s[j], mw2b.t[b, fc * 4 + j, :, :], r=[mw2b], w=[mw2s[j]],
                           q=("sync" if j % 2 == 0 else "gpsimd"))
                for p in range(NPAN):
                    sl_t = slice(p * PAN, (p + 1) * PAN)
                    for k in range(KT):
                        P.load(hlb[k], hlnT3b.t[b, k * 128:(k + 1) * 128, sl_t],
                               r=[hlnT3b], w=[hlb[k]],
                               q=("sync" if k % 2 == 0 else "gpsimd"))
                    for fmt in range(4):
                        pj = ps_u[fmt % 2]
                        for k in range(KT):
                            mm(P, pj.ap, mw1s[k][:, fmt * 128:(fmt + 1) * 128], hlb[k].ap,
                               r=[mw1s[k], hlb[k]], w=[pj], start=(k == 0),
                               stop=(k == KT - 1))
                        ftg = fc * 4 + fmt
                        act(P, ubf[fmt].ap, pj.ap, AF.Gelu_apprx_tanh, r=[pj, mb1s],
                            w=[ubf[fmt]], bias=mb1s[:, ftg:ftg + 1])
                    for mt in range(KT):
                        pj = ps_y[mt % 2]
                        for j in range(4):
                            mm(P, pj.ap, mw2s[j][:, mt * 128:(mt + 1) * 128], ubf[j].ap,
                               r=[mw2s[j], ubf[j]], w=[pj], start=(j == 0), stop=(j == 3))
                        if fc == 0:
                            vcopy(P, ya[mt][:, sl_t], pj.ap, r=[pj], w=[ya[mt]])
                        else:
                            tt(P, ya[mt][:, sl_t], ya[mt][:, sl_t], pj.ap, OP.add,
                               r=[ya[mt], pj], w=[ya[mt]])
            for p in range(NPAN):
                sl_t = slice(p * PAN, (p + 1) * PAN)
                for k in range(KT):
                    P.load(xps[k], x3T.t[b, k * 128:(k + 1) * 128, sl_t], r=[x3T],
                           w=[xps[k]], q=("sync" if k % 2 == 0 else "gpsimd"))
                for mt in range(KT):
                    def fx(e, mt=mt, sl_t=sl_t):
                        return e.scalar_tensor_tensor(
                            x4s.ap, ya[mt][:, sl_t], mb2s[:, mt:mt + 1], xps[mt].ap,
                            OP.add, OP.add)
                    P.v(fx, r=[ya[mt], mb2s, xps[mt]], w=[x4s])
                    P.store(x4T.t[b, mt * 128:(mt + 1) * 128, sl_t], x4s.ap, x4T, r=[x4s])
            P.barrier()
    P.barrier()


STAGES = None

def _default_launch_plan():
    return [("s_in", emit_s_in),
            ("s0a", emit_s0a),
            ("s0a2", emit_s0a2),
            ("s0b", emit_s0b),
            ("s0c1", emit_s0c1),
            ("s0c2", emit_s0c2),
            ("s1g", emit_s1g),
            ("s1w", emit_s1w),
            ("s1e", emit_s1e),
            ("s2", emit_s2),
            ("s3a", emit_s3a),
            ("s3b", emit_s3b),
            ("s3c", emit_s3c),
            ("s_out", emit_s_out)]

def _run_launch(name, emitters, carried, mode):
    nc = bass.Bass("TRN2", target_bir_lowering=False)
    with ExitStack() as stack:
        P = Prog(nc)
        ctx = Ctx(nc, stack, P, mode, carried)
        if mode == "multi":
            ctx._pre = carried
            def dscratch(nm, shape, dtype=F32, _c=ctx):
                if nm not in _c.dram:
                    if nm in carried:
                        t = _c.nc.dram_tensor(nm, list(shape), dtype, kind="ExternalInput")
                        _c.dram[nm] = TB(t.ap(), nm)
                    else:
                        t = _c.nc.dram_tensor(nm, list(shape), dtype, kind="ExternalOutput")
                        _c.dram[nm] = TB(t.ap(), nm)
                        _c.outputs.add(nm)
                return _c.dram[nm]
            ctx.dscratch = dscratch
        for nm in CONST_SPECS:
            ctx.const(nm)
        fv = ctx.const("fvals")
        P.fconst = fv
        for i, val in enumerate([1e-5, 1e-6, -1.0, 0.125, 1.0 / 256.0]):
            nc.const_aps.register(F32, val, fv.t[:, i:i + 1])
        for em in emitters:
            em(P, ctx)
        P.build(stack)
    in_map = {}
    for nm, tb in ctx.dram.items():
        if nm in carried:
            in_map[nm] = carried[nm]
        elif nm.startswith("c_"):
            in_map[nm] = np.ascontiguousarray(CONST_SPECS[nm[2:]])
    in_maps = [dict(in_map) if i == 0 else {k: v for k, v in in_map.items()}
               for i in range(NCORES)]
    for i in range(NCORES):
        for k in list(in_maps[i].keys()):
            v = in_maps[i][k]
            if isinstance(v, list):
                in_maps[i][k] = v[i]
    res = run_bass_kernel_spmd(nc, in_maps, core_ids=list(range(NCORES)))
    outs = {}
    for nm in ctx.outputs:
        outs[nm] = [res.results[i][nm] for i in range(NCORES)]
    return outs


def run_pipeline(inputs, plan):
    x = np.ascontiguousarray(inputs["L0_attn_hs"])
    carried = {"x": [np.ascontiguousarray(x[2 * i:2 * i + 2]) for i in range(NCORES)]}
    for k, v in inputs.items():
        carried[k] = np.ascontiguousarray(v)
    for name, em in plan:
        ems = em if isinstance(em, list) else [em]
        outs = _run_launch(name, ems, carried, mode="multi")
        for nm, percore in outs.items():
            carried[nm] = percore
    return carried


FUSED = True

def kernel(**inputs):
    plan = STAGES if STAGES is not None else _default_launch_plan()
    if FUSED and STAGES is None:
        x = np.ascontiguousarray(inputs["L0_attn_hs"])
        carried = {"x": [np.ascontiguousarray(x[2 * i:2 * i + 2])
                         for i in range(NCORES)]}
        for k, v in inputs.items():
            carried[k] = np.ascontiguousarray(v)
        outs = _run_launch("mega", [em for _, em in plan], carried, mode="fused")
        return np.concatenate(outs["out"], axis=0)
    carried = run_pipeline(inputs, plan)
    return np.concatenate(carried["out"], axis=0)


# version 07
# speedup vs baseline: 1.1461x; 1.1461x over previous
import numpy as np
import ml_dtypes
from contextlib import ExitStack

import concourse.bass as bass
import concourse.mybir as mybir
from concourse.bass_utils import run_bass_kernel_spmd

F32 = mybir.dt.float32
BF16 = mybir.dt.bfloat16
AF = mybir.ActivationFunctionType
OP = mybir.AluOpType
AX = mybir.AxisListType
NPBF = ml_dtypes.bfloat16

NCORES = 8
B = 2
T = 2048
D = 1024
KT = D // 128
PAN = 512
NPAN = T // PAN
NH, DK, DV = 4, 256, 256
CH = 128
NCH = T // CH
E0, F0 = 8, 4096
H1, DH1 = 16, 64
F1, E1 = 4096, 8


class _Op:
    __slots__ = ("eng", "fn", "waits", "inc", "amount", "val", "kind", "semkey")
    def __init__(self, eng, fn, kind):
        self.eng = eng; self.fn = fn; self.waits = []
        self.inc = False; self.amount = 1; self.val = None; self.kind = kind
        self.semkey = None

class Buf:
    __slots__ = ("name", "writer", "readers")
    def __init__(self, name):
        self.name = name; self.writer = None; self.readers = []

class TB:
    __slots__ = ("t", "b")
    def __init__(self, t, name):
        self.t = t; self.b = Buf(name)
    def __getitem__(self, idx):
        return self.t[idx]
    @property
    def ap(self):
        return self.t[:]

class View:
    __slots__ = ("_ap", "b")
    def __init__(self, ap, buf):
        self._ap = ap; self.b = buf
    @property
    def ap(self):
        return self._ap


def rr2(tb, pat, **kw):
    return View(tb.t[:, :].rearrange(pat, **kw), tb.b)

def _buf(x):
    return getattr(x, "b", x)

class Prog:
    ENGS = ("tensor", "vector", "scalar", "gpsimd", "sync")
    NDS = 4

    def __init__(self, nc):
        self.nc = nc
        self.ops = {e: [] for e in self.ENGS}
        self.dma_n = {e: 0 for e in self.ENGS}
        self.dma_sem_tot = {}
        self.pbank = {}
        self.pbufs = {}

    def _emit(self, eng, fn, r=(), w=(), kind="eng", amount=1):
        op = _Op(eng, fn, kind)
        op.amount = amount
        assert (kind == "dma") == (eng in ("sync", "gpsimd")), (kind, eng)
        if kind == "dma":
            idx = self.dma_n[eng]
            self.dma_n[eng] = idx + 1
            op.semkey = (eng, idx % self.NDS)
            op.val = 16 * (idx // self.NDS + 1)
            op.inc = True
            if op.val > 16:
                op.waits.append(("dma", op.semkey, op.val - 16))
            self.dma_sem_tot[op.semkey] = op.val
        deps = []
        for x in r:
            b = _buf(x)
            if b.name in self.pbufs:
                for kk in self.pbufs[b.name]:
                    if kk in self.pbank:
                        self.pbank[kk]["stale"] = True
            if b.writer is not None:
                deps.append(b.writer)
        for x in w:
            b = _buf(x)
            if b.writer is not None:
                deps.append(b.writer)
            deps.extend(b.readers)
        seen = set()
        dma_wait = {}
        for d in deps:
            if d is op or id(d) in seen:
                continue
            seen.add(id(d))
            if d.kind == "dma":
                k = d.semkey
                dma_wait[k] = max(dma_wait.get(k, 0), d.val)
                continue
            if d.kind == "eng" and d.eng == eng == "tensor":
                continue
            d.inc = True
            op.waits.append(d)
        for k, v in dma_wait.items():
            if op.kind == "dma" and k == op.semkey:
                op.waits = [wt for wt in op.waits
                            if not (isinstance(wt, tuple) and wt[1] == k and wt[2] < v)]
                if not any(isinstance(wt, tuple) and wt[1] == k and wt[2] >= v
                           for wt in op.waits):
                    op.waits.append(("dma", k, v))
            else:
                op.waits.append(("dma", k, v))
        self.ops[eng].append(op)
        for x in r:
            _buf(x).readers.append(op)
        for x in w:
            b = _buf(x)
            b.writer = op
            b.readers = []
        return op

    def t(self, fn, r=(), w=()):
        return self._emit("tensor", fn, r, w)
    def v(self, fn, r=(), w=()):
        return self._emit("vector", fn, r, w)
    def s(self, fn, r=(), w=()):
        return self._emit("scalar", fn, r, w)
    def g(self, fn, r=(), w=()):
        return self._emit("gpsimd", fn, r, w)
    def load(self, dst, src_ap, r=(), w=(), slow=False, q="sync"):
        def fn(e, dst=dst, src_ap=src_ap, slow=slow):
            return e.dma_start(out=(dst.ap if hasattr(dst, "ap") else dst),
                               in_=src_ap, allow_slow_non_contiguous=slow)
        return self._emit(q, fn, tuple(r), (dst,) + tuple(w), kind="dma", amount=16)
    def store(self, dst_ap, src_ap, dstbuf, r=(), slow=False, q="gpsimd"):
        def fn(e, dst_ap=dst_ap, src_ap=src_ap, slow=slow):
            return e.dma_start(out=dst_ap, in_=src_ap,
                               allow_slow_non_contiguous=slow)
        return self._emit(q, fn, tuple(r), (dstbuf,), kind="dma", amount=16)

    def barrier(self):
        self.close_banks()
        lasts = [("dma", k, v) for k, v in self.dma_sem_tot.items()]
        for e in self.ENGS:
            for op in reversed(self.ops[e]):
                if op.kind == "eng" and op.fn is not None:
                    op.inc = True
                    lasts.append(op)
                    break
        for e in self.ENGS:
            bop = _Op(e, None, "eng")
            bop.waits = [d for d in lasts
                         if not (not isinstance(d, tuple) and d.eng == e)]
            self.ops[e].append(bop)

    def mm_flags(self, out_ap, acc, bufname):
        off_b = out_ap.offset * mybir.dt.size(out_ap.tensor.dtype)
        key = (out_ap.tensor.name, off_b // 2048)
        flags = {"start": False, "stop": False}
        ent = self.pbank.get(key)
        if acc:
            if ent is None:
                flags["start"] = True
                self.pbank[key] = {"flags": flags, "stale": False}
                self.pbufs.setdefault(bufname, set()).add(key)
            else:
                ent["flags"] = flags
        else:
            if ent is None or ent["stale"]:
                if ent is not None:
                    ent["flags"]["stop"] = True
                flags["start"] = True
                self.pbank[key] = {"flags": flags, "stale": False}
            else:
                ent["flags"] = flags
        self.pbufs.setdefault(bufname, set()).add(key)
        return flags

    def close_banks(self):
        for ent in self.pbank.values():
            ent["flags"]["stop"] = True
        self.pbank.clear()
        self.pbufs.clear()

    def build(self, stack):
        nc = self.nc
        self.close_banks()
        self.barrier()
        sems = {}
        for e in ("tensor", "vector", "scalar"):
            c = 0
            for op in self.ops[e]:
                if op.inc:
                    c += op.amount
                    op.val = c
            sems[e] = stack.enter_context(nc.semaphore(f"sem_{e}"))
        dsems = {}
        for k in self.dma_sem_tot:
            dsems[k] = stack.enter_context(nc.semaphore(f"dsem_{k[0]}_{k[1]}"))
        block = stack.enter_context(nc.Block())
        def mk(e):
            def emitf(engq):
                waited = {}
                for op in self.ops[e]:
                    for d in op.waits:
                        if isinstance(d, tuple):
                            key, val, sem = d[1], d[2], dsems[d[1]]
                        else:
                            key, val, sem = d.eng, d.val, sems[d.eng]
                        if waited.get(key, -1) >= val:
                            continue
                        waited[key] = val
                        engq.wait_ge(sem, val)
                    if op.fn is not None:
                        instr = op.fn(engq)
                        if op.inc:
                            if op.kind == "dma":
                                instr.then_inc(dsems[op.semkey], 16)
                            else:
                                instr.then_inc(sems[e], op.amount)
            return emitf
        block.tensor(mk("tensor"))
        block.vector(mk("vector"))
        block.scalar(mk("scalar"))
        block.gpsimd(mk("gpsimd"))
        block.sync(mk("sync"))


def host_consts():
    i = np.arange(128)
    tril_s = (i[:, None] > i[None, :]).astype(np.float32)
    triu_s = (i[:, None] < i[None, :]).astype(np.float32)
    triu_i = (i[:, None] <= i[None, :]).astype(np.float32)
    c = {}
    c["eye_f"] = np.eye(128, dtype=np.float32)
    c["eye_b"] = np.eye(128, dtype=np.float32).astype(NPBF)
    c["ones_col_f"] = np.ones((128, 1), dtype=np.float32)
    c["ones_col_b"] = np.ones((128, 1), dtype=np.float32).astype(NPBF)
    c["ones_row_f"] = np.ones((1, 128), dtype=np.float32)
    c["mask_sl4"] = np.tile(tril_s, (1, 4)).astype(NPBF)
    c["mask_su4"] = np.tile(triu_s, (1, 4)).astype(NPBF)
    c["mask_ui4"] = np.tile(triu_i, (1, 4)).astype(NPBF)
    sr16 = np.zeros((16, 16 * 128), dtype=np.float32)
    for r in range(16):
        sr16[r, r * 128:(r + 1) * 128] = 1.0
    c["sel_row16"] = sr16
    sr8 = np.zeros((8, 8 * 128), dtype=np.float32)
    for r in range(8):
        sr8[r, r * 128:(r + 1) * 128] = 1.0
    c["sel_row8"] = sr8
    sr4 = np.zeros((4, 4 * 128), dtype=np.float32)
    for r in range(4):
        sr4[r, r * 128:(r + 1) * 128] = 1.0
    c["sel_row4"] = sr4
    sc = np.zeros((128, 16 * 12), dtype=np.float32)
    for br in range(4):
        sc[:, (3 * br + 0) * 16 + 4 * br + 0] = 1.0
        sc[:, (3 * br + 1) * 16 + 4 * br + 1] = 1.0
        sc[:, (3 * br + 1) * 16 + 4 * br + 3] = 1.0
        sc[:, (3 * br + 2) * 16 + 4 * br + 2] = 1.0
    c["selc_st"] = sc
    s4 = np.zeros((128, 16), dtype=np.float32)
    for h in range(4):
        s4[:, 4 * h + h] = 1.0
    c["selc4"] = s4
    c["selc_st_b"] = sc.astype(NPBF)
    st_scale = np.full((16, 1), 1.0 / 256.0, np.float32)
    for br in range(4):
        st_scale[4 * br + 3, 0] = 1.0
    c["st_scale"] = st_scale
    m3b = np.zeros((16, 512), np.float32)
    for br in range(4):
        m3b[4 * br + 3, :] = 1.0
    c["m3big"] = m3b
    r416 = np.zeros((4, 16), np.float32)
    for h in range(4):
        r416[h, 4 * h:4 * h + 4] = 1.0
    c["rep4_16"] = r416
    fv = np.array([1e-5, 1e-6, -1.0, 0.125, 1.0 / 256.0], np.float32)
    c["fvals"] = np.tile(fv[None, :], (128, 1))
    sh = np.zeros((16, 16), dtype=np.float32)
    for br in range(4):
        sh[4 * br + 0, 4 * br + 1] = 1.0
    c["shift16"] = sh
    m3 = np.zeros((16, 1), dtype=np.float32)
    for br in range(4):
        m3[4 * br + 3, 0] = 1.0
    c["m3row"] = m3
    return c

CONST_SPECS = {k: v for k, v in host_consts().items()}


class Ctx:
    def __init__(self, nc, stack, P, mode, carried):
        self.nc = nc; self.stack = stack; self.stack0 = stack; self.P = P
        self.mode = mode
        self.carried = carried
        self.dram = {}
        self.outputs = set()
        self.sb_cache = {}

    def din(self, name, shape, dtype=F32):
        if name not in self.dram:
            t = self.nc.dram_tensor(name, list(shape), dtype, kind="ExternalInput")
            self.dram[name] = TB(t.ap(), name)
        return self.dram[name]

    def dout(self, name, shape, dtype=F32):
        if name not in self.dram:
            t = self.nc.dram_tensor(name, list(shape), dtype, kind="ExternalOutput")
            self.dram[name] = TB(t.ap(), name)
            self.outputs.add(name)
        return self.dram[name]

    def dscratch(self, name, shape, dtype=F32):
        if name not in self.dram:
            kind = "Internal" if self.mode == "fused" else None
            if kind:
                t = self.nc.dram_tensor(name, list(shape), dtype, kind=kind)
                self.dram[name] = TB(t.ap(), name)
        return self.dram.get(name)

    def sb(self, name, p, f, dtype=F32):
        t = self.stack.enter_context(self.nc.sbuf_tensor(name, [p, f], dtype))
        return TB(t, name)

    def ps(self, name, p, f, dtype=F32):
        t = self.stack.enter_context(self.nc.psum_tensor(name, [p, f], dtype))
        return TB(t, name)

    def const(self, name):
        if name in self.sb_cache:
            return self.sb_cache[name]
        arr = CONST_SPECS[name]
        dt = BF16 if arr.dtype == np.dtype(NPBF) else F32
        d = self.din("c_" + name, arr.shape, dt)
        t = self.stack0.enter_context(self.nc.sbuf_tensor("s_" + name,
                                                          list(arr.shape), dt))
        s = TB(t, "s_" + name)
        self.P.load(s, d.ap, r=[d])
        self.sb_cache[name] = s
        return s


def mm(P, out_ap, lhsT_ap, rhs_ap, r, w, start=True, stop=True):
    bufname = _buf(w[0]).name
    flags = P.mm_flags(out_ap, acc=(not start), bufname=bufname)
    return P.t(lambda e: e.matmul(out_ap, lhsT_ap, rhs_ap,
                                  start=flags["start"], stop=flags["stop"]),
               r=r, w=w)

def transpose(P, out_ap, in_ap, eye_tb, r, w, kdim=128):
    bufname = _buf(w[0]).name
    flags = P.mm_flags(out_ap, acc=False, bufname=bufname)
    return P.t(lambda e: e.matmul(out_ap, in_ap, eye_tb[0:kdim, 0:kdim],
                                  start=flags["start"], stop=flags["stop"],
                                  is_transpose=True),
               r=list(r) + [eye_tb], w=w)

def tt(P, out_ap, a_ap, b_ap, op, r, w, eng="v"):
    f = lambda e: e.tensor_tensor(out_ap, a_ap, b_ap, op)
    return getattr(P, eng)(f, r=r, w=w)

def ts(P, out_ap, in_ap, s1, op0, r, w, s2=None, op1=None, eng="v"):
    def f(e):
        if s2 is None and op1 is None:
            return e.tensor_scalar(out_ap, in_ap, s1, None, op0)
        return e.tensor_scalar(out_ap, in_ap, s1, s2, op0, op1)
    return getattr(P, eng)(f, r=r, w=w)

def stt(P, out_ap, in0_ap, scalar, in1_ap, op0, op1, r, w):
    return P.v(lambda e: e.scalar_tensor_tensor(out_ap, in0_ap, scalar, in1_ap, op0, op1),
               r=r, w=w)

def act(P, out_ap, in_ap, func, r, w, bias=None, scale=None):
    r = list(r)
    if isinstance(bias, float) or isinstance(scale, float):
        fc = getattr(P, "fconst", None)
        if fc is not None:
            r.append(fc)
    def f(e):
        kw = {}
        if bias is not None:
            kw["bias"] = bias
        if scale is not None:
            kw["scale"] = scale
        return e.activation(out_ap, in_ap, func, **kw)
    return P.s(f, r=r, w=w)

def vcopy(P, out_ap, in_ap, r, w, eng="v"):
    if eng == "s":
        f = lambda e: e.copy(out_ap, in_ap)
    else:
        f = lambda e: e.tensor_copy(out_ap, in_ap)
    return getattr(P, eng)(f, r=r, w=w)

def memset(P, ap, val, w, eng="v"):
    return getattr(P, eng)(lambda e: e.memset(ap, val), r=(), w=w)

def reduce(P, out_ap, in_ap, op, r, w):
    return P.v(lambda e: e.tensor_reduce(out_ap, in_ap, AX.X, op), r=r, w=w)

def recip(P, out_ap, in_ap, r, w):
    return P.v(lambda e: e.reciprocal(out_ap, in_ap), r=r, w=w)

def emit_ln_panel(P, ctx, xs, dst, tmp, pr_sum, pr_sq, rowbuf, pr_bc):
    oc = ctx.const("ones_col_f")
    for k in range(KT):
        mm(P, pr_sum[0:1, :], oc[:, 0:1], xs[k].ap, r=[xs[k], oc], w=[pr_sum],
           start=(k == 0), stop=(k == KT - 1))
    for k in range(KT):
        act(P, tmp.ap, xs[k].ap, AF.Square, r=[xs[k]], w=[tmp])
        mm(P, pr_sq[0:1, :], oc[:, 0:1], tmp.ap, r=[tmp, oc], w=[pr_sq],
           start=(k == 0), stop=(k == KT - 1))
    mean = rowbuf[0:1, 0:PAN]; m2 = rowbuf[0:1, PAN:2 * PAN]
    var = rowbuf[0:1, 2 * PAN:3 * PAN]; rstd = rowbuf[0:1, 3 * PAN:4 * PAN]
    ts(P, mean, pr_sum[0:1, :], 1.0 / D, OP.mult, r=[pr_sum], w=[rowbuf])
    act(P, m2, mean, AF.Square, r=[rowbuf], w=[rowbuf])
    ts(P, var, pr_sq[0:1, :], 1.0 / D, OP.mult, r=[pr_sq, rowbuf], w=[rowbuf])
    tt(P, var, var, m2, OP.subtract, r=[rowbuf], w=[rowbuf])
    act(P, rstd, var, AF.Sqrt, r=[rowbuf], w=[rowbuf], bias=1e-5)
    recip(P, rstd, rstd, r=[rowbuf], w=[rowbuf])
    orow = ctx.const("ones_row_f")
    mm(P, pr_bc[:, 0:PAN], orow[0:1, :], mean, r=[rowbuf, orow], w=[pr_bc])
    mm(P, pr_bc[:, PAN:2 * PAN], orow[0:1, :], rstd, r=[rowbuf, orow], w=[pr_bc])
    for k in range(KT):
        tt(P, tmp.ap, xs[k].ap, pr_bc[:, 0:PAN], OP.subtract, r=[xs[k], pr_bc], w=[tmp])
        tt(P, dst[k].ap, tmp.ap, pr_bc[:, PAN:2 * PAN], OP.mult, r=[tmp, pr_bc], w=[dst[k]])


def emit_s_in(P, ctx):
    x_in = ctx.din("x", [B, T, D], F32)
    xT = ctx.dscratch("xT", [B, D, T], F32)
    eye = ctx.const("eye_f")
    _scope = scope(ctx); _scope.__enter__()
    ld = [ctx.sb(f"si_ld{i}", 128, D) for i in range(2)]
    pt = ctx.ps("si_ps", 128, 1024)
    gat = [ctx.sb(f"si_g{i}", 128, 1024) for i in range(2)]
    for b in range(B):
        for tt_ in range(T // 128):
            l = ld[tt_ % 2]; g = gat[tt_ % 2]
            P.load(l, x_in.t[b, tt_ * 128:(tt_ + 1) * 128, :], r=[x_in])
            for k in range(KT):
                transpose(P, pt[:, k * 128:(k + 1) * 128], l[:, k * 128:(k + 1) * 128],
                          eye, r=[l], w=[pt])
                vcopy(P, g[:, k * 128:(k + 1) * 128], pt[:, k * 128:(k + 1) * 128],
                      r=[pt], w=[g])
            for k in range(KT):
                P.store(xT.t[b, k * 128:(k + 1) * 128, tt_ * 128:(tt_ + 1) * 128],
                        g.t[:, k * 128:(k + 1) * 128], xT, r=[g])
    _scope.__exit__(None, None, None)
    P.barrier()

def emit_s_out(P, ctx, src="x4T"):
    x4T = ctx.dscratch(src, [B, D, T], F32)
    out = ctx.dout("out", [B, T, D], F32)
    eye = ctx.const("eye_f")
    _scope = scope(ctx); _scope.__enter__()
    ld = [ctx.sb(f"so_ld{i}", 128, 1024) for i in range(2)]
    pt = ctx.ps("so_ps", 128, 1024)
    gat = [ctx.sb(f"so_g{i}", 128, 1024) for i in range(2)]
    for b in range(B):
        for tt_ in range(T // 128):
            l = ld[tt_ % 2]; g = gat[tt_ % 2]
            for k in range(KT):
                P.load(TB(l.t[:, k * 128:(k + 1) * 128], f"so{k}"),
                       x4T.t[b, k * 128:(k + 1) * 128, tt_ * 128:(tt_ + 1) * 128],
                       r=[x4T], w=[l])
            for k in range(KT):
                transpose(P, pt[:, k * 128:(k + 1) * 128], l[:, k * 128:(k + 1) * 128],
                          eye, r=[l], w=[pt])
                vcopy(P, g[:, k * 128:(k + 1) * 128], pt[:, k * 128:(k + 1) * 128],
                      r=[pt], w=[g])
            P.store(out.t[b, tt_ * 128:(tt_ + 1) * 128, :], g.ap, out, r=[g])
    _scope.__exit__(None, None, None)
    P.barrier()


from contextlib import contextmanager

@contextmanager
def scope(ctx):
    old = ctx.stack
    with ExitStack() as st:
        ctx.stack = st
        yield
        ctx.P.barrier()
    ctx.stack = old

def _ln_alloc(ctx, tag):
    a = {}
    a["tmp"] = ctx.sb(f"{tag}_lntmp", 128, PAN)
    a["row"] = ctx.sb(f"{tag}_lnrow", 1, 4 * PAN)
    a["psum"] = ctx.ps(f"{tag}_lnpsS", 1, PAN)
    a["psq"] = ctx.ps(f"{tag}_lnpsQ", 1, PAN)
    a["pbc"] = ctx.ps(f"{tag}_lnpbc", 128, 2 * PAN)
    return a

def emit_ln_to(P, ctx, a, xs, dst):
    emit_ln_panel(P, ctx, xs, dst, a["tmp"], a["psum"], a["psq"], a["row"], a["pbc"])


def emit_s0a(P, ctx):
    xT = ctx.dscratch("xT", [B, D, T], F32)
    hlnT = ctx.dscratch("hlnT", [B, D, T], F32)
    qnT = ctx.dscratch("qnT", [B, D, T], F32)
    knT = ctx.dscratch("knT", [B, D, T], F32)
    vT = ctx.dscratch("vT", [B, D, T], F32)
    eye = ctx.const("eye_f")
    oc = ctx.const("ones_col_f")
    with scope(ctx):
        xs = [ctx.sb(f"s0a_x{k}", 128, PAN) for k in range(KT)]
        hl = [ctx.sb(f"s0a_h{k}", 128, PAN) for k in range(KT)]
        la = _ln_alloc(ctx, "s0a")
        for b in range(B):
            for p in range(NPAN):
                for k in range(KT):
                    P.load(xs[k], xT.t[b, k * 128:(k + 1) * 128, p * PAN:(p + 1) * PAN],
                           r=[xT], q=("sync" if k % 2 == 0 else "gpsimd"))
                emit_ln_to(P, ctx, la, xs, hl)
                for k in range(KT):
                    P.store(hlnT.t[b, k * 128:(k + 1) * 128, p * PAN:(p + 1) * PAN],
                            hl[k].ap, hlnT, r=[hl[k]])
        P.barrier()
        wnames = [("L0_attn_Wq", "L0_attn_cwq", qnT, True),
                  ("L0_attn_Wk", "L0_attn_cwk", knT, True),
                  ("L0_attn_Wv", "L0_attn_cwv", vT, False)]
        W_d = {nm: ctx.din(nm, [D, D]) for nm, _, _, _ in wnames}
        cw_d = {nm: ctx.din(nm, [D, 4]) for _, nm, _, _ in wnames}
        ws = [ctx.sb(f"s0a_w{k}", 128, D) for k in range(KT)]
        cw_sb = ctx.sb("s0a_cw", 128, 32)
        dg = [ctx.sb(f"s0a_dg{i}", 128, 128) for i in range(32)]
        pre = [ctx.sb(f"s0a_pre{k}", 128, PAN + 3) for k in range(KT)]
        post = [ctx.sb(f"s0a_post{k}", 128, PAN) for k in range(KT)]
        tail = [ctx.sb(f"s0a_tl{k}", 128, 3) for k in range(KT)]
        sq = ctx.sb("s0a_sq", 128, PAN)
        nrow = ctx.sb("s0a_nrow", 1, PAN)
        pp = [ctx.ps(f"s0a_pp{i}", 128, PAN) for i in range(2)]
        pn = ctx.ps("s0a_pn", 1, PAN)
        pb = ctx.ps("s0a_pb", 128, PAN)
        for wname, cwname, dstT, do_l2 in wnames:
            Wd = W_d[wname]
            for k in range(KT):
                P.load(ws[k], Wd.t[k * 128:(k + 1) * 128, :], r=[Wd],
                       q=("sync" if k % 2 == 0 else "gpsimd"))
            P.load(rr2(cw_sb, "p (a j) -> p a j", j=4),
                   cw_d[cwname].t[:, :].rearrange("(a p) j -> p a j", p=128),
                   r=[cw_d[cwname]], w=[cw_sb], slow=True)
            for k in range(KT):
                for tap in range(4):
                    ts(P, dg[k * 4 + tap].ap, eye.ap, cw_sb[:, k * 4 + tap:k * 4 + tap + 1],
                       OP.mult, r=[eye, cw_sb], w=[dg[k * 4 + tap]])
            for b in range(B):
                for k in range(KT):
                    memset(P, tail[k].ap, 0.0, w=[tail[k]])
                for p in range(NPAN):
                    for k in range(KT):
                        P.load(hl[k], hlnT.t[b, k * 128:(k + 1) * 128, p * PAN:(p + 1) * PAN],
                               r=[hlnT], q=("sync" if k % 2 == 0 else "gpsimd"))
                    for mk_ in range(KT):
                        pj = pp[mk_ % 2]
                        for k in range(KT):
                            mm(P, pj.ap, ws[k][:, mk_ * 128:(mk_ + 1) * 128], hl[k].ap,
                               r=[ws[k], hl[k]], w=[pj], start=(k == 0), stop=(k == KT - 1))
                        vcopy(P, pre[mk_][:, 3:3 + PAN], pj.ap, r=[pj], w=[pre[mk_]])
                        vcopy(P, pre[mk_][:, 0:3], tail[mk_].ap, r=[tail[mk_]], w=[pre[mk_]])
                        vcopy(P, tail[mk_].ap, pre[mk_][:, PAN:PAN + 3],
                              r=[pre[mk_]], w=[tail[mk_]], eng="s")
                    for k in range(KT):
                        pj = pp[k % 2]
                        for tap in range(4):
                            mm(P, pj.ap, dg[k * 4 + tap].ap, pre[k][:, tap:tap + PAN],
                               r=[dg[k * 4 + tap], pre[k]], w=[pj],
                               start=(tap == 0), stop=(tap == 3))
                        act(P, post[k].ap, pj.ap, AF.Silu, r=[pj], w=[post[k]])
                    if do_l2:
                        for h in range(NH):
                            for j in range(2):
                                k = 2 * h + j
                                act(P, sq.ap, post[k].ap, AF.Square, r=[post[k]], w=[sq])
                                mm(P, pn.ap, oc[:, 0:1], sq.ap, r=[sq, oc], w=[pn],
                                   start=(j == 0), stop=(j == 1))
                            act(P, nrow.ap, pn.ap, AF.Sqrt, r=[pn], w=[nrow], bias=1e-6)
                            recip(P, nrow.ap, nrow.ap, r=[nrow], w=[nrow])
                            orow = ctx.const("ones_row_f")
                            mm(P, pb.ap, orow[0:1, :], nrow.ap, r=[nrow, orow], w=[pb])
                            for j in range(2):
                                k = 2 * h + j
                                tt(P, post[k].ap, post[k].ap, pb.ap, OP.mult,
                                   r=[post[k], pb], w=[post[k]])
                    for k in range(KT):
                        P.store(dstT.t[b, k * 128:(k + 1) * 128, p * PAN:(p + 1) * PAN],
                                post[k].ap, dstT, r=[post[k]])
            P.barrier()
    P.barrier()


def emit_s0a2(P, ctx):
    hlnT = ctx.dscratch("hlnT", [B, D, T], F32)
    idT = ctx.dscratch("idT", [B, D, T], F32)
    betaT = ctx.dscratch("betaT", [B, NH, T], F32)
    Wid = ctx.din("L0_attn_Wid", [D, D])
    Wb = ctx.din("L0_attn_Wb", [D, NH])
    with scope(ctx):
        hl = [ctx.sb(f"s0a2_h{k}", 128, PAN) for k in range(KT)]
        ws = [ctx.sb(f"s0a2_w{k}", 128, D) for k in range(KT)]
        wb_sb = ctx.sb("s0a2_wb", 128, 32)
        out = [ctx.sb(f"s0a2_o{k}", 128, PAN) for k in range(KT)]
        brow = ctx.sb("s0a2_brow", 1, PAN)
        pp = [ctx.ps(f"s0a2_pp{i}", 128, PAN) for i in range(2)]
        pnb = ctx.ps("s0a2_pnb", 1, PAN)
        for k in range(KT):
            P.load(ws[k], Wid.t[k * 128:(k + 1) * 128, :], r=[Wid],
                   q=("sync" if k % 2 == 0 else "gpsimd"))
        P.load(rr2(wb_sb, "p (a j) -> p a j", j=4),
               Wb.t[:, :].rearrange("(a p) j -> p a j", p=128), r=[Wb], w=[wb_sb], slow=True)
        for b in range(B):
            for p in range(NPAN):
                for k in range(KT):
                    P.load(hl[k], hlnT.t[b, k * 128:(k + 1) * 128, p * PAN:(p + 1) * PAN],
                           r=[hlnT], q=("sync" if k % 2 == 0 else "gpsimd"))
                for mk_ in range(KT):
                    pj = pp[mk_ % 2]
                    for k in range(KT):
                        mm(P, pj.ap, ws[k][:, mk_ * 128:(mk_ + 1) * 128], hl[k].ap,
                           r=[ws[k], hl[k]], w=[pj], start=(k == 0), stop=(k == KT - 1))
                    vcopy(P, out[mk_].ap, pj.ap, r=[pj], w=[out[mk_]])
                    P.store(idT.t[b, mk_ * 128:(mk_ + 1) * 128, p * PAN:(p + 1) * PAN],
                            out[mk_].ap, idT, r=[out[mk_]])
                for h in range(NH):
                    for k in range(KT):
                        mm(P, pnb.ap, wb_sb[:, k * 4 + h:k * 4 + h + 1], hl[k].ap,
                           r=[wb_sb, hl[k]], w=[pnb], start=(k == 0), stop=(k == KT - 1))
                    act(P, brow.ap, pnb.ap, AF.Sigmoid, r=[pnb], w=[brow])
                    P.store(betaT.t[b, h:h + 1, p * PAN:(p + 1) * PAN],
                            brow.ap, betaT, r=[brow])
    P.barrier()

DBG_S0B = False


def emit_s0b(P, ctx):
    qnT = ctx.dscratch("qnT", [B, D, T], F32)
    knT = ctx.dscratch("knT", [B, D, T], F32)
    vT = ctx.dscratch("vT", [B, D, T], F32)
    betaT = ctx.dscratch("betaT", [B, NH, T], F32)
    doutT = ctx.dscratch("doutT", [B, D, T], F32)
    eye = ctx.const("eye_f")
    sr4 = ctx.const("sel_row4")
    m_sl = ctx.const("mask_sl4")
    m_su = ctx.const("mask_su4")
    m_ui = ctx.const("mask_ui4")
    with scope(ctx):
        qc = [ctx.sb(f"s0b_q{i}", 128, 1024) for i in range(2)]
        kc = [ctx.sb(f"s0b_k{i}", 128, 1024) for i in range(2)]
        vc = [ctx.sb(f"s0b_v{i}", 128, 1024) for i in range(2)]
        bc = [ctx.sb(f"s0b_b{i}", NH, CH) for i in range(2)]
        kb = ctx.sb("s0b_kb", 128, 1024)
        vn = ctx.sb("s0b_vn", 128, 1024)
        kn = ctx.sb("s0b_kn", 128, 1024)
        bn = ctx.sb("s0b_bn", 128, NH)
        Bp = [ctx.sb(f"s0b_B{i}", 128, 512) for i in range(2)]
        Cp = [ctx.sb(f"s0b_C{j}", 128, 512) for j in range(7)]
        za = ctx.sb("s0b_za", 128, 1024)
        zb = ctx.sb("s0b_zb", 128, 1024)
        Asb = ctx.sb("s0b_A", 128, 512)
        osb = ctx.sb("s0b_o", 128, 1024)
        Ssb = ctx.sb("s0b_S", 128, 2048)
        psS = ctx.ps("s0b_psS", 128, 2048)
        psa = ctx.ps("s0b_psa", 128, 512)
        psb = ctx.ps("s0b_psb", 128, 512)
        psw = ctx.ps("s0b_psw", 128, 1024)
        for b in range(B):
            for c in range(NCH):
                par = c % 2
                q_, k_, v_, b_ = qc[par], kc[par], vc[par], bc[par]
                lq = "sync" if par == 0 else "gpsimd"
                for k in range(KT):
                    P.load(TB(q_.t[:, k * 128:(k + 1) * 128], "l"),
                           qnT.t[b, k * 128:(k + 1) * 128, c * CH:(c + 1) * CH],
                           r=[qnT], w=[q_], q=lq)
                    P.load(TB(k_.t[:, k * 128:(k + 1) * 128], "l"),
                           knT.t[b, k * 128:(k + 1) * 128, c * CH:(c + 1) * CH],
                           r=[knT], w=[k_], q=lq)
                    P.load(TB(v_.t[:, k * 128:(k + 1) * 128], "l"),
                           vT.t[b, k * 128:(k + 1) * 128, c * CH:(c + 1) * CH],
                           r=[vT], w=[v_], q=lq)
                P.load(b_, betaT.t[b, :, c * CH:(c + 1) * CH], r=[betaT], q=lq)
                transpose(P, psb[:, 0:NH], b_.ap, eye, r=[b_], w=[psb], kdim=NH)
                vcopy(P, bn.ap, psb[:, 0:NH], r=[psb], w=[bn])
                for k in range(KT):
                    h = k // 2
                    mm(P, psw[:, k * 128:(k + 1) * 128], sr4[:, h * 128:(h + 1) * 128],
                       b_.ap, r=[b_, sr4], w=[psw])
                tt(P, kb.ap, k_.ap, psw.ap, OP.mult, r=[k_, psw], w=[kb])
                for h in range(NH):
                    for j in range(2):
                        k = 2 * h + j
                        mm(P, psa[:, h * 128:(h + 1) * 128], kb[:, k * 128:(k + 1) * 128],
                           k_.t[:, k * 128:(k + 1) * 128], r=[kb, k_], w=[psa],
                           start=(j == 0), stop=(j == 1))
                        mm(P, psb[:, h * 128:(h + 1) * 128], k_.t[:, k * 128:(k + 1) * 128],
                           kb[:, k * 128:(k + 1) * 128], r=[kb, k_], w=[psb],
                           start=(j == 0), stop=(j == 1))
                tt(P, Bp[0].ap, psa.ap, m_sl.ap, OP.mult, r=[psa, m_sl], w=[Bp[0]])
                tt(P, Cp[0].ap, psb.ap, m_su.ap, OP.mult, r=[psb, m_su], w=[Cp[0]])
                bcur = Bp[0]
                for j in range(1, 7):
                    bnext = Bp[j % 2]
                    for h in range(NH):
                        sl = slice(h * 128, (h + 1) * 128)
                        mm(P, psa[:, sl], Cp[j - 1][:, sl], bcur[:, sl],
                           r=[Cp[j - 1], bcur], w=[psa])
                        mm(P, psb[:, sl], bcur[:, sl], Cp[j - 1][:, sl],
                           r=[Cp[j - 1], bcur], w=[psb])
                    vcopy(P, bnext.ap, psa.ap, r=[psa], w=[bnext])
                    vcopy(P, Cp[j].ap, psb.ap, r=[psb], w=[Cp[j]], eng="s")
                    bcur = bnext
                for k in range(KT):
                    transpose(P, psw[:, k * 128:(k + 1) * 128],
                              v_.t[:, k * 128:(k + 1) * 128], eye, r=[v_], w=[psw])
                vcopy(P, vn.ap, psw.ap, r=[psw], w=[vn])
                for k in range(KT):
                    transpose(P, psw[:, k * 128:(k + 1) * 128],
                              k_.t[:, k * 128:(k + 1) * 128], eye, r=[k_], w=[psw])
                vcopy(P, kn.ap, psw.ap, r=[psw], w=[kn])
                if c == 0:
                    for h in range(NH):
                        sl = slice(h * 256, (h + 1) * 256)
                        ts(P, za.t[:, sl], vn.t[:, sl], bn[:, h:h + 1], OP.mult,
                           r=[vn, bn], w=[za])
                else:
                    for h in range(NH):
                        for j in range(2):
                            k = 2 * h + j
                            mm(P, psw[:, h * 256:(h + 1) * 256],
                               kb[:, k * 128:(k + 1) * 128],
                               Ssb.t[:, h * 512 + j * 256:h * 512 + (j + 1) * 256],
                               r=[kb, Ssb], w=[psw], start=(j == 0), stop=(j == 1))
                    if DBG_S0B:
                        dbgP = ctx.dscratch("dbgP", [B, NCH, 128, 1024], F32)
                        vcopy(P, osb.ap, psw.ap, r=[psw], w=[osb])
                        P.store(dbgP.t[b, c], osb.ap, dbgP, r=[osb], q="sync")
                    for h in range(NH):
                        sl = slice(h * 256, (h + 1) * 256)
                        def f(e, o=za.t[:, sl], i0=vn.t[:, sl], s=bn[:, h:h + 1],
                              i1=psw[:, sl]):
                            return e.scalar_tensor_tensor(o, i0, s, i1,
                                                          OP.mult, OP.subtract)
                        P.v(f, r=[vn, bn, psw], w=[za])
                if DBG_S0B:
                    dbgM = ctx.dscratch("dbgM", [B, NCH, 128, 1024], F32)
                    P.store(dbgM.t[b, c], za.ap, dbgM, r=[za])
                    dbgB = ctx.dscratch("dbgB", [B, NCH, 128, 512], F32)
                    P.store(dbgB.t[b, c], Cp[0].ap, dbgB, r=[Cp[0]])
                zc, zn = za, zb
                for j in range(6, -1, -1):
                    for h in range(NH):
                        mm(P, psw[:, h * 256:(h + 1) * 256], Cp[j][:, h * 128:(h + 1) * 128],
                           zc.t[:, h * 256:(h + 1) * 256], r=[Cp[j], zc], w=[psw])
                    op1 = OP.subtract if j == 0 else OP.add
                    sgn = -1.0 if j == 0 else 1.0
                    def f(e, sgn=sgn, zn=zn, zc=zc):
                        return e.scalar_tensor_tensor(zn.ap, psw.ap, sgn, zc.ap,
                                                      OP.mult, OP.add)
                    P.v(f, r=[psw, zc], w=[zn])
                    zc, zn = zn, zc
                uadj = zc
                for h in range(NH):
                    for j in range(2):
                        k = 2 * h + j
                        mm(P, psa[:, h * 128:(h + 1) * 128], k_.t[:, k * 128:(k + 1) * 128],
                           q_.t[:, k * 128:(k + 1) * 128], r=[k_, q_], w=[psa],
                           start=(j == 0), stop=(j == 1))
                tt(P, Asb.ap, psa.ap, m_ui.ap, OP.mult, r=[psa, m_ui], w=[Asb])
                for h in range(NH):
                    for dt in range(2):
                        k = 2 * h + dt
                        sl = slice(k * 128, (k + 1) * 128)
                        first = True
                        if c > 0:
                            for j in range(2):
                                mm(P, psw[:, sl],
                                   Ssb.t[:, h * 512 + j * 256 + dt * 128:h * 512 + j * 256 + (dt + 1) * 128],
                                   q_.t[:, (2 * h + j) * 128:(2 * h + j + 1) * 128],
                                   r=[Ssb, q_], w=[psw], start=(j == 0), stop=False)
                            first = False
                        mm(P, psw[:, sl], uadj.t[:, h * 256 + dt * 128:h * 256 + (dt + 1) * 128],
                           Asb[:, h * 128:(h + 1) * 128], r=[uadj, Asb], w=[psw],
                           start=first, stop=True)
                vcopy(P, osb.ap, psw.ap, r=[psw], w=[osb])
                for k in range(KT):
                    P.store(doutT.t[b, k * 128:(k + 1) * 128, c * CH:(c + 1) * CH],
                            osb.t[:, k * 128:(k + 1) * 128], doutT, r=[osb],
                            q=("gpsimd" if par == 0 else "sync"))
                if c < NCH - 1:
                    for h in range(NH):
                        for j in range(2):
                            mm(P, psS[:, h * 512 + j * 256:h * 512 + (j + 1) * 256],
                               kn.t[:, h * 256 + j * 128:h * 256 + (j + 1) * 128],
                               uadj.t[:, h * 256:(h + 1) * 256], r=[kn, uadj], w=[psS],
                               start=True)
                if c < NCH - 1:
                    if c == 0:
                        vcopy(P, Ssb.ap, psS.ap, r=[psS], w=[Ssb])
                    else:
                        tt(P, Ssb.ap, Ssb.ap, psS.ap, OP.add, r=[Ssb, psS], w=[Ssb])
                if DBG_S0B:
                    dbgU = ctx.dscratch("dbgU", [B, NCH, 128, 1024], F32)
                    dbgS = ctx.dscratch("dbgS", [B, NCH, 128, 2048], F32)
                    P.store(dbgU.t[b, c], uadj.ap, dbgU, r=[uadj])
                    P.store(dbgS.t[b, c], Ssb.ap, dbgS, r=[Ssb])
            P.barrier()
    P.barrier()


def emit_s0c1(P, ctx):
    vT = ctx.dscratch("vT", [B, D, T], F32)
    lsT = ctx.dscratch("lsT", [B, D, T], F32)
    llT = ctx.dscratch("llT", [B, D, T], F32)
    firs_d = ctx.din("L0_attn_firs", [4, 256, 3])
    firl_d = ctx.din("L0_attn_firl", [4, 256, 63])
    eyeb = ctx.const("eye_b")
    with scope(ctx):
        fls = ctx.sb("s0c1_fls", 128, 24)
        fll = ctx.sb("s0c1_fll", 128, 504)
        P.load(View(fls.t[:, :].rearrange("p (h a t) -> p h a t", h=4, a=2), fls.b),
               firs_d.t[:, :, :].rearrange("h (a p) t -> p h a t", p=128),
               r=[firs_d], slow=True)
        P.load(View(fll.t[:, :].rearrange("p (h a t) -> p h a t", h=4, a=2), fll.b),
               firl_d.t[:, :, :].rearrange("h (a p) t -> p h a t", p=128),
               r=[firl_d], slow=True)
        vbf = [ctx.sb(f"s0c1_vbf{k}", 128, 62 + T, BF16) for k in range(KT)]
        dgl = [ctx.sb(f"s0c1_dg{i}", 128, 128, BF16) for i in range(66)]
        vstg = [ctx.sb(f"s0c1_vstg{i}", 128, PAN) for i in range(2)]
        ous = [ctx.sb(f"s0c1_ou{i}", 128, PAN) for i in range(4)]
        psf = ctx.ps("s0c1_psf", 128, 1024)
        for b in range(B):
            for k in range(KT):
                memset(P, vbf[k][:, 0:62], 0.0, w=[vbf[k]])
                for p in range(NPAN):
                    vs = vstg[p % 2]
                    P.load(vs, vT.t[b, k * 128:(k + 1) * 128, p * PAN:(p + 1) * PAN],
                           r=[vT], w=[vs], q=("sync" if p % 2 == 0 else "gpsimd"))
                    vcopy(P, vbf[k][:, 62 + p * PAN:62 + (p + 1) * PAN], vs.ap,
                          r=[vs], w=[vbf[k]])
            for k in range(KT):
                for tp_ in range(63):
                    ts(P, dgl[tp_].ap, eyeb.ap, fll[:, (4 * k + 0) * 63 + tp_:(4 * k) * 63 + tp_ + 1]
                       if False else fll[:, k * 63 + tp_:k * 63 + tp_ + 1],
                       OP.mult, r=[eyeb, fll], w=[dgl[tp_]])
                for tp_ in range(3):
                    ts(P, dgl[63 + tp_].ap, eyeb.ap, fls[:, k * 3 + tp_:k * 3 + tp_ + 1],
                       OP.mult, r=[eyeb, fls], w=[dgl[63 + tp_]])
                for p in range(NPAN):
                    for tp_ in range(63):
                        mm(P, psf[:, 512:1024], dgl[tp_].ap,
                           vbf[k][:, p * PAN + tp_:p * PAN + tp_ + PAN],
                           r=[dgl[tp_], vbf[k]], w=[psf],
                           start=(tp_ == 0), stop=(tp_ == 62))
                    for tp_ in range(3):
                        mm(P, psf[:, 0:512], dgl[63 + tp_].ap,
                           vbf[k][:, 60 + p * PAN + tp_:60 + p * PAN + tp_ + PAN],
                           r=[dgl[63 + tp_], vbf[k]], w=[psf],
                           start=(tp_ == 0), stop=(tp_ == 2))
                    o1 = ous[2 * (p % 2)]; o2 = ous[2 * (p % 2) + 1]
                    vcopy(P, o1.ap, psf[:, 512:1024], r=[psf], w=[o1])
                    vcopy(P, o2.ap, psf[:, 0:512], r=[psf], w=[o2], eng="s")
                    P.store(llT.t[b, k * 128:(k + 1) * 128, p * PAN:(p + 1) * PAN],
                            o1.ap, llT, r=[o1])
                    P.store(lsT.t[b, k * 128:(k + 1) * 128, p * PAN:(p + 1) * PAN],
                            o2.ap, lsT, r=[o2], q="sync")
    P.barrier()


def emit_s0c2(P, ctx):
    xT = ctx.dscratch("xT", [B, D, T], F32)
    hlnT = ctx.dscratch("hlnT", [B, D, T], F32)
    vT = ctx.dscratch("vT", [B, D, T], F32)
    doutT = ctx.dscratch("doutT", [B, D, T], F32)
    idT = ctx.dscratch("idT", [B, D, T], F32)
    lsT = ctx.dscratch("lsT", [B, D, T], F32)
    llT = ctx.dscratch("llT", [B, D, T], F32)
    x1T = ctx.dscratch("x1T", [B, D, T], F32)
    gW1 = ctx.din("L0_attn_gW1", [1040, 1024])
    gW2 = ctx.din("L0_attn_gW2", [1024, 4])
    gb1 = ctx.din("L0_attn_gb1", [1024])
    gb2 = ctx.din("L0_attn_gb2", [4])
    ltm = ctx.din("L0_attn_log_temp", [4, 4])
    flp = ctx.din("L0_attn_floor_param", [4, 4])
    crl = ctx.din("L0_attn_conv_res_logit", [4])
    alp = ctx.din("L0_attn_alpha_id", [4])
    onw = ctx.din("L0_attn_o_norm_w", [256])
    Wo = ctx.din("L0_attn_Wo", [D, D])
    eye = ctx.const("eye_f")
    orow = ctx.const("ones_row_f")
    sr16 = ctx.const("sel_row16")
    sr4 = ctx.const("sel_row4")
    selc = ctx.const("selc_st")
    stsc = ctx.const("st_scale")
    m3b = ctx.const("m3big")
    r416 = ctx.const("rep4_16")
    sh16 = ctx.const("shift16")
    selc4 = ctx.const("selc4")
    with scope(ctx):
        gwa = [ctx.sb(f"s0c2_gwa{k}", 128, 1024) for k in range(KT)]
        for k in range(KT):
            P.load(gwa[k], gW1.t[k * 128:(k + 1) * 128, :], r=[gW1],
                   q=("sync" if k % 2 == 0 else "gpsimd"))
        gwb = ctx.sb("s0c2_gwb", 16, 1024)
        P.load(gwb, gW1.t[1024:1040, :], r=[gW1])
        gw2 = ctx.sb("s0c2_gw2", 128, 32)
        P.load(rr2(gw2, "p (a j) -> p a j", j=4),
               gW2.t[:, :].rearrange("(a p) j -> p a j", p=128), r=[gW2], w=[gw2], slow=True)
        gb1s = ctx.sb("s0c2_gb1", 128, KT)
        P.load(gb1s, gb1.t[:].rearrange("(a p) -> p a", p=128), r=[gb1], slow=True)
        gb2s = ctx.sb("s0c2_gb2", 4, 1)
        P.load(gb2s, gb2.t[:].rearrange("(j o) -> j o", o=1), r=[gb2], slow=True)
        wstg = ctx.sb("s0c2_wstg", 128, 1024)
        wob = [ctx.sb(f"s0c2_wob{k}", 128, 1024, BF16) for k in range(KT)]
        for k in range(KT):
            P.load(wstg, Wo.t[k * 128:(k + 1) * 128, :], r=[Wo], w=[wstg])
            vcopy(P, wob[k].ap, wstg.ap, r=[wstg], w=[wob[k]])
        onws = ctx.sb("s0c2_onw", 128, 2)
        P.load(onws, onw.t[:].rearrange("(a p) -> p a", p=128), r=[onw], slow=True)
        r16 = ctx.sb("s0c2_r16", 1, 16)
        r16b = ctx.sb("s0c2_r16b", 1, 16)
        r4 = ctx.sb("s0c2_r4", 1, 4)
        c4 = ctx.sb("s0c2_c4", 4, 1)
        invt = ctx.sb("s0c2_invt", 128, 64)
        fl64 = ctx.sb("s0c2_fl64", 128, 64)
        omf64 = ctx.sb("s0c2_omf64", 128, 64)
        crlb = ctx.sb("s0c2_crlb", 128, 4)
        alpb = ctx.sb("s0c2_alpb", 128, 4)
        ps_nat = ctx.ps("s0c2_psnat", 128, 64)
        ps_g = ctx.ps("s0c2_psg", 128, 512)
        ps_s1 = ctx.ps("s0c2_pss1", 128, 512)
        ps_t4 = ctx.ps("s0c2_pst4", 16, 512)
        ps_big = ctx.ps("s0c2_psbig", 128, 2048)
        P.load(r16, ltm.t[:, :].rearrange("(o h) j -> o (h j)", o=1), r=[ltm])
        act(P, r16.ap, r16.ap, AF.Exp, r=[r16], w=[r16], scale=-1.0)
        mm(P, ps_nat[:, 0:16], orow[0:1, :], r16.ap, r=[r16, orow], w=[ps_nat])
        for j in range(4):
            vcopy(P, invt[:, 16 * j:16 * (j + 1)], ps_nat[:, 0:16], r=[ps_nat], w=[invt])
        P.load(r16b, flp.t[:, :].rearrange("(o h) j -> o (h j)", o=1), r=[flp])
        act(P, r16b.ap, r16b.ap, AF.Sigmoid, r=[r16b], w=[r16b])
        ts(P, r16b.ap, r16b.ap, 0.05, OP.mult, r=[r16b], w=[r16b])
        mm(P, ps_nat[:, 0:16], orow[0:1, :], r16b.ap, r=[r16b, orow], w=[ps_nat])
        for j in range(4):
            vcopy(P, fl64[:, 16 * j:16 * (j + 1)], ps_nat[:, 0:16], r=[ps_nat], w=[fl64])
        reduce(P, r4.ap, r16b.t[0:1, :].rearrange("o (h j) -> o h j", j=4), OP.add,
               r=[r16b], w=[r4])
        ts(P, r4.ap, r4.ap, -1.0, OP.mult, r=[r4], w=[r4], s2=1.0, op1=OP.add)
        transpose(P, ps_t4[0:4, 0:1], r4.ap, eye, r=[r4], w=[ps_t4], kdim=1)
        vcopy(P, c4.ap, ps_t4[0:4, 0:1], r=[ps_t4], w=[c4])
        mm(P, ps_nat[0:1, 0:16], c4.ap, r416.ap, r=[c4, r416], w=[ps_nat])
        vcopy(P, r16.ap, ps_nat[0:1, 0:16], r=[ps_nat], w=[r16])
        mm(P, ps_nat[:, 16:32], orow[0:1, :], r16.ap, r=[r16, orow], w=[ps_nat])
        for j in range(4):
            vcopy(P, omf64[:, 16 * j:16 * (j + 1)], ps_nat[:, 16:32], r=[ps_nat], w=[omf64])
        P.load(r4, crl.t[:].rearrange("(o h) -> o h", o=1), r=[crl], w=[r4])
        act(P, r4.ap, r4.ap, AF.Sigmoid, r=[r4], w=[r4])
        mm(P, ps_nat[:, 0:4], orow[0:1, :], r4.ap, r=[r4, orow], w=[ps_nat])
        vcopy(P, crlb.ap, ps_nat[:, 0:4], r=[ps_nat], w=[crlb])
        P.load(r4, alp.t[:].rearrange("(o h) -> o h", o=1), r=[alp], w=[r4])
        mm(P, ps_nat[:, 0:4], orow[0:1, :], r4.ap, r=[r4, orow], w=[ps_nat])
        vcopy(P, alpb.ap, ps_nat[:, 0:4], r=[ps_nat], w=[alpb])
        hl = [ctx.sb(f"s0c2_hl{k}", 128, PAN) for k in range(KT)]
        xps = [ctx.sb(f"s0c2_xp{k}", 128, PAN) for k in range(KT)]
        onT = [ctx.sb(f"s0c2_on{k}", 128, PAN, BF16) for k in range(KT)]
        brt = {nm: [ctx.sb(f"s0c2_{nm}{j}", 128, PAN) for j in range(2)]
               for nm in ("ls", "ll", "do", "vp", "id")}
        sqt = ctx.sb("s0c2_sqt", 128, PAN)
        abt = ctx.sb("s0c2_abt", 128, PAN)
        Gsb = ctx.sb("s0c2_G", 16, PAN)
        m2sb = ctx.sb("s0c2_m2", 16, PAN)
        sts = [ctx.sb(f"s0c2_st{h}", 16, PAN) for h in range(NH)]
        shsb = ctx.sb("s0c2_sh", 128, PAN)
        tg = ctx.sb("s0c2_tg", 128, PAN)
        ghm = ctx.sb("s0c2_ghm", 128, PAN)
        lgs = [ctx.sb(f"s0c2_lg{h}", 4, PAN) for h in range(NH)]
        lgt = ctx.sb("s0c2_lgt", 4, PAN)
        nats = ctx.sb("s0c2_nats", 128, 64)
        ex = ctx.sb("s0c2_ex", 128, 64)
        sums = ctx.sb("s0c2_sums", 128, 16)
        prT = ctx.sb("s0c2_prT", 16, PAN)
        oac = [ctx.sb(f"s0c2_oac{d}", 128, PAN) for d in range(2)]
        tmx = ctx.sb("s0c2_tmx", 128, PAN)
        rr = ctx.sb("s0c2_rr", 4, PAN)
        x1s = ctx.sb("s0c2_x1s", 128, PAN)

        def load_branches(b, h, sl_t):
            for j in range(2):
                k = 2 * h + j
                for nm, src in (("ls", lsT), ("ll", llT), ("do", doutT), ("vp", vT),
                                ("id", idT)):
                    P.load(brt[nm][j], src.t[b, k * 128:(k + 1) * 128, sl_t], r=[src],
                           q=("sync" if (j + ord(nm[0])) % 2 == 0 else "gpsimd"))

        for b in range(B):
            for p in range(NPAN):
                sl_t = slice(p * PAN, (p + 1) * PAN)
                for k in range(KT):
                    P.load(hl[k], hlnT.t[b, k * 128:(k + 1) * 128, sl_t], r=[hlnT],
                           q=("sync" if k % 2 == 0 else "gpsimd"))
                for h in range(NH):
                    load_branches(b, h, sl_t)
                    for br, nm in enumerate(("ls", "ll", "do", "vp")):
                        for j in range(2):
                            tb_ = brt[nm][j]
                            first = (br == 0 and j == 0)
                            mm(P, ps_t4.ap, selc[:, (3 * br + 0) * 16:(3 * br + 1) * 16],
                               tb_.ap, r=[tb_, selc], w=[ps_t4], start=first, stop=False)
                            act(P, sqt.ap, tb_.ap, AF.Square, r=[tb_], w=[sqt])
                            mm(P, ps_t4.ap, selc[:, (3 * br + 1) * 16:(3 * br + 2) * 16],
                               sqt.ap, r=[sqt, selc], w=[ps_t4], start=False, stop=False)
                            act(P, abt.ap, tb_.ap, AF.Abs, r=[tb_], w=[abt])
                            mm(P, ps_t4.ap, selc[:, (3 * br + 2) * 16:(3 * br + 3) * 16],
                               abt.ap, r=[abt, selc], w=[ps_t4], start=False,
                               stop=(br == 3 and j == 1))
                    ts(P, Gsb.ap, ps_t4.ap, stsc.t[:, 0:1], OP.mult, r=[ps_t4, stsc], w=[Gsb])
                    mm(P, ps_t4.ap, sh16.ap, Gsb.ap, r=[sh16, Gsb], w=[ps_t4])
                    act(P, m2sb.ap, ps_t4.ap, AF.Square, r=[ps_t4], w=[m2sb])
                    tt(P, Gsb.ap, Gsb.ap, m2sb.ap, OP.subtract, r=[Gsb, m2sb], w=[Gsb])
                    ts(P, m2sb.ap, Gsb.ap, 0.0, OP.max, r=[Gsb], w=[m2sb])
                    act(P, m2sb.ap, m2sb.ap, AF.Sqrt, r=[m2sb], w=[m2sb])
                    tt(P, m2sb.ap, m2sb.ap, Gsb.ap, OP.subtract, r=[m2sb, Gsb], w=[m2sb])
                    tt(P, m2sb.ap, m2sb.ap, m3b[:, 0:PAN], OP.mult, r=[m2sb, m3b], w=[m2sb])
                    tt(P, sts[h].ap, Gsb.ap, m2sb.ap, OP.add, r=[Gsb, m2sb], w=[sts[h]])
                for h in range(NH):
                    memset(P, lgs[h].ap, 0.0, w=[lgs[h]])
                for mt in range(KT):
                    for k in range(KT):
                        mm(P, ps_g.ap, gwa[k][:, mt * 128:(mt + 1) * 128], hl[k].ap,
                           r=[gwa[k], hl[k]], w=[ps_g], start=(k == 0), stop=(k == KT - 1))
                    vcopy(P, shsb.ap, ps_g.ap, r=[ps_g], w=[shsb])
                    for h in range(NH):
                        mm(P, ps_s1.ap, gwb[:, mt * 128:(mt + 1) * 128], sts[h].ap,
                           r=[gwb, sts[h]], w=[ps_s1])
                        tt(P, tg.ap, ps_s1.ap, shsb.ap, OP.add, r=[ps_s1, shsb], w=[tg])
                        act(P, ghm.ap, tg.ap, AF.Gelu, r=[tg, gb1s], w=[ghm],
                            bias=gb1s[:, mt:mt + 1])
                        mm(P, ps_s1[0:4, :], gw2[:, mt * 4:(mt + 1) * 4], ghm.ap,
                           r=[gw2, ghm], w=[ps_s1])
                        vcopy(P, lgt.ap, ps_s1[0:4, :], r=[ps_s1], w=[lgt])
                        tt(P, lgs[h].ap, lgs[h].ap, lgt.ap, OP.add, r=[lgs[h], lgt],
                           w=[lgs[h]])
                for h in range(NH):
                    ts(P, lgs[h].ap, lgs[h].ap, gb2s.t[:, 0:1], OP.add, r=[lgs[h], gb2s],
                       w=[lgs[h]])
                for s in range(4):
                    for h in range(NH):
                        transpose(P, ps_nat[:, s * 16 + 4 * h:s * 16 + 4 * h + 4],
                                  lgs[h][:, s * 128:(s + 1) * 128], eye,
                                  r=[lgs[h]], w=[ps_nat], kdim=4)
                vcopy(P, nats.ap, ps_nat.ap, r=[ps_nat], w=[nats])
                tt(P, nats.ap, nats.ap, invt.ap, OP.mult, r=[nats, invt], w=[nats])
                act(P, ex.ap, nats.ap, AF.Exp, r=[nats], w=[ex])
                reduce(P, sums.ap, ex.t[:, :].rearrange("p (s j) -> p s j", j=4), OP.add,
                       r=[ex], w=[sums])
                recip(P, sums.ap, sums.ap, r=[sums], w=[sums])
                for s in range(4):
                    for h in range(NH):
                        c_ = s * 4 + h
                        ts(P, ex[:, 16 * s + 4 * h:16 * s + 4 * h + 4],
                           ex[:, 16 * s + 4 * h:16 * s + 4 * h + 4],
                           sums[:, c_:c_ + 1], OP.mult, r=[ex, sums], w=[ex])
                tt(P, ex.ap, ex.ap, omf64.ap, OP.mult, r=[ex, omf64], w=[ex])
                tt(P, ex.ap, ex.ap, fl64.ap, OP.add, r=[ex, fl64], w=[ex])
                for s in range(4):
                    transpose(P, ps_t4[:, s * 128:(s + 1) * 128], ex[:, s * 16:(s + 1) * 16],
                              eye, r=[ex], w=[ps_t4], kdim=128)
                vcopy(P, prT.ap, ps_t4.ap, r=[ps_t4], w=[prT])
                for k in range(KT):
                    P.load(xps[k], xT.t[b, k * 128:(k + 1) * 128, sl_t], r=[xT],
                           q=("sync" if k % 2 == 0 else "gpsimd"))
                for h in range(NH):
                    load_branches(b, h, sl_t)
                    for j in range(4):
                        mm(P, ps_big[:, j * 512:(j + 1) * 512],
                           sr16[:, (4 * h + j) * 128:(4 * h + j + 1) * 128], prT.ap,
                           r=[sr16, prT], w=[ps_big])
                    for dt in range(2):
                        def f0(e, dt=dt, h=h):
                            return e.scalar_tensor_tensor(
                                oac[dt].ap, ps_big[:, 0:512], crlb[:, h:h + 1],
                                brt["ls"][dt].ap, OP.add, OP.mult)
                        P.v(f0, r=[ps_big, crlb, brt["ls"][dt]], w=[oac[dt]])
                        for (jj, nm) in ((1, "ll"), (2, "do"), (3, "vp")):
                            tt(P, tmx.ap, ps_big[:, jj * 512:(jj + 1) * 512],
                               brt[nm][dt].ap, OP.mult, r=[ps_big, brt[nm][dt]], w=[tmx])
                            tt(P, oac[dt].ap, oac[dt].ap, tmx.ap, OP.add,
                               r=[oac[dt], tmx], w=[oac[dt]])
                        ts(P, tmx.ap, brt["id"][dt].ap, alpb[:, h:h + 1], OP.mult,
                           r=[brt["id"][dt], alpb], w=[tmx])
                        tt(P, oac[dt].ap, oac[dt].ap, tmx.ap, OP.add, r=[oac[dt], tmx],
                           w=[oac[dt]])
                        act(P, sqt.ap, oac[dt].ap, AF.Square, r=[oac[dt]], w=[sqt])
                        mm(P, ps_t4[0:4, :], selc4[:, 4 * h:4 * h + 4], sqt.ap,
                           r=[sqt, selc4], w=[ps_t4], start=(dt == 0), stop=(dt == 1))
                    act(P, rr.ap, ps_t4[0:4, :], AF.Sqrt, r=[ps_t4], w=[rr],
                        scale=1.0 / 256.0, bias=1e-5)
                    recip(P, rr.ap, rr.ap, r=[rr], w=[rr])
                    mm(P, ps_g.ap, sr4[:, h * 128:(h + 1) * 128], rr.ap, r=[rr, sr4],
                       w=[ps_g])
                    for dt in range(2):
                        k = 2 * h + dt
                        def f1(e, k=k, dt=dt):
                            return e.scalar_tensor_tensor(
                                onT[k].ap, oac[dt].ap, onws[:, dt:dt + 1], ps_g.ap,
                                OP.mult, OP.mult)
                        P.v(f1, r=[oac[dt], onws, ps_g], w=[onT[k]])
                for mt in range(KT):
                    pj = ps_s1 if mt % 2 == 0 else ps_g
                    for k in range(KT):
                        mm(P, pj.ap, wob[k][:, mt * 128:(mt + 1) * 128], onT[k].ap,
                           r=[wob[k], onT[k]], w=[pj], start=(k == 0), stop=(k == KT - 1))
                    tt(P, x1s.ap, pj.ap, xps[mt].ap, OP.add, r=[pj, xps[mt]], w=[x1s])
                    P.store(x1T.t[b, mt * 128:(mt + 1) * 128, sl_t], x1s.ap, x1T, r=[x1s])
            P.barrier()
    P.barrier()


def emit_s1g(P, ctx):
    x1T = ctx.dscratch("x1T", [B, D, T], F32)
    hlnT1b = ctx.dscratch("hlnT1b", [B, D, T], BF16)
    smaskT = ctx.dscratch("smaskT", [B, E0, T], F32)
    wred = ctx.din("L0_ffn_w_red", [D, 16])
    wg_d = ctx.din("L0_ffn_wg", [E0, 16])
    eye = ctx.const("eye_f")
    eyeb = ctx.const("eye_b")
    with scope(ctx):
        xs = [ctx.sb(f"s1g_x{k}", 128, PAN) for k in range(KT)]
        hlb = [ctx.sb(f"s1g_h{k}", 128, PAN, BF16) for k in range(KT)]
        la = _ln_alloc(ctx, "s1g")
        wrs = ctx.sb("s1g_wrs", 128, 128)
        wrb = ctx.sb("s1g_wrb", 128, 128, BF16)
        P.load(rr2(wrs, "p (a j) -> p a j", j=16),
               wred.t[:, :].rearrange("(a p) j -> p a j", p=128), r=[wred], w=[wrs], slow=True)
        vcopy(P, wrb.ap, wrs.ap, r=[wrs], w=[wrb])
        wgs = ctx.sb("s1g_wgs", E0, 16)
        P.load(wgs, wg_d.ap, r=[wg_d])
        wsq = ctx.sb("s1g_wsq", E0, 16)
        wnr = ctx.sb("s1g_wnr", E0, 1)
        wgn = ctx.sb("s1g_wgn", E0, 16, BF16)
        wgnT = ctx.sb("s1g_wgnT", 16, E0, BF16)
        act(P, wsq.ap, wgs.ap, AF.Square, r=[wgs], w=[wsq])
        reduce(P, wnr.ap, wsq.ap, OP.add, r=[wsq], w=[wnr])
        act(P, wnr.ap, wnr.ap, AF.Sqrt, r=[wnr], w=[wnr])
        ts(P, wnr.ap, wnr.ap, 1e-4, OP.max, r=[wnr], w=[wnr])
        recip(P, wnr.ap, wnr.ap, r=[wnr], w=[wnr])
        ts(P, wgn.ap, wgs.ap, wnr.t[:, 0:1], OP.mult, r=[wgs, wnr], w=[wgn])
        ps_w = ctx.ps("s1g_psw", 16, E0, BF16)
        transpose(P, ps_w.ap, wgn.ap, eyeb, r=[wgn], w=[ps_w], kdim=E0)
        vcopy(P, wgnT.ap, ps_w.ap, r=[ps_w], w=[wgnT])
        red = ctx.sb("s1g_red", 16, PAN, BF16)
        lg8 = ctx.sb("s1g_lg8", E0, PAN)
        nat = ctx.sb("s1g_nat", 128, 32)
        exl = ctx.sb("s1g_exl", 128, 32)
        exs = ctx.sb("s1g_exs", 128, 32)
        msk = ctx.sb("s1g_msk", 128, 32)
        lmx = ctx.sb("s1g_lmx", 128, 4)
        ssum = ctx.sb("s1g_ssum", 128, 4)
        smT = ctx.sb("s1g_smT", E0, PAN)
        ps16 = ctx.ps("s1g_ps16", 16, PAN)
        ps8 = ctx.ps("s1g_ps8", E0, PAN)
        psn = ctx.ps("s1g_psn", 128, 32)
        for b in range(B):
            for p in range(NPAN):
                sl_t = slice(p * PAN, (p + 1) * PAN)
                for k in range(KT):
                    P.load(xs[k], x1T.t[b, k * 128:(k + 1) * 128, sl_t], r=[x1T],
                           q=("sync" if k % 2 == 0 else "gpsimd"))
                emit_ln_to(P, ctx, la, xs, hlb)
                for k in range(KT):
                    P.store(hlnT1b.t[b, k * 128:(k + 1) * 128, sl_t], hlb[k].ap,
                            hlnT1b, r=[hlb[k]])
                for k in range(KT):
                    mm(P, ps16.ap, wrb[:, k * 16:(k + 1) * 16], hlb[k].ap,
                       r=[wrb, hlb[k]], w=[ps16], start=(k == 0), stop=(k == KT - 1))
                vcopy(P, red.ap, ps16.ap, r=[ps16], w=[red])
                mm(P, ps8.ap, wgnT.ap, red.ap, r=[wgnT, red], w=[ps8])
                vcopy(P, lg8.ap, ps8.ap, r=[ps8], w=[lg8])
                for s in range(4):
                    transpose(P, psn[:, s * 8:(s + 1) * 8], lg8[:, s * 128:(s + 1) * 128],
                              eye, r=[lg8], w=[psn], kdim=8)
                vcopy(P, nat.ap, psn.ap, r=[psn], w=[nat])
                for s in range(4):
                    sl8 = slice(s * 8, (s + 1) * 8)
                    def fr(e, sl8=sl8, s=s):
                        return e.tensor_reduce(lmx[:, s:s + 1], nat[:, sl8], AX.X, OP.max)
                    P.v(fr, r=[nat], w=[lmx])
                    ts(P, exl[:, sl8], nat[:, sl8], lmx[:, s:s + 1], OP.subtract,
                       r=[nat, lmx], w=[exl])
                act(P, exs.ap, exl.ap, AF.Exp, r=[exl], w=[exs])
                for s in range(4):
                    sl8 = slice(s * 8, (s + 1) * 8)
                    def fr2(e, sl8=sl8, s=s):
                        return e.tensor_reduce(ssum[:, s:s + 1], exs[:, sl8], AX.X, OP.add)
                    P.v(fr2, r=[exs], w=[ssum])
                recip(P, ssum.ap, ssum.ap, r=[ssum], w=[ssum])
                for s in range(4):
                    sl8 = slice(s * 8, (s + 1) * 8)
                    ts(P, msk[:, sl8], nat[:, sl8], lmx[:, s:s + 1], OP.is_equal,
                       r=[nat, lmx], w=[msk])
                    ts(P, msk[:, sl8], msk[:, sl8], ssum[:, s:s + 1], OP.mult,
                       r=[msk, ssum], w=[msk])
                for s in range(4):
                    transpose(P, ps8[:, s * 128:(s + 1) * 128], msk[:, s * 8:(s + 1) * 8],
                              eye, r=[msk], w=[ps8], kdim=128)
                vcopy(P, smT.ap, ps8.ap, r=[ps8], w=[smT])
                P.store(smaskT.t[b, :, sl_t], smT.ap, smaskT, r=[smT])
    P.barrier()


def emit_s1w(P, ctx):
    w1 = ctx.din("L0_ffn_w1", [E0, D, F0])
    w2 = ctx.din("L0_ffn_w2", [E0, F0, D])
    w1b = ctx.dscratch("w1b", [E0, KT, 128, F0], BF16)
    w2b = ctx.dscratch("w2b", [E0, 32, 128, D], BF16)
    with scope(ctx):
        stg = [ctx.sb(f"s1w_stg{i}", 128, F0) for i in range(2)]
        ob = [ctx.sb(f"s1w_ob{i}", 128, F0, BF16) for i in range(2)]
        i = 0
        for e in range(E0):
            for k in range(KT):
                s_ = stg[i % 2]; o_ = ob[i % 2]
                P.load(s_, w1.t[e, k * 128:(k + 1) * 128, :], r=[w1], w=[s_],
                       q=("sync" if i % 2 == 0 else "gpsimd"))
                vcopy(P, o_.ap, s_.ap, r=[s_], w=[o_], eng=("v" if i % 2 == 0 else "s"))
                P.store(w1b.t[e, k, :, :], o_.ap, w1b, r=[o_],
                        q=("gpsimd" if i % 2 == 0 else "sync"))
                i += 1
        for e in range(E0):
            for f in range(32):
                s_ = stg[i % 2]; o_ = ob[i % 2]
                P.load(TB(s_.t[:, 0:D], "stg"), w2.t[e, f * 128:(f + 1) * 128, :],
                       r=[w2], w=[s_], q=("sync" if i % 2 == 0 else "gpsimd"))
                vcopy(P, o_.t[:, 0:D], s_.t[:, 0:D], r=[s_], w=[o_],
                      eng=("v" if i % 2 == 0 else "s"))
                P.store(w2b.t[e, f, :, :], o_.t[:, 0:D], w2b, r=[o_],
                        q=("gpsimd" if i % 2 == 0 else "sync"))
                i += 1
    P.barrier()


def emit_s1e(P, ctx):
    x1T = ctx.dscratch("x1T", [B, D, T], F32)
    hlnT1b = ctx.dscratch("hlnT1b", [B, D, T], BF16)
    smaskT = ctx.dscratch("smaskT", [B, E0, T], F32)
    w1b = ctx.dscratch("w1b", [E0, KT, 128, F0], BF16)
    w2b = ctx.dscratch("w2b", [E0, 32, 128, D], BF16)
    x2T = ctx.dscratch("x2T", [B, D, T], F32)
    b1d = ctx.din("L0_ffn_b1", [E0, F0])
    b2d = ctx.din("L0_ffn_b2", [E0, D])
    sr8 = ctx.const("sel_row8")
    with scope(ctx):
        x2a = [ctx.sb(f"s1e_x2a{k}", 128, T) for k in range(KT)]
        smk = ctx.sb("s1e_smk", E0, PAN)
        b1a = ctx.sb("s1e_b1a", 128, 256)
        b2a = ctx.sb("s1e_b2a", 128, 64)
        P.load(rr2(b1a, "p (e a) -> p e a", e=8),
               b1d.t[:, :].rearrange("e (a p) -> p e a", p=128), r=[b1d], w=[b1a], slow=True)
        P.load(rr2(b2a, "p (e a) -> p e a", e=8),
               b2d.t[:, :].rearrange("e (a p) -> p e a", p=128), r=[b2d], w=[b2a], slow=True)
        w1h = [ctx.sb(f"s1e_w1h{k}", 128, 2048, BF16) for k in range(KT)]
        w2h = [ctx.sb(f"s1e_w2h{j}", 128, D, BF16) for j in range(16)]
        hlb = [ctx.sb(f"s1e_hl{k}", 128, T, BF16) for k in range(KT)]
        hbf = [ctx.sb(f"s1e_hb{j}", 128, PAN, BF16) for j in range(16)]
        smbc = ctx.sb("s1e_smbc", 128, PAN)
        tmx = ctx.sb("s1e_tmx", 128, PAN)
        ps_h = [ctx.ps(f"s1e_psh{i}", 128, PAN) for i in range(2)]
        ps_y = [ctx.ps(f"s1e_psy{i}", 128, PAN) for i in range(2)]
        ps_sm = ctx.ps("s1e_pssm", 128, PAN)
        for b in range(B):
            for k in range(KT):
                P.load(x2a[k], x1T.t[b, k * 128:(k + 1) * 128, :], r=[x1T], w=[x2a[k]],
                       q=("sync" if k % 2 == 0 else "gpsimd"))

            for k in range(KT):
                P.load(hlb[k], hlnT1b.t[b, k * 128:(k + 1) * 128, :],
                       r=[hlnT1b], w=[hlb[k]],
                       q=("sync" if k % 2 == 0 else "gpsimd"))
            for e in range(E0):
                for fh in range(2):
                    for k in range(KT):
                        P.load(w1h[k], w1b.t[e, k, :, fh * 2048:(fh + 1) * 2048],
                               r=[w1b], w=[w1h[k]], q=("sync" if k % 2 == 0 else "gpsimd"))
                    for j in range(16):
                        P.load(w2h[j], w2b.t[e, fh * 16 + j, :, :], r=[w2b], w=[w2h[j]],
                               q=("sync" if j % 2 == 0 else "gpsimd"))
                    for p in range(NPAN):
                        sl_t = slice(p * PAN, (p + 1) * PAN)
                        P.load(smk, smaskT.t[b, :, sl_t], r=[smaskT], w=[smk])
                        mm(P, ps_sm.ap, sr8[:, e * 128:(e + 1) * 128], smk.ap,
                           r=[sr8, smk], w=[ps_sm])
                        vcopy(P, smbc.ap, ps_sm.ap, r=[ps_sm], w=[smbc])
                        for ft in range(16):
                            pj = ps_h[ft % 2]
                            for k in range(KT):
                                mm(P, pj.ap, w1h[k][:, ft * 128:(ft + 1) * 128],
                                   hlb[k][:, sl_t],
                                   r=[w1h[k], hlb[k]], w=[pj], start=(k == 0),
                                   stop=(k == KT - 1))
                            act(P, hbf[ft].ap, pj.ap, AF.Gelu, r=[pj, b1a], w=[hbf[ft]],
                                bias=b1a[:, e * 32 + fh * 16 + ft:e * 32 + fh * 16 + ft + 1])
                        for mt in range(KT):
                            pj = ps_y[mt % 2]
                            for j in range(16):
                                mm(P, pj.ap, w2h[j][:, mt * 128:(mt + 1) * 128], hbf[j].ap,
                                   r=[w2h[j], hbf[j]], w=[pj], start=(j == 0),
                                   stop=(j == 15))
                            if fh == 1:
                                def fb(e_, pj=pj, mt=mt, e=e):
                                    return e_.scalar_tensor_tensor(
                                        tmx.ap, pj.ap, b2a[:, e * 8 + mt:e * 8 + mt + 1],
                                        smbc.ap, OP.add, OP.mult)
                                P.v(fb, r=[pj, b2a, smbc], w=[tmx])
                            else:
                                tt(P, tmx.ap, pj.ap, smbc.ap, OP.mult, r=[pj, smbc],
                                   w=[tmx])
                            tt(P, x2a[mt][:, sl_t], x2a[mt][:, sl_t], tmx.ap, OP.add,
                               r=[x2a[mt], tmx], w=[x2a[mt]])
            for k in range(KT):
                P.store(x2T.t[b, k * 128:(k + 1) * 128, :], x2a[k].ap, x2T, r=[x2a[k]],
                        q=("gpsimd" if k % 2 == 0 else "sync"))
            P.barrier()
    P.barrier()


def emit_s2(P, ctx):
    x2T = ctx.dscratch("x2T", [B, D, T], F32)
    x3T = ctx.dscratch("x3T", [B, D, T], F32)
    Wq = ctx.din("L1_attn_Wq", [H1, D, DH1])
    Wk = ctx.din("L1_attn_Wk", [H1, D, DH1])
    Wv = ctx.din("L1_attn_Wv", [H1, D, DH1])
    Wo = ctx.din("L1_attn_Wo", [D, D])
    bo = ctx.din("L1_attn_bo", [D])
    eye = ctx.const("eye_f")
    eyeb = ctx.const("eye_b")
    ocb = ctx.const("ones_col_b")
    orow = ctx.const("ones_row_f")
    m_ui = ctx.const("mask_ui4")
    with scope(ctx):
        ps_s = [ctx.ps(f"s2_pss{i}", 128, PAN) for i in range(2)]
        ps_w2 = ctx.ps("s2_psw2", 128, 1024)
        ps_sum = ctx.ps("s2_pssum", 1, PAN)
        ps_t = ctx.ps("s2_pst", 128, 64, BF16)
        xs = [ctx.sb(f"s2_x{k}", 128, PAN) for k in range(KT)]
        hlb = [ctx.sb(f"s2_hl{k}", 128, T, BF16) for k in range(KT)]
        hpan = [TB(hlb[k].t, f"alias_hl{k}") for k in range(KT)]
        la = {"tmp": ctx.sb("s2_lntmp", 128, PAN), "row": ctx.sb("s2_lnrow", 1, 4 * PAN),
              "psum": ps_sum, "psq": ps_s[1], "pbc": ps_w2}
        qw = {}
        wstg = ctx.sb("s2_wstg", 128, 512)
        for nm, Wd in (("q", Wq), ("k", Wk), ("v", Wv)):
            for h in range(H1):
                t_ = ctx.sb(f"s2_w{nm}{h}", 128, 512, BF16)
                P.load(rr2(wstg, "p (a j) -> p a j", j=DH1),
                       Wd.t[h].rearrange("(a p) j -> p a j", p=128),
                       r=[Wd], w=[wstg], slow=True,
                       q=("sync" if h % 2 == 0 else "gpsimd"))
                vcopy(P, t_.ap, wstg.ap, r=[wstg], w=[t_],
                      eng=("v" if h % 2 == 0 else "s"))
                qw[(nm, h)] = t_
        woT = [ctx.sb(f"s2_woT{k}", 128, 1024, BF16) for k in range(KT)]
        wst2 = ctx.sb("s2_wst2", 128, 1024)
        for m in range(KT):
            P.load(wst2, Wo.t[m * 128:(m + 1) * 128, :], r=[Wo], w=[wst2])
            for k in range(KT):
                transpose(P, ps_w2[:, 0:128], wst2[:, k * 128:(k + 1) * 128], eye,
                          r=[wst2], w=[ps_w2])
                vcopy(P, woT[k][:, m * 128:(m + 1) * 128], ps_w2[:, 0:128],
                      r=[ps_w2], w=[woT[k]])
        bos = ctx.sb("s2_bos", 128, KT)
        P.load(bos, bo.t[:].rearrange("(a p) -> p a", p=128), r=[bo], slow=True)
        qT = ctx.sb("s2_qT", 64, T, BF16)
        kT = ctx.sb("s2_kT", 64, T, BF16)
        vTs = ctx.sb("s2_vT", 64, T, BF16)
        vn = [ctx.sb(f"s2_vn{tt_}", 128, 64, BF16) for tt_ in range(T // 128)]
        pex = ctx.sb("s2_pex", 128, PAN, BF16)
        srow = ctx.sb("s2_srow", 1, PAN)
        of32 = ctx.sb("s2_of32", 64, PAN)
        oall = [ctx.sb(f"s2_oa{k}", 128, T, BF16) for k in range(KT)]
        x3s = ctx.sb("s2_x3s", 128, PAN)
        for b in range(B):
            for p in range(NPAN):
                for k in range(KT):
                    P.load(xs[k], x2T.t[b, k * 128:(k + 1) * 128, p * PAN:(p + 1) * PAN],
                           r=[x2T], q=("sync" if k % 2 == 0 else "gpsimd"))
                dsts = [View(hlb[k].t[:, p * PAN:(p + 1) * PAN], hlb[k].b)
                        for k in range(KT)]
                emit_ln_panel(P, ctx, xs, dsts, la["tmp"], la["psum"], la["psq"],
                              la["row"], la["pbc"])
            for h in range(H1):
                for nm, dst in (("q", qT), ("k", kT), ("v", vTs)):
                    w_ = qw[(nm, h)]
                    for p in range(NPAN):
                        pj = ps_s[p % 2]
                        for k in range(KT):
                            mm(P, pj[0:64, :], w_[:, k * 64:(k + 1) * 64],
                               hlb[k][:, p * PAN:(p + 1) * PAN], r=[w_, hlb[k]], w=[pj],
                               start=(k == 0), stop=(k == KT - 1))
                        vcopy(P, dst[0:64, p * PAN:(p + 1) * PAN], pj[0:64, :],
                              r=[pj], w=[dst])
                for tt_ in range(T // 128):
                    transpose(P, ps_t[:, 0:64], vTs[0:64, tt_ * 128:(tt_ + 1) * 128],
                              eyeb, r=[vTs], w=[ps_t], kdim=64)
                    vcopy(P, vn[tt_].ap, ps_t[:, 0:64], r=[ps_t], w=[vn[tt_]])
                for ip in range(NPAN):
                    for jt in range(4 * ip + 4):
                        off = max(0, jt - 4 * ip)
                        cols = PAN - off * 128
                        pj = ps_s[jt % 2]
                        mm(P, pj[:, 0:cols], kT[0:64, jt * 128:(jt + 1) * 128],
                           qT[0:64, ip * PAN + off * 128:(ip + 1) * PAN],
                           r=[kT, qT], w=[pj])
                        act(P, pex[:, 0:cols], pj[:, 0:cols], AF.Exp, r=[pj], w=[pex],
                            scale=0.125)
                        if jt >= 4 * ip:
                            tt(P, pex[:, 0:128], pex[:, 0:128], m_ui[:, 0:128], OP.mult,
                               r=[pex, m_ui], w=[pex])
                        mm(P, ps_sum[0:1, off * 128:PAN], ocb[:, 0:1], pex[:, 0:cols],
                           r=[pex, ocb], w=[ps_sum], start=(jt == 0), stop=True)
                        mm(P, ps_w2[0:64, off * 128:PAN], vn[jt].ap, pex[:, 0:cols],
                           r=[vn[jt], pex], w=[ps_w2], start=(jt == 0), stop=True)
                    vcopy(P, srow.ap, ps_sum[0:1, :], r=[ps_sum], w=[srow])
                    recip(P, srow.ap, srow.ap, r=[srow], w=[srow])
                    mm(P, ps_w2[0:64, 512:1024], orow[0:1, 0:64], srow.ap,
                       r=[srow, orow], w=[ps_w2])
                    vcopy(P, of32.ap, ps_w2[0:64, 0:512], r=[ps_w2], w=[of32])
                    kk = h // 2; po = (h % 2) * 64
                    tt(P, oall[kk].t[po:po + 64, ip * PAN:(ip + 1) * PAN], of32.ap,
                       ps_w2[0:64, 512:1024], OP.mult, r=[of32, ps_w2], w=[oall[kk]])
            for p in range(NPAN):
                sl_t = slice(p * PAN, (p + 1) * PAN)
                for k in range(KT):
                    P.load(xs[k], x2T.t[b, k * 128:(k + 1) * 128, sl_t], r=[x2T],
                           q=("sync" if k % 2 == 0 else "gpsimd"))
                for mt in range(KT):
                    pj = ps_s[mt % 2]
                    for k in range(KT):
                        mm(P, pj.ap, woT[k][:, mt * 128:(mt + 1) * 128],
                           oall[k][:, sl_t], r=[woT[k], oall[k]], w=[pj],
                           start=(k == 0), stop=(k == KT - 1))
                    def fo(e, pj=pj, mt=mt):
                        return e.scalar_tensor_tensor(
                            x3s.ap, pj.ap, bos[:, mt:mt + 1], xs[mt].ap, OP.add, OP.add)
                    P.v(fo, r=[pj, bos, xs[mt]], w=[x3s])
                    P.store(x3T.t[b, mt * 128:(mt + 1) * 128, sl_t], x3s.ap, x3T, r=[x3s])
            P.barrier()
    P.barrier()


def emit_s3a(P, ctx):
    x3T = ctx.dscratch("x3T", [B, D, T], F32)
    hlnT3b = ctx.dscratch("hlnT3b", [B, D, T], BF16)
    gsc = ctx.dscratch("gsc", [B, E1, 1], F32)
    gw1 = ctx.din("L1_ffn_gate_w1", [D, D])
    gb1_ = ctx.din("L1_ffn_gate_b1", [D])
    gw2_ = ctx.din("L1_ffn_gate_w2", [E1, D])
    gb2_ = ctx.din("L1_ffn_gate_b2", [E1])
    eye = ctx.const("eye_f")
    with scope(ctx):
        xs = [ctx.sb(f"s3a_x{k}", 128, PAN) for k in range(KT)]
        hlb = [ctx.sb(f"s3a_h{k}", 128, PAN, BF16) for k in range(KT)]
        la = _ln_alloc(ctx, "s3a")
        wst = ctx.sb("s3a_wst", 128, 1024)
        gw1T = [ctx.sb(f"s3a_g1T{k}", 128, 1024, BF16) for k in range(KT)]
        pst = ctx.ps("s3a_pst", 128, 128)
        for o in range(KT):
            P.load(wst, gw1.t[o * 128:(o + 1) * 128, :], r=[gw1], w=[wst])
            for k in range(KT):
                transpose(P, pst.ap, wst[:, k * 128:(k + 1) * 128], eye, r=[wst], w=[pst])
                vcopy(P, gw1T[k][:, o * 128:(o + 1) * 128], pst.ap, r=[pst], w=[gw1T[k]])
        gw2s = ctx.sb("s3a_gw2s", E1, 1024)
        P.load(gw2s, gw2_.ap, r=[gw2_])
        gw2T = [ctx.sb(f"s3a_g2T{k}", 128, E1, BF16) for k in range(KT)]
        for k in range(KT):
            transpose(P, pst[:, 0:E1], gw2s[:, k * 128:(k + 1) * 128], eye,
                      r=[gw2s], w=[pst], kdim=E1)
            vcopy(P, gw2T[k].ap, pst[:, 0:E1], r=[pst], w=[gw2T[k]])
        gb1c = ctx.sb("s3a_gb1c", 128, KT)
        P.load(gb1c, gb1_.t[:].rearrange("(a p) -> p a", p=128), r=[gb1_], slow=True)
        gb2c = ctx.sb("s3a_gb2c", E1, 1)
        P.load(gb2c, gb2_.t[:].rearrange("(e o) -> e o", o=1), r=[gb2_], slow=True)
        h1b = [ctx.sb(f"s3a_h1{k}", 128, PAN, BF16) for k in range(KT)]
        gac = ctx.sb("s3a_gac", E1, 1)
        gtmp = ctx.sb("s3a_gtmp", E1, 1)
        ps_m = [ctx.ps(f"s3a_psm{i}", 128, PAN) for i in range(2)]
        ps_g8 = ctx.ps("s3a_psg8", E1, PAN)
        for b in range(B):
            memset(P, gac.ap, 0.0, w=[gac])
            for p in range(NPAN):
                sl_t = slice(p * PAN, (p + 1) * PAN)
                for k in range(KT):
                    P.load(xs[k], x3T.t[b, k * 128:(k + 1) * 128, sl_t], r=[x3T],
                           q=("sync" if k % 2 == 0 else "gpsimd"))
                emit_ln_to(P, ctx, la, xs, hlb)
                for k in range(KT):
                    P.store(hlnT3b.t[b, k * 128:(k + 1) * 128, sl_t], hlb[k].ap,
                            hlnT3b, r=[hlb[k]])
                for mt in range(KT):
                    pj = ps_m[mt % 2]
                    for k in range(KT):
                        mm(P, pj.ap, gw1T[k][:, mt * 128:(mt + 1) * 128], hlb[k].ap,
                           r=[gw1T[k], hlb[k]], w=[pj], start=(k == 0), stop=(k == KT - 1))
                    act(P, h1b[mt].ap, pj.ap, AF.Relu, r=[pj, gb1c], w=[h1b[mt]],
                        bias=gb1c[:, mt:mt + 1])
                for k in range(KT):
                    mm(P, ps_g8.ap, gw2T[k].ap, h1b[k].ap, r=[gw2T[k], h1b[k]], w=[ps_g8],
                       start=(k == 0), stop=(k == KT - 1))
                reduce(P, gtmp.ap, ps_g8.ap, OP.add, r=[ps_g8], w=[gtmp])
                tt(P, gac.ap, gac.ap, gtmp.ap, OP.add, r=[gac, gtmp], w=[gac])
            ts(P, gac.ap, gac.ap, 1.0 / T, OP.mult, r=[gac], w=[gac])
            tt(P, gac.ap, gac.ap, gb2c.ap, OP.add, r=[gac, gb2c], w=[gac])
            P.store(gsc.t[b], gac.ap, gsc, r=[gac])
    P.barrier()


def emit_s3b(P, ctx):
    gsc = ctx.dscratch("gsc", [B, E1, 1], F32)
    mw1b = ctx.dscratch("mw1b", [B, KT, 128, F1], BF16)
    mw2b = ctx.dscratch("mw2b", [B, 32, 128, D], BF16)
    mb1d = ctx.dscratch("mb1d", [B, 128, 32], F32)
    mb2d = ctx.dscratch("mb2d", [B, 128, E1], F32)
    W1 = ctx.din("L1_ffn_W1", [F1, D])
    W2 = ctx.din("L1_ffn_W2", [D, F1])
    TW1 = ctx.din("L1_ffn_TW1", [E1, F1, D])
    TW2 = ctx.din("L1_ffn_TW2", [E1, D, F1])
    b1_ = ctx.din("L1_ffn_b1", [F1])
    b2_ = ctx.din("L1_ffn_b2", [D])
    Tb1 = ctx.din("L1_ffn_Tb1", [E1, F1])
    Tb2 = ctx.din("L1_ffn_Tb2", [E1, D])
    eye = ctx.const("eye_f")
    sr8 = ctx.const("sel_row8")
    with scope(ctx):
        g8 = ctx.sb("s3b_g8", E1, 1)
        gcol = ctx.sb("s3b_gcol", 128, E1)
        dgs = [ctx.sb(f"s3b_dg{e}", 128, 128) for e in range(E1)]
        stg = [ctx.sb(f"s3b_stg{i}", 128, F1) for i in range(2)]
        gat = ctx.sb("s3b_gat", 128, F1, BF16)
        ps32 = ctx.ps("s3b_ps32", 128, 4096)
        tb1s = ctx.sb("s3b_tb1s", E1, F1)
        tb2s = ctx.sb("s3b_tb2s", E1, D)
        bvec = ctx.sb("s3b_bvec", 128, 32)
        mb = ctx.sb("s3b_mb", 128, 32)
        P.load(tb1s, Tb1.ap, r=[Tb1])
        P.load(tb2s, Tb2.ap, r=[Tb2])
        for b in range(B):
            P.load(g8, gsc.t[b], r=[gsc], w=[g8])
            for e in range(E1):
                mm(P, ps32[:, e:e + 1], sr8[:, e * 128:(e + 1) * 128], g8.ap,
                   r=[g8, sr8], w=[ps32])
            vcopy(P, gcol.ap, ps32[:, 0:E1], r=[ps32], w=[gcol])
            for e in range(E1):
                ts(P, dgs[e].ap, eye.ap, gcol[:, e:e + 1], OP.mult, r=[eye, gcol],
                   w=[dgs[e]])
            for ft in range(32):
                s_ = stg[0]
                P.load(View(s_.t[:, 0:D], s_.b), W1.t[ft * 128:(ft + 1) * 128, :],
                       r=[W1], w=[s_])
                for hc in range(KT):
                    mm(P, ps32[:, hc * 128:(hc + 1) * 128],
                       s_[:, hc * 128:(hc + 1) * 128], eye.ap, r=[s_, eye], w=[ps32],
                       start=True, stop=False)
                for e in range(E1):
                    s_ = stg[(e + 1) % 2]
                    P.load(View(s_.t[:, 0:D], s_.b),
                           TW1.t[e, ft * 128:(ft + 1) * 128, :], r=[TW1], w=[s_],
                           q=("sync" if e % 2 == 0 else "gpsimd"))
                    for hc in range(KT):
                        mm(P, ps32[:, hc * 128:(hc + 1) * 128],
                           s_[:, hc * 128:(hc + 1) * 128], dgs[e].ap, r=[s_, dgs[e]],
                           w=[ps32], start=False, stop=(e == E1 - 1))
                vcopy(P, gat[:, 0:1024], ps32[:, 0:1024], r=[ps32], w=[gat])
                for hc in range(KT):
                    P.store(mw1b.t[b, hc, :, ft * 128:(ft + 1) * 128],
                            gat.t[:, hc * 128:(hc + 1) * 128], mw1b, r=[gat],
                            q=("gpsimd" if hc % 2 == 0 else "sync"))
            for ht in range(KT):
                s_ = stg[0]
                P.load(s_, W2.t[ht * 128:(ht + 1) * 128, :], r=[W2], w=[s_])
                for fc in range(32):
                    mm(P, ps32[:, fc * 128:(fc + 1) * 128],
                       s_[:, fc * 128:(fc + 1) * 128], eye.ap, r=[s_, eye], w=[ps32],
                       start=True, stop=False)
                for e in range(E1):
                    s_ = stg[(e + 1) % 2]
                    P.load(s_, TW2.t[e, ht * 128:(ht + 1) * 128, :], r=[TW2], w=[s_],
                           q=("sync" if e % 2 == 0 else "gpsimd"))
                    for fc in range(32):
                        mm(P, ps32[:, fc * 128:(fc + 1) * 128],
                           s_[:, fc * 128:(fc + 1) * 128], dgs[e].ap, r=[s_, dgs[e]],
                           w=[ps32], start=False, stop=(e == E1 - 1))
                vcopy(P, gat.ap, ps32.ap, r=[ps32], w=[gat])
                for fc in range(32):
                    P.store(mw2b.t[b, fc, :, ht * 128:(ht + 1) * 128],
                            gat.t[:, fc * 128:(fc + 1) * 128], mw2b, r=[gat],
                            q=("gpsimd" if fc % 2 == 0 else "sync"))
            P.load(bvec, b1_.t[:].rearrange("(a p) -> p a", p=128), r=[b1_], w=[bvec],
                   slow=True)
            for ft in range(32):
                mm(P, ps32[:, ft:ft + 1], tb1s[:, ft * 128:(ft + 1) * 128], g8.ap,
                   r=[tb1s, g8], w=[ps32], start=True, stop=True)
            tt(P, mb.ap, ps32[:, 0:32], bvec.ap, OP.add, r=[ps32, bvec], w=[mb])
            P.store(mb1d.t[b], mb.ap, mb1d, r=[mb])
            P.load(View(bvec.t[:, 0:E1], bvec.b),
                   b2_.t[:].rearrange("(a p) -> p a", p=128), r=[b2_], w=[bvec],
                   slow=True)
            for ht in range(KT):
                mm(P, ps32[:, ht:ht + 1], tb2s[:, ht * 128:(ht + 1) * 128], g8.ap,
                   r=[tb2s, g8], w=[ps32], start=True, stop=True)
            tt(P, mb[:, 0:E1], ps32[:, 0:E1], bvec[:, 0:E1], OP.add, r=[ps32, bvec],
               w=[mb])
            P.store(mb2d.t[b], mb.t[:, 0:E1], mb2d, r=[mb])
            P.barrier()
    P.barrier()


def emit_s3c(P, ctx):
    x3T = ctx.dscratch("x3T", [B, D, T], F32)
    hlnT3b = ctx.dscratch("hlnT3b", [B, D, T], BF16)
    mw1b = ctx.dscratch("mw1b", [B, KT, 128, F1], BF16)
    mw2b = ctx.dscratch("mw2b", [B, 32, 128, D], BF16)
    mb1d = ctx.dscratch("mb1d", [B, 128, 32], F32)
    mb2d = ctx.dscratch("mb2d", [B, 128, E1], F32)
    x4T = ctx.dscratch("x4T", [B, D, T], F32)
    with scope(ctx):
        ya = [ctx.sb(f"s3c_ya{k}", 128, T) for k in range(KT)]
        mw1s = [ctx.sb(f"s3c_w1{k}", 128, PAN, BF16) for k in range(KT)]
        mw2s = [ctx.sb(f"s3c_w2{j}", 128, D, BF16) for j in range(4)]
        hlb = [ctx.sb(f"s3c_hl{k}", 128, PAN, BF16) for k in range(KT)]
        ubf = [ctx.sb(f"s3c_u{j}", 128, PAN, BF16) for j in range(4)]
        mb1s = ctx.sb("s3c_mb1", 128, 32)
        mb2s = ctx.sb("s3c_mb2", 128, E1)
        xps = [ctx.sb(f"s3c_xp{k}", 128, PAN) for k in range(KT)]
        x4s = ctx.sb("s3c_x4s", 128, PAN)
        ps_u = [ctx.ps(f"s3c_psu{i}", 128, PAN) for i in range(2)]
        ps_y = [ctx.ps(f"s3c_psy{i}", 128, PAN) for i in range(2)]
        for b in range(B):
            P.load(mb1s, mb1d.t[b], r=[mb1d], w=[mb1s])
            P.load(mb2s, mb2d.t[b], r=[mb2d], w=[mb2s])
            for fc in range(KT):
                for k in range(KT):
                    P.load(mw1s[k], mw1b.t[b, k, :, fc * PAN:(fc + 1) * PAN],
                           r=[mw1b], w=[mw1s[k]], q=("sync" if k % 2 == 0 else "gpsimd"))
                for j in range(4):
                    P.load(mw2s[j], mw2b.t[b, fc * 4 + j, :, :], r=[mw2b], w=[mw2s[j]],
                           q=("sync" if j % 2 == 0 else "gpsimd"))
                for p in range(NPAN):
                    sl_t = slice(p * PAN, (p + 1) * PAN)
                    for k in range(KT):
                        P.load(hlb[k], hlnT3b.t[b, k * 128:(k + 1) * 128, sl_t],
                               r=[hlnT3b], w=[hlb[k]],
                               q=("sync" if k % 2 == 0 else "gpsimd"))
                    for fmt in range(4):
                        pj = ps_u[fmt % 2]
                        for k in range(KT):
                            mm(P, pj.ap, mw1s[k][:, fmt * 128:(fmt + 1) * 128], hlb[k].ap,
                               r=[mw1s[k], hlb[k]], w=[pj], start=(k == 0),
                               stop=(k == KT - 1))
                        ftg = fc * 4 + fmt
                        act(P, ubf[fmt].ap, pj.ap, AF.Gelu_apprx_tanh, r=[pj, mb1s],
                            w=[ubf[fmt]], bias=mb1s[:, ftg:ftg + 1])
                    for mt in range(KT):
                        pj = ps_y[mt % 2]
                        for j in range(4):
                            mm(P, pj.ap, mw2s[j][:, mt * 128:(mt + 1) * 128], ubf[j].ap,
                               r=[mw2s[j], ubf[j]], w=[pj], start=(j == 0), stop=(j == 3))
                        if fc == 0:
                            vcopy(P, ya[mt][:, sl_t], pj.ap, r=[pj], w=[ya[mt]])
                        else:
                            tt(P, ya[mt][:, sl_t], ya[mt][:, sl_t], pj.ap, OP.add,
                               r=[ya[mt], pj], w=[ya[mt]])
            for p in range(NPAN):
                sl_t = slice(p * PAN, (p + 1) * PAN)
                for k in range(KT):
                    P.load(xps[k], x3T.t[b, k * 128:(k + 1) * 128, sl_t], r=[x3T],
                           w=[xps[k]], q=("sync" if k % 2 == 0 else "gpsimd"))
                for mt in range(KT):
                    def fx(e, mt=mt, sl_t=sl_t):
                        return e.scalar_tensor_tensor(
                            x4s.ap, ya[mt][:, sl_t], mb2s[:, mt:mt + 1], xps[mt].ap,
                            OP.add, OP.add)
                    P.v(fx, r=[ya[mt], mb2s, xps[mt]], w=[x4s])
                    P.store(x4T.t[b, mt * 128:(mt + 1) * 128, sl_t], x4s.ap, x4T, r=[x4s])
            P.barrier()
    P.barrier()


STAGES = None

def _default_launch_plan():
    return [("s_in", emit_s_in),
            ("s0a", emit_s0a),
            ("s0a2", emit_s0a2),
            ("s0b", emit_s0b),
            ("s0c1", emit_s0c1),
            ("s0c2", emit_s0c2),
            ("s1g", emit_s1g),
            ("s1w", emit_s1w),
            ("s1e", emit_s1e),
            ("s2", emit_s2),
            ("s3a", emit_s3a),
            ("s3b", emit_s3b),
            ("s3c", emit_s3c),
            ("s_out", emit_s_out)]

def _run_launch(name, emitters, carried, mode):
    nc = bass.Bass("TRN2", target_bir_lowering=False)
    with ExitStack() as stack:
        P = Prog(nc)
        ctx = Ctx(nc, stack, P, mode, carried)
        if mode == "multi":
            ctx._pre = carried
            def dscratch(nm, shape, dtype=F32, _c=ctx):
                if nm not in _c.dram:
                    if nm in carried:
                        t = _c.nc.dram_tensor(nm, list(shape), dtype, kind="ExternalInput")
                        _c.dram[nm] = TB(t.ap(), nm)
                    else:
                        t = _c.nc.dram_tensor(nm, list(shape), dtype, kind="ExternalOutput")
                        _c.dram[nm] = TB(t.ap(), nm)
                        _c.outputs.add(nm)
                return _c.dram[nm]
            ctx.dscratch = dscratch
        for nm in CONST_SPECS:
            ctx.const(nm)
        fv = ctx.const("fvals")
        P.fconst = fv
        for i, val in enumerate([1e-5, 1e-6, -1.0, 0.125, 1.0 / 256.0]):
            nc.const_aps.register(F32, val, fv.t[:, i:i + 1])
        for em in emitters:
            em(P, ctx)
        P.build(stack)
    in_map = {}
    for nm, tb in ctx.dram.items():
        if nm in carried:
            in_map[nm] = carried[nm]
        elif nm.startswith("c_"):
            in_map[nm] = np.ascontiguousarray(CONST_SPECS[nm[2:]])
    in_maps = [dict(in_map) if i == 0 else {k: v for k, v in in_map.items()}
               for i in range(NCORES)]
    for i in range(NCORES):
        for k in list(in_maps[i].keys()):
            v = in_maps[i][k]
            if isinstance(v, list):
                in_maps[i][k] = v[i]
    res = run_bass_kernel_spmd(nc, in_maps, core_ids=list(range(NCORES)))
    outs = {}
    for nm in ctx.outputs:
        outs[nm] = [res.results[i][nm] for i in range(NCORES)]
    return outs


def run_pipeline(inputs, plan):
    x = np.ascontiguousarray(inputs["L0_attn_hs"])
    carried = {"x": [np.ascontiguousarray(x[2 * i:2 * i + 2]) for i in range(NCORES)]}
    for k, v in inputs.items():
        carried[k] = np.ascontiguousarray(v)
    for name, em in plan:
        ems = em if isinstance(em, list) else [em]
        outs = _run_launch(name, ems, carried, mode="multi")
        for nm, percore in outs.items():
            carried[nm] = percore
    return carried


FUSED = True

def kernel(**inputs):
    plan = STAGES if STAGES is not None else _default_launch_plan()
    if FUSED and STAGES is None:
        x = np.ascontiguousarray(inputs["L0_attn_hs"])
        carried = {"x": [np.ascontiguousarray(x[2 * i:2 * i + 2])
                         for i in range(NCORES)]}
        for k, v in inputs.items():
            carried[k] = np.ascontiguousarray(v)
        outs = _run_launch("mega", [em for _, em in plan], carried, mode="fused")
        return np.concatenate(outs["out"], axis=0)
    carried = run_pipeline(inputs, plan)
    return np.concatenate(carried["out"], axis=0)


# version 08
# speedup vs baseline: 1.1700x; 1.0209x over previous
import numpy as np
import ml_dtypes
from contextlib import ExitStack

import concourse.bass as bass
import concourse.mybir as mybir
from concourse.bass_utils import run_bass_kernel_spmd

F32 = mybir.dt.float32
BF16 = mybir.dt.bfloat16
AF = mybir.ActivationFunctionType
OP = mybir.AluOpType
AX = mybir.AxisListType
NPBF = ml_dtypes.bfloat16

NCORES = 8
B = 2
T = 2048
D = 1024
KT = D // 128
PAN = 512
NPAN = T // PAN
NH, DK, DV = 4, 256, 256
CH = 128
NCH = T // CH
E0, F0 = 8, 4096
H1, DH1 = 16, 64
F1, E1 = 4096, 8


class _Op:
    __slots__ = ("eng", "fn", "waits", "inc", "amount", "val", "kind", "semkey")
    def __init__(self, eng, fn, kind):
        self.eng = eng; self.fn = fn; self.waits = []
        self.inc = False; self.amount = 1; self.val = None; self.kind = kind
        self.semkey = None

class Buf:
    __slots__ = ("name", "writer", "readers")
    def __init__(self, name):
        self.name = name; self.writer = None; self.readers = []

class TB:
    __slots__ = ("t", "b")
    def __init__(self, t, name):
        self.t = t; self.b = Buf(name)
    def __getitem__(self, idx):
        return self.t[idx]
    @property
    def ap(self):
        return self.t[:]

class View:
    __slots__ = ("_ap", "b")
    def __init__(self, ap, buf):
        self._ap = ap; self.b = buf
    @property
    def ap(self):
        return self._ap


def rr2(tb, pat, **kw):
    return View(tb.t[:, :].rearrange(pat, **kw), tb.b)

def _buf(x):
    return getattr(x, "b", x)

class Prog:
    ENGS = ("tensor", "vector", "scalar", "gpsimd", "sync")
    NDS = 8

    def __init__(self, nc):
        self.nc = nc
        self.ops = {e: [] for e in self.ENGS}
        self.dma_n = {e: 0 for e in self.ENGS}
        self.dma_sem_tot = {}
        self.pbank = {}
        self.pbufs = {}

    def _emit(self, eng, fn, r=(), w=(), kind="eng", amount=1):
        op = _Op(eng, fn, kind)
        op.amount = amount
        assert (kind == "dma") == (eng in ("sync", "gpsimd")), (kind, eng)
        if kind == "dma":
            idx = self.dma_n[eng]
            self.dma_n[eng] = idx + 1
            op.semkey = (eng, idx % self.NDS)
            op.val = 16 * (idx // self.NDS + 1)
            op.inc = True
            if op.val > 16:
                op.waits.append(("dma", op.semkey, op.val - 16))
            self.dma_sem_tot[op.semkey] = op.val
        deps = []
        for x in r:
            b = _buf(x)
            if b.name in self.pbufs:
                for kk in self.pbufs[b.name]:
                    if kk in self.pbank:
                        self.pbank[kk]["stale"] = True
            if b.writer is not None:
                deps.append(b.writer)
        for x in w:
            b = _buf(x)
            if b.writer is not None:
                deps.append(b.writer)
            deps.extend(b.readers)
        seen = set()
        dma_wait = {}
        for d in deps:
            if d is op or id(d) in seen:
                continue
            seen.add(id(d))
            if d.kind == "dma":
                k = d.semkey
                dma_wait[k] = max(dma_wait.get(k, 0), d.val)
                continue
            if d.kind == "eng" and d.eng == eng == "tensor":
                continue
            d.inc = True
            op.waits.append(d)
        for k, v in dma_wait.items():
            if op.kind == "dma" and k == op.semkey:
                op.waits = [wt for wt in op.waits
                            if not (isinstance(wt, tuple) and wt[1] == k and wt[2] < v)]
                if not any(isinstance(wt, tuple) and wt[1] == k and wt[2] >= v
                           for wt in op.waits):
                    op.waits.append(("dma", k, v))
            else:
                op.waits.append(("dma", k, v))
        self.ops[eng].append(op)
        for x in r:
            _buf(x).readers.append(op)
        for x in w:
            b = _buf(x)
            b.writer = op
            b.readers = []
        return op

    def t(self, fn, r=(), w=()):
        return self._emit("tensor", fn, r, w)
    def v(self, fn, r=(), w=()):
        return self._emit("vector", fn, r, w)
    def s(self, fn, r=(), w=()):
        return self._emit("scalar", fn, r, w)
    def g(self, fn, r=(), w=()):
        return self._emit("gpsimd", fn, r, w)
    def load(self, dst, src_ap, r=(), w=(), slow=False, q="sync"):
        def fn(e, dst=dst, src_ap=src_ap, slow=slow):
            return e.dma_start(out=(dst.ap if hasattr(dst, "ap") else dst),
                               in_=src_ap, allow_slow_non_contiguous=slow)
        return self._emit(q, fn, tuple(r), (dst,) + tuple(w), kind="dma", amount=16)
    def store(self, dst_ap, src_ap, dstbuf, r=(), slow=False, q="gpsimd"):
        def fn(e, dst_ap=dst_ap, src_ap=src_ap, slow=slow):
            return e.dma_start(out=dst_ap, in_=src_ap,
                               allow_slow_non_contiguous=slow)
        return self._emit(q, fn, tuple(r), (dstbuf,), kind="dma", amount=16)

    def barrier(self):
        self.close_banks()
        lasts = [("dma", k, v) for k, v in self.dma_sem_tot.items()]
        for e in self.ENGS:
            for op in reversed(self.ops[e]):
                if op.kind == "eng" and op.fn is not None:
                    op.inc = True
                    lasts.append(op)
                    break
        for e in self.ENGS:
            bop = _Op(e, None, "eng")
            bop.waits = [d for d in lasts
                         if not (not isinstance(d, tuple) and d.eng == e)]
            self.ops[e].append(bop)

    def mm_flags(self, out_ap, acc, bufname):
        off_b = out_ap.offset * mybir.dt.size(out_ap.tensor.dtype)
        key = (out_ap.tensor.name, off_b // 2048)
        flags = {"start": False, "stop": False}
        ent = self.pbank.get(key)
        if acc:
            if ent is None:
                flags["start"] = True
                self.pbank[key] = {"flags": flags, "stale": False}
                self.pbufs.setdefault(bufname, set()).add(key)
            else:
                ent["flags"] = flags
        else:
            if ent is None or ent["stale"]:
                if ent is not None:
                    ent["flags"]["stop"] = True
                flags["start"] = True
                self.pbank[key] = {"flags": flags, "stale": False}
            else:
                ent["flags"] = flags
        self.pbufs.setdefault(bufname, set()).add(key)
        return flags

    def close_banks(self):
        for ent in self.pbank.values():
            ent["flags"]["stop"] = True
        self.pbank.clear()
        self.pbufs.clear()

    def build(self, stack):
        nc = self.nc
        self.close_banks()
        self.barrier()
        sems = {}
        for e in ("tensor", "vector", "scalar"):
            c = 0
            for op in self.ops[e]:
                if op.inc:
                    c += op.amount
                    op.val = c
            sems[e] = stack.enter_context(nc.semaphore(f"sem_{e}"))
        dsems = {}
        for k in self.dma_sem_tot:
            dsems[k] = stack.enter_context(nc.semaphore(f"dsem_{k[0]}_{k[1]}"))
        block = stack.enter_context(nc.Block())
        def mk(e):
            def emitf(engq):
                waited = {}
                for op in self.ops[e]:
                    for d in op.waits:
                        if isinstance(d, tuple):
                            key, val, sem = d[1], d[2], dsems[d[1]]
                        else:
                            key, val, sem = d.eng, d.val, sems[d.eng]
                        if waited.get(key, -1) >= val:
                            continue
                        waited[key] = val
                        engq.wait_ge(sem, val)
                    if op.fn is not None:
                        instr = op.fn(engq)
                        if op.inc:
                            if op.kind == "dma":
                                instr.then_inc(dsems[op.semkey], 16)
                            else:
                                instr.then_inc(sems[e], op.amount)
            return emitf
        block.tensor(mk("tensor"))
        block.vector(mk("vector"))
        block.scalar(mk("scalar"))
        block.gpsimd(mk("gpsimd"))
        block.sync(mk("sync"))


def host_consts():
    i = np.arange(128)
    tril_s = (i[:, None] > i[None, :]).astype(np.float32)
    triu_s = (i[:, None] < i[None, :]).astype(np.float32)
    triu_i = (i[:, None] <= i[None, :]).astype(np.float32)
    c = {}
    c["eye_f"] = np.eye(128, dtype=np.float32)
    c["eye_b"] = np.eye(128, dtype=np.float32).astype(NPBF)
    c["ones_col_f"] = np.ones((128, 1), dtype=np.float32)
    c["ones_col_b"] = np.ones((128, 1), dtype=np.float32).astype(NPBF)
    c["ones_row_f"] = np.ones((1, 128), dtype=np.float32)
    c["mask_sl4"] = np.tile(tril_s, (1, 4)).astype(NPBF)
    c["mask_su4"] = np.tile(triu_s, (1, 4)).astype(NPBF)
    c["mask_ui4"] = np.tile(triu_i, (1, 4)).astype(NPBF)
    sr16 = np.zeros((16, 16 * 128), dtype=np.float32)
    for r in range(16):
        sr16[r, r * 128:(r + 1) * 128] = 1.0
    c["sel_row16"] = sr16
    sr8 = np.zeros((8, 8 * 128), dtype=np.float32)
    for r in range(8):
        sr8[r, r * 128:(r + 1) * 128] = 1.0
    c["sel_row8"] = sr8
    sr4 = np.zeros((4, 4 * 128), dtype=np.float32)
    for r in range(4):
        sr4[r, r * 128:(r + 1) * 128] = 1.0
    c["sel_row4"] = sr4
    sc = np.zeros((128, 16 * 12), dtype=np.float32)
    for br in range(4):
        sc[:, (3 * br + 0) * 16 + 4 * br + 0] = 1.0
        sc[:, (3 * br + 1) * 16 + 4 * br + 1] = 1.0
        sc[:, (3 * br + 1) * 16 + 4 * br + 3] = 1.0
        sc[:, (3 * br + 2) * 16 + 4 * br + 2] = 1.0
    c["selc_st"] = sc
    s4 = np.zeros((128, 16), dtype=np.float32)
    for h in range(4):
        s4[:, 4 * h + h] = 1.0
    c["selc4"] = s4
    c["selc_st_b"] = sc.astype(NPBF)
    st_scale = np.full((16, 1), 1.0 / 256.0, np.float32)
    for br in range(4):
        st_scale[4 * br + 3, 0] = 1.0
    c["st_scale"] = st_scale
    m3b = np.zeros((16, 512), np.float32)
    for br in range(4):
        m3b[4 * br + 3, :] = 1.0
    c["m3big"] = m3b
    r416 = np.zeros((4, 16), np.float32)
    for h in range(4):
        r416[h, 4 * h:4 * h + 4] = 1.0
    c["rep4_16"] = r416
    fv = np.array([1e-5, 1e-6, -1.0, 0.125, 1.0 / 256.0], np.float32)
    c["fvals"] = np.tile(fv[None, :], (128, 1))
    sh = np.zeros((16, 16), dtype=np.float32)
    for br in range(4):
        sh[4 * br + 0, 4 * br + 1] = 1.0
    c["shift16"] = sh
    m3 = np.zeros((16, 1), dtype=np.float32)
    for br in range(4):
        m3[4 * br + 3, 0] = 1.0
    c["m3row"] = m3
    return c

CONST_SPECS = {k: v for k, v in host_consts().items()}


class Ctx:
    def __init__(self, nc, stack, P, mode, carried):
        self.nc = nc; self.stack = stack; self.stack0 = stack; self.P = P
        self.mode = mode
        self.carried = carried
        self.dram = {}
        self.outputs = set()
        self.sb_cache = {}

    def din(self, name, shape, dtype=F32):
        if name not in self.dram:
            t = self.nc.dram_tensor(name, list(shape), dtype, kind="ExternalInput")
            self.dram[name] = TB(t.ap(), name)
        return self.dram[name]

    def dout(self, name, shape, dtype=F32):
        if name not in self.dram:
            t = self.nc.dram_tensor(name, list(shape), dtype, kind="ExternalOutput")
            self.dram[name] = TB(t.ap(), name)
            self.outputs.add(name)
        return self.dram[name]

    def dscratch(self, name, shape, dtype=F32):
        if name not in self.dram:
            kind = "Internal" if self.mode == "fused" else None
            if kind:
                t = self.nc.dram_tensor(name, list(shape), dtype, kind=kind)
                self.dram[name] = TB(t.ap(), name)
        return self.dram.get(name)

    def sb(self, name, p, f, dtype=F32):
        t = self.stack.enter_context(self.nc.sbuf_tensor(name, [p, f], dtype))
        return TB(t, name)

    def ps(self, name, p, f, dtype=F32):
        t = self.stack.enter_context(self.nc.psum_tensor(name, [p, f], dtype))
        return TB(t, name)

    def const(self, name):
        if name in self.sb_cache:
            return self.sb_cache[name]
        arr = CONST_SPECS[name]
        dt = BF16 if arr.dtype == np.dtype(NPBF) else F32
        d = self.din("c_" + name, arr.shape, dt)
        t = self.stack0.enter_context(self.nc.sbuf_tensor("s_" + name,
                                                          list(arr.shape), dt))
        s = TB(t, "s_" + name)
        self.P.load(s, d.ap, r=[d])
        self.sb_cache[name] = s
        return s


def mm(P, out_ap, lhsT_ap, rhs_ap, r, w, start=True, stop=True):
    bufname = _buf(w[0]).name
    flags = P.mm_flags(out_ap, acc=(not start), bufname=bufname)
    return P.t(lambda e: e.matmul(out_ap, lhsT_ap, rhs_ap,
                                  start=flags["start"], stop=flags["stop"]),
               r=r, w=w)

def transpose(P, out_ap, in_ap, eye_tb, r, w, kdim=128):
    bufname = _buf(w[0]).name
    flags = P.mm_flags(out_ap, acc=False, bufname=bufname)
    return P.t(lambda e: e.matmul(out_ap, in_ap, eye_tb[0:kdim, 0:kdim],
                                  start=flags["start"], stop=flags["stop"],
                                  is_transpose=True),
               r=list(r) + [eye_tb], w=w)

def tt(P, out_ap, a_ap, b_ap, op, r, w, eng="v"):
    f = lambda e: e.tensor_tensor(out_ap, a_ap, b_ap, op)
    return getattr(P, eng)(f, r=r, w=w)

def ts(P, out_ap, in_ap, s1, op0, r, w, s2=None, op1=None, eng="v"):
    def f(e):
        if s2 is None and op1 is None:
            return e.tensor_scalar(out_ap, in_ap, s1, None, op0)
        return e.tensor_scalar(out_ap, in_ap, s1, s2, op0, op1)
    return getattr(P, eng)(f, r=r, w=w)

def stt(P, out_ap, in0_ap, scalar, in1_ap, op0, op1, r, w):
    return P.v(lambda e: e.scalar_tensor_tensor(out_ap, in0_ap, scalar, in1_ap, op0, op1),
               r=r, w=w)

def act(P, out_ap, in_ap, func, r, w, bias=None, scale=None):
    r = list(r)
    if isinstance(bias, float) or isinstance(scale, float):
        fc = getattr(P, "fconst", None)
        if fc is not None:
            r.append(fc)
    def f(e):
        kw = {}
        if bias is not None:
            kw["bias"] = bias
        if scale is not None:
            kw["scale"] = scale
        return e.activation(out_ap, in_ap, func, **kw)
    return P.s(f, r=r, w=w)

def vcopy(P, out_ap, in_ap, r, w, eng="v"):
    if eng == "s":
        f = lambda e: e.copy(out_ap, in_ap)
    else:
        f = lambda e: e.tensor_copy(out_ap, in_ap)
    return getattr(P, eng)(f, r=r, w=w)

def memset(P, ap, val, w, eng="v"):
    return getattr(P, eng)(lambda e: e.memset(ap, val), r=(), w=w)

def reduce(P, out_ap, in_ap, op, r, w):
    return P.v(lambda e: e.tensor_reduce(out_ap, in_ap, AX.X, op), r=r, w=w)

def recip(P, out_ap, in_ap, r, w):
    return P.v(lambda e: e.reciprocal(out_ap, in_ap), r=r, w=w)

def emit_ln_panel(P, ctx, xs, dst, tmp, pr_sum, pr_sq, rowbuf, pr_bc):
    oc = ctx.const("ones_col_f")
    for k in range(KT):
        mm(P, pr_sum[0:1, :], oc[:, 0:1], xs[k].ap, r=[xs[k], oc], w=[pr_sum],
           start=(k == 0), stop=(k == KT - 1))
    for k in range(KT):
        act(P, tmp.ap, xs[k].ap, AF.Square, r=[xs[k]], w=[tmp])
        mm(P, pr_sq[0:1, :], oc[:, 0:1], tmp.ap, r=[tmp, oc], w=[pr_sq],
           start=(k == 0), stop=(k == KT - 1))
    mean = rowbuf[0:1, 0:PAN]; m2 = rowbuf[0:1, PAN:2 * PAN]
    var = rowbuf[0:1, 2 * PAN:3 * PAN]; rstd = rowbuf[0:1, 3 * PAN:4 * PAN]
    ts(P, mean, pr_sum[0:1, :], 1.0 / D, OP.mult, r=[pr_sum], w=[rowbuf])
    act(P, m2, mean, AF.Square, r=[rowbuf], w=[rowbuf])
    ts(P, var, pr_sq[0:1, :], 1.0 / D, OP.mult, r=[pr_sq, rowbuf], w=[rowbuf])
    tt(P, var, var, m2, OP.subtract, r=[rowbuf], w=[rowbuf])
    act(P, rstd, var, AF.Sqrt, r=[rowbuf], w=[rowbuf], bias=1e-5)
    recip(P, rstd, rstd, r=[rowbuf], w=[rowbuf])
    orow = ctx.const("ones_row_f")
    mm(P, pr_bc[:, 0:PAN], orow[0:1, :], mean, r=[rowbuf, orow], w=[pr_bc])
    mm(P, pr_bc[:, PAN:2 * PAN], orow[0:1, :], rstd, r=[rowbuf, orow], w=[pr_bc])
    for k in range(KT):
        tt(P, tmp.ap, xs[k].ap, pr_bc[:, 0:PAN], OP.subtract, r=[xs[k], pr_bc], w=[tmp])
        tt(P, dst[k].ap, tmp.ap, pr_bc[:, PAN:2 * PAN], OP.mult, r=[tmp, pr_bc], w=[dst[k]])


def emit_s_in(P, ctx):
    x_in = ctx.din("x", [B, T, D], F32)
    xT = ctx.dscratch("xT", [B, D, T], F32)
    eye = ctx.const("eye_f")
    _scope = scope(ctx); _scope.__enter__()
    ld = [ctx.sb(f"si_ld{i}", 128, D) for i in range(2)]
    pt = ctx.ps("si_ps", 128, 1024)
    gat = [ctx.sb(f"si_g{i}", 128, 1024) for i in range(2)]
    for b in range(B):
        for tt_ in range(T // 128):
            l = ld[tt_ % 2]; g = gat[tt_ % 2]
            P.load(l, x_in.t[b, tt_ * 128:(tt_ + 1) * 128, :], r=[x_in])
            for k in range(KT):
                transpose(P, pt[:, k * 128:(k + 1) * 128], l[:, k * 128:(k + 1) * 128],
                          eye, r=[l], w=[pt])
                vcopy(P, g[:, k * 128:(k + 1) * 128], pt[:, k * 128:(k + 1) * 128],
                      r=[pt], w=[g])
            for k in range(KT):
                P.store(xT.t[b, k * 128:(k + 1) * 128, tt_ * 128:(tt_ + 1) * 128],
                        g.t[:, k * 128:(k + 1) * 128], xT, r=[g])
    _scope.__exit__(None, None, None)
    P.barrier()

def emit_s_out(P, ctx, src="x4T"):
    x4T = ctx.dscratch(src, [B, D, T], F32)
    out = ctx.dout("out", [B, T, D], F32)
    eye = ctx.const("eye_f")
    _scope = scope(ctx); _scope.__enter__()
    ld = [ctx.sb(f"so_ld{i}", 128, 1024) for i in range(2)]
    pt = ctx.ps("so_ps", 128, 1024)
    gat = [ctx.sb(f"so_g{i}", 128, 1024) for i in range(2)]
    for b in range(B):
        for tt_ in range(T // 128):
            l = ld[tt_ % 2]; g = gat[tt_ % 2]
            for k in range(KT):
                P.load(TB(l.t[:, k * 128:(k + 1) * 128], f"so{k}"),
                       x4T.t[b, k * 128:(k + 1) * 128, tt_ * 128:(tt_ + 1) * 128],
                       r=[x4T], w=[l])
            for k in range(KT):
                transpose(P, pt[:, k * 128:(k + 1) * 128], l[:, k * 128:(k + 1) * 128],
                          eye, r=[l], w=[pt])
                vcopy(P, g[:, k * 128:(k + 1) * 128], pt[:, k * 128:(k + 1) * 128],
                      r=[pt], w=[g])
            P.store(out.t[b, tt_ * 128:(tt_ + 1) * 128, :], g.ap, out, r=[g])
    _scope.__exit__(None, None, None)
    P.barrier()


from contextlib import contextmanager

@contextmanager
def scope(ctx):
    old = ctx.stack
    with ExitStack() as st:
        ctx.stack = st
        yield
        ctx.P.barrier()
    ctx.stack = old

def _ln_alloc(ctx, tag):
    a = {}
    a["tmp"] = ctx.sb(f"{tag}_lntmp", 128, PAN)
    a["row"] = ctx.sb(f"{tag}_lnrow", 1, 4 * PAN)
    a["psum"] = ctx.ps(f"{tag}_lnpsS", 1, PAN)
    a["psq"] = ctx.ps(f"{tag}_lnpsQ", 1, PAN)
    a["pbc"] = ctx.ps(f"{tag}_lnpbc", 128, 2 * PAN)
    return a

def emit_ln_to(P, ctx, a, xs, dst):
    emit_ln_panel(P, ctx, xs, dst, a["tmp"], a["psum"], a["psq"], a["row"], a["pbc"])


def emit_s0a(P, ctx):
    xT = ctx.dscratch("xT", [B, D, T], F32)
    hlnT = ctx.dscratch("hlnT", [B, D, T], F32)
    qnT = ctx.dscratch("qnT", [B, D, T], F32)
    knT = ctx.dscratch("knT", [B, D, T], F32)
    vT = ctx.dscratch("vT", [B, D, T], F32)
    eye = ctx.const("eye_f")
    oc = ctx.const("ones_col_f")
    with scope(ctx):
        xs = [ctx.sb(f"s0a_x{k}", 128, PAN) for k in range(KT)]
        hl = [ctx.sb(f"s0a_h{k}", 128, PAN) for k in range(KT)]
        la = _ln_alloc(ctx, "s0a")
        for b in range(B):
            for p in range(NPAN):
                for k in range(KT):
                    P.load(xs[k], xT.t[b, k * 128:(k + 1) * 128, p * PAN:(p + 1) * PAN],
                           r=[xT], q=("sync" if k % 2 == 0 else "gpsimd"))
                emit_ln_to(P, ctx, la, xs, hl)
                for k in range(KT):
                    P.store(hlnT.t[b, k * 128:(k + 1) * 128, p * PAN:(p + 1) * PAN],
                            hl[k].ap, hlnT, r=[hl[k]])
        P.barrier()
        wnames = [("L0_attn_Wq", "L0_attn_cwq", qnT, True),
                  ("L0_attn_Wk", "L0_attn_cwk", knT, True),
                  ("L0_attn_Wv", "L0_attn_cwv", vT, False)]
        W_d = {nm: ctx.din(nm, [D, D]) for nm, _, _, _ in wnames}
        cw_d = {nm: ctx.din(nm, [D, 4]) for _, nm, _, _ in wnames}
        ws = [ctx.sb(f"s0a_w{k}", 128, D) for k in range(KT)]
        wsb = [ctx.sb(f"s0a_wb{k}", 128, D, BF16) for k in range(KT)]
        hlb16 = [ctx.sb(f"s0a_hb{k}", 128, PAN, BF16) for k in range(KT)]
        cw_sb = ctx.sb("s0a_cw", 128, 32)
        dg = [ctx.sb(f"s0a_dg{i}", 128, 128) for i in range(32)]
        pre = [ctx.sb(f"s0a_pre{k}", 128, PAN + 3) for k in range(KT)]
        post = [ctx.sb(f"s0a_post{k}", 128, PAN) for k in range(KT)]
        tail = [ctx.sb(f"s0a_tl{k}", 128, 3) for k in range(KT)]
        sq = ctx.sb("s0a_sq", 128, PAN)
        nrow = ctx.sb("s0a_nrow", 1, PAN)
        pp = [ctx.ps(f"s0a_pp{i}", 128, PAN) for i in range(2)]
        pn = ctx.ps("s0a_pn", 1, PAN)
        pb = ctx.ps("s0a_pb", 128, PAN)
        for wname, cwname, dstT, do_l2 in wnames:
            Wd = W_d[wname]
            for k in range(KT):
                P.load(ws[k], Wd.t[k * 128:(k + 1) * 128, :], r=[Wd],
                       q=("sync" if k % 2 == 0 else "gpsimd"))
                vcopy(P, wsb[k].ap, ws[k].ap, r=[ws[k]], w=[wsb[k]],
                      eng=("v" if k % 2 == 0 else "s"))
            P.load(rr2(cw_sb, "p (a j) -> p a j", j=4),
                   cw_d[cwname].t[:, :].rearrange("(a p) j -> p a j", p=128),
                   r=[cw_d[cwname]], w=[cw_sb], slow=True)
            for k in range(KT):
                for tap in range(4):
                    ts(P, dg[k * 4 + tap].ap, eye.ap, cw_sb[:, k * 4 + tap:k * 4 + tap + 1],
                       OP.mult, r=[eye, cw_sb], w=[dg[k * 4 + tap]])
            for b in range(B):
                for k in range(KT):
                    memset(P, tail[k].ap, 0.0, w=[tail[k]])
                for p in range(NPAN):
                    for k in range(KT):
                        P.load(hl[k], hlnT.t[b, k * 128:(k + 1) * 128, p * PAN:(p + 1) * PAN],
                               r=[hlnT], q=("sync" if k % 2 == 0 else "gpsimd"))
                    for k in range(KT):
                        vcopy(P, hlb16[k].ap, hl[k].ap, r=[hl[k]], w=[hlb16[k]],
                              eng=("v" if k % 2 == 0 else "s"))
                    for mk_ in range(KT):
                        pj = pp[mk_ % 2]
                        for k in range(KT):
                            mm(P, pj.ap, wsb[k][:, mk_ * 128:(mk_ + 1) * 128], hlb16[k].ap,
                               r=[wsb[k], hlb16[k]], w=[pj], start=(k == 0), stop=(k == KT - 1))
                        vcopy(P, pre[mk_][:, 3:3 + PAN], pj.ap, r=[pj], w=[pre[mk_]])
                        vcopy(P, pre[mk_][:, 0:3], tail[mk_].ap, r=[tail[mk_]], w=[pre[mk_]])
                        vcopy(P, tail[mk_].ap, pre[mk_][:, PAN:PAN + 3],
                              r=[pre[mk_]], w=[tail[mk_]], eng="s")
                    for k in range(KT):
                        pj = pp[k % 2]
                        for tap in range(4):
                            mm(P, pj.ap, dg[k * 4 + tap].ap, pre[k][:, tap:tap + PAN],
                               r=[dg[k * 4 + tap], pre[k]], w=[pj],
                               start=(tap == 0), stop=(tap == 3))
                        act(P, post[k].ap, pj.ap, AF.Silu, r=[pj], w=[post[k]])
                    if do_l2:
                        for h in range(NH):
                            for j in range(2):
                                k = 2 * h + j
                                act(P, sq.ap, post[k].ap, AF.Square, r=[post[k]], w=[sq])
                                mm(P, pn.ap, oc[:, 0:1], sq.ap, r=[sq, oc], w=[pn],
                                   start=(j == 0), stop=(j == 1))
                            act(P, nrow.ap, pn.ap, AF.Sqrt, r=[pn], w=[nrow], bias=1e-6)
                            recip(P, nrow.ap, nrow.ap, r=[nrow], w=[nrow])
                            orow = ctx.const("ones_row_f")
                            mm(P, pb.ap, orow[0:1, :], nrow.ap, r=[nrow, orow], w=[pb])
                            for j in range(2):
                                k = 2 * h + j
                                tt(P, post[k].ap, post[k].ap, pb.ap, OP.mult,
                                   r=[post[k], pb], w=[post[k]])
                    for k in range(KT):
                        P.store(dstT.t[b, k * 128:(k + 1) * 128, p * PAN:(p + 1) * PAN],
                                post[k].ap, dstT, r=[post[k]])
            P.barrier()
    P.barrier()


def emit_s0a2(P, ctx):
    hlnT = ctx.dscratch("hlnT", [B, D, T], F32)
    idT = ctx.dscratch("idT", [B, D, T], F32)
    betaT = ctx.dscratch("betaT", [B, NH, T], F32)
    Wid = ctx.din("L0_attn_Wid", [D, D])
    Wb = ctx.din("L0_attn_Wb", [D, NH])
    with scope(ctx):
        hl = [ctx.sb(f"s0a2_h{k}", 128, PAN) for k in range(KT)]
        hlb16 = [ctx.sb(f"s0a2_hb{k}", 128, PAN, BF16) for k in range(KT)]
        ws = [ctx.sb(f"s0a2_w{k}", 128, D) for k in range(KT)]
        wsb = [ctx.sb(f"s0a2_wb{k}", 128, D, BF16) for k in range(KT)]
        wb_sb = ctx.sb("s0a2_wb", 128, 32)
        out = [ctx.sb(f"s0a2_o{k}", 128, PAN) for k in range(KT)]
        brow = ctx.sb("s0a2_brow", 1, PAN)
        pp = [ctx.ps(f"s0a2_pp{i}", 128, PAN) for i in range(2)]
        pnb = ctx.ps("s0a2_pnb", 1, PAN)
        for k in range(KT):
            P.load(ws[k], Wid.t[k * 128:(k + 1) * 128, :], r=[Wid],
                   q=("sync" if k % 2 == 0 else "gpsimd"))
            vcopy(P, wsb[k].ap, ws[k].ap, r=[ws[k]], w=[wsb[k]],
                  eng=("v" if k % 2 == 0 else "s"))
        P.load(rr2(wb_sb, "p (a j) -> p a j", j=4),
               Wb.t[:, :].rearrange("(a p) j -> p a j", p=128), r=[Wb], w=[wb_sb], slow=True)
        for b in range(B):
            for p in range(NPAN):
                for k in range(KT):
                    P.load(hl[k], hlnT.t[b, k * 128:(k + 1) * 128, p * PAN:(p + 1) * PAN],
                           r=[hlnT], q=("sync" if k % 2 == 0 else "gpsimd"))
                for k in range(KT):
                    vcopy(P, hlb16[k].ap, hl[k].ap, r=[hl[k]], w=[hlb16[k]],
                          eng=("v" if k % 2 == 0 else "s"))
                for mk_ in range(KT):
                    pj = pp[mk_ % 2]
                    for k in range(KT):
                        mm(P, pj.ap, wsb[k][:, mk_ * 128:(mk_ + 1) * 128], hlb16[k].ap,
                           r=[wsb[k], hlb16[k]], w=[pj], start=(k == 0), stop=(k == KT - 1))
                    vcopy(P, out[mk_].ap, pj.ap, r=[pj], w=[out[mk_]])
                    P.store(idT.t[b, mk_ * 128:(mk_ + 1) * 128, p * PAN:(p + 1) * PAN],
                            out[mk_].ap, idT, r=[out[mk_]])
                for h in range(NH):
                    for k in range(KT):
                        mm(P, pnb.ap, wb_sb[:, k * 4 + h:k * 4 + h + 1], hl[k].ap,
                           r=[wb_sb, hl[k]], w=[pnb], start=(k == 0), stop=(k == KT - 1))
                    act(P, brow.ap, pnb.ap, AF.Sigmoid, r=[pnb], w=[brow])
                    P.store(betaT.t[b, h:h + 1, p * PAN:(p + 1) * PAN],
                            brow.ap, betaT, r=[brow])
    P.barrier()

DBG_S0B = False


def emit_s0b(P, ctx):
    qnT = ctx.dscratch("qnT", [B, D, T], F32)
    knT = ctx.dscratch("knT", [B, D, T], F32)
    vT = ctx.dscratch("vT", [B, D, T], F32)
    betaT = ctx.dscratch("betaT", [B, NH, T], F32)
    doutT = ctx.dscratch("doutT", [B, D, T], F32)
    eye = ctx.const("eye_f")
    sr4 = ctx.const("sel_row4")
    m_sl = ctx.const("mask_sl4")
    m_su = ctx.const("mask_su4")
    m_ui = ctx.const("mask_ui4")
    with scope(ctx):
        qc = [ctx.sb(f"s0b_q{i}", 128, 1024) for i in range(2)]
        kc = [ctx.sb(f"s0b_k{i}", 128, 1024) for i in range(2)]
        vc = [ctx.sb(f"s0b_v{i}", 128, 1024) for i in range(2)]
        bc = [ctx.sb(f"s0b_b{i}", NH, CH) for i in range(2)]
        kb = ctx.sb("s0b_kb", 128, 1024)
        vn = ctx.sb("s0b_vn", 128, 1024)
        kn = ctx.sb("s0b_kn", 128, 1024)
        bn = ctx.sb("s0b_bn", 128, NH)
        Bp = [ctx.sb(f"s0b_B{i}", 128, 512) for i in range(2)]
        Cp = [ctx.sb(f"s0b_C{j}", 128, 512) for j in range(7)]
        za = ctx.sb("s0b_za", 128, 1024)
        zb = ctx.sb("s0b_zb", 128, 1024)
        Asb = ctx.sb("s0b_A", 128, 512)
        osb = ctx.sb("s0b_o", 128, 1024)
        Ssb = ctx.sb("s0b_S", 128, 2048)
        psS = ctx.ps("s0b_psS", 128, 2048)
        psa = ctx.ps("s0b_psa", 128, 512)
        psb = ctx.ps("s0b_psb", 128, 512)
        psw = ctx.ps("s0b_psw", 128, 1024)
        for b in range(B):
            for c in range(NCH):
                par = c % 2
                q_, k_, v_, b_ = qc[par], kc[par], vc[par], bc[par]
                lq = "sync" if par == 0 else "gpsimd"
                for k in range(KT):
                    P.load(TB(q_.t[:, k * 128:(k + 1) * 128], "l"),
                           qnT.t[b, k * 128:(k + 1) * 128, c * CH:(c + 1) * CH],
                           r=[qnT], w=[q_], q=lq)
                    P.load(TB(k_.t[:, k * 128:(k + 1) * 128], "l"),
                           knT.t[b, k * 128:(k + 1) * 128, c * CH:(c + 1) * CH],
                           r=[knT], w=[k_], q=lq)
                    P.load(TB(v_.t[:, k * 128:(k + 1) * 128], "l"),
                           vT.t[b, k * 128:(k + 1) * 128, c * CH:(c + 1) * CH],
                           r=[vT], w=[v_], q=lq)
                P.load(b_, betaT.t[b, :, c * CH:(c + 1) * CH], r=[betaT], q=lq)
                transpose(P, psb[:, 0:NH], b_.ap, eye, r=[b_], w=[psb], kdim=NH)
                vcopy(P, bn.ap, psb[:, 0:NH], r=[psb], w=[bn])
                for k in range(KT):
                    h = k // 2
                    mm(P, psw[:, k * 128:(k + 1) * 128], sr4[:, h * 128:(h + 1) * 128],
                       b_.ap, r=[b_, sr4], w=[psw])
                tt(P, kb.ap, k_.ap, psw.ap, OP.mult, r=[k_, psw], w=[kb])
                for h in range(NH):
                    for j in range(2):
                        k = 2 * h + j
                        mm(P, psa[:, h * 128:(h + 1) * 128], kb[:, k * 128:(k + 1) * 128],
                           k_.t[:, k * 128:(k + 1) * 128], r=[kb, k_], w=[psa],
                           start=(j == 0), stop=(j == 1))
                        mm(P, psb[:, h * 128:(h + 1) * 128], k_.t[:, k * 128:(k + 1) * 128],
                           kb[:, k * 128:(k + 1) * 128], r=[kb, k_], w=[psb],
                           start=(j == 0), stop=(j == 1))
                tt(P, Bp[0].ap, psa.ap, m_sl.ap, OP.mult, r=[psa, m_sl], w=[Bp[0]])
                tt(P, Cp[0].ap, psb.ap, m_su.ap, OP.mult, r=[psb, m_su], w=[Cp[0]])
                bcur = Bp[0]
                for j in range(1, 7):
                    bnext = Bp[j % 2]
                    for h in range(NH):
                        sl = slice(h * 128, (h + 1) * 128)
                        mm(P, psa[:, sl], Cp[j - 1][:, sl], bcur[:, sl],
                           r=[Cp[j - 1], bcur], w=[psa])
                        mm(P, psb[:, sl], bcur[:, sl], Cp[j - 1][:, sl],
                           r=[Cp[j - 1], bcur], w=[psb])
                    vcopy(P, bnext.ap, psa.ap, r=[psa], w=[bnext])
                    vcopy(P, Cp[j].ap, psb.ap, r=[psb], w=[Cp[j]], eng="s")
                    bcur = bnext
                for k in range(KT):
                    transpose(P, psw[:, k * 128:(k + 1) * 128],
                              v_.t[:, k * 128:(k + 1) * 128], eye, r=[v_], w=[psw])
                vcopy(P, vn.ap, psw.ap, r=[psw], w=[vn])
                for k in range(KT):
                    transpose(P, psw[:, k * 128:(k + 1) * 128],
                              k_.t[:, k * 128:(k + 1) * 128], eye, r=[k_], w=[psw])
                vcopy(P, kn.ap, psw.ap, r=[psw], w=[kn])
                if c == 0:
                    for h in range(NH):
                        sl = slice(h * 256, (h + 1) * 256)
                        ts(P, za.t[:, sl], vn.t[:, sl], bn[:, h:h + 1], OP.mult,
                           r=[vn, bn], w=[za])
                else:
                    for h in range(NH):
                        for j in range(2):
                            k = 2 * h + j
                            mm(P, psw[:, h * 256:(h + 1) * 256],
                               kb[:, k * 128:(k + 1) * 128],
                               Ssb.t[:, h * 512 + j * 256:h * 512 + (j + 1) * 256],
                               r=[kb, Ssb], w=[psw], start=(j == 0), stop=(j == 1))
                    if DBG_S0B:
                        dbgP = ctx.dscratch("dbgP", [B, NCH, 128, 1024], F32)
                        vcopy(P, osb.ap, psw.ap, r=[psw], w=[osb])
                        P.store(dbgP.t[b, c], osb.ap, dbgP, r=[osb], q="sync")
                    for h in range(NH):
                        sl = slice(h * 256, (h + 1) * 256)
                        def f(e, o=za.t[:, sl], i0=vn.t[:, sl], s=bn[:, h:h + 1],
                              i1=psw[:, sl]):
                            return e.scalar_tensor_tensor(o, i0, s, i1,
                                                          OP.mult, OP.subtract)
                        P.v(f, r=[vn, bn, psw], w=[za])
                if DBG_S0B:
                    dbgM = ctx.dscratch("dbgM", [B, NCH, 128, 1024], F32)
                    P.store(dbgM.t[b, c], za.ap, dbgM, r=[za])
                    dbgB = ctx.dscratch("dbgB", [B, NCH, 128, 512], F32)
                    P.store(dbgB.t[b, c], Cp[0].ap, dbgB, r=[Cp[0]])
                zc, zn = za, zb
                for j in range(6, -1, -1):
                    for h in range(NH):
                        mm(P, psw[:, h * 256:(h + 1) * 256], Cp[j][:, h * 128:(h + 1) * 128],
                           zc.t[:, h * 256:(h + 1) * 256], r=[Cp[j], zc], w=[psw])
                    op1 = OP.subtract if j == 0 else OP.add
                    sgn = -1.0 if j == 0 else 1.0
                    def f(e, sgn=sgn, zn=zn, zc=zc):
                        return e.scalar_tensor_tensor(zn.ap, psw.ap, sgn, zc.ap,
                                                      OP.mult, OP.add)
                    P.v(f, r=[psw, zc], w=[zn])
                    zc, zn = zn, zc
                uadj = zc
                for h in range(NH):
                    for j in range(2):
                        k = 2 * h + j
                        mm(P, psa[:, h * 128:(h + 1) * 128], k_.t[:, k * 128:(k + 1) * 128],
                           q_.t[:, k * 128:(k + 1) * 128], r=[k_, q_], w=[psa],
                           start=(j == 0), stop=(j == 1))
                tt(P, Asb.ap, psa.ap, m_ui.ap, OP.mult, r=[psa, m_ui], w=[Asb])
                for h in range(NH):
                    for dt in range(2):
                        k = 2 * h + dt
                        sl = slice(k * 128, (k + 1) * 128)
                        first = True
                        if c > 0:
                            for j in range(2):
                                mm(P, psw[:, sl],
                                   Ssb.t[:, h * 512 + j * 256 + dt * 128:h * 512 + j * 256 + (dt + 1) * 128],
                                   q_.t[:, (2 * h + j) * 128:(2 * h + j + 1) * 128],
                                   r=[Ssb, q_], w=[psw], start=(j == 0), stop=False)
                            first = False
                        mm(P, psw[:, sl], uadj.t[:, h * 256 + dt * 128:h * 256 + (dt + 1) * 128],
                           Asb[:, h * 128:(h + 1) * 128], r=[uadj, Asb], w=[psw],
                           start=first, stop=True)
                vcopy(P, osb.ap, psw.ap, r=[psw], w=[osb])
                for k in range(KT):
                    P.store(doutT.t[b, k * 128:(k + 1) * 128, c * CH:(c + 1) * CH],
                            osb.t[:, k * 128:(k + 1) * 128], doutT, r=[osb],
                            q=("gpsimd" if par == 0 else "sync"))
                if c < NCH - 1:
                    for h in range(NH):
                        for j in range(2):
                            mm(P, psS[:, h * 512 + j * 256:h * 512 + (j + 1) * 256],
                               kn.t[:, h * 256 + j * 128:h * 256 + (j + 1) * 128],
                               uadj.t[:, h * 256:(h + 1) * 256], r=[kn, uadj], w=[psS],
                               start=True)
                if c < NCH - 1:
                    if c == 0:
                        vcopy(P, Ssb.ap, psS.ap, r=[psS], w=[Ssb])
                    else:
                        tt(P, Ssb.ap, Ssb.ap, psS.ap, OP.add, r=[Ssb, psS], w=[Ssb])
                if DBG_S0B:
                    dbgU = ctx.dscratch("dbgU", [B, NCH, 128, 1024], F32)
                    dbgS = ctx.dscratch("dbgS", [B, NCH, 128, 2048], F32)
                    P.store(dbgU.t[b, c], uadj.ap, dbgU, r=[uadj])
                    P.store(dbgS.t[b, c], Ssb.ap, dbgS, r=[Ssb])
            P.barrier()
    P.barrier()


def emit_s0c1(P, ctx):
    vT = ctx.dscratch("vT", [B, D, T], F32)
    lsT = ctx.dscratch("lsT", [B, D, T], F32)
    llT = ctx.dscratch("llT", [B, D, T], F32)
    firs_d = ctx.din("L0_attn_firs", [4, 256, 3])
    firl_d = ctx.din("L0_attn_firl", [4, 256, 63])
    eyeb = ctx.const("eye_b")
    with scope(ctx):
        fls = ctx.sb("s0c1_fls", 128, 24)
        fll = ctx.sb("s0c1_fll", 128, 504)
        P.load(View(fls.t[:, :].rearrange("p (h a t) -> p h a t", h=4, a=2), fls.b),
               firs_d.t[:, :, :].rearrange("h (a p) t -> p h a t", p=128),
               r=[firs_d], slow=True)
        P.load(View(fll.t[:, :].rearrange("p (h a t) -> p h a t", h=4, a=2), fll.b),
               firl_d.t[:, :, :].rearrange("h (a p) t -> p h a t", p=128),
               r=[firl_d], slow=True)
        vbf = [ctx.sb(f"s0c1_vbf{k}", 128, 62 + T, BF16) for k in range(KT)]
        dgl = [ctx.sb(f"s0c1_dg{i}", 128, 128, BF16) for i in range(66)]
        vstg = [ctx.sb(f"s0c1_vstg{i}", 128, PAN) for i in range(2)]
        ous = [ctx.sb(f"s0c1_ou{i}", 128, PAN) for i in range(4)]
        psf = ctx.ps("s0c1_psf", 128, 1024)
        for b in range(B):
            for k in range(KT):
                memset(P, vbf[k][:, 0:62], 0.0, w=[vbf[k]])
                for p in range(NPAN):
                    vs = vstg[p % 2]
                    P.load(vs, vT.t[b, k * 128:(k + 1) * 128, p * PAN:(p + 1) * PAN],
                           r=[vT], w=[vs], q=("sync" if p % 2 == 0 else "gpsimd"))
                    vcopy(P, vbf[k][:, 62 + p * PAN:62 + (p + 1) * PAN], vs.ap,
                          r=[vs], w=[vbf[k]])
            for k in range(KT):
                for tp_ in range(63):
                    ts(P, dgl[tp_].ap, eyeb.ap, fll[:, (4 * k + 0) * 63 + tp_:(4 * k) * 63 + tp_ + 1]
                       if False else fll[:, k * 63 + tp_:k * 63 + tp_ + 1],
                       OP.mult, r=[eyeb, fll], w=[dgl[tp_]])
                for tp_ in range(3):
                    ts(P, dgl[63 + tp_].ap, eyeb.ap, fls[:, k * 3 + tp_:k * 3 + tp_ + 1],
                       OP.mult, r=[eyeb, fls], w=[dgl[63 + tp_]])
                for p in range(NPAN):
                    for tp_ in range(63):
                        mm(P, psf[:, 512:1024], dgl[tp_].ap,
                           vbf[k][:, p * PAN + tp_:p * PAN + tp_ + PAN],
                           r=[dgl[tp_], vbf[k]], w=[psf],
                           start=(tp_ == 0), stop=(tp_ == 62))
                    for tp_ in range(3):
                        mm(P, psf[:, 0:512], dgl[63 + tp_].ap,
                           vbf[k][:, 60 + p * PAN + tp_:60 + p * PAN + tp_ + PAN],
                           r=[dgl[63 + tp_], vbf[k]], w=[psf],
                           start=(tp_ == 0), stop=(tp_ == 2))
                    o1 = ous[2 * (p % 2)]; o2 = ous[2 * (p % 2) + 1]
                    vcopy(P, o1.ap, psf[:, 512:1024], r=[psf], w=[o1])
                    vcopy(P, o2.ap, psf[:, 0:512], r=[psf], w=[o2], eng="s")
                    P.store(llT.t[b, k * 128:(k + 1) * 128, p * PAN:(p + 1) * PAN],
                            o1.ap, llT, r=[o1])
                    P.store(lsT.t[b, k * 128:(k + 1) * 128, p * PAN:(p + 1) * PAN],
                            o2.ap, lsT, r=[o2], q="sync")
    P.barrier()


def emit_s0c2(P, ctx):
    xT = ctx.dscratch("xT", [B, D, T], F32)
    hlnT = ctx.dscratch("hlnT", [B, D, T], F32)
    vT = ctx.dscratch("vT", [B, D, T], F32)
    doutT = ctx.dscratch("doutT", [B, D, T], F32)
    idT = ctx.dscratch("idT", [B, D, T], F32)
    lsT = ctx.dscratch("lsT", [B, D, T], F32)
    llT = ctx.dscratch("llT", [B, D, T], F32)
    x1T = ctx.dscratch("x1T", [B, D, T], F32)
    gW1 = ctx.din("L0_attn_gW1", [1040, 1024])
    gW2 = ctx.din("L0_attn_gW2", [1024, 4])
    gb1 = ctx.din("L0_attn_gb1", [1024])
    gb2 = ctx.din("L0_attn_gb2", [4])
    ltm = ctx.din("L0_attn_log_temp", [4, 4])
    flp = ctx.din("L0_attn_floor_param", [4, 4])
    crl = ctx.din("L0_attn_conv_res_logit", [4])
    alp = ctx.din("L0_attn_alpha_id", [4])
    onw = ctx.din("L0_attn_o_norm_w", [256])
    Wo = ctx.din("L0_attn_Wo", [D, D])
    eye = ctx.const("eye_f")
    orow = ctx.const("ones_row_f")
    sr16 = ctx.const("sel_row16")
    sr4 = ctx.const("sel_row4")
    selc = ctx.const("selc_st")
    stsc = ctx.const("st_scale")
    m3b = ctx.const("m3big")
    r416 = ctx.const("rep4_16")
    sh16 = ctx.const("shift16")
    selc4 = ctx.const("selc4")
    with scope(ctx):
        gwa = [ctx.sb(f"s0c2_gwa{k}", 128, 1024) for k in range(KT)]
        for k in range(KT):
            P.load(gwa[k], gW1.t[k * 128:(k + 1) * 128, :], r=[gW1],
                   q=("sync" if k % 2 == 0 else "gpsimd"))
        gwb = ctx.sb("s0c2_gwb", 16, 1024)
        P.load(gwb, gW1.t[1024:1040, :], r=[gW1])
        gw2 = ctx.sb("s0c2_gw2", 128, 32)
        P.load(rr2(gw2, "p (a j) -> p a j", j=4),
               gW2.t[:, :].rearrange("(a p) j -> p a j", p=128), r=[gW2], w=[gw2], slow=True)
        gb1s = ctx.sb("s0c2_gb1", 128, KT)
        P.load(gb1s, gb1.t[:].rearrange("(a p) -> p a", p=128), r=[gb1], slow=True)
        gb2s = ctx.sb("s0c2_gb2", 4, 1)
        P.load(gb2s, gb2.t[:].rearrange("(j o) -> j o", o=1), r=[gb2], slow=True)
        wstg = ctx.sb("s0c2_wstg", 128, 1024)
        wob = [ctx.sb(f"s0c2_wob{k}", 128, 1024, BF16) for k in range(KT)]
        for k in range(KT):
            P.load(wstg, Wo.t[k * 128:(k + 1) * 128, :], r=[Wo], w=[wstg])
            vcopy(P, wob[k].ap, wstg.ap, r=[wstg], w=[wob[k]])
        onws = ctx.sb("s0c2_onw", 128, 2)
        P.load(onws, onw.t[:].rearrange("(a p) -> p a", p=128), r=[onw], slow=True)
        r16 = ctx.sb("s0c2_r16", 1, 16)
        r16b = ctx.sb("s0c2_r16b", 1, 16)
        r4 = ctx.sb("s0c2_r4", 1, 4)
        c4 = ctx.sb("s0c2_c4", 4, 1)
        invt = ctx.sb("s0c2_invt", 128, 64)
        fl64 = ctx.sb("s0c2_fl64", 128, 64)
        omf64 = ctx.sb("s0c2_omf64", 128, 64)
        crlb = ctx.sb("s0c2_crlb", 128, 4)
        alpb = ctx.sb("s0c2_alpb", 128, 4)
        ps_nat = ctx.ps("s0c2_psnat", 128, 64)
        ps_g = ctx.ps("s0c2_psg", 128, 512)
        ps_s1 = ctx.ps("s0c2_pss1", 128, 512)
        ps_t4 = ctx.ps("s0c2_pst4", 16, 512)
        ps_big = ctx.ps("s0c2_psbig", 128, 2048)
        P.load(r16, ltm.t[:, :].rearrange("(o h) j -> o (h j)", o=1), r=[ltm])
        act(P, r16.ap, r16.ap, AF.Exp, r=[r16], w=[r16], scale=-1.0)
        mm(P, ps_nat[:, 0:16], orow[0:1, :], r16.ap, r=[r16, orow], w=[ps_nat])
        for j in range(4):
            vcopy(P, invt[:, 16 * j:16 * (j + 1)], ps_nat[:, 0:16], r=[ps_nat], w=[invt])
        P.load(r16b, flp.t[:, :].rearrange("(o h) j -> o (h j)", o=1), r=[flp])
        act(P, r16b.ap, r16b.ap, AF.Sigmoid, r=[r16b], w=[r16b])
        ts(P, r16b.ap, r16b.ap, 0.05, OP.mult, r=[r16b], w=[r16b])
        mm(P, ps_nat[:, 0:16], orow[0:1, :], r16b.ap, r=[r16b, orow], w=[ps_nat])
        for j in range(4):
            vcopy(P, fl64[:, 16 * j:16 * (j + 1)], ps_nat[:, 0:16], r=[ps_nat], w=[fl64])
        reduce(P, r4.ap, r16b.t[0:1, :].rearrange("o (h j) -> o h j", j=4), OP.add,
               r=[r16b], w=[r4])
        ts(P, r4.ap, r4.ap, -1.0, OP.mult, r=[r4], w=[r4], s2=1.0, op1=OP.add)
        transpose(P, ps_t4[0:4, 0:1], r4.ap, eye, r=[r4], w=[ps_t4], kdim=1)
        vcopy(P, c4.ap, ps_t4[0:4, 0:1], r=[ps_t4], w=[c4])
        mm(P, ps_nat[0:1, 0:16], c4.ap, r416.ap, r=[c4, r416], w=[ps_nat])
        vcopy(P, r16.ap, ps_nat[0:1, 0:16], r=[ps_nat], w=[r16])
        mm(P, ps_nat[:, 16:32], orow[0:1, :], r16.ap, r=[r16, orow], w=[ps_nat])
        for j in range(4):
            vcopy(P, omf64[:, 16 * j:16 * (j + 1)], ps_nat[:, 16:32], r=[ps_nat], w=[omf64])
        P.load(r4, crl.t[:].rearrange("(o h) -> o h", o=1), r=[crl], w=[r4])
        act(P, r4.ap, r4.ap, AF.Sigmoid, r=[r4], w=[r4])
        mm(P, ps_nat[:, 0:4], orow[0:1, :], r4.ap, r=[r4, orow], w=[ps_nat])
        vcopy(P, crlb.ap, ps_nat[:, 0:4], r=[ps_nat], w=[crlb])
        P.load(r4, alp.t[:].rearrange("(o h) -> o h", o=1), r=[alp], w=[r4])
        mm(P, ps_nat[:, 0:4], orow[0:1, :], r4.ap, r=[r4, orow], w=[ps_nat])
        vcopy(P, alpb.ap, ps_nat[:, 0:4], r=[ps_nat], w=[alpb])
        hl = [ctx.sb(f"s0c2_hl{k}", 128, PAN) for k in range(KT)]
        xps = [ctx.sb(f"s0c2_xp{k}", 128, PAN) for k in range(KT)]
        onT = [ctx.sb(f"s0c2_on{k}", 128, PAN, BF16) for k in range(KT)]
        brt = {nm: [ctx.sb(f"s0c2_{nm}{j}", 128, PAN) for j in range(2)]
               for nm in ("ls", "ll", "do", "vp", "id")}
        sqt = ctx.sb("s0c2_sqt", 128, PAN)
        abt = ctx.sb("s0c2_abt", 128, PAN)
        Gsb = ctx.sb("s0c2_G", 16, PAN)
        m2sb = ctx.sb("s0c2_m2", 16, PAN)
        sts = [ctx.sb(f"s0c2_st{h}", 16, PAN) for h in range(NH)]
        shsb = ctx.sb("s0c2_sh", 128, PAN)
        tg = ctx.sb("s0c2_tg", 128, PAN)
        ghm = ctx.sb("s0c2_ghm", 128, PAN)
        lgs = [ctx.sb(f"s0c2_lg{h}", 4, PAN) for h in range(NH)]
        lgt = ctx.sb("s0c2_lgt", 4, PAN)
        nats = ctx.sb("s0c2_nats", 128, 64)
        ex = ctx.sb("s0c2_ex", 128, 64)
        sums = ctx.sb("s0c2_sums", 128, 16)
        prT = ctx.sb("s0c2_prT", 16, PAN)
        oac = [ctx.sb(f"s0c2_oac{d}", 128, PAN) for d in range(2)]
        tmx = ctx.sb("s0c2_tmx", 128, PAN)
        rr = ctx.sb("s0c2_rr", 4, PAN)
        x1s = ctx.sb("s0c2_x1s", 128, PAN)

        def load_branches(b, h, sl_t):
            for j in range(2):
                k = 2 * h + j
                for nm, src in (("ls", lsT), ("ll", llT), ("do", doutT), ("vp", vT),
                                ("id", idT)):
                    P.load(brt[nm][j], src.t[b, k * 128:(k + 1) * 128, sl_t], r=[src],
                           q=("sync" if (j + ord(nm[0])) % 2 == 0 else "gpsimd"))

        for b in range(B):
            for p in range(NPAN):
                sl_t = slice(p * PAN, (p + 1) * PAN)
                for k in range(KT):
                    P.load(hl[k], hlnT.t[b, k * 128:(k + 1) * 128, sl_t], r=[hlnT],
                           q=("sync" if k % 2 == 0 else "gpsimd"))
                for h in range(NH):
                    load_branches(b, h, sl_t)
                    for br, nm in enumerate(("ls", "ll", "do", "vp")):
                        for j in range(2):
                            tb_ = brt[nm][j]
                            first = (br == 0 and j == 0)
                            mm(P, ps_t4.ap, selc[:, (3 * br + 0) * 16:(3 * br + 1) * 16],
                               tb_.ap, r=[tb_, selc], w=[ps_t4], start=first, stop=False)
                            act(P, sqt.ap, tb_.ap, AF.Square, r=[tb_], w=[sqt])
                            mm(P, ps_t4.ap, selc[:, (3 * br + 1) * 16:(3 * br + 2) * 16],
                               sqt.ap, r=[sqt, selc], w=[ps_t4], start=False, stop=False)
                            act(P, abt.ap, tb_.ap, AF.Abs, r=[tb_], w=[abt])
                            mm(P, ps_t4.ap, selc[:, (3 * br + 2) * 16:(3 * br + 3) * 16],
                               abt.ap, r=[abt, selc], w=[ps_t4], start=False,
                               stop=(br == 3 and j == 1))
                    ts(P, Gsb.ap, ps_t4.ap, stsc.t[:, 0:1], OP.mult, r=[ps_t4, stsc], w=[Gsb])
                    mm(P, ps_t4.ap, sh16.ap, Gsb.ap, r=[sh16, Gsb], w=[ps_t4])
                    act(P, m2sb.ap, ps_t4.ap, AF.Square, r=[ps_t4], w=[m2sb])
                    tt(P, Gsb.ap, Gsb.ap, m2sb.ap, OP.subtract, r=[Gsb, m2sb], w=[Gsb])
                    ts(P, m2sb.ap, Gsb.ap, 0.0, OP.max, r=[Gsb], w=[m2sb])
                    act(P, m2sb.ap, m2sb.ap, AF.Sqrt, r=[m2sb], w=[m2sb])
                    tt(P, m2sb.ap, m2sb.ap, Gsb.ap, OP.subtract, r=[m2sb, Gsb], w=[m2sb])
                    tt(P, m2sb.ap, m2sb.ap, m3b[:, 0:PAN], OP.mult, r=[m2sb, m3b], w=[m2sb])
                    tt(P, sts[h].ap, Gsb.ap, m2sb.ap, OP.add, r=[Gsb, m2sb], w=[sts[h]])
                for h in range(NH):
                    memset(P, lgs[h].ap, 0.0, w=[lgs[h]])
                for mt in range(KT):
                    for k in range(KT):
                        mm(P, ps_g.ap, gwa[k][:, mt * 128:(mt + 1) * 128], hl[k].ap,
                           r=[gwa[k], hl[k]], w=[ps_g], start=(k == 0), stop=(k == KT - 1))
                    vcopy(P, shsb.ap, ps_g.ap, r=[ps_g], w=[shsb])
                    for h in range(NH):
                        mm(P, ps_s1.ap, gwb[:, mt * 128:(mt + 1) * 128], sts[h].ap,
                           r=[gwb, sts[h]], w=[ps_s1])
                        tt(P, tg.ap, ps_s1.ap, shsb.ap, OP.add, r=[ps_s1, shsb], w=[tg])
                        act(P, ghm.ap, tg.ap, AF.Gelu, r=[tg, gb1s], w=[ghm],
                            bias=gb1s[:, mt:mt + 1])
                        mm(P, ps_s1[0:4, :], gw2[:, mt * 4:(mt + 1) * 4], ghm.ap,
                           r=[gw2, ghm], w=[ps_s1])
                        vcopy(P, lgt.ap, ps_s1[0:4, :], r=[ps_s1], w=[lgt])
                        tt(P, lgs[h].ap, lgs[h].ap, lgt.ap, OP.add, r=[lgs[h], lgt],
                           w=[lgs[h]])
                for h in range(NH):
                    ts(P, lgs[h].ap, lgs[h].ap, gb2s.t[:, 0:1], OP.add, r=[lgs[h], gb2s],
                       w=[lgs[h]])
                for s in range(4):
                    for h in range(NH):
                        transpose(P, ps_nat[:, s * 16 + 4 * h:s * 16 + 4 * h + 4],
                                  lgs[h][:, s * 128:(s + 1) * 128], eye,
                                  r=[lgs[h]], w=[ps_nat], kdim=4)
                vcopy(P, nats.ap, ps_nat.ap, r=[ps_nat], w=[nats])
                tt(P, nats.ap, nats.ap, invt.ap, OP.mult, r=[nats, invt], w=[nats])
                act(P, ex.ap, nats.ap, AF.Exp, r=[nats], w=[ex])
                reduce(P, sums.ap, ex.t[:, :].rearrange("p (s j) -> p s j", j=4), OP.add,
                       r=[ex], w=[sums])
                recip(P, sums.ap, sums.ap, r=[sums], w=[sums])
                for s in range(4):
                    for h in range(NH):
                        c_ = s * 4 + h
                        ts(P, ex[:, 16 * s + 4 * h:16 * s + 4 * h + 4],
                           ex[:, 16 * s + 4 * h:16 * s + 4 * h + 4],
                           sums[:, c_:c_ + 1], OP.mult, r=[ex, sums], w=[ex])
                tt(P, ex.ap, ex.ap, omf64.ap, OP.mult, r=[ex, omf64], w=[ex])
                tt(P, ex.ap, ex.ap, fl64.ap, OP.add, r=[ex, fl64], w=[ex])
                for s in range(4):
                    transpose(P, ps_t4[:, s * 128:(s + 1) * 128], ex[:, s * 16:(s + 1) * 16],
                              eye, r=[ex], w=[ps_t4], kdim=128)
                vcopy(P, prT.ap, ps_t4.ap, r=[ps_t4], w=[prT])
                for k in range(KT):
                    P.load(xps[k], xT.t[b, k * 128:(k + 1) * 128, sl_t], r=[xT],
                           q=("sync" if k % 2 == 0 else "gpsimd"))
                for h in range(NH):
                    load_branches(b, h, sl_t)
                    for j in range(4):
                        mm(P, ps_big[:, j * 512:(j + 1) * 512],
                           sr16[:, (4 * h + j) * 128:(4 * h + j + 1) * 128], prT.ap,
                           r=[sr16, prT], w=[ps_big])
                    for dt in range(2):
                        def f0(e, dt=dt, h=h):
                            return e.scalar_tensor_tensor(
                                oac[dt].ap, ps_big[:, 0:512], crlb[:, h:h + 1],
                                brt["ls"][dt].ap, OP.add, OP.mult)
                        P.v(f0, r=[ps_big, crlb, brt["ls"][dt]], w=[oac[dt]])
                        for (jj, nm) in ((1, "ll"), (2, "do"), (3, "vp")):
                            tt(P, tmx.ap, ps_big[:, jj * 512:(jj + 1) * 512],
                               brt[nm][dt].ap, OP.mult, r=[ps_big, brt[nm][dt]], w=[tmx])
                            tt(P, oac[dt].ap, oac[dt].ap, tmx.ap, OP.add,
                               r=[oac[dt], tmx], w=[oac[dt]])
                        ts(P, tmx.ap, brt["id"][dt].ap, alpb[:, h:h + 1], OP.mult,
                           r=[brt["id"][dt], alpb], w=[tmx])
                        tt(P, oac[dt].ap, oac[dt].ap, tmx.ap, OP.add, r=[oac[dt], tmx],
                           w=[oac[dt]])
                        act(P, sqt.ap, oac[dt].ap, AF.Square, r=[oac[dt]], w=[sqt])
                        mm(P, ps_t4[0:4, :], selc4[:, 4 * h:4 * h + 4], sqt.ap,
                           r=[sqt, selc4], w=[ps_t4], start=(dt == 0), stop=(dt == 1))
                    act(P, rr.ap, ps_t4[0:4, :], AF.Sqrt, r=[ps_t4], w=[rr],
                        scale=1.0 / 256.0, bias=1e-5)
                    recip(P, rr.ap, rr.ap, r=[rr], w=[rr])
                    mm(P, ps_g.ap, sr4[:, h * 128:(h + 1) * 128], rr.ap, r=[rr, sr4],
                       w=[ps_g])
                    for dt in range(2):
                        k = 2 * h + dt
                        def f1(e, k=k, dt=dt):
                            return e.scalar_tensor_tensor(
                                onT[k].ap, oac[dt].ap, onws[:, dt:dt + 1], ps_g.ap,
                                OP.mult, OP.mult)
                        P.v(f1, r=[oac[dt], onws, ps_g], w=[onT[k]])
                for mt in range(KT):
                    pj = ps_s1 if mt % 2 == 0 else ps_g
                    for k in range(KT):
                        mm(P, pj.ap, wob[k][:, mt * 128:(mt + 1) * 128], onT[k].ap,
                           r=[wob[k], onT[k]], w=[pj], start=(k == 0), stop=(k == KT - 1))
                    tt(P, x1s.ap, pj.ap, xps[mt].ap, OP.add, r=[pj, xps[mt]], w=[x1s])
                    P.store(x1T.t[b, mt * 128:(mt + 1) * 128, sl_t], x1s.ap, x1T, r=[x1s])
            P.barrier()
    P.barrier()


def emit_s1g(P, ctx):
    x1T = ctx.dscratch("x1T", [B, D, T], F32)
    hlnT1b = ctx.dscratch("hlnT1b", [B, D, T], BF16)
    smaskT = ctx.dscratch("smaskT", [B, E0, T], F32)
    wred = ctx.din("L0_ffn_w_red", [D, 16])
    wg_d = ctx.din("L0_ffn_wg", [E0, 16])
    eye = ctx.const("eye_f")
    eyeb = ctx.const("eye_b")
    with scope(ctx):
        xs = [ctx.sb(f"s1g_x{k}", 128, PAN) for k in range(KT)]
        hlb = [ctx.sb(f"s1g_h{k}", 128, PAN, BF16) for k in range(KT)]
        la = _ln_alloc(ctx, "s1g")
        wrs = ctx.sb("s1g_wrs", 128, 128)
        wrb = ctx.sb("s1g_wrb", 128, 128, BF16)
        P.load(rr2(wrs, "p (a j) -> p a j", j=16),
               wred.t[:, :].rearrange("(a p) j -> p a j", p=128), r=[wred], w=[wrs], slow=True)
        vcopy(P, wrb.ap, wrs.ap, r=[wrs], w=[wrb])
        wgs = ctx.sb("s1g_wgs", E0, 16)
        P.load(wgs, wg_d.ap, r=[wg_d])
        wsq = ctx.sb("s1g_wsq", E0, 16)
        wnr = ctx.sb("s1g_wnr", E0, 1)
        wgn = ctx.sb("s1g_wgn", E0, 16, BF16)
        wgnT = ctx.sb("s1g_wgnT", 16, E0, BF16)
        act(P, wsq.ap, wgs.ap, AF.Square, r=[wgs], w=[wsq])
        reduce(P, wnr.ap, wsq.ap, OP.add, r=[wsq], w=[wnr])
        act(P, wnr.ap, wnr.ap, AF.Sqrt, r=[wnr], w=[wnr])
        ts(P, wnr.ap, wnr.ap, 1e-4, OP.max, r=[wnr], w=[wnr])
        recip(P, wnr.ap, wnr.ap, r=[wnr], w=[wnr])
        ts(P, wgn.ap, wgs.ap, wnr.t[:, 0:1], OP.mult, r=[wgs, wnr], w=[wgn])
        ps_w = ctx.ps("s1g_psw", 16, E0, BF16)
        transpose(P, ps_w.ap, wgn.ap, eyeb, r=[wgn], w=[ps_w], kdim=E0)
        vcopy(P, wgnT.ap, ps_w.ap, r=[ps_w], w=[wgnT])
        red = ctx.sb("s1g_red", 16, PAN, BF16)
        lg8 = ctx.sb("s1g_lg8", E0, PAN)
        nat = ctx.sb("s1g_nat", 128, 32)
        exl = ctx.sb("s1g_exl", 128, 32)
        exs = ctx.sb("s1g_exs", 128, 32)
        msk = ctx.sb("s1g_msk", 128, 32)
        lmx = ctx.sb("s1g_lmx", 128, 4)
        ssum = ctx.sb("s1g_ssum", 128, 4)
        smT = ctx.sb("s1g_smT", E0, PAN)
        ps16 = ctx.ps("s1g_ps16", 16, PAN)
        ps8 = ctx.ps("s1g_ps8", E0, PAN)
        psn = ctx.ps("s1g_psn", 128, 32)
        for b in range(B):
            for p in range(NPAN):
                sl_t = slice(p * PAN, (p + 1) * PAN)
                for k in range(KT):
                    P.load(xs[k], x1T.t[b, k * 128:(k + 1) * 128, sl_t], r=[x1T],
                           q=("sync" if k % 2 == 0 else "gpsimd"))
                emit_ln_to(P, ctx, la, xs, hlb)
                for k in range(KT):
                    P.store(hlnT1b.t[b, k * 128:(k + 1) * 128, sl_t], hlb[k].ap,
                            hlnT1b, r=[hlb[k]])
                for k in range(KT):
                    mm(P, ps16.ap, wrb[:, k * 16:(k + 1) * 16], hlb[k].ap,
                       r=[wrb, hlb[k]], w=[ps16], start=(k == 0), stop=(k == KT - 1))
                vcopy(P, red.ap, ps16.ap, r=[ps16], w=[red])
                mm(P, ps8.ap, wgnT.ap, red.ap, r=[wgnT, red], w=[ps8])
                vcopy(P, lg8.ap, ps8.ap, r=[ps8], w=[lg8])
                for s in range(4):
                    transpose(P, psn[:, s * 8:(s + 1) * 8], lg8[:, s * 128:(s + 1) * 128],
                              eye, r=[lg8], w=[psn], kdim=8)
                vcopy(P, nat.ap, psn.ap, r=[psn], w=[nat])
                for s in range(4):
                    sl8 = slice(s * 8, (s + 1) * 8)
                    def fr(e, sl8=sl8, s=s):
                        return e.tensor_reduce(lmx[:, s:s + 1], nat[:, sl8], AX.X, OP.max)
                    P.v(fr, r=[nat], w=[lmx])
                    ts(P, exl[:, sl8], nat[:, sl8], lmx[:, s:s + 1], OP.subtract,
                       r=[nat, lmx], w=[exl])
                act(P, exs.ap, exl.ap, AF.Exp, r=[exl], w=[exs])
                for s in range(4):
                    sl8 = slice(s * 8, (s + 1) * 8)
                    def fr2(e, sl8=sl8, s=s):
                        return e.tensor_reduce(ssum[:, s:s + 1], exs[:, sl8], AX.X, OP.add)
                    P.v(fr2, r=[exs], w=[ssum])
                recip(P, ssum.ap, ssum.ap, r=[ssum], w=[ssum])
                for s in range(4):
                    sl8 = slice(s * 8, (s + 1) * 8)
                    ts(P, msk[:, sl8], nat[:, sl8], lmx[:, s:s + 1], OP.is_equal,
                       r=[nat, lmx], w=[msk])
                    ts(P, msk[:, sl8], msk[:, sl8], ssum[:, s:s + 1], OP.mult,
                       r=[msk, ssum], w=[msk])
                for s in range(4):
                    transpose(P, ps8[:, s * 128:(s + 1) * 128], msk[:, s * 8:(s + 1) * 8],
                              eye, r=[msk], w=[ps8], kdim=128)
                vcopy(P, smT.ap, ps8.ap, r=[ps8], w=[smT])
                P.store(smaskT.t[b, :, sl_t], smT.ap, smaskT, r=[smT])
    P.barrier()


def emit_s1w(P, ctx):
    w1 = ctx.din("L0_ffn_w1", [E0, D, F0])
    w2 = ctx.din("L0_ffn_w2", [E0, F0, D])
    w1b = ctx.dscratch("w1b", [E0, KT, 128, F0], BF16)
    w2b = ctx.dscratch("w2b", [E0, 32, 128, D], BF16)
    with scope(ctx):
        stg = [ctx.sb(f"s1w_stg{i}", 128, F0) for i in range(2)]
        ob = [ctx.sb(f"s1w_ob{i}", 128, F0, BF16) for i in range(2)]
        i = 0
        for e in range(E0):
            for k in range(KT):
                s_ = stg[i % 2]; o_ = ob[i % 2]
                P.load(s_, w1.t[e, k * 128:(k + 1) * 128, :], r=[w1], w=[s_],
                       q=("sync" if i % 2 == 0 else "gpsimd"))
                vcopy(P, o_.ap, s_.ap, r=[s_], w=[o_], eng=("v" if i % 2 == 0 else "s"))
                P.store(w1b.t[e, k, :, :], o_.ap, w1b, r=[o_],
                        q=("gpsimd" if i % 2 == 0 else "sync"))
                i += 1
        for e in range(E0):
            for f in range(32):
                s_ = stg[i % 2]; o_ = ob[i % 2]
                P.load(TB(s_.t[:, 0:D], "stg"), w2.t[e, f * 128:(f + 1) * 128, :],
                       r=[w2], w=[s_], q=("sync" if i % 2 == 0 else "gpsimd"))
                vcopy(P, o_.t[:, 0:D], s_.t[:, 0:D], r=[s_], w=[o_],
                      eng=("v" if i % 2 == 0 else "s"))
                P.store(w2b.t[e, f, :, :], o_.t[:, 0:D], w2b, r=[o_],
                        q=("gpsimd" if i % 2 == 0 else "sync"))
                i += 1
    P.barrier()


def emit_s1e(P, ctx):
    x1T = ctx.dscratch("x1T", [B, D, T], F32)
    hlnT1b = ctx.dscratch("hlnT1b", [B, D, T], BF16)
    smaskT = ctx.dscratch("smaskT", [B, E0, T], F32)
    w1b = ctx.dscratch("w1b", [E0, KT, 128, F0], BF16)
    w2b = ctx.dscratch("w2b", [E0, 32, 128, D], BF16)
    x2T = ctx.dscratch("x2T", [B, D, T], F32)
    b1d = ctx.din("L0_ffn_b1", [E0, F0])
    b2d = ctx.din("L0_ffn_b2", [E0, D])
    sr8 = ctx.const("sel_row8")
    with scope(ctx):
        x2a = [ctx.sb(f"s1e_x2a{k}", 128, T) for k in range(KT)]
        smk = ctx.sb("s1e_smk", E0, PAN)
        b1a = ctx.sb("s1e_b1a", 128, 256)
        b2a = ctx.sb("s1e_b2a", 128, 64)
        P.load(rr2(b1a, "p (e a) -> p e a", e=8),
               b1d.t[:, :].rearrange("e (a p) -> p e a", p=128), r=[b1d], w=[b1a], slow=True)
        P.load(rr2(b2a, "p (e a) -> p e a", e=8),
               b2d.t[:, :].rearrange("e (a p) -> p e a", p=128), r=[b2d], w=[b2a], slow=True)
        w1h = [ctx.sb(f"s1e_w1h{k}", 128, 2048, BF16) for k in range(KT)]
        w2h = [ctx.sb(f"s1e_w2h{j}", 128, D, BF16) for j in range(16)]
        hlb = [ctx.sb(f"s1e_hl{k}", 128, T, BF16) for k in range(KT)]
        hbf = [ctx.sb(f"s1e_hb{j}", 128, PAN, BF16) for j in range(16)]
        smbc = ctx.sb("s1e_smbc", 128, PAN)
        tmx = ctx.sb("s1e_tmx", 128, PAN)
        ps_h = [ctx.ps(f"s1e_psh{i}", 128, PAN) for i in range(2)]
        ps_y = [ctx.ps(f"s1e_psy{i}", 128, PAN) for i in range(2)]
        ps_sm = ctx.ps("s1e_pssm", 128, PAN)
        for b in range(B):
            for k in range(KT):
                P.load(x2a[k], x1T.t[b, k * 128:(k + 1) * 128, :], r=[x1T], w=[x2a[k]],
                       q=("sync" if k % 2 == 0 else "gpsimd"))

            for k in range(KT):
                P.load(hlb[k], hlnT1b.t[b, k * 128:(k + 1) * 128, :],
                       r=[hlnT1b], w=[hlb[k]],
                       q=("sync" if k % 2 == 0 else "gpsimd"))
            for e in range(E0):
                for fh in range(2):
                    for k in range(KT):
                        P.load(w1h[k], w1b.t[e, k, :, fh * 2048:(fh + 1) * 2048],
                               r=[w1b], w=[w1h[k]], q=("sync" if k % 2 == 0 else "gpsimd"))
                    for j in range(16):
                        P.load(w2h[j], w2b.t[e, fh * 16 + j, :, :], r=[w2b], w=[w2h[j]],
                               q=("sync" if j % 2 == 0 else "gpsimd"))
                    for p in range(NPAN):
                        sl_t = slice(p * PAN, (p + 1) * PAN)
                        P.load(smk, smaskT.t[b, :, sl_t], r=[smaskT], w=[smk])
                        mm(P, ps_sm.ap, sr8[:, e * 128:(e + 1) * 128], smk.ap,
                           r=[sr8, smk], w=[ps_sm])
                        vcopy(P, smbc.ap, ps_sm.ap, r=[ps_sm], w=[smbc])
                        for ft in range(16):
                            pj = ps_h[ft % 2]
                            for k in range(KT):
                                mm(P, pj.ap, w1h[k][:, ft * 128:(ft + 1) * 128],
                                   hlb[k][:, sl_t],
                                   r=[w1h[k], hlb[k]], w=[pj], start=(k == 0),
                                   stop=(k == KT - 1))
                            act(P, hbf[ft].ap, pj.ap, AF.Gelu, r=[pj, b1a], w=[hbf[ft]],
                                bias=b1a[:, e * 32 + fh * 16 + ft:e * 32 + fh * 16 + ft + 1])
                        for mt in range(KT):
                            pj = ps_y[mt % 2]
                            for j in range(16):
                                mm(P, pj.ap, w2h[j][:, mt * 128:(mt + 1) * 128], hbf[j].ap,
                                   r=[w2h[j], hbf[j]], w=[pj], start=(j == 0),
                                   stop=(j == 15))
                            if fh == 1:
                                def fb(e_, pj=pj, mt=mt, e=e):
                                    return e_.scalar_tensor_tensor(
                                        tmx.ap, pj.ap, b2a[:, e * 8 + mt:e * 8 + mt + 1],
                                        smbc.ap, OP.add, OP.mult)
                                P.v(fb, r=[pj, b2a, smbc], w=[tmx])
                            else:
                                tt(P, tmx.ap, pj.ap, smbc.ap, OP.mult, r=[pj, smbc],
                                   w=[tmx])
                            tt(P, x2a[mt][:, sl_t], x2a[mt][:, sl_t], tmx.ap, OP.add,
                               r=[x2a[mt], tmx], w=[x2a[mt]])
            for k in range(KT):
                P.store(x2T.t[b, k * 128:(k + 1) * 128, :], x2a[k].ap, x2T, r=[x2a[k]],
                        q=("gpsimd" if k % 2 == 0 else "sync"))
            P.barrier()
    P.barrier()


def emit_s2(P, ctx):
    x2T = ctx.dscratch("x2T", [B, D, T], F32)
    x3T = ctx.dscratch("x3T", [B, D, T], F32)
    Wq = ctx.din("L1_attn_Wq", [H1, D, DH1])
    Wk = ctx.din("L1_attn_Wk", [H1, D, DH1])
    Wv = ctx.din("L1_attn_Wv", [H1, D, DH1])
    Wo = ctx.din("L1_attn_Wo", [D, D])
    bo = ctx.din("L1_attn_bo", [D])
    eye = ctx.const("eye_f")
    eyeb = ctx.const("eye_b")
    ocb = ctx.const("ones_col_b")
    orow = ctx.const("ones_row_f")
    m_ui = ctx.const("mask_ui4")
    with scope(ctx):
        ps_s = [ctx.ps(f"s2_pss{i}", 128, PAN) for i in range(2)]
        ps_w2 = ctx.ps("s2_psw2", 128, 1024)
        ps_sum = ctx.ps("s2_pssum", 1, PAN)
        ps_t = ctx.ps("s2_pst", 128, 64, BF16)
        xs = [ctx.sb(f"s2_x{k}", 128, PAN) for k in range(KT)]
        hlb = [ctx.sb(f"s2_hl{k}", 128, T, BF16) for k in range(KT)]
        hpan = [TB(hlb[k].t, f"alias_hl{k}") for k in range(KT)]
        la = {"tmp": ctx.sb("s2_lntmp", 128, PAN), "row": ctx.sb("s2_lnrow", 1, 4 * PAN),
              "psum": ps_sum, "psq": ps_s[1], "pbc": ps_w2}
        qw = {}
        wstg = ctx.sb("s2_wstg", 128, 512)
        for nm, Wd in (("q", Wq), ("k", Wk), ("v", Wv)):
            for h in range(H1):
                t_ = ctx.sb(f"s2_w{nm}{h}", 128, 512, BF16)
                P.load(rr2(wstg, "p (a j) -> p a j", j=DH1),
                       Wd.t[h].rearrange("(a p) j -> p a j", p=128),
                       r=[Wd], w=[wstg], slow=True,
                       q=("sync" if h % 2 == 0 else "gpsimd"))
                vcopy(P, t_.ap, wstg.ap, r=[wstg], w=[t_],
                      eng=("v" if h % 2 == 0 else "s"))
                qw[(nm, h)] = t_
        woT = [ctx.sb(f"s2_woT{k}", 128, 1024, BF16) for k in range(KT)]
        wst2 = ctx.sb("s2_wst2", 128, 1024)
        for m in range(KT):
            P.load(wst2, Wo.t[m * 128:(m + 1) * 128, :], r=[Wo], w=[wst2])
            for k in range(KT):
                transpose(P, ps_w2[:, 0:128], wst2[:, k * 128:(k + 1) * 128], eye,
                          r=[wst2], w=[ps_w2])
                vcopy(P, woT[k][:, m * 128:(m + 1) * 128], ps_w2[:, 0:128],
                      r=[ps_w2], w=[woT[k]])
        bos = ctx.sb("s2_bos", 128, KT)
        P.load(bos, bo.t[:].rearrange("(a p) -> p a", p=128), r=[bo], slow=True)
        qT = ctx.sb("s2_qT", 64, T, BF16)
        kT = ctx.sb("s2_kT", 64, T, BF16)
        vTs = ctx.sb("s2_vT", 64, T, BF16)
        vn = [ctx.sb(f"s2_vn{tt_}", 128, 64, BF16) for tt_ in range(T // 128)]
        pex = ctx.sb("s2_pex", 128, PAN, BF16)
        srow = ctx.sb("s2_srow", 1, PAN)
        of32 = ctx.sb("s2_of32", 64, PAN)
        oall = [ctx.sb(f"s2_oa{k}", 128, T, BF16) for k in range(KT)]
        x3s = ctx.sb("s2_x3s", 128, PAN)
        for b in range(B):
            for p in range(NPAN):
                for k in range(KT):
                    P.load(xs[k], x2T.t[b, k * 128:(k + 1) * 128, p * PAN:(p + 1) * PAN],
                           r=[x2T], q=("sync" if k % 2 == 0 else "gpsimd"))
                dsts = [View(hlb[k].t[:, p * PAN:(p + 1) * PAN], hlb[k].b)
                        for k in range(KT)]
                emit_ln_panel(P, ctx, xs, dsts, la["tmp"], la["psum"], la["psq"],
                              la["row"], la["pbc"])
            for h in range(H1):
                for nm, dst in (("q", qT), ("k", kT), ("v", vTs)):
                    w_ = qw[(nm, h)]
                    for p in range(NPAN):
                        pj = ps_s[p % 2]
                        for k in range(KT):
                            mm(P, pj[0:64, :], w_[:, k * 64:(k + 1) * 64],
                               hlb[k][:, p * PAN:(p + 1) * PAN], r=[w_, hlb[k]], w=[pj],
                               start=(k == 0), stop=(k == KT - 1))
                        vcopy(P, dst[0:64, p * PAN:(p + 1) * PAN], pj[0:64, :],
                              r=[pj], w=[dst])
                for tt_ in range(T // 128):
                    transpose(P, ps_t[:, 0:64], vTs[0:64, tt_ * 128:(tt_ + 1) * 128],
                              eyeb, r=[vTs], w=[ps_t], kdim=64)
                    vcopy(P, vn[tt_].ap, ps_t[:, 0:64], r=[ps_t], w=[vn[tt_]])
                for ip in range(NPAN):
                    for jt in range(4 * ip + 4):
                        off = max(0, jt - 4 * ip)
                        cols = PAN - off * 128
                        pj = ps_s[jt % 2]
                        mm(P, pj[:, 0:cols], kT[0:64, jt * 128:(jt + 1) * 128],
                           qT[0:64, ip * PAN + off * 128:(ip + 1) * PAN],
                           r=[kT, qT], w=[pj])
                        act(P, pex[:, 0:cols], pj[:, 0:cols], AF.Exp, r=[pj], w=[pex],
                            scale=0.125)
                        if jt >= 4 * ip:
                            tt(P, pex[:, 0:128], pex[:, 0:128], m_ui[:, 0:128], OP.mult,
                               r=[pex, m_ui], w=[pex])
                        mm(P, ps_sum[0:1, off * 128:PAN], ocb[:, 0:1], pex[:, 0:cols],
                           r=[pex, ocb], w=[ps_sum], start=(jt == 0), stop=True)
                        mm(P, ps_w2[0:64, off * 128:PAN], vn[jt].ap, pex[:, 0:cols],
                           r=[vn[jt], pex], w=[ps_w2], start=(jt == 0), stop=True)
                    vcopy(P, srow.ap, ps_sum[0:1, :], r=[ps_sum], w=[srow])
                    recip(P, srow.ap, srow.ap, r=[srow], w=[srow])
                    mm(P, ps_w2[0:64, 512:1024], orow[0:1, 0:64], srow.ap,
                       r=[srow, orow], w=[ps_w2])
                    vcopy(P, of32.ap, ps_w2[0:64, 0:512], r=[ps_w2], w=[of32])
                    kk = h // 2; po = (h % 2) * 64
                    tt(P, oall[kk].t[po:po + 64, ip * PAN:(ip + 1) * PAN], of32.ap,
                       ps_w2[0:64, 512:1024], OP.mult, r=[of32, ps_w2], w=[oall[kk]])
            for p in range(NPAN):
                sl_t = slice(p * PAN, (p + 1) * PAN)
                for k in range(KT):
                    P.load(xs[k], x2T.t[b, k * 128:(k + 1) * 128, sl_t], r=[x2T],
                           q=("sync" if k % 2 == 0 else "gpsimd"))
                for mt in range(KT):
                    pj = ps_s[mt % 2]
                    for k in range(KT):
                        mm(P, pj.ap, woT[k][:, mt * 128:(mt + 1) * 128],
                           oall[k][:, sl_t], r=[woT[k], oall[k]], w=[pj],
                           start=(k == 0), stop=(k == KT - 1))
                    def fo(e, pj=pj, mt=mt):
                        return e.scalar_tensor_tensor(
                            x3s.ap, pj.ap, bos[:, mt:mt + 1], xs[mt].ap, OP.add, OP.add)
                    P.v(fo, r=[pj, bos, xs[mt]], w=[x3s])
                    P.store(x3T.t[b, mt * 128:(mt + 1) * 128, sl_t], x3s.ap, x3T, r=[x3s])
            P.barrier()
    P.barrier()


def emit_s3a(P, ctx):
    x3T = ctx.dscratch("x3T", [B, D, T], F32)
    hlnT3b = ctx.dscratch("hlnT3b", [B, D, T], BF16)
    gsc = ctx.dscratch("gsc", [B, E1, 1], F32)
    gw1 = ctx.din("L1_ffn_gate_w1", [D, D])
    gb1_ = ctx.din("L1_ffn_gate_b1", [D])
    gw2_ = ctx.din("L1_ffn_gate_w2", [E1, D])
    gb2_ = ctx.din("L1_ffn_gate_b2", [E1])
    eye = ctx.const("eye_f")
    with scope(ctx):
        xs = [ctx.sb(f"s3a_x{k}", 128, PAN) for k in range(KT)]
        hlb = [ctx.sb(f"s3a_h{k}", 128, PAN, BF16) for k in range(KT)]
        la = _ln_alloc(ctx, "s3a")
        wst = ctx.sb("s3a_wst", 128, 1024)
        gw1T = [ctx.sb(f"s3a_g1T{k}", 128, 1024, BF16) for k in range(KT)]
        pst = ctx.ps("s3a_pst", 128, 128)
        for o in range(KT):
            P.load(wst, gw1.t[o * 128:(o + 1) * 128, :], r=[gw1], w=[wst])
            for k in range(KT):
                transpose(P, pst.ap, wst[:, k * 128:(k + 1) * 128], eye, r=[wst], w=[pst])
                vcopy(P, gw1T[k][:, o * 128:(o + 1) * 128], pst.ap, r=[pst], w=[gw1T[k]])
        gw2s = ctx.sb("s3a_gw2s", E1, 1024)
        P.load(gw2s, gw2_.ap, r=[gw2_])
        gw2T = [ctx.sb(f"s3a_g2T{k}", 128, E1, BF16) for k in range(KT)]
        for k in range(KT):
            transpose(P, pst[:, 0:E1], gw2s[:, k * 128:(k + 1) * 128], eye,
                      r=[gw2s], w=[pst], kdim=E1)
            vcopy(P, gw2T[k].ap, pst[:, 0:E1], r=[pst], w=[gw2T[k]])
        gb1c = ctx.sb("s3a_gb1c", 128, KT)
        P.load(gb1c, gb1_.t[:].rearrange("(a p) -> p a", p=128), r=[gb1_], slow=True)
        gb2c = ctx.sb("s3a_gb2c", E1, 1)
        P.load(gb2c, gb2_.t[:].rearrange("(e o) -> e o", o=1), r=[gb2_], slow=True)
        h1b = [ctx.sb(f"s3a_h1{k}", 128, PAN, BF16) for k in range(KT)]
        gac = ctx.sb("s3a_gac", E1, 1)
        gtmp = ctx.sb("s3a_gtmp", E1, 1)
        ps_m = [ctx.ps(f"s3a_psm{i}", 128, PAN) for i in range(2)]
        ps_g8 = ctx.ps("s3a_psg8", E1, PAN)
        for b in range(B):
            memset(P, gac.ap, 0.0, w=[gac])
            for p in range(NPAN):
                sl_t = slice(p * PAN, (p + 1) * PAN)
                for k in range(KT):
                    P.load(xs[k], x3T.t[b, k * 128:(k + 1) * 128, sl_t], r=[x3T],
                           q=("sync" if k % 2 == 0 else "gpsimd"))
                emit_ln_to(P, ctx, la, xs, hlb)
                for k in range(KT):
                    P.store(hlnT3b.t[b, k * 128:(k + 1) * 128, sl_t], hlb[k].ap,
                            hlnT3b, r=[hlb[k]])
                for mt in range(KT):
                    pj = ps_m[mt % 2]
                    for k in range(KT):
                        mm(P, pj.ap, gw1T[k][:, mt * 128:(mt + 1) * 128], hlb[k].ap,
                           r=[gw1T[k], hlb[k]], w=[pj], start=(k == 0), stop=(k == KT - 1))
                    act(P, h1b[mt].ap, pj.ap, AF.Relu, r=[pj, gb1c], w=[h1b[mt]],
                        bias=gb1c[:, mt:mt + 1])
                for k in range(KT):
                    mm(P, ps_g8.ap, gw2T[k].ap, h1b[k].ap, r=[gw2T[k], h1b[k]], w=[ps_g8],
                       start=(k == 0), stop=(k == KT - 1))
                reduce(P, gtmp.ap, ps_g8.ap, OP.add, r=[ps_g8], w=[gtmp])
                tt(P, gac.ap, gac.ap, gtmp.ap, OP.add, r=[gac, gtmp], w=[gac])
            ts(P, gac.ap, gac.ap, 1.0 / T, OP.mult, r=[gac], w=[gac])
            tt(P, gac.ap, gac.ap, gb2c.ap, OP.add, r=[gac, gb2c], w=[gac])
            P.store(gsc.t[b], gac.ap, gsc, r=[gac])
    P.barrier()


def emit_s3b(P, ctx):
    gsc = ctx.dscratch("gsc", [B, E1, 1], F32)
    mw1b = ctx.dscratch("mw1b", [B, KT, 128, F1], BF16)
    mw2b = ctx.dscratch("mw2b", [B, 32, 128, D], BF16)
    mb1d = ctx.dscratch("mb1d", [B, 128, 32], F32)
    mb2d = ctx.dscratch("mb2d", [B, 128, E1], F32)
    W1 = ctx.din("L1_ffn_W1", [F1, D])
    W2 = ctx.din("L1_ffn_W2", [D, F1])
    TW1 = ctx.din("L1_ffn_TW1", [E1, F1, D])
    TW2 = ctx.din("L1_ffn_TW2", [E1, D, F1])
    b1_ = ctx.din("L1_ffn_b1", [F1])
    b2_ = ctx.din("L1_ffn_b2", [D])
    Tb1 = ctx.din("L1_ffn_Tb1", [E1, F1])
    Tb2 = ctx.din("L1_ffn_Tb2", [E1, D])
    eye = ctx.const("eye_f")
    sr8 = ctx.const("sel_row8")
    with scope(ctx):
        g8 = ctx.sb("s3b_g8", E1, 1)
        gcol = ctx.sb("s3b_gcol", 128, E1)
        dgs = [ctx.sb(f"s3b_dg{e}", 128, 128) for e in range(E1)]
        stg = [ctx.sb(f"s3b_stg{i}", 128, F1) for i in range(2)]
        gat = ctx.sb("s3b_gat", 128, F1, BF16)
        ps32 = ctx.ps("s3b_ps32", 128, 4096)
        tb1s = ctx.sb("s3b_tb1s", E1, F1)
        tb2s = ctx.sb("s3b_tb2s", E1, D)
        bvec = ctx.sb("s3b_bvec", 128, 32)
        mb = ctx.sb("s3b_mb", 128, 32)
        P.load(tb1s, Tb1.ap, r=[Tb1])
        P.load(tb2s, Tb2.ap, r=[Tb2])
        for b in range(B):
            P.load(g8, gsc.t[b], r=[gsc], w=[g8])
            for e in range(E1):
                mm(P, ps32[:, e:e + 1], sr8[:, e * 128:(e + 1) * 128], g8.ap,
                   r=[g8, sr8], w=[ps32])
            vcopy(P, gcol.ap, ps32[:, 0:E1], r=[ps32], w=[gcol])
            for e in range(E1):
                ts(P, dgs[e].ap, eye.ap, gcol[:, e:e + 1], OP.mult, r=[eye, gcol],
                   w=[dgs[e]])
            for ft in range(32):
                s_ = stg[0]
                P.load(View(s_.t[:, 0:D], s_.b), W1.t[ft * 128:(ft + 1) * 128, :],
                       r=[W1], w=[s_])
                for hc in range(KT):
                    mm(P, ps32[:, hc * 128:(hc + 1) * 128],
                       s_[:, hc * 128:(hc + 1) * 128], eye.ap, r=[s_, eye], w=[ps32],
                       start=True, stop=False)
                for e in range(E1):
                    s_ = stg[(e + 1) % 2]
                    P.load(View(s_.t[:, 0:D], s_.b),
                           TW1.t[e, ft * 128:(ft + 1) * 128, :], r=[TW1], w=[s_],
                           q=("sync" if e % 2 == 0 else "gpsimd"))
                    for hc in range(KT):
                        mm(P, ps32[:, hc * 128:(hc + 1) * 128],
                           s_[:, hc * 128:(hc + 1) * 128], dgs[e].ap, r=[s_, dgs[e]],
                           w=[ps32], start=False, stop=(e == E1 - 1))
                vcopy(P, gat[:, 0:1024], ps32[:, 0:1024], r=[ps32], w=[gat])
                for hc in range(KT):
                    P.store(mw1b.t[b, hc, :, ft * 128:(ft + 1) * 128],
                            gat.t[:, hc * 128:(hc + 1) * 128], mw1b, r=[gat],
                            q=("gpsimd" if hc % 2 == 0 else "sync"))
            for ht in range(KT):
                s_ = stg[0]
                P.load(s_, W2.t[ht * 128:(ht + 1) * 128, :], r=[W2], w=[s_])
                for fc in range(32):
                    mm(P, ps32[:, fc * 128:(fc + 1) * 128],
                       s_[:, fc * 128:(fc + 1) * 128], eye.ap, r=[s_, eye], w=[ps32],
                       start=True, stop=False)
                for e in range(E1):
                    s_ = stg[(e + 1) % 2]
                    P.load(s_, TW2.t[e, ht * 128:(ht + 1) * 128, :], r=[TW2], w=[s_],
                           q=("sync" if e % 2 == 0 else "gpsimd"))
                    for fc in range(32):
                        mm(P, ps32[:, fc * 128:(fc + 1) * 128],
                           s_[:, fc * 128:(fc + 1) * 128], dgs[e].ap, r=[s_, dgs[e]],
                           w=[ps32], start=False, stop=(e == E1 - 1))
                vcopy(P, gat.ap, ps32.ap, r=[ps32], w=[gat])
                for fc in range(32):
                    P.store(mw2b.t[b, fc, :, ht * 128:(ht + 1) * 128],
                            gat.t[:, fc * 128:(fc + 1) * 128], mw2b, r=[gat],
                            q=("gpsimd" if fc % 2 == 0 else "sync"))
            P.load(bvec, b1_.t[:].rearrange("(a p) -> p a", p=128), r=[b1_], w=[bvec],
                   slow=True)
            for ft in range(32):
                mm(P, ps32[:, ft:ft + 1], tb1s[:, ft * 128:(ft + 1) * 128], g8.ap,
                   r=[tb1s, g8], w=[ps32], start=True, stop=True)
            tt(P, mb.ap, ps32[:, 0:32], bvec.ap, OP.add, r=[ps32, bvec], w=[mb])
            P.store(mb1d.t[b], mb.ap, mb1d, r=[mb])
            P.load(View(bvec.t[:, 0:E1], bvec.b),
                   b2_.t[:].rearrange("(a p) -> p a", p=128), r=[b2_], w=[bvec],
                   slow=True)
            for ht in range(KT):
                mm(P, ps32[:, ht:ht + 1], tb2s[:, ht * 128:(ht + 1) * 128], g8.ap,
                   r=[tb2s, g8], w=[ps32], start=True, stop=True)
            tt(P, mb[:, 0:E1], ps32[:, 0:E1], bvec[:, 0:E1], OP.add, r=[ps32, bvec],
               w=[mb])
            P.store(mb2d.t[b], mb.t[:, 0:E1], mb2d, r=[mb])
            P.barrier()
    P.barrier()


def emit_s3c(P, ctx):
    x3T = ctx.dscratch("x3T", [B, D, T], F32)
    hlnT3b = ctx.dscratch("hlnT3b", [B, D, T], BF16)
    mw1b = ctx.dscratch("mw1b", [B, KT, 128, F1], BF16)
    mw2b = ctx.dscratch("mw2b", [B, 32, 128, D], BF16)
    mb1d = ctx.dscratch("mb1d", [B, 128, 32], F32)
    mb2d = ctx.dscratch("mb2d", [B, 128, E1], F32)
    x4T = ctx.dscratch("x4T", [B, D, T], F32)
    with scope(ctx):
        ya = [ctx.sb(f"s3c_ya{k}", 128, T) for k in range(KT)]
        mw1s = [ctx.sb(f"s3c_w1{k}", 128, PAN, BF16) for k in range(KT)]
        mw2s = [ctx.sb(f"s3c_w2{j}", 128, D, BF16) for j in range(4)]
        hlb = [ctx.sb(f"s3c_hl{k}", 128, PAN, BF16) for k in range(KT)]
        ubf = [ctx.sb(f"s3c_u{j}", 128, PAN, BF16) for j in range(4)]
        mb1s = ctx.sb("s3c_mb1", 128, 32)
        mb2s = ctx.sb("s3c_mb2", 128, E1)
        xps = [ctx.sb(f"s3c_xp{k}", 128, PAN) for k in range(KT)]
        x4s = ctx.sb("s3c_x4s", 128, PAN)
        ps_u = [ctx.ps(f"s3c_psu{i}", 128, PAN) for i in range(2)]
        ps_y = [ctx.ps(f"s3c_psy{i}", 128, PAN) for i in range(2)]
        for b in range(B):
            P.load(mb1s, mb1d.t[b], r=[mb1d], w=[mb1s])
            P.load(mb2s, mb2d.t[b], r=[mb2d], w=[mb2s])
            for fc in range(KT):
                for k in range(KT):
                    P.load(mw1s[k], mw1b.t[b, k, :, fc * PAN:(fc + 1) * PAN],
                           r=[mw1b], w=[mw1s[k]], q=("sync" if k % 2 == 0 else "gpsimd"))
                for j in range(4):
                    P.load(mw2s[j], mw2b.t[b, fc * 4 + j, :, :], r=[mw2b], w=[mw2s[j]],
                           q=("sync" if j % 2 == 0 else "gpsimd"))
                for p in range(NPAN):
                    sl_t = slice(p * PAN, (p + 1) * PAN)
                    for k in range(KT):
                        P.load(hlb[k], hlnT3b.t[b, k * 128:(k + 1) * 128, sl_t],
                               r=[hlnT3b], w=[hlb[k]],
                               q=("sync" if k % 2 == 0 else "gpsimd"))
                    for fmt in range(4):
                        pj = ps_u[fmt % 2]
                        for k in range(KT):
                            mm(P, pj.ap, mw1s[k][:, fmt * 128:(fmt + 1) * 128], hlb[k].ap,
                               r=[mw1s[k], hlb[k]], w=[pj], start=(k == 0),
                               stop=(k == KT - 1))
                        ftg = fc * 4 + fmt
                        act(P, ubf[fmt].ap, pj.ap, AF.Gelu_apprx_tanh, r=[pj, mb1s],
                            w=[ubf[fmt]], bias=mb1s[:, ftg:ftg + 1])
                    for mt in range(KT):
                        pj = ps_y[mt % 2]
                        for j in range(4):
                            mm(P, pj.ap, mw2s[j][:, mt * 128:(mt + 1) * 128], ubf[j].ap,
                               r=[mw2s[j], ubf[j]], w=[pj], start=(j == 0), stop=(j == 3))
                        if fc == 0:
                            vcopy(P, ya[mt][:, sl_t], pj.ap, r=[pj], w=[ya[mt]])
                        else:
                            tt(P, ya[mt][:, sl_t], ya[mt][:, sl_t], pj.ap, OP.add,
                               r=[ya[mt], pj], w=[ya[mt]])
            for p in range(NPAN):
                sl_t = slice(p * PAN, (p + 1) * PAN)
                for k in range(KT):
                    P.load(xps[k], x3T.t[b, k * 128:(k + 1) * 128, sl_t], r=[x3T],
                           w=[xps[k]], q=("sync" if k % 2 == 0 else "gpsimd"))
                for mt in range(KT):
                    def fx(e, mt=mt, sl_t=sl_t):
                        return e.scalar_tensor_tensor(
                            x4s.ap, ya[mt][:, sl_t], mb2s[:, mt:mt + 1], xps[mt].ap,
                            OP.add, OP.add)
                    P.v(fx, r=[ya[mt], mb2s, xps[mt]], w=[x4s])
                    P.store(x4T.t[b, mt * 128:(mt + 1) * 128, sl_t], x4s.ap, x4T, r=[x4s])
            P.barrier()
    P.barrier()


STAGES = None

def _default_launch_plan():
    return [("s_in", emit_s_in),
            ("s0a", emit_s0a),
            ("s0a2", emit_s0a2),
            ("s0b", emit_s0b),
            ("s0c1", emit_s0c1),
            ("s0c2", emit_s0c2),
            ("s1g", emit_s1g),
            ("s1w", emit_s1w),
            ("s1e", emit_s1e),
            ("s2", emit_s2),
            ("s3a", emit_s3a),
            ("s3b", emit_s3b),
            ("s3c", emit_s3c),
            ("s_out", emit_s_out)]

def _run_launch(name, emitters, carried, mode):
    nc = bass.Bass("TRN2", target_bir_lowering=False)
    with ExitStack() as stack:
        P = Prog(nc)
        ctx = Ctx(nc, stack, P, mode, carried)
        if mode == "multi":
            ctx._pre = carried
            def dscratch(nm, shape, dtype=F32, _c=ctx):
                if nm not in _c.dram:
                    if nm in carried:
                        t = _c.nc.dram_tensor(nm, list(shape), dtype, kind="ExternalInput")
                        _c.dram[nm] = TB(t.ap(), nm)
                    else:
                        t = _c.nc.dram_tensor(nm, list(shape), dtype, kind="ExternalOutput")
                        _c.dram[nm] = TB(t.ap(), nm)
                        _c.outputs.add(nm)
                return _c.dram[nm]
            ctx.dscratch = dscratch
        for nm in CONST_SPECS:
            ctx.const(nm)
        fv = ctx.const("fvals")
        P.fconst = fv
        for i, val in enumerate([1e-5, 1e-6, -1.0, 0.125, 1.0 / 256.0]):
            nc.const_aps.register(F32, val, fv.t[:, i:i + 1])
        for em in emitters:
            em(P, ctx)
        P.build(stack)
    in_map = {}
    for nm, tb in ctx.dram.items():
        if nm in carried:
            in_map[nm] = carried[nm]
        elif nm.startswith("c_"):
            in_map[nm] = np.ascontiguousarray(CONST_SPECS[nm[2:]])
    in_maps = [dict(in_map) if i == 0 else {k: v for k, v in in_map.items()}
               for i in range(NCORES)]
    for i in range(NCORES):
        for k in list(in_maps[i].keys()):
            v = in_maps[i][k]
            if isinstance(v, list):
                in_maps[i][k] = v[i]
    res = run_bass_kernel_spmd(nc, in_maps, core_ids=list(range(NCORES)))
    outs = {}
    for nm in ctx.outputs:
        outs[nm] = [res.results[i][nm] for i in range(NCORES)]
    return outs


def run_pipeline(inputs, plan):
    x = np.ascontiguousarray(inputs["L0_attn_hs"])
    carried = {"x": [np.ascontiguousarray(x[2 * i:2 * i + 2]) for i in range(NCORES)]}
    for k, v in inputs.items():
        carried[k] = np.ascontiguousarray(v)
    for name, em in plan:
        ems = em if isinstance(em, list) else [em]
        outs = _run_launch(name, ems, carried, mode="multi")
        for nm, percore in outs.items():
            carried[nm] = percore
    return carried


FUSED = True

def kernel(**inputs):
    plan = STAGES if STAGES is not None else _default_launch_plan()
    if FUSED and STAGES is None:
        x = np.ascontiguousarray(inputs["L0_attn_hs"])
        carried = {"x": [np.ascontiguousarray(x[2 * i:2 * i + 2])
                         for i in range(NCORES)]}
        for k, v in inputs.items():
            carried[k] = np.ascontiguousarray(v)
        outs = _run_launch("mega", [em for _, em in plan], carried, mode="fused")
        return np.concatenate(outs["out"], axis=0)
    carried = run_pipeline(inputs, plan)
    return np.concatenate(carried["out"], axis=0)


# version 09 (final)
# speedup vs baseline: 1.2103x; 1.0344x over previous
import numpy as np
import ml_dtypes
from contextlib import ExitStack

import concourse.bass as bass
import concourse.mybir as mybir
from concourse.bass_utils import run_bass_kernel_spmd

F32 = mybir.dt.float32
BF16 = mybir.dt.bfloat16
AF = mybir.ActivationFunctionType
OP = mybir.AluOpType
AX = mybir.AxisListType
NPBF = ml_dtypes.bfloat16

NCORES = 8
B = 2
T = 2048
D = 1024
KT = D // 128
PAN = 512
NPAN = T // PAN
NH, DK, DV = 4, 256, 256
CH = 128
NCH = T // CH
E0, F0 = 8, 4096
H1, DH1 = 16, 64
F1, E1 = 4096, 8


class _Op:
    __slots__ = ("eng", "fn", "waits", "inc", "amount", "val", "kind", "semkey")
    def __init__(self, eng, fn, kind):
        self.eng = eng; self.fn = fn; self.waits = []
        self.inc = False; self.amount = 1; self.val = None; self.kind = kind
        self.semkey = None

class Buf:
    __slots__ = ("name", "writer", "readers")
    def __init__(self, name):
        self.name = name; self.writer = None; self.readers = []

class TB:
    __slots__ = ("t", "b")
    def __init__(self, t, name):
        self.t = t; self.b = Buf(name)
    def __getitem__(self, idx):
        return self.t[idx]
    @property
    def ap(self):
        return self.t[:]

class View:
    __slots__ = ("_ap", "b")
    def __init__(self, ap, buf):
        self._ap = ap; self.b = buf
    @property
    def ap(self):
        return self._ap


def rr2(tb, pat, **kw):
    return View(tb.t[:, :].rearrange(pat, **kw), tb.b)

def _buf(x):
    return getattr(x, "b", x)

class Prog:
    ENGS = ("tensor", "vector", "scalar", "gpsimd", "sync")
    NDS = 8

    def __init__(self, nc):
        self.nc = nc
        self.ops = {e: [] for e in self.ENGS}
        self.dma_n = {e: 0 for e in self.ENGS}
        self.dma_sem_tot = {}
        self.pbank = {}
        self.pbufs = {}

    def _emit(self, eng, fn, r=(), w=(), kind="eng", amount=1):
        op = _Op(eng, fn, kind)
        op.amount = amount
        assert (kind == "dma") == (eng in ("sync", "gpsimd")), (kind, eng)
        if kind == "dma":
            idx = self.dma_n[eng]
            self.dma_n[eng] = idx + 1
            op.semkey = (eng, idx % self.NDS)
            op.val = 16 * (idx // self.NDS + 1)
            op.inc = True
            if op.val > 16:
                op.waits.append(("dma", op.semkey, op.val - 16))
            self.dma_sem_tot[op.semkey] = op.val
        deps = []
        for x in r:
            b = _buf(x)
            if b.name in self.pbufs:
                for kk in self.pbufs[b.name]:
                    if kk in self.pbank:
                        self.pbank[kk]["stale"] = True
            if b.writer is not None:
                deps.append(b.writer)
        for x in w:
            b = _buf(x)
            if b.writer is not None:
                deps.append(b.writer)
            deps.extend(b.readers)
        seen = set()
        dma_wait = {}
        for d in deps:
            if d is op or id(d) in seen:
                continue
            seen.add(id(d))
            if d.kind == "dma":
                k = d.semkey
                dma_wait[k] = max(dma_wait.get(k, 0), d.val)
                continue
            if d.kind == "eng" and d.eng == eng == "tensor":
                continue
            d.inc = True
            op.waits.append(d)
        for k, v in dma_wait.items():
            if op.kind == "dma" and k == op.semkey:
                op.waits = [wt for wt in op.waits
                            if not (isinstance(wt, tuple) and wt[1] == k and wt[2] < v)]
                if not any(isinstance(wt, tuple) and wt[1] == k and wt[2] >= v
                           for wt in op.waits):
                    op.waits.append(("dma", k, v))
            else:
                op.waits.append(("dma", k, v))
        self.ops[eng].append(op)
        for x in r:
            _buf(x).readers.append(op)
        for x in w:
            b = _buf(x)
            b.writer = op
            b.readers = []
        return op

    def t(self, fn, r=(), w=()):
        return self._emit("tensor", fn, r, w)
    def v(self, fn, r=(), w=()):
        return self._emit("vector", fn, r, w)
    def s(self, fn, r=(), w=()):
        return self._emit("scalar", fn, r, w)
    def g(self, fn, r=(), w=()):
        return self._emit("gpsimd", fn, r, w)
    def load(self, dst, src_ap, r=(), w=(), slow=False, q="sync"):
        def fn(e, dst=dst, src_ap=src_ap, slow=slow):
            return e.dma_start(out=(dst.ap if hasattr(dst, "ap") else dst),
                               in_=src_ap, allow_slow_non_contiguous=slow)
        return self._emit(q, fn, tuple(r), (dst,) + tuple(w), kind="dma", amount=16)
    def store(self, dst_ap, src_ap, dstbuf, r=(), slow=False, q="gpsimd"):
        def fn(e, dst_ap=dst_ap, src_ap=src_ap, slow=slow):
            return e.dma_start(out=dst_ap, in_=src_ap,
                               allow_slow_non_contiguous=slow)
        return self._emit(q, fn, tuple(r), (dstbuf,), kind="dma", amount=16)

    def barrier(self):
        self.close_banks()
        lasts = [("dma", k, v) for k, v in self.dma_sem_tot.items()]
        for e in self.ENGS:
            for op in reversed(self.ops[e]):
                if op.kind == "eng" and op.fn is not None:
                    op.inc = True
                    lasts.append(op)
                    break
        for e in self.ENGS:
            bop = _Op(e, None, "eng")
            bop.waits = [d for d in lasts
                         if not (not isinstance(d, tuple) and d.eng == e)]
            self.ops[e].append(bop)

    def mm_flags(self, out_ap, acc, bufname):
        off_b = out_ap.offset * mybir.dt.size(out_ap.tensor.dtype)
        key = (out_ap.tensor.name, off_b // 2048)
        flags = {"start": False, "stop": False}
        ent = self.pbank.get(key)
        if acc:
            if ent is None:
                flags["start"] = True
                self.pbank[key] = {"flags": flags, "stale": False}
                self.pbufs.setdefault(bufname, set()).add(key)
            else:
                ent["flags"] = flags
        else:
            if ent is None or ent["stale"]:
                if ent is not None:
                    ent["flags"]["stop"] = True
                flags["start"] = True
                self.pbank[key] = {"flags": flags, "stale": False}
            else:
                ent["flags"] = flags
        self.pbufs.setdefault(bufname, set()).add(key)
        return flags

    def close_banks(self):
        for ent in self.pbank.values():
            ent["flags"]["stop"] = True
        self.pbank.clear()
        self.pbufs.clear()

    def build(self, stack):
        nc = self.nc
        self.close_banks()
        self.barrier()
        sems = {}
        for e in ("tensor", "vector", "scalar"):
            c = 0
            for op in self.ops[e]:
                if op.inc:
                    c += op.amount
                    op.val = c
            sems[e] = stack.enter_context(nc.semaphore(f"sem_{e}"))
        dsems = {}
        for k in self.dma_sem_tot:
            dsems[k] = stack.enter_context(nc.semaphore(f"dsem_{k[0]}_{k[1]}"))
        block = stack.enter_context(nc.Block())
        def mk(e):
            def emitf(engq):
                waited = {}
                for op in self.ops[e]:
                    for d in op.waits:
                        if isinstance(d, tuple):
                            key, val, sem = d[1], d[2], dsems[d[1]]
                        else:
                            key, val, sem = d.eng, d.val, sems[d.eng]
                        if waited.get(key, -1) >= val:
                            continue
                        waited[key] = val
                        engq.wait_ge(sem, val)
                    if op.fn is not None:
                        instr = op.fn(engq)
                        if op.inc:
                            if op.kind == "dma":
                                instr.then_inc(dsems[op.semkey], 16)
                            else:
                                instr.then_inc(sems[e], op.amount)
            return emitf
        block.tensor(mk("tensor"))
        block.vector(mk("vector"))
        block.scalar(mk("scalar"))
        block.gpsimd(mk("gpsimd"))
        block.sync(mk("sync"))


def host_consts():
    i = np.arange(128)
    tril_s = (i[:, None] > i[None, :]).astype(np.float32)
    triu_s = (i[:, None] < i[None, :]).astype(np.float32)
    triu_i = (i[:, None] <= i[None, :]).astype(np.float32)
    c = {}
    c["eye_f"] = np.eye(128, dtype=np.float32)
    c["eye_b"] = np.eye(128, dtype=np.float32).astype(NPBF)
    c["ones_col_f"] = np.ones((128, 1), dtype=np.float32)
    c["ones_col_b"] = np.ones((128, 1), dtype=np.float32).astype(NPBF)
    c["ones_row_f"] = np.ones((1, 128), dtype=np.float32)
    c["mask_sl4"] = np.tile(tril_s, (1, 4)).astype(NPBF)
    c["mask_su4"] = np.tile(triu_s, (1, 4)).astype(NPBF)
    c["mask_ui4"] = np.tile(triu_i, (1, 4)).astype(NPBF)
    sr16 = np.zeros((16, 16 * 128), dtype=np.float32)
    for r in range(16):
        sr16[r, r * 128:(r + 1) * 128] = 1.0
    c["sel_row16"] = sr16
    sr8 = np.zeros((8, 8 * 128), dtype=np.float32)
    for r in range(8):
        sr8[r, r * 128:(r + 1) * 128] = 1.0
    c["sel_row8"] = sr8
    sr4 = np.zeros((4, 4 * 128), dtype=np.float32)
    for r in range(4):
        sr4[r, r * 128:(r + 1) * 128] = 1.0
    c["sel_row4"] = sr4
    sc = np.zeros((128, 16 * 12), dtype=np.float32)
    for br in range(4):
        sc[:, (3 * br + 0) * 16 + 4 * br + 0] = 1.0
        sc[:, (3 * br + 1) * 16 + 4 * br + 1] = 1.0
        sc[:, (3 * br + 1) * 16 + 4 * br + 3] = 1.0
        sc[:, (3 * br + 2) * 16 + 4 * br + 2] = 1.0
    c["selc_st"] = sc
    s4 = np.zeros((128, 16), dtype=np.float32)
    for h in range(4):
        s4[:, 4 * h + h] = 1.0
    c["selc4"] = s4
    c["selc_st_b"] = sc.astype(NPBF)
    st_scale = np.full((16, 1), 1.0 / 256.0, np.float32)
    for br in range(4):
        st_scale[4 * br + 3, 0] = 1.0
    c["st_scale"] = st_scale
    m3b = np.zeros((16, 512), np.float32)
    for br in range(4):
        m3b[4 * br + 3, :] = 1.0
    c["m3big"] = m3b
    r416 = np.zeros((4, 16), np.float32)
    for h in range(4):
        r416[h, 4 * h:4 * h + 4] = 1.0
    c["rep4_16"] = r416
    fv = np.array([1e-5, 1e-6, -1.0, 0.125, 1.0 / 256.0], np.float32)
    c["fvals"] = np.tile(fv[None, :], (128, 1))
    sh = np.zeros((16, 16), dtype=np.float32)
    for br in range(4):
        sh[4 * br + 0, 4 * br + 1] = 1.0
    c["shift16"] = sh
    m3 = np.zeros((16, 1), dtype=np.float32)
    for br in range(4):
        m3[4 * br + 3, 0] = 1.0
    c["m3row"] = m3
    return c

CONST_SPECS = {k: v for k, v in host_consts().items()}


class Ctx:
    def __init__(self, nc, stack, P, mode, carried):
        self.nc = nc; self.stack = stack; self.stack0 = stack; self.P = P
        self.mode = mode
        self.carried = carried
        self.dram = {}
        self.outputs = set()
        self.sb_cache = {}

    def din(self, name, shape, dtype=F32):
        if name not in self.dram:
            t = self.nc.dram_tensor(name, list(shape), dtype, kind="ExternalInput")
            self.dram[name] = TB(t.ap(), name)
        return self.dram[name]

    def dout(self, name, shape, dtype=F32):
        if name not in self.dram:
            t = self.nc.dram_tensor(name, list(shape), dtype, kind="ExternalOutput")
            self.dram[name] = TB(t.ap(), name)
            self.outputs.add(name)
        return self.dram[name]

    def dscratch(self, name, shape, dtype=F32):
        if name not in self.dram:
            kind = "Internal" if self.mode == "fused" else None
            if kind:
                t = self.nc.dram_tensor(name, list(shape), dtype, kind=kind)
                self.dram[name] = TB(t.ap(), name)
        return self.dram.get(name)

    def sb(self, name, p, f, dtype=F32):
        t = self.stack.enter_context(self.nc.sbuf_tensor(name, [p, f], dtype))
        return TB(t, name)

    def ps(self, name, p, f, dtype=F32):
        t = self.stack.enter_context(self.nc.psum_tensor(name, [p, f], dtype))
        return TB(t, name)

    def const(self, name):
        if name in self.sb_cache:
            return self.sb_cache[name]
        arr = CONST_SPECS[name]
        dt = BF16 if arr.dtype == np.dtype(NPBF) else F32
        d = self.din("c_" + name, arr.shape, dt)
        t = self.stack0.enter_context(self.nc.sbuf_tensor("s_" + name,
                                                          list(arr.shape), dt))
        s = TB(t, "s_" + name)
        self.P.load(s, d.ap, r=[d])
        self.sb_cache[name] = s
        return s


def mm(P, out_ap, lhsT_ap, rhs_ap, r, w, start=True, stop=True):
    bufname = _buf(w[0]).name
    flags = P.mm_flags(out_ap, acc=(not start), bufname=bufname)
    return P.t(lambda e: e.matmul(out_ap, lhsT_ap, rhs_ap,
                                  start=flags["start"], stop=flags["stop"]),
               r=r, w=w)

def transpose(P, out_ap, in_ap, eye_tb, r, w, kdim=128):
    bufname = _buf(w[0]).name
    flags = P.mm_flags(out_ap, acc=False, bufname=bufname)
    return P.t(lambda e: e.matmul(out_ap, in_ap, eye_tb[0:kdim, 0:kdim],
                                  start=flags["start"], stop=flags["stop"],
                                  is_transpose=True),
               r=list(r) + [eye_tb], w=w)

def tt(P, out_ap, a_ap, b_ap, op, r, w, eng="v"):
    f = lambda e: e.tensor_tensor(out_ap, a_ap, b_ap, op)
    return getattr(P, eng)(f, r=r, w=w)

def ts(P, out_ap, in_ap, s1, op0, r, w, s2=None, op1=None, eng="v"):
    def f(e):
        if s2 is None and op1 is None:
            return e.tensor_scalar(out_ap, in_ap, s1, None, op0)
        return e.tensor_scalar(out_ap, in_ap, s1, s2, op0, op1)
    return getattr(P, eng)(f, r=r, w=w)

def stt(P, out_ap, in0_ap, scalar, in1_ap, op0, op1, r, w):
    return P.v(lambda e: e.scalar_tensor_tensor(out_ap, in0_ap, scalar, in1_ap, op0, op1),
               r=r, w=w)

def act(P, out_ap, in_ap, func, r, w, bias=None, scale=None):
    r = list(r)
    if isinstance(bias, float) or isinstance(scale, float):
        fc = getattr(P, "fconst", None)
        if fc is not None:
            r.append(fc)
    def f(e):
        kw = {}
        if bias is not None:
            kw["bias"] = bias
        if scale is not None:
            kw["scale"] = scale
        return e.activation(out_ap, in_ap, func, **kw)
    return P.s(f, r=r, w=w)

def vcopy(P, out_ap, in_ap, r, w, eng="v"):
    if eng == "s":
        f = lambda e: e.copy(out_ap, in_ap)
    else:
        f = lambda e: e.tensor_copy(out_ap, in_ap)
    return getattr(P, eng)(f, r=r, w=w)

def memset(P, ap, val, w, eng="v"):
    return getattr(P, eng)(lambda e: e.memset(ap, val), r=(), w=w)

def reduce(P, out_ap, in_ap, op, r, w):
    return P.v(lambda e: e.tensor_reduce(out_ap, in_ap, AX.X, op), r=r, w=w)

def recip(P, out_ap, in_ap, r, w):
    return P.v(lambda e: e.reciprocal(out_ap, in_ap), r=r, w=w)

def emit_ln_panel(P, ctx, xs, dst, tmp, pr_sum, pr_sq, rowbuf, pr_bc):
    oc = ctx.const("ones_col_f")
    for k in range(KT):
        mm(P, pr_sum[0:1, :], oc[:, 0:1], xs[k].ap, r=[xs[k], oc], w=[pr_sum],
           start=(k == 0), stop=(k == KT - 1))
    for k in range(KT):
        act(P, tmp.ap, xs[k].ap, AF.Square, r=[xs[k]], w=[tmp])
        mm(P, pr_sq[0:1, :], oc[:, 0:1], tmp.ap, r=[tmp, oc], w=[pr_sq],
           start=(k == 0), stop=(k == KT - 1))
    mean = rowbuf[0:1, 0:PAN]; m2 = rowbuf[0:1, PAN:2 * PAN]
    var = rowbuf[0:1, 2 * PAN:3 * PAN]; rstd = rowbuf[0:1, 3 * PAN:4 * PAN]
    ts(P, mean, pr_sum[0:1, :], 1.0 / D, OP.mult, r=[pr_sum], w=[rowbuf])
    act(P, m2, mean, AF.Square, r=[rowbuf], w=[rowbuf])
    ts(P, var, pr_sq[0:1, :], 1.0 / D, OP.mult, r=[pr_sq, rowbuf], w=[rowbuf])
    tt(P, var, var, m2, OP.subtract, r=[rowbuf], w=[rowbuf])
    act(P, rstd, var, AF.Sqrt, r=[rowbuf], w=[rowbuf], bias=1e-5)
    recip(P, rstd, rstd, r=[rowbuf], w=[rowbuf])
    orow = ctx.const("ones_row_f")
    mm(P, pr_bc[:, 0:PAN], orow[0:1, :], mean, r=[rowbuf, orow], w=[pr_bc])
    mm(P, pr_bc[:, PAN:2 * PAN], orow[0:1, :], rstd, r=[rowbuf, orow], w=[pr_bc])
    for k in range(KT):
        tt(P, tmp.ap, xs[k].ap, pr_bc[:, 0:PAN], OP.subtract, r=[xs[k], pr_bc], w=[tmp])
        tt(P, dst[k].ap, tmp.ap, pr_bc[:, PAN:2 * PAN], OP.mult, r=[tmp, pr_bc], w=[dst[k]])


def emit_s_in(P, ctx):
    x_in = ctx.din("x", [B, T, D], F32)
    xT = ctx.dscratch("xT", [B, D, T], F32)
    eye = ctx.const("eye_f")
    _scope = scope(ctx); _scope.__enter__()
    ld = [ctx.sb(f"si_ld{i}", 128, D) for i in range(2)]
    pt = ctx.ps("si_ps", 128, 1024)
    gat = [ctx.sb(f"si_g{i}", 128, 1024) for i in range(2)]
    for b in range(B):
        for tt_ in range(T // 128):
            l = ld[tt_ % 2]; g = gat[tt_ % 2]
            P.load(l, x_in.t[b, tt_ * 128:(tt_ + 1) * 128, :], r=[x_in])
            for k in range(KT):
                transpose(P, pt[:, k * 128:(k + 1) * 128], l[:, k * 128:(k + 1) * 128],
                          eye, r=[l], w=[pt])
                vcopy(P, g[:, k * 128:(k + 1) * 128], pt[:, k * 128:(k + 1) * 128],
                      r=[pt], w=[g])
            for k in range(KT):
                P.store(xT.t[b, k * 128:(k + 1) * 128, tt_ * 128:(tt_ + 1) * 128],
                        g.t[:, k * 128:(k + 1) * 128], xT, r=[g])
    _scope.__exit__(None, None, None)
    P.barrier()

def emit_s_out(P, ctx, src="x4T"):
    x4T = ctx.dscratch(src, [B, D, T], F32)
    out = ctx.dout("out", [B, T, D], F32)
    eye = ctx.const("eye_f")
    _scope = scope(ctx); _scope.__enter__()
    ld = [ctx.sb(f"so_ld{i}", 128, 1024) for i in range(2)]
    pt = ctx.ps("so_ps", 128, 1024)
    gat = [ctx.sb(f"so_g{i}", 128, 1024) for i in range(2)]
    for b in range(B):
        for tt_ in range(T // 128):
            l = ld[tt_ % 2]; g = gat[tt_ % 2]
            for k in range(KT):
                P.load(TB(l.t[:, k * 128:(k + 1) * 128], f"so{k}"),
                       x4T.t[b, k * 128:(k + 1) * 128, tt_ * 128:(tt_ + 1) * 128],
                       r=[x4T], w=[l])
            for k in range(KT):
                transpose(P, pt[:, k * 128:(k + 1) * 128], l[:, k * 128:(k + 1) * 128],
                          eye, r=[l], w=[pt])
                vcopy(P, g[:, k * 128:(k + 1) * 128], pt[:, k * 128:(k + 1) * 128],
                      r=[pt], w=[g])
            P.store(out.t[b, tt_ * 128:(tt_ + 1) * 128, :], g.ap, out, r=[g])
    _scope.__exit__(None, None, None)
    P.barrier()


from contextlib import contextmanager

@contextmanager
def scope(ctx):
    old = ctx.stack
    with ExitStack() as st:
        ctx.stack = st
        yield
        ctx.P.barrier()
    ctx.stack = old

def _ln_alloc(ctx, tag):
    a = {}
    a["tmp"] = ctx.sb(f"{tag}_lntmp", 128, PAN)
    a["row"] = ctx.sb(f"{tag}_lnrow", 1, 4 * PAN)
    a["psum"] = ctx.ps(f"{tag}_lnpsS", 1, PAN)
    a["psq"] = ctx.ps(f"{tag}_lnpsQ", 1, PAN)
    a["pbc"] = ctx.ps(f"{tag}_lnpbc", 128, 2 * PAN)
    return a

def emit_ln_to(P, ctx, a, xs, dst):
    emit_ln_panel(P, ctx, xs, dst, a["tmp"], a["psum"], a["psq"], a["row"], a["pbc"])


def emit_s0a(P, ctx):
    xT = ctx.dscratch("xT", [B, D, T], F32)
    hlnT = ctx.dscratch("hlnT", [B, D, T], F32)
    qnT = ctx.dscratch("qnT", [B, D, T], F32)
    knT = ctx.dscratch("knT", [B, D, T], F32)
    vT = ctx.dscratch("vT", [B, D, T], F32)
    eye = ctx.const("eye_f")
    oc = ctx.const("ones_col_f")
    with scope(ctx):
        xs = [ctx.sb(f"s0a_x{k}", 128, PAN) for k in range(KT)]
        hl = [ctx.sb(f"s0a_h{k}", 128, PAN) for k in range(KT)]
        la = _ln_alloc(ctx, "s0a")
        for b in range(B):
            for p in range(NPAN):
                for k in range(KT):
                    P.load(xs[k], xT.t[b, k * 128:(k + 1) * 128, p * PAN:(p + 1) * PAN],
                           r=[xT], q=("sync" if k % 2 == 0 else "gpsimd"))
                emit_ln_to(P, ctx, la, xs, hl)
                for k in range(KT):
                    P.store(hlnT.t[b, k * 128:(k + 1) * 128, p * PAN:(p + 1) * PAN],
                            hl[k].ap, hlnT, r=[hl[k]])
        P.barrier()
        wnames = [("L0_attn_Wq", "L0_attn_cwq", qnT, True),
                  ("L0_attn_Wk", "L0_attn_cwk", knT, True),
                  ("L0_attn_Wv", "L0_attn_cwv", vT, False)]
        W_d = {nm: ctx.din(nm, [D, D]) for nm, _, _, _ in wnames}
        cw_d = {nm: ctx.din(nm, [D, 4]) for _, nm, _, _ in wnames}
        ws = [ctx.sb(f"s0a_w{k}", 128, D) for k in range(KT)]
        wsb = [ctx.sb(f"s0a_wb{k}", 128, D, BF16) for k in range(KT)]
        hlb16 = [ctx.sb(f"s0a_hb{k}", 128, PAN, BF16) for k in range(KT)]
        cw_sb = ctx.sb("s0a_cw", 128, 32)
        dg = [ctx.sb(f"s0a_dg{i}", 128, 128) for i in range(32)]
        pre = [ctx.sb(f"s0a_pre{k}", 128, PAN + 3) for k in range(KT)]
        post = [ctx.sb(f"s0a_post{k}", 128, PAN) for k in range(KT)]
        tail = [ctx.sb(f"s0a_tl{k}", 128, 3) for k in range(KT)]
        sq = ctx.sb("s0a_sq", 128, PAN)
        nrow = ctx.sb("s0a_nrow", 1, PAN)
        pp = [ctx.ps(f"s0a_pp{i}", 128, PAN) for i in range(2)]
        pn = ctx.ps("s0a_pn", 1, PAN)
        pb = ctx.ps("s0a_pb", 128, PAN)
        for wname, cwname, dstT, do_l2 in wnames:
            Wd = W_d[wname]
            for k in range(KT):
                P.load(ws[k], Wd.t[k * 128:(k + 1) * 128, :], r=[Wd],
                       q=("sync" if k % 2 == 0 else "gpsimd"))
                vcopy(P, wsb[k].ap, ws[k].ap, r=[ws[k]], w=[wsb[k]],
                      eng=("v" if k % 2 == 0 else "s"))
            P.load(rr2(cw_sb, "p (a j) -> p a j", j=4),
                   cw_d[cwname].t[:, :].rearrange("(a p) j -> p a j", p=128),
                   r=[cw_d[cwname]], w=[cw_sb], slow=True)
            for k in range(KT):
                for tap in range(4):
                    ts(P, dg[k * 4 + tap].ap, eye.ap, cw_sb[:, k * 4 + tap:k * 4 + tap + 1],
                       OP.mult, r=[eye, cw_sb], w=[dg[k * 4 + tap]])
            for b in range(B):
                for k in range(KT):
                    memset(P, tail[k].ap, 0.0, w=[tail[k]])
                for p in range(NPAN):
                    for k in range(KT):
                        P.load(hl[k], hlnT.t[b, k * 128:(k + 1) * 128, p * PAN:(p + 1) * PAN],
                               r=[hlnT], q=("sync" if k % 2 == 0 else "gpsimd"))
                    for k in range(KT):
                        vcopy(P, hlb16[k].ap, hl[k].ap, r=[hl[k]], w=[hlb16[k]],
                              eng=("v" if k % 2 == 0 else "s"))
                    for mk_ in range(KT):
                        pj = pp[mk_ % 2]
                        for k in range(KT):
                            mm(P, pj.ap, wsb[k][:, mk_ * 128:(mk_ + 1) * 128], hlb16[k].ap,
                               r=[wsb[k], hlb16[k]], w=[pj], start=(k == 0), stop=(k == KT - 1))
                        vcopy(P, pre[mk_][:, 3:3 + PAN], pj.ap, r=[pj], w=[pre[mk_]])
                        vcopy(P, pre[mk_][:, 0:3], tail[mk_].ap, r=[tail[mk_]], w=[pre[mk_]])
                        vcopy(P, tail[mk_].ap, pre[mk_][:, PAN:PAN + 3],
                              r=[pre[mk_]], w=[tail[mk_]], eng="s")
                    for k in range(KT):
                        pj = pp[k % 2]
                        for tap in range(4):
                            mm(P, pj.ap, dg[k * 4 + tap].ap, pre[k][:, tap:tap + PAN],
                               r=[dg[k * 4 + tap], pre[k]], w=[pj],
                               start=(tap == 0), stop=(tap == 3))
                        act(P, post[k].ap, pj.ap, AF.Silu, r=[pj], w=[post[k]])
                    if do_l2:
                        for h in range(NH):
                            for j in range(2):
                                k = 2 * h + j
                                act(P, sq.ap, post[k].ap, AF.Square, r=[post[k]], w=[sq])
                                mm(P, pn.ap, oc[:, 0:1], sq.ap, r=[sq, oc], w=[pn],
                                   start=(j == 0), stop=(j == 1))
                            act(P, nrow.ap, pn.ap, AF.Sqrt, r=[pn], w=[nrow], bias=1e-6)
                            recip(P, nrow.ap, nrow.ap, r=[nrow], w=[nrow])
                            orow = ctx.const("ones_row_f")
                            mm(P, pb.ap, orow[0:1, :], nrow.ap, r=[nrow, orow], w=[pb])
                            for j in range(2):
                                k = 2 * h + j
                                tt(P, post[k].ap, post[k].ap, pb.ap, OP.mult,
                                   r=[post[k], pb], w=[post[k]])
                    for k in range(KT):
                        P.store(dstT.t[b, k * 128:(k + 1) * 128, p * PAN:(p + 1) * PAN],
                                post[k].ap, dstT, r=[post[k]])
            P.barrier()
    P.barrier()


def emit_s0a2(P, ctx):
    hlnT = ctx.dscratch("hlnT", [B, D, T], F32)
    idT = ctx.dscratch("idT", [B, D, T], F32)
    betaT = ctx.dscratch("betaT", [B, NH, T], F32)
    Wid = ctx.din("L0_attn_Wid", [D, D])
    Wb = ctx.din("L0_attn_Wb", [D, NH])
    with scope(ctx):
        hl = [ctx.sb(f"s0a2_h{k}", 128, PAN) for k in range(KT)]
        hlb16 = [ctx.sb(f"s0a2_hb{k}", 128, PAN, BF16) for k in range(KT)]
        ws = [ctx.sb(f"s0a2_w{k}", 128, D) for k in range(KT)]
        wsb = [ctx.sb(f"s0a2_wb{k}", 128, D, BF16) for k in range(KT)]
        wb_sb = ctx.sb("s0a2_wb", 128, 32)
        out = [ctx.sb(f"s0a2_o{k}", 128, PAN) for k in range(KT)]
        brow = ctx.sb("s0a2_brow", 1, PAN)
        pp = [ctx.ps(f"s0a2_pp{i}", 128, PAN) for i in range(2)]
        pnb = ctx.ps("s0a2_pnb", 1, PAN)
        for k in range(KT):
            P.load(ws[k], Wid.t[k * 128:(k + 1) * 128, :], r=[Wid],
                   q=("sync" if k % 2 == 0 else "gpsimd"))
            vcopy(P, wsb[k].ap, ws[k].ap, r=[ws[k]], w=[wsb[k]],
                  eng=("v" if k % 2 == 0 else "s"))
        P.load(rr2(wb_sb, "p (a j) -> p a j", j=4),
               Wb.t[:, :].rearrange("(a p) j -> p a j", p=128), r=[Wb], w=[wb_sb], slow=True)
        for b in range(B):
            for p in range(NPAN):
                for k in range(KT):
                    P.load(hl[k], hlnT.t[b, k * 128:(k + 1) * 128, p * PAN:(p + 1) * PAN],
                           r=[hlnT], q=("sync" if k % 2 == 0 else "gpsimd"))
                for k in range(KT):
                    vcopy(P, hlb16[k].ap, hl[k].ap, r=[hl[k]], w=[hlb16[k]],
                          eng=("v" if k % 2 == 0 else "s"))
                for mk_ in range(KT):
                    pj = pp[mk_ % 2]
                    for k in range(KT):
                        mm(P, pj.ap, wsb[k][:, mk_ * 128:(mk_ + 1) * 128], hlb16[k].ap,
                           r=[wsb[k], hlb16[k]], w=[pj], start=(k == 0), stop=(k == KT - 1))
                    vcopy(P, out[mk_].ap, pj.ap, r=[pj], w=[out[mk_]])
                    P.store(idT.t[b, mk_ * 128:(mk_ + 1) * 128, p * PAN:(p + 1) * PAN],
                            out[mk_].ap, idT, r=[out[mk_]])
                for h in range(NH):
                    for k in range(KT):
                        mm(P, pnb.ap, wb_sb[:, k * 4 + h:k * 4 + h + 1], hl[k].ap,
                           r=[wb_sb, hl[k]], w=[pnb], start=(k == 0), stop=(k == KT - 1))
                    act(P, brow.ap, pnb.ap, AF.Sigmoid, r=[pnb], w=[brow])
                    P.store(betaT.t[b, h:h + 1, p * PAN:(p + 1) * PAN],
                            brow.ap, betaT, r=[brow])
    P.barrier()

DBG_S0B = False


def emit_s0b(P, ctx):
    qnT = ctx.dscratch("qnT", [B, D, T], F32)
    knT = ctx.dscratch("knT", [B, D, T], F32)
    vT = ctx.dscratch("vT", [B, D, T], F32)
    betaT = ctx.dscratch("betaT", [B, NH, T], F32)
    doutT = ctx.dscratch("doutT", [B, D, T], F32)
    eye = ctx.const("eye_f")
    sr4 = ctx.const("sel_row4")
    m_sl = ctx.const("mask_sl4")
    m_su = ctx.const("mask_su4")
    m_ui = ctx.const("mask_ui4")
    with scope(ctx):
        qc = [ctx.sb(f"s0b_q{i}", 128, 1024) for i in range(2)]
        kc = [ctx.sb(f"s0b_k{i}", 128, 1024) for i in range(2)]
        vc = [ctx.sb(f"s0b_v{i}", 128, 1024) for i in range(2)]
        bc = [ctx.sb(f"s0b_b{i}", NH, CH) for i in range(2)]
        kb = ctx.sb("s0b_kb", 128, 1024)
        vn = ctx.sb("s0b_vn", 128, 1024)
        kn = ctx.sb("s0b_kn", 128, 1024)
        bn = ctx.sb("s0b_bn", 128, NH)
        Bp = [ctx.sb(f"s0b_B{i}", 128, 512) for i in range(2)]
        Cp = [ctx.sb(f"s0b_C{j}", 128, 512) for j in range(7)]
        za = ctx.sb("s0b_za", 128, 1024)
        zb = ctx.sb("s0b_zb", 128, 1024)
        Asb = ctx.sb("s0b_A", 128, 512)
        osb = ctx.sb("s0b_o", 128, 1024)
        Ssb = ctx.sb("s0b_S", 128, 2048)
        psS = ctx.ps("s0b_psS", 128, 2048)
        psa = ctx.ps("s0b_psa", 128, 512)
        psb = ctx.ps("s0b_psb", 128, 512)
        psw = ctx.ps("s0b_psw", 128, 1024)
        for b in range(B):
            for c in range(NCH):
                par = c % 2
                q_, k_, v_, b_ = qc[par], kc[par], vc[par], bc[par]
                lq = "sync" if par == 0 else "gpsimd"
                for k in range(KT):
                    P.load(TB(q_.t[:, k * 128:(k + 1) * 128], "l"),
                           qnT.t[b, k * 128:(k + 1) * 128, c * CH:(c + 1) * CH],
                           r=[qnT], w=[q_], q=lq)
                    P.load(TB(k_.t[:, k * 128:(k + 1) * 128], "l"),
                           knT.t[b, k * 128:(k + 1) * 128, c * CH:(c + 1) * CH],
                           r=[knT], w=[k_], q=lq)
                    P.load(TB(v_.t[:, k * 128:(k + 1) * 128], "l"),
                           vT.t[b, k * 128:(k + 1) * 128, c * CH:(c + 1) * CH],
                           r=[vT], w=[v_], q=lq)
                P.load(b_, betaT.t[b, :, c * CH:(c + 1) * CH], r=[betaT], q=lq)
                transpose(P, psb[:, 0:NH], b_.ap, eye, r=[b_], w=[psb], kdim=NH)
                vcopy(P, bn.ap, psb[:, 0:NH], r=[psb], w=[bn])
                for k in range(KT):
                    h = k // 2
                    mm(P, psw[:, k * 128:(k + 1) * 128], sr4[:, h * 128:(h + 1) * 128],
                       b_.ap, r=[b_, sr4], w=[psw])
                tt(P, kb.ap, k_.ap, psw.ap, OP.mult, r=[k_, psw], w=[kb])
                for h in range(NH):
                    for j in range(2):
                        k = 2 * h + j
                        mm(P, psa[:, h * 128:(h + 1) * 128], kb[:, k * 128:(k + 1) * 128],
                           k_.t[:, k * 128:(k + 1) * 128], r=[kb, k_], w=[psa],
                           start=(j == 0), stop=(j == 1))
                        mm(P, psb[:, h * 128:(h + 1) * 128], k_.t[:, k * 128:(k + 1) * 128],
                           kb[:, k * 128:(k + 1) * 128], r=[kb, k_], w=[psb],
                           start=(j == 0), stop=(j == 1))
                tt(P, Bp[0].ap, psa.ap, m_sl.ap, OP.mult, r=[psa, m_sl], w=[Bp[0]])
                tt(P, Cp[0].ap, psb.ap, m_su.ap, OP.mult, r=[psb, m_su], w=[Cp[0]])
                bcur = Bp[0]
                for j in range(1, 7):
                    bnext = Bp[j % 2]
                    for h in range(NH):
                        sl = slice(h * 128, (h + 1) * 128)
                        mm(P, psa[:, sl], Cp[j - 1][:, sl], bcur[:, sl],
                           r=[Cp[j - 1], bcur], w=[psa])
                        mm(P, psb[:, sl], bcur[:, sl], Cp[j - 1][:, sl],
                           r=[Cp[j - 1], bcur], w=[psb])
                    vcopy(P, bnext.ap, psa.ap, r=[psa], w=[bnext])
                    vcopy(P, Cp[j].ap, psb.ap, r=[psb], w=[Cp[j]], eng="s")
                    bcur = bnext
                for k in range(KT):
                    transpose(P, psw[:, k * 128:(k + 1) * 128],
                              v_.t[:, k * 128:(k + 1) * 128], eye, r=[v_], w=[psw])
                vcopy(P, vn.ap, psw.ap, r=[psw], w=[vn])
                for k in range(KT):
                    transpose(P, psw[:, k * 128:(k + 1) * 128],
                              k_.t[:, k * 128:(k + 1) * 128], eye, r=[k_], w=[psw])
                vcopy(P, kn.ap, psw.ap, r=[psw], w=[kn])
                if c == 0:
                    for h in range(NH):
                        sl = slice(h * 256, (h + 1) * 256)
                        ts(P, za.t[:, sl], vn.t[:, sl], bn[:, h:h + 1], OP.mult,
                           r=[vn, bn], w=[za])
                else:
                    for h in range(NH):
                        for j in range(2):
                            k = 2 * h + j
                            mm(P, psw[:, h * 256:(h + 1) * 256],
                               kb[:, k * 128:(k + 1) * 128],
                               Ssb.t[:, h * 512 + j * 256:h * 512 + (j + 1) * 256],
                               r=[kb, Ssb], w=[psw], start=(j == 0), stop=(j == 1))
                    if DBG_S0B:
                        dbgP = ctx.dscratch("dbgP", [B, NCH, 128, 1024], F32)
                        vcopy(P, osb.ap, psw.ap, r=[psw], w=[osb])
                        P.store(dbgP.t[b, c], osb.ap, dbgP, r=[osb], q="sync")
                    for h in range(NH):
                        sl = slice(h * 256, (h + 1) * 256)
                        def f(e, o=za.t[:, sl], i0=vn.t[:, sl], s=bn[:, h:h + 1],
                              i1=psw[:, sl]):
                            return e.scalar_tensor_tensor(o, i0, s, i1,
                                                          OP.mult, OP.subtract)
                        P.v(f, r=[vn, bn, psw], w=[za])
                if DBG_S0B:
                    dbgM = ctx.dscratch("dbgM", [B, NCH, 128, 1024], F32)
                    P.store(dbgM.t[b, c], za.ap, dbgM, r=[za])
                    dbgB = ctx.dscratch("dbgB", [B, NCH, 128, 512], F32)
                    P.store(dbgB.t[b, c], Cp[0].ap, dbgB, r=[Cp[0]])
                zc, zn = za, zb
                for j in range(6, -1, -1):
                    for h in range(NH):
                        mm(P, psw[:, h * 256:(h + 1) * 256], Cp[j][:, h * 128:(h + 1) * 128],
                           zc.t[:, h * 256:(h + 1) * 256], r=[Cp[j], zc], w=[psw])
                    op1 = OP.subtract if j == 0 else OP.add
                    sgn = -1.0 if j == 0 else 1.0
                    def f(e, sgn=sgn, zn=zn, zc=zc):
                        return e.scalar_tensor_tensor(zn.ap, psw.ap, sgn, zc.ap,
                                                      OP.mult, OP.add)
                    P.v(f, r=[psw, zc], w=[zn])
                    zc, zn = zn, zc
                uadj = zc
                for h in range(NH):
                    for j in range(2):
                        k = 2 * h + j
                        mm(P, psa[:, h * 128:(h + 1) * 128], k_.t[:, k * 128:(k + 1) * 128],
                           q_.t[:, k * 128:(k + 1) * 128], r=[k_, q_], w=[psa],
                           start=(j == 0), stop=(j == 1))
                tt(P, Asb.ap, psa.ap, m_ui.ap, OP.mult, r=[psa, m_ui], w=[Asb])
                for h in range(NH):
                    for dt in range(2):
                        k = 2 * h + dt
                        sl = slice(k * 128, (k + 1) * 128)
                        first = True
                        if c > 0:
                            for j in range(2):
                                mm(P, psw[:, sl],
                                   Ssb.t[:, h * 512 + j * 256 + dt * 128:h * 512 + j * 256 + (dt + 1) * 128],
                                   q_.t[:, (2 * h + j) * 128:(2 * h + j + 1) * 128],
                                   r=[Ssb, q_], w=[psw], start=(j == 0), stop=False)
                            first = False
                        mm(P, psw[:, sl], uadj.t[:, h * 256 + dt * 128:h * 256 + (dt + 1) * 128],
                           Asb[:, h * 128:(h + 1) * 128], r=[uadj, Asb], w=[psw],
                           start=first, stop=True)
                vcopy(P, osb.ap, psw.ap, r=[psw], w=[osb])
                for k in range(KT):
                    P.store(doutT.t[b, k * 128:(k + 1) * 128, c * CH:(c + 1) * CH],
                            osb.t[:, k * 128:(k + 1) * 128], doutT, r=[osb],
                            q=("gpsimd" if par == 0 else "sync"))
                if c < NCH - 1:
                    for h in range(NH):
                        for j in range(2):
                            mm(P, psS[:, h * 512 + j * 256:h * 512 + (j + 1) * 256],
                               kn.t[:, h * 256 + j * 128:h * 256 + (j + 1) * 128],
                               uadj.t[:, h * 256:(h + 1) * 256], r=[kn, uadj], w=[psS],
                               start=True)
                if c < NCH - 1:
                    if c == 0:
                        vcopy(P, Ssb.ap, psS.ap, r=[psS], w=[Ssb])
                    else:
                        tt(P, Ssb.ap, Ssb.ap, psS.ap, OP.add, r=[Ssb, psS], w=[Ssb])
                if DBG_S0B:
                    dbgU = ctx.dscratch("dbgU", [B, NCH, 128, 1024], F32)
                    dbgS = ctx.dscratch("dbgS", [B, NCH, 128, 2048], F32)
                    P.store(dbgU.t[b, c], uadj.ap, dbgU, r=[uadj])
                    P.store(dbgS.t[b, c], Ssb.ap, dbgS, r=[Ssb])
            P.barrier()
    P.barrier()


def emit_s0c1(P, ctx):
    vT = ctx.dscratch("vT", [B, D, T], F32)
    lsT = ctx.dscratch("lsT", [B, D, T], F32)
    llT = ctx.dscratch("llT", [B, D, T], F32)
    firs_d = ctx.din("L0_attn_firs", [4, 256, 3])
    firl_d = ctx.din("L0_attn_firl", [4, 256, 63])
    eyeb = ctx.const("eye_b")
    with scope(ctx):
        fls = ctx.sb("s0c1_fls", 128, 24)
        fll = ctx.sb("s0c1_fll", 128, 504)
        P.load(View(fls.t[:, :].rearrange("p (h a t) -> p h a t", h=4, a=2), fls.b),
               firs_d.t[:, :, :].rearrange("h (a p) t -> p h a t", p=128),
               r=[firs_d], slow=True)
        P.load(View(fll.t[:, :].rearrange("p (h a t) -> p h a t", h=4, a=2), fll.b),
               firl_d.t[:, :, :].rearrange("h (a p) t -> p h a t", p=128),
               r=[firl_d], slow=True)
        vbf = [ctx.sb(f"s0c1_vbf{k}", 128, 62 + T, BF16) for k in range(KT)]
        dgl = [ctx.sb(f"s0c1_dg{i}", 128, 128, BF16) for i in range(66)]
        vstg = [ctx.sb(f"s0c1_vstg{i}", 128, PAN) for i in range(2)]
        ous = [ctx.sb(f"s0c1_ou{i}", 128, PAN) for i in range(4)]
        psf = ctx.ps("s0c1_psf", 128, 1024)
        for b in range(B):
            for k in range(KT):
                memset(P, vbf[k][:, 0:62], 0.0, w=[vbf[k]])
                for p in range(NPAN):
                    vs = vstg[p % 2]
                    P.load(vs, vT.t[b, k * 128:(k + 1) * 128, p * PAN:(p + 1) * PAN],
                           r=[vT], w=[vs], q=("sync" if p % 2 == 0 else "gpsimd"))
                    vcopy(P, vbf[k][:, 62 + p * PAN:62 + (p + 1) * PAN], vs.ap,
                          r=[vs], w=[vbf[k]])
            for k in range(KT):
                for tp_ in range(63):
                    ts(P, dgl[tp_].ap, eyeb.ap, fll[:, (4 * k + 0) * 63 + tp_:(4 * k) * 63 + tp_ + 1]
                       if False else fll[:, k * 63 + tp_:k * 63 + tp_ + 1],
                       OP.mult, r=[eyeb, fll], w=[dgl[tp_]])
                for tp_ in range(3):
                    ts(P, dgl[63 + tp_].ap, eyeb.ap, fls[:, k * 3 + tp_:k * 3 + tp_ + 1],
                       OP.mult, r=[eyeb, fls], w=[dgl[63 + tp_]])
                for p in range(NPAN):
                    for tp_ in range(63):
                        mm(P, psf[:, 512:1024], dgl[tp_].ap,
                           vbf[k][:, p * PAN + tp_:p * PAN + tp_ + PAN],
                           r=[dgl[tp_], vbf[k]], w=[psf],
                           start=(tp_ == 0), stop=(tp_ == 62))
                    for tp_ in range(3):
                        mm(P, psf[:, 0:512], dgl[63 + tp_].ap,
                           vbf[k][:, 60 + p * PAN + tp_:60 + p * PAN + tp_ + PAN],
                           r=[dgl[63 + tp_], vbf[k]], w=[psf],
                           start=(tp_ == 0), stop=(tp_ == 2))
                    o1 = ous[2 * (p % 2)]; o2 = ous[2 * (p % 2) + 1]
                    vcopy(P, o1.ap, psf[:, 512:1024], r=[psf], w=[o1])
                    vcopy(P, o2.ap, psf[:, 0:512], r=[psf], w=[o2], eng="s")
                    P.store(llT.t[b, k * 128:(k + 1) * 128, p * PAN:(p + 1) * PAN],
                            o1.ap, llT, r=[o1])
                    P.store(lsT.t[b, k * 128:(k + 1) * 128, p * PAN:(p + 1) * PAN],
                            o2.ap, lsT, r=[o2], q="sync")
    P.barrier()


def emit_s0c2(P, ctx):
    xT = ctx.dscratch("xT", [B, D, T], F32)
    hlnT = ctx.dscratch("hlnT", [B, D, T], F32)
    vT = ctx.dscratch("vT", [B, D, T], F32)
    doutT = ctx.dscratch("doutT", [B, D, T], F32)
    idT = ctx.dscratch("idT", [B, D, T], F32)
    lsT = ctx.dscratch("lsT", [B, D, T], F32)
    llT = ctx.dscratch("llT", [B, D, T], F32)
    x1T = ctx.dscratch("x1T", [B, D, T], F32)
    gW1 = ctx.din("L0_attn_gW1", [1040, 1024])
    gW2 = ctx.din("L0_attn_gW2", [1024, 4])
    gb1 = ctx.din("L0_attn_gb1", [1024])
    gb2 = ctx.din("L0_attn_gb2", [4])
    ltm = ctx.din("L0_attn_log_temp", [4, 4])
    flp = ctx.din("L0_attn_floor_param", [4, 4])
    crl = ctx.din("L0_attn_conv_res_logit", [4])
    alp = ctx.din("L0_attn_alpha_id", [4])
    onw = ctx.din("L0_attn_o_norm_w", [256])
    Wo = ctx.din("L0_attn_Wo", [D, D])
    eye = ctx.const("eye_f")
    orow = ctx.const("ones_row_f")
    sr16 = ctx.const("sel_row16")
    sr4 = ctx.const("sel_row4")
    selc = ctx.const("selc_st")
    stsc = ctx.const("st_scale")
    m3b = ctx.const("m3big")
    r416 = ctx.const("rep4_16")
    sh16 = ctx.const("shift16")
    selc4 = ctx.const("selc4")
    with scope(ctx):
        gwa = [ctx.sb(f"s0c2_gwa{k}", 128, 1024, BF16) for k in range(KT)]
        gstg = ctx.sb("s0c2_gstg", 128, 1024)
        for k in range(KT):
            P.load(gstg, gW1.t[k * 128:(k + 1) * 128, :], r=[gW1], w=[gstg],
                   q=("sync" if k % 2 == 0 else "gpsimd"))
            vcopy(P, gwa[k].ap, gstg.ap, r=[gstg], w=[gwa[k]],
                  eng=("v" if k % 2 == 0 else "s"))
        gwbf = ctx.sb("s0c2_gwbf", 16, 1024)
        gwb = ctx.sb("s0c2_gwb", 16, 1024, BF16)
        P.load(gwbf, gW1.t[1024:1040, :], r=[gW1])
        vcopy(P, gwb.ap, gwbf.ap, r=[gwbf], w=[gwb])
        gw2 = ctx.sb("s0c2_gw2", 128, 32)
        P.load(rr2(gw2, "p (a j) -> p a j", j=4),
               gW2.t[:, :].rearrange("(a p) j -> p a j", p=128), r=[gW2], w=[gw2], slow=True)
        gw2b = ctx.sb("s0c2_gw2b", 128, 32, BF16)
        vcopy(P, gw2b.ap, gw2.ap, r=[gw2], w=[gw2b])
        gb1s = ctx.sb("s0c2_gb1", 128, KT)
        P.load(gb1s, gb1.t[:].rearrange("(a p) -> p a", p=128), r=[gb1], slow=True)
        gb2s = ctx.sb("s0c2_gb2", 4, 1)
        P.load(gb2s, gb2.t[:].rearrange("(j o) -> j o", o=1), r=[gb2], slow=True)
        wstg = ctx.sb("s0c2_wstg", 128, 1024)
        wob = [ctx.sb(f"s0c2_wob{k}", 128, 1024, BF16) for k in range(KT)]
        for k in range(KT):
            P.load(wstg, Wo.t[k * 128:(k + 1) * 128, :], r=[Wo], w=[wstg])
            vcopy(P, wob[k].ap, wstg.ap, r=[wstg], w=[wob[k]])
        onws = ctx.sb("s0c2_onw", 128, 2)
        P.load(onws, onw.t[:].rearrange("(a p) -> p a", p=128), r=[onw], slow=True)
        r16 = ctx.sb("s0c2_r16", 1, 16)
        r16b = ctx.sb("s0c2_r16b", 1, 16)
        r4 = ctx.sb("s0c2_r4", 1, 4)
        c4 = ctx.sb("s0c2_c4", 4, 1)
        invt = ctx.sb("s0c2_invt", 128, 64)
        fl64 = ctx.sb("s0c2_fl64", 128, 64)
        omf64 = ctx.sb("s0c2_omf64", 128, 64)
        crlb = ctx.sb("s0c2_crlb", 128, 4)
        alpb = ctx.sb("s0c2_alpb", 128, 4)
        ps_nat = ctx.ps("s0c2_psnat", 128, 64)
        ps_g = ctx.ps("s0c2_psg", 128, 512)
        ps_s1 = ctx.ps("s0c2_pss1", 128, 512)
        ps_t4 = ctx.ps("s0c2_pst4", 16, 512)
        ps_big = ctx.ps("s0c2_psbig", 128, 2048)
        P.load(r16, ltm.t[:, :].rearrange("(o h) j -> o (h j)", o=1), r=[ltm])
        act(P, r16.ap, r16.ap, AF.Exp, r=[r16], w=[r16], scale=-1.0)
        mm(P, ps_nat[:, 0:16], orow[0:1, :], r16.ap, r=[r16, orow], w=[ps_nat])
        for j in range(4):
            vcopy(P, invt[:, 16 * j:16 * (j + 1)], ps_nat[:, 0:16], r=[ps_nat], w=[invt])
        P.load(r16b, flp.t[:, :].rearrange("(o h) j -> o (h j)", o=1), r=[flp])
        act(P, r16b.ap, r16b.ap, AF.Sigmoid, r=[r16b], w=[r16b])
        ts(P, r16b.ap, r16b.ap, 0.05, OP.mult, r=[r16b], w=[r16b])
        mm(P, ps_nat[:, 0:16], orow[0:1, :], r16b.ap, r=[r16b, orow], w=[ps_nat])
        for j in range(4):
            vcopy(P, fl64[:, 16 * j:16 * (j + 1)], ps_nat[:, 0:16], r=[ps_nat], w=[fl64])
        reduce(P, r4.ap, r16b.t[0:1, :].rearrange("o (h j) -> o h j", j=4), OP.add,
               r=[r16b], w=[r4])
        ts(P, r4.ap, r4.ap, -1.0, OP.mult, r=[r4], w=[r4], s2=1.0, op1=OP.add)
        transpose(P, ps_t4[0:4, 0:1], r4.ap, eye, r=[r4], w=[ps_t4], kdim=1)
        vcopy(P, c4.ap, ps_t4[0:4, 0:1], r=[ps_t4], w=[c4])
        mm(P, ps_nat[0:1, 0:16], c4.ap, r416.ap, r=[c4, r416], w=[ps_nat])
        vcopy(P, r16.ap, ps_nat[0:1, 0:16], r=[ps_nat], w=[r16])
        mm(P, ps_nat[:, 16:32], orow[0:1, :], r16.ap, r=[r16, orow], w=[ps_nat])
        for j in range(4):
            vcopy(P, omf64[:, 16 * j:16 * (j + 1)], ps_nat[:, 16:32], r=[ps_nat], w=[omf64])
        P.load(r4, crl.t[:].rearrange("(o h) -> o h", o=1), r=[crl], w=[r4])
        act(P, r4.ap, r4.ap, AF.Sigmoid, r=[r4], w=[r4])
        mm(P, ps_nat[:, 0:4], orow[0:1, :], r4.ap, r=[r4, orow], w=[ps_nat])
        vcopy(P, crlb.ap, ps_nat[:, 0:4], r=[ps_nat], w=[crlb])
        P.load(r4, alp.t[:].rearrange("(o h) -> o h", o=1), r=[alp], w=[r4])
        mm(P, ps_nat[:, 0:4], orow[0:1, :], r4.ap, r=[r4, orow], w=[ps_nat])
        vcopy(P, alpb.ap, ps_nat[:, 0:4], r=[ps_nat], w=[alpb])
        hl = [ctx.sb(f"s0c2_hl{k}", 128, PAN) for k in range(KT)]
        hlb16 = [ctx.sb(f"s0c2_hlb{k}", 128, PAN, BF16) for k in range(KT)]
        xps = [ctx.sb(f"s0c2_xp{k}", 128, PAN) for k in range(KT)]
        onT = [ctx.sb(f"s0c2_on{k}", 128, PAN, BF16) for k in range(KT)]
        brt = {nm: [ctx.sb(f"s0c2_{nm}{j}", 128, PAN) for j in range(2)]
               for nm in ("ls", "ll", "do", "vp", "id")}
        sqt = ctx.sb("s0c2_sqt", 128, PAN)
        abt = ctx.sb("s0c2_abt", 128, PAN)
        Gsb = ctx.sb("s0c2_G", 16, PAN)
        m2sb = ctx.sb("s0c2_m2", 16, PAN)
        sts = [ctx.sb(f"s0c2_st{h}", 16, PAN, BF16) for h in range(NH)]
        shsb = ctx.sb("s0c2_sh", 128, PAN)
        tg = ctx.sb("s0c2_tg", 128, PAN)
        ghm = ctx.sb("s0c2_ghm", 128, PAN, BF16)
        lgs = [ctx.sb(f"s0c2_lg{h}", 4, PAN) for h in range(NH)]
        lgt = ctx.sb("s0c2_lgt", 4, PAN)
        nats = ctx.sb("s0c2_nats", 128, 64)
        ex = ctx.sb("s0c2_ex", 128, 64)
        sums = ctx.sb("s0c2_sums", 128, 16)
        prT = ctx.sb("s0c2_prT", 16, PAN)
        oac = [ctx.sb(f"s0c2_oac{d}", 128, PAN) for d in range(2)]
        tmx = ctx.sb("s0c2_tmx", 128, PAN)
        rr = ctx.sb("s0c2_rr", 4, PAN)
        x1s = ctx.sb("s0c2_x1s", 128, PAN)

        def load_branches(b, h, sl_t):
            for j in range(2):
                k = 2 * h + j
                for nm, src in (("ls", lsT), ("ll", llT), ("do", doutT), ("vp", vT),
                                ("id", idT)):
                    P.load(brt[nm][j], src.t[b, k * 128:(k + 1) * 128, sl_t], r=[src],
                           q=("sync" if (j + ord(nm[0])) % 2 == 0 else "gpsimd"))

        for b in range(B):
            for p in range(NPAN):
                sl_t = slice(p * PAN, (p + 1) * PAN)
                for k in range(KT):
                    P.load(hl[k], hlnT.t[b, k * 128:(k + 1) * 128, sl_t], r=[hlnT],
                           q=("sync" if k % 2 == 0 else "gpsimd"))
                    vcopy(P, hlb16[k].ap, hl[k].ap, r=[hl[k]], w=[hlb16[k]],
                          eng=("v" if k % 2 == 0 else "s"))
                for h in range(NH):
                    load_branches(b, h, sl_t)
                    for br, nm in enumerate(("ls", "ll", "do", "vp")):
                        for j in range(2):
                            tb_ = brt[nm][j]
                            first = (br == 0 and j == 0)
                            mm(P, ps_t4.ap, selc[:, (3 * br + 0) * 16:(3 * br + 1) * 16],
                               tb_.ap, r=[tb_, selc], w=[ps_t4], start=first, stop=False)
                            act(P, sqt.ap, tb_.ap, AF.Square, r=[tb_], w=[sqt])
                            mm(P, ps_t4.ap, selc[:, (3 * br + 1) * 16:(3 * br + 2) * 16],
                               sqt.ap, r=[sqt, selc], w=[ps_t4], start=False, stop=False)
                            act(P, abt.ap, tb_.ap, AF.Abs, r=[tb_], w=[abt])
                            mm(P, ps_t4.ap, selc[:, (3 * br + 2) * 16:(3 * br + 3) * 16],
                               abt.ap, r=[abt, selc], w=[ps_t4], start=False,
                               stop=(br == 3 and j == 1))
                    ts(P, Gsb.ap, ps_t4.ap, stsc.t[:, 0:1], OP.mult, r=[ps_t4, stsc], w=[Gsb])
                    mm(P, ps_t4.ap, sh16.ap, Gsb.ap, r=[sh16, Gsb], w=[ps_t4])
                    act(P, m2sb.ap, ps_t4.ap, AF.Square, r=[ps_t4], w=[m2sb])
                    tt(P, Gsb.ap, Gsb.ap, m2sb.ap, OP.subtract, r=[Gsb, m2sb], w=[Gsb])
                    ts(P, m2sb.ap, Gsb.ap, 0.0, OP.max, r=[Gsb], w=[m2sb])
                    act(P, m2sb.ap, m2sb.ap, AF.Sqrt, r=[m2sb], w=[m2sb])
                    tt(P, m2sb.ap, m2sb.ap, Gsb.ap, OP.subtract, r=[m2sb, Gsb], w=[m2sb])
                    tt(P, m2sb.ap, m2sb.ap, m3b[:, 0:PAN], OP.mult, r=[m2sb, m3b], w=[m2sb])
                    tt(P, sts[h].ap, Gsb.ap, m2sb.ap, OP.add, r=[Gsb, m2sb], w=[sts[h]])
                for h in range(NH):
                    memset(P, lgs[h].ap, 0.0, w=[lgs[h]])
                for mt in range(KT):
                    for k in range(KT):
                        mm(P, ps_g.ap, gwa[k][:, mt * 128:(mt + 1) * 128], hlb16[k].ap,
                           r=[gwa[k], hlb16[k]], w=[ps_g], start=(k == 0), stop=(k == KT - 1))
                    vcopy(P, shsb.ap, ps_g.ap, r=[ps_g], w=[shsb])
                    for h in range(NH):
                        mm(P, ps_s1.ap, gwb[:, mt * 128:(mt + 1) * 128], sts[h].ap,
                           r=[gwb, sts[h]], w=[ps_s1])
                        tt(P, tg.ap, ps_s1.ap, shsb.ap, OP.add, r=[ps_s1, shsb], w=[tg])
                        act(P, ghm.ap, tg.ap, AF.Gelu, r=[tg, gb1s], w=[ghm],
                            bias=gb1s[:, mt:mt + 1])
                        mm(P, ps_s1[0:4, :], gw2b[:, mt * 4:(mt + 1) * 4], ghm.ap,
                           r=[gw2b, ghm], w=[ps_s1])
                        vcopy(P, lgt.ap, ps_s1[0:4, :], r=[ps_s1], w=[lgt])
                        tt(P, lgs[h].ap, lgs[h].ap, lgt.ap, OP.add, r=[lgs[h], lgt],
                           w=[lgs[h]])
                for h in range(NH):
                    ts(P, lgs[h].ap, lgs[h].ap, gb2s.t[:, 0:1], OP.add, r=[lgs[h], gb2s],
                       w=[lgs[h]])
                for s in range(4):
                    for h in range(NH):
                        transpose(P, ps_nat[:, s * 16 + 4 * h:s * 16 + 4 * h + 4],
                                  lgs[h][:, s * 128:(s + 1) * 128], eye,
                                  r=[lgs[h]], w=[ps_nat], kdim=4)
                vcopy(P, nats.ap, ps_nat.ap, r=[ps_nat], w=[nats])
                tt(P, nats.ap, nats.ap, invt.ap, OP.mult, r=[nats, invt], w=[nats])
                act(P, ex.ap, nats.ap, AF.Exp, r=[nats], w=[ex])
                reduce(P, sums.ap, ex.t[:, :].rearrange("p (s j) -> p s j", j=4), OP.add,
                       r=[ex], w=[sums])
                recip(P, sums.ap, sums.ap, r=[sums], w=[sums])
                for s in range(4):
                    for h in range(NH):
                        c_ = s * 4 + h
                        ts(P, ex[:, 16 * s + 4 * h:16 * s + 4 * h + 4],
                           ex[:, 16 * s + 4 * h:16 * s + 4 * h + 4],
                           sums[:, c_:c_ + 1], OP.mult, r=[ex, sums], w=[ex])
                tt(P, ex.ap, ex.ap, omf64.ap, OP.mult, r=[ex, omf64], w=[ex])
                tt(P, ex.ap, ex.ap, fl64.ap, OP.add, r=[ex, fl64], w=[ex])
                for s in range(4):
                    transpose(P, ps_t4[:, s * 128:(s + 1) * 128], ex[:, s * 16:(s + 1) * 16],
                              eye, r=[ex], w=[ps_t4], kdim=128)
                vcopy(P, prT.ap, ps_t4.ap, r=[ps_t4], w=[prT])
                for k in range(KT):
                    P.load(xps[k], xT.t[b, k * 128:(k + 1) * 128, sl_t], r=[xT],
                           q=("sync" if k % 2 == 0 else "gpsimd"))
                for h in range(NH):
                    load_branches(b, h, sl_t)
                    for j in range(4):
                        mm(P, ps_big[:, j * 512:(j + 1) * 512],
                           sr16[:, (4 * h + j) * 128:(4 * h + j + 1) * 128], prT.ap,
                           r=[sr16, prT], w=[ps_big])
                    for dt in range(2):
                        def f0(e, dt=dt, h=h):
                            return e.scalar_tensor_tensor(
                                oac[dt].ap, ps_big[:, 0:512], crlb[:, h:h + 1],
                                brt["ls"][dt].ap, OP.add, OP.mult)
                        P.v(f0, r=[ps_big, crlb, brt["ls"][dt]], w=[oac[dt]])
                        for (jj, nm) in ((1, "ll"), (2, "do"), (3, "vp")):
                            tt(P, tmx.ap, ps_big[:, jj * 512:(jj + 1) * 512],
                               brt[nm][dt].ap, OP.mult, r=[ps_big, brt[nm][dt]], w=[tmx])
                            tt(P, oac[dt].ap, oac[dt].ap, tmx.ap, OP.add,
                               r=[oac[dt], tmx], w=[oac[dt]])
                        ts(P, tmx.ap, brt["id"][dt].ap, alpb[:, h:h + 1], OP.mult,
                           r=[brt["id"][dt], alpb], w=[tmx])
                        tt(P, oac[dt].ap, oac[dt].ap, tmx.ap, OP.add, r=[oac[dt], tmx],
                           w=[oac[dt]])
                        act(P, sqt.ap, oac[dt].ap, AF.Square, r=[oac[dt]], w=[sqt])
                        mm(P, ps_t4[0:4, :], selc4[:, 4 * h:4 * h + 4], sqt.ap,
                           r=[sqt, selc4], w=[ps_t4], start=(dt == 0), stop=(dt == 1))
                    act(P, rr.ap, ps_t4[0:4, :], AF.Sqrt, r=[ps_t4], w=[rr],
                        scale=1.0 / 256.0, bias=1e-5)
                    recip(P, rr.ap, rr.ap, r=[rr], w=[rr])
                    mm(P, ps_g.ap, sr4[:, h * 128:(h + 1) * 128], rr.ap, r=[rr, sr4],
                       w=[ps_g])
                    for dt in range(2):
                        k = 2 * h + dt
                        def f1(e, k=k, dt=dt):
                            return e.scalar_tensor_tensor(
                                onT[k].ap, oac[dt].ap, onws[:, dt:dt + 1], ps_g.ap,
                                OP.mult, OP.mult)
                        P.v(f1, r=[oac[dt], onws, ps_g], w=[onT[k]])
                for mt in range(KT):
                    pj = ps_s1 if mt % 2 == 0 else ps_g
                    for k in range(KT):
                        mm(P, pj.ap, wob[k][:, mt * 128:(mt + 1) * 128], onT[k].ap,
                           r=[wob[k], onT[k]], w=[pj], start=(k == 0), stop=(k == KT - 1))
                    tt(P, x1s.ap, pj.ap, xps[mt].ap, OP.add, r=[pj, xps[mt]], w=[x1s])
                    P.store(x1T.t[b, mt * 128:(mt + 1) * 128, sl_t], x1s.ap, x1T, r=[x1s])
            P.barrier()
    P.barrier()


def emit_s1g(P, ctx):
    x1T = ctx.dscratch("x1T", [B, D, T], F32)
    hlnT1b = ctx.dscratch("hlnT1b", [B, D, T], BF16)
    smaskT = ctx.dscratch("smaskT", [B, E0, T], F32)
    wred = ctx.din("L0_ffn_w_red", [D, 16])
    wg_d = ctx.din("L0_ffn_wg", [E0, 16])
    eye = ctx.const("eye_f")
    eyeb = ctx.const("eye_b")
    with scope(ctx):
        xs = [ctx.sb(f"s1g_x{k}", 128, PAN) for k in range(KT)]
        hlb = [ctx.sb(f"s1g_h{k}", 128, PAN, BF16) for k in range(KT)]
        la = _ln_alloc(ctx, "s1g")
        wrs = ctx.sb("s1g_wrs", 128, 128)
        wrb = ctx.sb("s1g_wrb", 128, 128, BF16)
        P.load(rr2(wrs, "p (a j) -> p a j", j=16),
               wred.t[:, :].rearrange("(a p) j -> p a j", p=128), r=[wred], w=[wrs], slow=True)
        vcopy(P, wrb.ap, wrs.ap, r=[wrs], w=[wrb])
        wgs = ctx.sb("s1g_wgs", E0, 16)
        P.load(wgs, wg_d.ap, r=[wg_d])
        wsq = ctx.sb("s1g_wsq", E0, 16)
        wnr = ctx.sb("s1g_wnr", E0, 1)
        wgn = ctx.sb("s1g_wgn", E0, 16, BF16)
        wgnT = ctx.sb("s1g_wgnT", 16, E0, BF16)
        act(P, wsq.ap, wgs.ap, AF.Square, r=[wgs], w=[wsq])
        reduce(P, wnr.ap, wsq.ap, OP.add, r=[wsq], w=[wnr])
        act(P, wnr.ap, wnr.ap, AF.Sqrt, r=[wnr], w=[wnr])
        ts(P, wnr.ap, wnr.ap, 1e-4, OP.max, r=[wnr], w=[wnr])
        recip(P, wnr.ap, wnr.ap, r=[wnr], w=[wnr])
        ts(P, wgn.ap, wgs.ap, wnr.t[:, 0:1], OP.mult, r=[wgs, wnr], w=[wgn])
        ps_w = ctx.ps("s1g_psw", 16, E0, BF16)
        transpose(P, ps_w.ap, wgn.ap, eyeb, r=[wgn], w=[ps_w], kdim=E0)
        vcopy(P, wgnT.ap, ps_w.ap, r=[ps_w], w=[wgnT])
        red = ctx.sb("s1g_red", 16, PAN, BF16)
        lg8 = ctx.sb("s1g_lg8", E0, PAN)
        nat = ctx.sb("s1g_nat", 128, 32)
        exl = ctx.sb("s1g_exl", 128, 32)
        exs = ctx.sb("s1g_exs", 128, 32)
        msk = ctx.sb("s1g_msk", 128, 32)
        lmx = ctx.sb("s1g_lmx", 128, 4)
        ssum = ctx.sb("s1g_ssum", 128, 4)
        smT = ctx.sb("s1g_smT", E0, PAN)
        ps16 = ctx.ps("s1g_ps16", 16, PAN)
        ps8 = ctx.ps("s1g_ps8", E0, PAN)
        psn = ctx.ps("s1g_psn", 128, 32)
        for b in range(B):
            for p in range(NPAN):
                sl_t = slice(p * PAN, (p + 1) * PAN)
                for k in range(KT):
                    P.load(xs[k], x1T.t[b, k * 128:(k + 1) * 128, sl_t], r=[x1T],
                           q=("sync" if k % 2 == 0 else "gpsimd"))
                emit_ln_to(P, ctx, la, xs, hlb)
                for k in range(KT):
                    P.store(hlnT1b.t[b, k * 128:(k + 1) * 128, sl_t], hlb[k].ap,
                            hlnT1b, r=[hlb[k]])
                for k in range(KT):
                    mm(P, ps16.ap, wrb[:, k * 16:(k + 1) * 16], hlb[k].ap,
                       r=[wrb, hlb[k]], w=[ps16], start=(k == 0), stop=(k == KT - 1))
                vcopy(P, red.ap, ps16.ap, r=[ps16], w=[red])
                mm(P, ps8.ap, wgnT.ap, red.ap, r=[wgnT, red], w=[ps8])
                vcopy(P, lg8.ap, ps8.ap, r=[ps8], w=[lg8])
                for s in range(4):
                    transpose(P, psn[:, s * 8:(s + 1) * 8], lg8[:, s * 128:(s + 1) * 128],
                              eye, r=[lg8], w=[psn], kdim=8)
                vcopy(P, nat.ap, psn.ap, r=[psn], w=[nat])
                for s in range(4):
                    sl8 = slice(s * 8, (s + 1) * 8)
                    def fr(e, sl8=sl8, s=s):
                        return e.tensor_reduce(lmx[:, s:s + 1], nat[:, sl8], AX.X, OP.max)
                    P.v(fr, r=[nat], w=[lmx])
                    ts(P, exl[:, sl8], nat[:, sl8], lmx[:, s:s + 1], OP.subtract,
                       r=[nat, lmx], w=[exl])
                act(P, exs.ap, exl.ap, AF.Exp, r=[exl], w=[exs])
                for s in range(4):
                    sl8 = slice(s * 8, (s + 1) * 8)
                    def fr2(e, sl8=sl8, s=s):
                        return e.tensor_reduce(ssum[:, s:s + 1], exs[:, sl8], AX.X, OP.add)
                    P.v(fr2, r=[exs], w=[ssum])
                recip(P, ssum.ap, ssum.ap, r=[ssum], w=[ssum])
                for s in range(4):
                    sl8 = slice(s * 8, (s + 1) * 8)
                    ts(P, msk[:, sl8], nat[:, sl8], lmx[:, s:s + 1], OP.is_equal,
                       r=[nat, lmx], w=[msk])
                    ts(P, msk[:, sl8], msk[:, sl8], ssum[:, s:s + 1], OP.mult,
                       r=[msk, ssum], w=[msk])
                for s in range(4):
                    transpose(P, ps8[:, s * 128:(s + 1) * 128], msk[:, s * 8:(s + 1) * 8],
                              eye, r=[msk], w=[ps8], kdim=128)
                vcopy(P, smT.ap, ps8.ap, r=[ps8], w=[smT])
                P.store(smaskT.t[b, :, sl_t], smT.ap, smaskT, r=[smT])
    P.barrier()


def emit_s1w(P, ctx):
    w1 = ctx.din("L0_ffn_w1", [E0, D, F0])
    w2 = ctx.din("L0_ffn_w2", [E0, F0, D])
    w1b = ctx.dscratch("w1b", [E0, KT, 128, F0], BF16)
    w2b = ctx.dscratch("w2b", [E0, 32, 128, D], BF16)
    with scope(ctx):
        stg = [ctx.sb(f"s1w_stg{i}", 128, F0) for i in range(2)]
        ob = [ctx.sb(f"s1w_ob{i}", 128, F0, BF16) for i in range(2)]
        i = 0
        for e in range(E0):
            for k in range(KT):
                s_ = stg[i % 2]; o_ = ob[i % 2]
                P.load(s_, w1.t[e, k * 128:(k + 1) * 128, :], r=[w1], w=[s_],
                       q=("sync" if i % 2 == 0 else "gpsimd"))
                vcopy(P, o_.ap, s_.ap, r=[s_], w=[o_], eng=("v" if i % 2 == 0 else "s"))
                P.store(w1b.t[e, k, :, :], o_.ap, w1b, r=[o_],
                        q=("gpsimd" if i % 2 == 0 else "sync"))
                i += 1
        for e in range(E0):
            for f in range(32):
                s_ = stg[i % 2]; o_ = ob[i % 2]
                P.load(TB(s_.t[:, 0:D], "stg"), w2.t[e, f * 128:(f + 1) * 128, :],
                       r=[w2], w=[s_], q=("sync" if i % 2 == 0 else "gpsimd"))
                vcopy(P, o_.t[:, 0:D], s_.t[:, 0:D], r=[s_], w=[o_],
                      eng=("v" if i % 2 == 0 else "s"))
                P.store(w2b.t[e, f, :, :], o_.t[:, 0:D], w2b, r=[o_],
                        q=("gpsimd" if i % 2 == 0 else "sync"))
                i += 1
    P.barrier()


def emit_s1e(P, ctx):
    x1T = ctx.dscratch("x1T", [B, D, T], F32)
    hlnT1b = ctx.dscratch("hlnT1b", [B, D, T], BF16)
    smaskT = ctx.dscratch("smaskT", [B, E0, T], F32)
    w1b = ctx.dscratch("w1b", [E0, KT, 128, F0], BF16)
    w2b = ctx.dscratch("w2b", [E0, 32, 128, D], BF16)
    x2T = ctx.dscratch("x2T", [B, D, T], F32)
    b1d = ctx.din("L0_ffn_b1", [E0, F0])
    b2d = ctx.din("L0_ffn_b2", [E0, D])
    sr8 = ctx.const("sel_row8")
    with scope(ctx):
        x2a = [ctx.sb(f"s1e_x2a{k}", 128, T) for k in range(KT)]
        smk = ctx.sb("s1e_smk", E0, PAN)
        b1a = ctx.sb("s1e_b1a", 128, 256)
        b2a = ctx.sb("s1e_b2a", 128, 64)
        P.load(rr2(b1a, "p (e a) -> p e a", e=8),
               b1d.t[:, :].rearrange("e (a p) -> p e a", p=128), r=[b1d], w=[b1a], slow=True)
        P.load(rr2(b2a, "p (e a) -> p e a", e=8),
               b2d.t[:, :].rearrange("e (a p) -> p e a", p=128), r=[b2d], w=[b2a], slow=True)
        w1h = [ctx.sb(f"s1e_w1h{k}", 128, 2048, BF16) for k in range(KT)]
        w2h = [ctx.sb(f"s1e_w2h{j}", 128, D, BF16) for j in range(16)]
        hlb = [ctx.sb(f"s1e_hl{k}", 128, T, BF16) for k in range(KT)]
        hbf = [ctx.sb(f"s1e_hb{j}", 128, PAN, BF16) for j in range(16)]
        smbc = ctx.sb("s1e_smbc", 128, PAN)
        tmx = ctx.sb("s1e_tmx", 128, PAN)
        ps_h = [ctx.ps(f"s1e_psh{i}", 128, PAN) for i in range(2)]
        ps_y = [ctx.ps(f"s1e_psy{i}", 128, PAN) for i in range(2)]
        ps_sm = ctx.ps("s1e_pssm", 128, PAN)
        for b in range(B):
            for k in range(KT):
                P.load(x2a[k], x1T.t[b, k * 128:(k + 1) * 128, :], r=[x1T], w=[x2a[k]],
                       q=("sync" if k % 2 == 0 else "gpsimd"))

            for k in range(KT):
                P.load(hlb[k], hlnT1b.t[b, k * 128:(k + 1) * 128, :],
                       r=[hlnT1b], w=[hlb[k]],
                       q=("sync" if k % 2 == 0 else "gpsimd"))
            for e in range(E0):
                for fh in range(2):
                    for k in range(KT):
                        P.load(w1h[k], w1b.t[e, k, :, fh * 2048:(fh + 1) * 2048],
                               r=[w1b], w=[w1h[k]], q=("sync" if k % 2 == 0 else "gpsimd"))
                    for j in range(16):
                        P.load(w2h[j], w2b.t[e, fh * 16 + j, :, :], r=[w2b], w=[w2h[j]],
                               q=("sync" if j % 2 == 0 else "gpsimd"))
                    for p in range(NPAN):
                        sl_t = slice(p * PAN, (p + 1) * PAN)
                        P.load(smk, smaskT.t[b, :, sl_t], r=[smaskT], w=[smk])
                        mm(P, ps_sm.ap, sr8[:, e * 128:(e + 1) * 128], smk.ap,
                           r=[sr8, smk], w=[ps_sm])
                        vcopy(P, smbc.ap, ps_sm.ap, r=[ps_sm], w=[smbc])
                        for ft in range(16):
                            pj = ps_h[ft % 2]
                            for k in range(KT):
                                mm(P, pj.ap, w1h[k][:, ft * 128:(ft + 1) * 128],
                                   hlb[k][:, sl_t],
                                   r=[w1h[k], hlb[k]], w=[pj], start=(k == 0),
                                   stop=(k == KT - 1))
                            act(P, hbf[ft].ap, pj.ap, AF.Gelu, r=[pj, b1a], w=[hbf[ft]],
                                bias=b1a[:, e * 32 + fh * 16 + ft:e * 32 + fh * 16 + ft + 1])
                        for mt in range(KT):
                            pj = ps_y[mt % 2]
                            for j in range(16):
                                mm(P, pj.ap, w2h[j][:, mt * 128:(mt + 1) * 128], hbf[j].ap,
                                   r=[w2h[j], hbf[j]], w=[pj], start=(j == 0),
                                   stop=(j == 15))
                            if fh == 1:
                                def fb(e_, pj=pj, mt=mt, e=e):
                                    return e_.scalar_tensor_tensor(
                                        tmx.ap, pj.ap, b2a[:, e * 8 + mt:e * 8 + mt + 1],
                                        smbc.ap, OP.add, OP.mult)
                                P.v(fb, r=[pj, b2a, smbc], w=[tmx])
                            else:
                                tt(P, tmx.ap, pj.ap, smbc.ap, OP.mult, r=[pj, smbc],
                                   w=[tmx])
                            tt(P, x2a[mt][:, sl_t], x2a[mt][:, sl_t], tmx.ap, OP.add,
                               r=[x2a[mt], tmx], w=[x2a[mt]])
            for k in range(KT):
                P.store(x2T.t[b, k * 128:(k + 1) * 128, :], x2a[k].ap, x2T, r=[x2a[k]],
                        q=("gpsimd" if k % 2 == 0 else "sync"))
            P.barrier()
    P.barrier()


def emit_s2(P, ctx):
    x2T = ctx.dscratch("x2T", [B, D, T], F32)
    x3T = ctx.dscratch("x3T", [B, D, T], F32)
    Wq = ctx.din("L1_attn_Wq", [H1, D, DH1])
    Wk = ctx.din("L1_attn_Wk", [H1, D, DH1])
    Wv = ctx.din("L1_attn_Wv", [H1, D, DH1])
    Wo = ctx.din("L1_attn_Wo", [D, D])
    bo = ctx.din("L1_attn_bo", [D])
    eye = ctx.const("eye_f")
    eyeb = ctx.const("eye_b")
    ocb = ctx.const("ones_col_b")
    orow = ctx.const("ones_row_f")
    m_ui = ctx.const("mask_ui4")
    with scope(ctx):
        ps_s = [ctx.ps(f"s2_pss{i}", 128, PAN) for i in range(2)]
        ps_w2 = ctx.ps("s2_psw2", 128, 1024)
        ps_sum = ctx.ps("s2_pssum", 1, PAN)
        ps_t = ctx.ps("s2_pst", 128, 64, BF16)
        xs = [ctx.sb(f"s2_x{k}", 128, PAN) for k in range(KT)]
        hlb = [ctx.sb(f"s2_hl{k}", 128, T, BF16) for k in range(KT)]
        hpan = [TB(hlb[k].t, f"alias_hl{k}") for k in range(KT)]
        la = {"tmp": ctx.sb("s2_lntmp", 128, PAN), "row": ctx.sb("s2_lnrow", 1, 4 * PAN),
              "psum": ps_sum, "psq": ps_s[1], "pbc": ps_w2}
        qw = {}
        wstg = ctx.sb("s2_wstg", 128, 512)
        for nm, Wd in (("q", Wq), ("k", Wk), ("v", Wv)):
            for h in range(H1):
                t_ = ctx.sb(f"s2_w{nm}{h}", 128, 512, BF16)
                P.load(rr2(wstg, "p (a j) -> p a j", j=DH1),
                       Wd.t[h].rearrange("(a p) j -> p a j", p=128),
                       r=[Wd], w=[wstg], slow=True,
                       q=("sync" if h % 2 == 0 else "gpsimd"))
                vcopy(P, t_.ap, wstg.ap, r=[wstg], w=[t_],
                      eng=("v" if h % 2 == 0 else "s"))
                qw[(nm, h)] = t_
        woT = [ctx.sb(f"s2_woT{k}", 128, 1024, BF16) for k in range(KT)]
        wst2 = ctx.sb("s2_wst2", 128, 1024)
        for m in range(KT):
            P.load(wst2, Wo.t[m * 128:(m + 1) * 128, :], r=[Wo], w=[wst2])
            for k in range(KT):
                transpose(P, ps_w2[:, 0:128], wst2[:, k * 128:(k + 1) * 128], eye,
                          r=[wst2], w=[ps_w2])
                vcopy(P, woT[k][:, m * 128:(m + 1) * 128], ps_w2[:, 0:128],
                      r=[ps_w2], w=[woT[k]])
        bos = ctx.sb("s2_bos", 128, KT)
        P.load(bos, bo.t[:].rearrange("(a p) -> p a", p=128), r=[bo], slow=True)
        qT = ctx.sb("s2_qT", 64, T, BF16)
        kT = ctx.sb("s2_kT", 64, T, BF16)
        vTs = ctx.sb("s2_vT", 64, T, BF16)
        vn = [ctx.sb(f"s2_vn{tt_}", 128, 64, BF16) for tt_ in range(T // 128)]
        pex = ctx.sb("s2_pex", 128, PAN, BF16)
        srow = ctx.sb("s2_srow", 1, PAN)
        of32 = ctx.sb("s2_of32", 64, PAN)
        oall = [ctx.sb(f"s2_oa{k}", 128, T, BF16) for k in range(KT)]
        x3s = ctx.sb("s2_x3s", 128, PAN)
        for b in range(B):
            for p in range(NPAN):
                for k in range(KT):
                    P.load(xs[k], x2T.t[b, k * 128:(k + 1) * 128, p * PAN:(p + 1) * PAN],
                           r=[x2T], q=("sync" if k % 2 == 0 else "gpsimd"))
                dsts = [View(hlb[k].t[:, p * PAN:(p + 1) * PAN], hlb[k].b)
                        for k in range(KT)]
                emit_ln_panel(P, ctx, xs, dsts, la["tmp"], la["psum"], la["psq"],
                              la["row"], la["pbc"])
            for h in range(H1):
                for nm, dst in (("q", qT), ("k", kT), ("v", vTs)):
                    w_ = qw[(nm, h)]
                    for p in range(NPAN):
                        pj = ps_s[p % 2]
                        for k in range(KT):
                            mm(P, pj[0:64, :], w_[:, k * 64:(k + 1) * 64],
                               hlb[k][:, p * PAN:(p + 1) * PAN], r=[w_, hlb[k]], w=[pj],
                               start=(k == 0), stop=(k == KT - 1))
                        vcopy(P, dst[0:64, p * PAN:(p + 1) * PAN], pj[0:64, :],
                              r=[pj], w=[dst])
                for tt_ in range(T // 128):
                    transpose(P, ps_t[:, 0:64], vTs[0:64, tt_ * 128:(tt_ + 1) * 128],
                              eyeb, r=[vTs], w=[ps_t], kdim=64)
                    vcopy(P, vn[tt_].ap, ps_t[:, 0:64], r=[ps_t], w=[vn[tt_]])
                for ip in range(NPAN):
                    for jt in range(4 * ip + 4):
                        off = max(0, jt - 4 * ip)
                        cols = PAN - off * 128
                        pj = ps_s[jt % 2]
                        mm(P, pj[:, 0:cols], kT[0:64, jt * 128:(jt + 1) * 128],
                           qT[0:64, ip * PAN + off * 128:(ip + 1) * PAN],
                           r=[kT, qT], w=[pj])
                        act(P, pex[:, 0:cols], pj[:, 0:cols], AF.Exp, r=[pj], w=[pex],
                            scale=0.125)
                        if jt >= 4 * ip:
                            tt(P, pex[:, 0:128], pex[:, 0:128], m_ui[:, 0:128], OP.mult,
                               r=[pex, m_ui], w=[pex])
                        mm(P, ps_sum[0:1, off * 128:PAN], ocb[:, 0:1], pex[:, 0:cols],
                           r=[pex, ocb], w=[ps_sum], start=(jt == 0), stop=True)
                        mm(P, ps_w2[0:64, off * 128:PAN], vn[jt].ap, pex[:, 0:cols],
                           r=[vn[jt], pex], w=[ps_w2], start=(jt == 0), stop=True)
                    vcopy(P, srow.ap, ps_sum[0:1, :], r=[ps_sum], w=[srow])
                    recip(P, srow.ap, srow.ap, r=[srow], w=[srow])
                    mm(P, ps_w2[0:64, 512:1024], orow[0:1, 0:64], srow.ap,
                       r=[srow, orow], w=[ps_w2])
                    vcopy(P, of32.ap, ps_w2[0:64, 0:512], r=[ps_w2], w=[of32])
                    kk = h // 2; po = (h % 2) * 64
                    tt(P, oall[kk].t[po:po + 64, ip * PAN:(ip + 1) * PAN], of32.ap,
                       ps_w2[0:64, 512:1024], OP.mult, r=[of32, ps_w2], w=[oall[kk]])
            for p in range(NPAN):
                sl_t = slice(p * PAN, (p + 1) * PAN)
                for k in range(KT):
                    P.load(xs[k], x2T.t[b, k * 128:(k + 1) * 128, sl_t], r=[x2T],
                           q=("sync" if k % 2 == 0 else "gpsimd"))
                for mt in range(KT):
                    pj = ps_s[mt % 2]
                    for k in range(KT):
                        mm(P, pj.ap, woT[k][:, mt * 128:(mt + 1) * 128],
                           oall[k][:, sl_t], r=[woT[k], oall[k]], w=[pj],
                           start=(k == 0), stop=(k == KT - 1))
                    def fo(e, pj=pj, mt=mt):
                        return e.scalar_tensor_tensor(
                            x3s.ap, pj.ap, bos[:, mt:mt + 1], xs[mt].ap, OP.add, OP.add)
                    P.v(fo, r=[pj, bos, xs[mt]], w=[x3s])
                    P.store(x3T.t[b, mt * 128:(mt + 1) * 128, sl_t], x3s.ap, x3T, r=[x3s])
            P.barrier()
    P.barrier()


def emit_s3a(P, ctx):
    x3T = ctx.dscratch("x3T", [B, D, T], F32)
    hlnT3b = ctx.dscratch("hlnT3b", [B, D, T], BF16)
    gsc = ctx.dscratch("gsc", [B, E1, 1], F32)
    gw1 = ctx.din("L1_ffn_gate_w1", [D, D])
    gb1_ = ctx.din("L1_ffn_gate_b1", [D])
    gw2_ = ctx.din("L1_ffn_gate_w2", [E1, D])
    gb2_ = ctx.din("L1_ffn_gate_b2", [E1])
    eye = ctx.const("eye_f")
    with scope(ctx):
        xs = [ctx.sb(f"s3a_x{k}", 128, PAN) for k in range(KT)]
        hlb = [ctx.sb(f"s3a_h{k}", 128, PAN, BF16) for k in range(KT)]
        la = _ln_alloc(ctx, "s3a")
        wst = ctx.sb("s3a_wst", 128, 1024)
        gw1T = [ctx.sb(f"s3a_g1T{k}", 128, 1024, BF16) for k in range(KT)]
        pst = ctx.ps("s3a_pst", 128, 128)
        for o in range(KT):
            P.load(wst, gw1.t[o * 128:(o + 1) * 128, :], r=[gw1], w=[wst])
            for k in range(KT):
                transpose(P, pst.ap, wst[:, k * 128:(k + 1) * 128], eye, r=[wst], w=[pst])
                vcopy(P, gw1T[k][:, o * 128:(o + 1) * 128], pst.ap, r=[pst], w=[gw1T[k]])
        gw2s = ctx.sb("s3a_gw2s", E1, 1024)
        P.load(gw2s, gw2_.ap, r=[gw2_])
        gw2T = [ctx.sb(f"s3a_g2T{k}", 128, E1, BF16) for k in range(KT)]
        for k in range(KT):
            transpose(P, pst[:, 0:E1], gw2s[:, k * 128:(k + 1) * 128], eye,
                      r=[gw2s], w=[pst], kdim=E1)
            vcopy(P, gw2T[k].ap, pst[:, 0:E1], r=[pst], w=[gw2T[k]])
        gb1c = ctx.sb("s3a_gb1c", 128, KT)
        P.load(gb1c, gb1_.t[:].rearrange("(a p) -> p a", p=128), r=[gb1_], slow=True)
        gb2c = ctx.sb("s3a_gb2c", E1, 1)
        P.load(gb2c, gb2_.t[:].rearrange("(e o) -> e o", o=1), r=[gb2_], slow=True)
        h1b = [ctx.sb(f"s3a_h1{k}", 128, PAN, BF16) for k in range(KT)]
        gac = ctx.sb("s3a_gac", E1, 1)
        gtmp = ctx.sb("s3a_gtmp", E1, 1)
        ps_m = [ctx.ps(f"s3a_psm{i}", 128, PAN) for i in range(2)]
        ps_g8 = ctx.ps("s3a_psg8", E1, PAN)
        for b in range(B):
            memset(P, gac.ap, 0.0, w=[gac])
            for p in range(NPAN):
                sl_t = slice(p * PAN, (p + 1) * PAN)
                for k in range(KT):
                    P.load(xs[k], x3T.t[b, k * 128:(k + 1) * 128, sl_t], r=[x3T],
                           q=("sync" if k % 2 == 0 else "gpsimd"))
                emit_ln_to(P, ctx, la, xs, hlb)
                for k in range(KT):
                    P.store(hlnT3b.t[b, k * 128:(k + 1) * 128, sl_t], hlb[k].ap,
                            hlnT3b, r=[hlb[k]])
                for mt in range(KT):
                    pj = ps_m[mt % 2]
                    for k in range(KT):
                        mm(P, pj.ap, gw1T[k][:, mt * 128:(mt + 1) * 128], hlb[k].ap,
                           r=[gw1T[k], hlb[k]], w=[pj], start=(k == 0), stop=(k == KT - 1))
                    act(P, h1b[mt].ap, pj.ap, AF.Relu, r=[pj, gb1c], w=[h1b[mt]],
                        bias=gb1c[:, mt:mt + 1])
                for k in range(KT):
                    mm(P, ps_g8.ap, gw2T[k].ap, h1b[k].ap, r=[gw2T[k], h1b[k]], w=[ps_g8],
                       start=(k == 0), stop=(k == KT - 1))
                reduce(P, gtmp.ap, ps_g8.ap, OP.add, r=[ps_g8], w=[gtmp])
                tt(P, gac.ap, gac.ap, gtmp.ap, OP.add, r=[gac, gtmp], w=[gac])
            ts(P, gac.ap, gac.ap, 1.0 / T, OP.mult, r=[gac], w=[gac])
            tt(P, gac.ap, gac.ap, gb2c.ap, OP.add, r=[gac, gb2c], w=[gac])
            P.store(gsc.t[b], gac.ap, gsc, r=[gac])
    P.barrier()


def emit_s3b(P, ctx):
    gsc = ctx.dscratch("gsc", [B, E1, 1], F32)
    mw1b = ctx.dscratch("mw1b", [B, KT, 128, F1], BF16)
    mw2b = ctx.dscratch("mw2b", [B, 32, 128, D], BF16)
    mb1d = ctx.dscratch("mb1d", [B, 128, 32], F32)
    mb2d = ctx.dscratch("mb2d", [B, 128, E1], F32)
    W1 = ctx.din("L1_ffn_W1", [F1, D])
    W2 = ctx.din("L1_ffn_W2", [D, F1])
    TW1 = ctx.din("L1_ffn_TW1", [E1, F1, D])
    TW2 = ctx.din("L1_ffn_TW2", [E1, D, F1])
    b1_ = ctx.din("L1_ffn_b1", [F1])
    b2_ = ctx.din("L1_ffn_b2", [D])
    Tb1 = ctx.din("L1_ffn_Tb1", [E1, F1])
    Tb2 = ctx.din("L1_ffn_Tb2", [E1, D])
    eye = ctx.const("eye_f")
    sr8 = ctx.const("sel_row8")
    with scope(ctx):
        g8 = ctx.sb("s3b_g8", E1, 1)
        gcol = ctx.sb("s3b_gcol", 128, E1)
        dgs = [ctx.sb(f"s3b_dg{e}", 128, 128) for e in range(E1)]
        stg = [ctx.sb(f"s3b_stg{i}", 128, F1) for i in range(2)]
        gat = ctx.sb("s3b_gat", 128, F1, BF16)
        ps32 = ctx.ps("s3b_ps32", 128, 4096)
        tb1s = ctx.sb("s3b_tb1s", E1, F1)
        tb2s = ctx.sb("s3b_tb2s", E1, D)
        bvec = ctx.sb("s3b_bvec", 128, 32)
        mb = ctx.sb("s3b_mb", 128, 32)
        P.load(tb1s, Tb1.ap, r=[Tb1])
        P.load(tb2s, Tb2.ap, r=[Tb2])
        for b in range(B):
            P.load(g8, gsc.t[b], r=[gsc], w=[g8])
            for e in range(E1):
                mm(P, ps32[:, e:e + 1], sr8[:, e * 128:(e + 1) * 128], g8.ap,
                   r=[g8, sr8], w=[ps32])
            vcopy(P, gcol.ap, ps32[:, 0:E1], r=[ps32], w=[gcol])
            for e in range(E1):
                ts(P, dgs[e].ap, eye.ap, gcol[:, e:e + 1], OP.mult, r=[eye, gcol],
                   w=[dgs[e]])
            for ft in range(32):
                s_ = stg[0]
                P.load(View(s_.t[:, 0:D], s_.b), W1.t[ft * 128:(ft + 1) * 128, :],
                       r=[W1], w=[s_])
                for hc in range(KT):
                    mm(P, ps32[:, hc * 128:(hc + 1) * 128],
                       s_[:, hc * 128:(hc + 1) * 128], eye.ap, r=[s_, eye], w=[ps32],
                       start=True, stop=False)
                for e in range(E1):
                    s_ = stg[(e + 1) % 2]
                    P.load(View(s_.t[:, 0:D], s_.b),
                           TW1.t[e, ft * 128:(ft + 1) * 128, :], r=[TW1], w=[s_],
                           q=("sync" if e % 2 == 0 else "gpsimd"))
                    for hc in range(KT):
                        mm(P, ps32[:, hc * 128:(hc + 1) * 128],
                           s_[:, hc * 128:(hc + 1) * 128], dgs[e].ap, r=[s_, dgs[e]],
                           w=[ps32], start=False, stop=(e == E1 - 1))
                vcopy(P, gat[:, 0:1024], ps32[:, 0:1024], r=[ps32], w=[gat])
                for hc in range(KT):
                    P.store(mw1b.t[b, hc, :, ft * 128:(ft + 1) * 128],
                            gat.t[:, hc * 128:(hc + 1) * 128], mw1b, r=[gat],
                            q=("gpsimd" if hc % 2 == 0 else "sync"))
            for ht in range(KT):
                s_ = stg[0]
                P.load(s_, W2.t[ht * 128:(ht + 1) * 128, :], r=[W2], w=[s_])
                for fc in range(32):
                    mm(P, ps32[:, fc * 128:(fc + 1) * 128],
                       s_[:, fc * 128:(fc + 1) * 128], eye.ap, r=[s_, eye], w=[ps32],
                       start=True, stop=False)
                for e in range(E1):
                    s_ = stg[(e + 1) % 2]
                    P.load(s_, TW2.t[e, ht * 128:(ht + 1) * 128, :], r=[TW2], w=[s_],
                           q=("sync" if e % 2 == 0 else "gpsimd"))
                    for fc in range(32):
                        mm(P, ps32[:, fc * 128:(fc + 1) * 128],
                           s_[:, fc * 128:(fc + 1) * 128], dgs[e].ap, r=[s_, dgs[e]],
                           w=[ps32], start=False, stop=(e == E1 - 1))
                vcopy(P, gat.ap, ps32.ap, r=[ps32], w=[gat])
                for fc in range(32):
                    P.store(mw2b.t[b, fc, :, ht * 128:(ht + 1) * 128],
                            gat.t[:, fc * 128:(fc + 1) * 128], mw2b, r=[gat],
                            q=("gpsimd" if fc % 2 == 0 else "sync"))
            P.load(bvec, b1_.t[:].rearrange("(a p) -> p a", p=128), r=[b1_], w=[bvec],
                   slow=True)
            for ft in range(32):
                mm(P, ps32[:, ft:ft + 1], tb1s[:, ft * 128:(ft + 1) * 128], g8.ap,
                   r=[tb1s, g8], w=[ps32], start=True, stop=True)
            tt(P, mb.ap, ps32[:, 0:32], bvec.ap, OP.add, r=[ps32, bvec], w=[mb])
            P.store(mb1d.t[b], mb.ap, mb1d, r=[mb])
            P.load(View(bvec.t[:, 0:E1], bvec.b),
                   b2_.t[:].rearrange("(a p) -> p a", p=128), r=[b2_], w=[bvec],
                   slow=True)
            for ht in range(KT):
                mm(P, ps32[:, ht:ht + 1], tb2s[:, ht * 128:(ht + 1) * 128], g8.ap,
                   r=[tb2s, g8], w=[ps32], start=True, stop=True)
            tt(P, mb[:, 0:E1], ps32[:, 0:E1], bvec[:, 0:E1], OP.add, r=[ps32, bvec],
               w=[mb])
            P.store(mb2d.t[b], mb.t[:, 0:E1], mb2d, r=[mb])
            P.barrier()
    P.barrier()


def emit_s3c(P, ctx):
    x3T = ctx.dscratch("x3T", [B, D, T], F32)
    hlnT3b = ctx.dscratch("hlnT3b", [B, D, T], BF16)
    mw1b = ctx.dscratch("mw1b", [B, KT, 128, F1], BF16)
    mw2b = ctx.dscratch("mw2b", [B, 32, 128, D], BF16)
    mb1d = ctx.dscratch("mb1d", [B, 128, 32], F32)
    mb2d = ctx.dscratch("mb2d", [B, 128, E1], F32)
    x4T = ctx.dscratch("x4T", [B, D, T], F32)
    with scope(ctx):
        ya = [ctx.sb(f"s3c_ya{k}", 128, T) for k in range(KT)]
        mw1s = [ctx.sb(f"s3c_w1{k}", 128, PAN, BF16) for k in range(KT)]
        mw2s = [ctx.sb(f"s3c_w2{j}", 128, D, BF16) for j in range(4)]
        hlb = [ctx.sb(f"s3c_hl{k}", 128, PAN, BF16) for k in range(KT)]
        ubf = [ctx.sb(f"s3c_u{j}", 128, PAN, BF16) for j in range(4)]
        mb1s = ctx.sb("s3c_mb1", 128, 32)
        mb2s = ctx.sb("s3c_mb2", 128, E1)
        xps = [ctx.sb(f"s3c_xp{k}", 128, PAN) for k in range(KT)]
        x4s = ctx.sb("s3c_x4s", 128, PAN)
        ps_u = [ctx.ps(f"s3c_psu{i}", 128, PAN) for i in range(2)]
        ps_y = [ctx.ps(f"s3c_psy{i}", 128, PAN) for i in range(2)]
        for b in range(B):
            P.load(mb1s, mb1d.t[b], r=[mb1d], w=[mb1s])
            P.load(mb2s, mb2d.t[b], r=[mb2d], w=[mb2s])
            for fc in range(KT):
                for k in range(KT):
                    P.load(mw1s[k], mw1b.t[b, k, :, fc * PAN:(fc + 1) * PAN],
                           r=[mw1b], w=[mw1s[k]], q=("sync" if k % 2 == 0 else "gpsimd"))
                for j in range(4):
                    P.load(mw2s[j], mw2b.t[b, fc * 4 + j, :, :], r=[mw2b], w=[mw2s[j]],
                           q=("sync" if j % 2 == 0 else "gpsimd"))
                for p in range(NPAN):
                    sl_t = slice(p * PAN, (p + 1) * PAN)
                    for k in range(KT):
                        P.load(hlb[k], hlnT3b.t[b, k * 128:(k + 1) * 128, sl_t],
                               r=[hlnT3b], w=[hlb[k]],
                               q=("sync" if k % 2 == 0 else "gpsimd"))
                    for fmt in range(4):
                        pj = ps_u[fmt % 2]
                        for k in range(KT):
                            mm(P, pj.ap, mw1s[k][:, fmt * 128:(fmt + 1) * 128], hlb[k].ap,
                               r=[mw1s[k], hlb[k]], w=[pj], start=(k == 0),
                               stop=(k == KT - 1))
                        ftg = fc * 4 + fmt
                        act(P, ubf[fmt].ap, pj.ap, AF.Gelu_apprx_tanh, r=[pj, mb1s],
                            w=[ubf[fmt]], bias=mb1s[:, ftg:ftg + 1])
                    for mt in range(KT):
                        pj = ps_y[mt % 2]
                        for j in range(4):
                            mm(P, pj.ap, mw2s[j][:, mt * 128:(mt + 1) * 128], ubf[j].ap,
                               r=[mw2s[j], ubf[j]], w=[pj], start=(j == 0), stop=(j == 3))
                        if fc == 0:
                            vcopy(P, ya[mt][:, sl_t], pj.ap, r=[pj], w=[ya[mt]])
                        else:
                            tt(P, ya[mt][:, sl_t], ya[mt][:, sl_t], pj.ap, OP.add,
                               r=[ya[mt], pj], w=[ya[mt]])
            for p in range(NPAN):
                sl_t = slice(p * PAN, (p + 1) * PAN)
                for k in range(KT):
                    P.load(xps[k], x3T.t[b, k * 128:(k + 1) * 128, sl_t], r=[x3T],
                           w=[xps[k]], q=("sync" if k % 2 == 0 else "gpsimd"))
                for mt in range(KT):
                    def fx(e, mt=mt, sl_t=sl_t):
                        return e.scalar_tensor_tensor(
                            x4s.ap, ya[mt][:, sl_t], mb2s[:, mt:mt + 1], xps[mt].ap,
                            OP.add, OP.add)
                    P.v(fx, r=[ya[mt], mb2s, xps[mt]], w=[x4s])
                    P.store(x4T.t[b, mt * 128:(mt + 1) * 128, sl_t], x4s.ap, x4T, r=[x4s])
            P.barrier()
    P.barrier()


STAGES = None

def _default_launch_plan():
    return [("s_in", emit_s_in),
            ("s0a", emit_s0a),
            ("s0a2", emit_s0a2),
            ("s0b", emit_s0b),
            ("s0c1", emit_s0c1),
            ("s0c2", emit_s0c2),
            ("s1g", emit_s1g),
            ("s1w", emit_s1w),
            ("s1e", emit_s1e),
            ("s2", emit_s2),
            ("s3a", emit_s3a),
            ("s3b", emit_s3b),
            ("s3c", emit_s3c),
            ("s_out", emit_s_out)]

def _run_launch(name, emitters, carried, mode):
    nc = bass.Bass("TRN2", target_bir_lowering=False)
    with ExitStack() as stack:
        P = Prog(nc)
        ctx = Ctx(nc, stack, P, mode, carried)
        if mode == "multi":
            ctx._pre = carried
            def dscratch(nm, shape, dtype=F32, _c=ctx):
                if nm not in _c.dram:
                    if nm in carried:
                        t = _c.nc.dram_tensor(nm, list(shape), dtype, kind="ExternalInput")
                        _c.dram[nm] = TB(t.ap(), nm)
                    else:
                        t = _c.nc.dram_tensor(nm, list(shape), dtype, kind="ExternalOutput")
                        _c.dram[nm] = TB(t.ap(), nm)
                        _c.outputs.add(nm)
                return _c.dram[nm]
            ctx.dscratch = dscratch
        for nm in CONST_SPECS:
            ctx.const(nm)
        fv = ctx.const("fvals")
        P.fconst = fv
        for i, val in enumerate([1e-5, 1e-6, -1.0, 0.125, 1.0 / 256.0]):
            nc.const_aps.register(F32, val, fv.t[:, i:i + 1])
        for em in emitters:
            em(P, ctx)
        P.build(stack)
    in_map = {}
    for nm, tb in ctx.dram.items():
        if nm in carried:
            in_map[nm] = carried[nm]
        elif nm.startswith("c_"):
            in_map[nm] = np.ascontiguousarray(CONST_SPECS[nm[2:]])
    in_maps = [dict(in_map) if i == 0 else {k: v for k, v in in_map.items()}
               for i in range(NCORES)]
    for i in range(NCORES):
        for k in list(in_maps[i].keys()):
            v = in_maps[i][k]
            if isinstance(v, list):
                in_maps[i][k] = v[i]
    res = run_bass_kernel_spmd(nc, in_maps, core_ids=list(range(NCORES)))
    outs = {}
    for nm in ctx.outputs:
        outs[nm] = [res.results[i][nm] for i in range(NCORES)]
    return outs


def run_pipeline(inputs, plan):
    x = np.ascontiguousarray(inputs["L0_attn_hs"])
    carried = {"x": [np.ascontiguousarray(x[2 * i:2 * i + 2]) for i in range(NCORES)]}
    for k, v in inputs.items():
        carried[k] = np.ascontiguousarray(v)
    for name, em in plan:
        ems = em if isinstance(em, list) else [em]
        outs = _run_launch(name, ems, carried, mode="multi")
        for nm, percore in outs.items():
            carried[nm] = percore
    return carried


FUSED = True

def kernel(**inputs):
    plan = STAGES if STAGES is not None else _default_launch_plan()
    if FUSED and STAGES is None:
        x = np.ascontiguousarray(inputs["L0_attn_hs"])
        carried = {"x": [np.ascontiguousarray(x[2 * i:2 * i + 2])
                         for i in range(NCORES)]}
        for k, v in inputs.items():
            carried[k] = np.ascontiguousarray(v)
        outs = _run_launch("mega", [em for _, em in plan], carried, mode="fused")
        return np.concatenate(outs["out"], axis=0)
    carried = run_pipeline(inputs, plan)
    return np.concatenate(carried["out"], axis=0)
